# version 03
# baseline (speedup 1.0000x reference)
import math
from contextlib import ExitStack

import numpy as np
import ml_dtypes

import concourse.bass as bass
import concourse.mybir as mybir
from concourse import library_config
from concourse.library_overlay import lower_extended_insts
from concourse.aot_env import allow_ici_all_to_all
from concourse.bass_utils import run_bass_kernel_spmd

F32 = mybir.dt.float32
BF16 = mybir.dt.bfloat16
I16 = mybir.dt.int16
AF = mybir.ActivationFunctionType
ALU = mybir.AluOpType
AX = mybir.AxisListType
BF = ml_dtypes.bfloat16

D = 2048
H_PER = 8
HD = 128
HC = H_PER * HD
M = 4
SCALE = HD ** (-0.5)
EPS = 1e-6
LN_EPS = 1e-5
E = 8
I_FF = 1408
J_FF = 5632
NEG_BIG = -1.0e30


class Cfg:
    def __init__(self, seq=4096):
        self.S = seq
        self.T = seq // 2
        self.NCH = self.S // 128
        self.NSC = self.S // 512
        self.OCH = self.T // 128
        cap = int(math.ceil(self.T * 2 * 1.25 / E))
        self.CAP = ((cap + 127) // 128) * 128
        self.CC = self.CAP // 128
        self.CREF = int(math.ceil(8 * self.T * 2 / E * 1.25))
        self.NSLOT = self.T * 2
        self.TROW = 64
        self.NTBL = E * self.CAP + self.NSLOT
        self.SHG = min(512, self.T)
        self.NSHG = self.T // self.SHG


def _wrap16(vals, rows=128):
    n = len(vals)
    cols = (n + 15) // 16
    out = np.zeros((rows, cols), np.int16)
    for i, v in enumerate(vals):
        out[i % 16, i // 16] = v
    out[16:32] = out[0:16]
    return out


def make_consts(cfg):
    c = {}
    c["ident"] = np.eye(128, dtype=BF)
    c["identf"] = np.eye(128, dtype=np.float32)
    c["triu"] = np.triu(np.ones((128, 128), np.float32), 1).astype(BF)
    mt = np.triu(np.ones((128, 128), np.float32), 0)
    c["maskT"] = np.tile(mt, (1, H_PER)).astype(BF)
    c["onesb"] = np.ones((128, 1), dtype=BF)
    c["onesrow"] = np.ones((1, 128), np.float32)
    pa = np.zeros((128, 8, 16), np.float32)
    for a in range(8):
        for b in range(16):
            pa[16 * a + b, a, b] = 1.0
    c["pa"] = pa
    c["iotae"] = np.tile(np.arange(8, dtype=np.float32)[None, :], (128, 1))
    c["iotat"] = (np.arange(cfg.OCH, dtype=np.float32)[None, :] * 128
                  + np.arange(128, dtype=np.float32)[:, None])
    c["iotas"] = (np.arange(2 * cfg.OCH, dtype=np.float32)[None, :] * 128
                  + np.arange(128, dtype=np.float32)[:, None])
    c["gidxsh"] = _wrap16(np.arange(cfg.T, dtype=np.int16))
    sel = np.zeros((M * 16, M), np.float32)
    for m in range(M):
        sel[m * 16:(m + 1) * 16, m] = 1.0
    c["esel"] = sel
    return c


def const_specs(c):
    return [
        ("ident", [128, 128], BF16), ("identf", [128, 128], F32),
        ("triu", [128, 128], BF16),
        ("maskT", [128, H_PER * 128], BF16), ("onesb", [128, 1], BF16),
        ("onesrow", [1, 128], F32), ("pa", [128, 8, 16], F32),
        ("iotae", [128, 8], F32), ("iotat", [128, c.OCH], F32),
        ("iotas", [128, 2 * c.OCH], F32), ("gidxsh", [128, c.T // 16], I16),
        ("esel", [M * 16, M], F32),
    ]


class EM:

    def __init__(self, nc, block, stack, names):
        self.nc = nc
        self.block = block
        self.h = {}
        self.v = {}
        for n in names:
            self.h[n] = stack.enter_context(nc.semaphore("s_" + n))
            self.v[n] = 0

    def inc(self, n, by=1):
        self.v[n] += by
        return self.v[n]

    def W(self, eng, n, val):
        assert val <= self.v[n], f"wait on future {n}: {val} > {self.v[n]}"
        if val > 0:
            eng.wait_ge(self.h[n], val)

    def on(self, engine):
        def deco(fn):
            getattr(self.block, engine)(fn)
        return deco

    def qw(self, eng, n):
        if self.v[n] > 0:
            eng.wait_ge(self.h[n], self.v[n])


SEMS = ["ld", "wl", "scr", "tp", "vp", "sp", "gq", "cc", "ost", "bar",
        "gl", "gs", "wla", "wlb", "qa", "qb", "sca", "scb", "mk",
        "wsa", "wsb"]


def full_barrier(env):
    em = env["em"]
    tgt = em.v["bar"] + 5
    dmas = {k: em.v[k] for k in ["scr", "ld", "wl", "ost", "gq", "cc", "gl",
                                 "gs", "wla", "wlb", "qa", "qb", "sca",
                                 "scb", "wsa", "wsb"]}
    prods = {k: em.v[k] for k in ["tp", "vp", "sp"]}

    @em.on("sync")
    def _(sy):
        for k, v in dmas.items():
            em.W(sy, k, v)
        for k, v in prods.items():
            em.W(sy, k, v)
        sy.sem_inc(em.h["bar"], 1)
        sy.wait_ge(em.h["bar"], tgt)

    for name in ["gpsimd", "tensor", "vector", "scalar"]:
        @em.on(name)
        def _(eng):
            eng.sem_inc(em.h["bar"], 1)
            eng.wait_ge(em.h["bar"], tgt)
    em.v["bar"] = tgt


def build_program(cfg, dbg=(), upto=99):
    nc = bass.Bass("TRN2", target_bir_lowering=False)
    S, T, OCH, NSC, CAP, CC = cfg.S, cfg.T, cfg.OCH, cfg.NSC, cfg.CAP, cfg.CC

    ein = lambda n, s, d: nc.dram_tensor(n, s, d, kind="ExternalInput").ap()
    env = {"cfg": cfg, "nc": nc}
    env["xb"] = ein("xb", [S, D], F32)
    env["xown"] = ein("xown", [T, D], F32)
    env["wqT"] = ein("wqT", [D, HC], F32)
    env["wkT"] = ein("wkT", [D, HC], F32)
    env["wvT"] = ein("wvT", [D, HC], F32)
    env["woT"] = ein("woT", [D, D], F32)
    env["memi"] = ein("memi", [M * H_PER, HD * HD], F32)
    env["nrmT"] = ein("nrmT", [H_PER, HD, M], F32)
    env["nrmF"] = ein("nrmF", [M * 16, HD], F32)
    env["gatei"] = ein("gatei", [1, H_PER], F32)
    env["routeri"] = ein("routeri", [D, E], F32)
    env["wug"] = ein("wug", [E, D, 2 * I_FF], F32)
    env["wdn"] = ein("wdn", [E, I_FF, D], F32)
    env["wsu"] = ein("wsu", [D, 2 * J_FF], F32)
    env["wsd"] = ein("wsd", [J_FF, D], F32)
    env["prior_sel"] = ein("prior_sel", [8, 1], F32)
    env["gidxwo"] = ein("gidxwo", [128, cfg.OCH * 16], I16)
    cst_sp = const_specs(cfg)
    env["cst"] = {n: ein("cin_" + n, s, d) for n, s, d in cst_sp}
    env["out"] = nc.dram_tensor("out", [T, D], F32, kind="ExternalOutput").ap()
    env["dbg"] = {}
    for name, shape, dt in dbg:
        env["dbg"][name] = nc.dram_tensor("dbg_" + name, shape, dt,
                                          kind="ExternalOutput").ap()

    env["XBH"] = nc.dram_tensor("XBH", [S, D], BF16).ap()
    env["SQT"] = nc.dram_tensor("SQT", [H_PER, HD, S], BF16).ap()
    env["PART"] = nc.dram_tensor("PART", [S, HC], BF16).ap()
    env["XP"] = nc.dram_tensor("XP", [T, D], F32).ap()
    env["XPH"] = nc.dram_tensor("XPH", [T, D], BF16).ap()
    env["TBL"] = nc.dram_tensor("TBL", [cfg.NTBL, cfg.TROW], F32).ap()
    env["RACC"] = nc.dram_tensor("RACC", [T, D], F32).ap()
    env["WSUB"] = nc.dram_tensor("WSUB", [D, 2 * J_FF], BF16).ap()
    env["WSDB"] = nc.dram_tensor("WSDB", [J_FF, D], BF16).ap()
    env["WUGB"] = nc.dram_tensor("WUGB", [E, D, 2 * I_FF], BF16).ap()
    env["WDNB"] = nc.dram_tensor("WDNB", [E, I_FF, D], BF16).ap()
    env["agin"] = nc.dram_tensor("agin", [S, HC], BF16)
    env["agout"] = nc.dram_tensor("agout", [2 * S, HC], BF16)
    env["cnt_in"] = nc.dram_tensor("cnt_in", [1, E], F32)
    env["cnt_out"] = nc.dram_tensor("cnt_out", [8, E], F32)

    with ExitStack() as stack:
        with nc.Block() as block:
            em = EM(nc, block, stack, SEMS)
            env["em"] = em
            sb = {}
            env["sb"] = sb

            def sbuf(name, shape, dt, st=stack):
                t = st.enter_context(nc.sbuf_tensor(name, shape, dt))
                sb[name] = t
                return t
            env["sbuf"] = sbuf

            for n, s, d in cst_sp:
                sbuf("c_" + n, s, d)
            for n, s, d in [
                ("nrmTf", [128, H_PER, M], F32), ("nrmTb", [128, H_PER, M], BF16),
                ("nrmFs", [M * 16, HD], F32), ("grow", [1, H_PER], F32),
                ("g1m", [128, H_PER], F32), ("weff", [128, H_PER * M], F32),
                ("relacc", [128, H_PER], F32), ("relaccb", [128, H_PER], BF16),
                ("priorS", [8, 1], F32), ("priorrow", [1, E], F32),
                ("priorrep", [128, E], F32), ("routerS", [128, 16, E], F32),
                ("zero", [128, 512], BF16),
                ("S", [128, HC], F32), ("Sb", [128, HC], BF16),
                ("Z", [128, H_PER], F32), ("Zb", [128, H_PER], BF16),
                ("rstash", [128, OCH, 8], F32),
                ("ctot", [OCH, E], F32), ("ctotb", [OCH, E], BF16),
                ("ctotrow", [1, OCH * E], F32),
                ("cprerow", [1, 128], F32), ("cnt", [8, E], F32),
                ("tmp", [128, 64], F32), ("tmpb", [128, 64], BF16),
                ("rel", [16, 40], F32),
                ("pay", [128, 2 * OCH, 2], F32), ("gdst", [128, 2 * OCH], F32),
                ("tidx", [128, (2 * OCH * 128) // 16], I16),
                ("xstage", [128, 2048], BF16), ("xstage2", [128, 2048], BF16),
            ]:
                sbuf(n, s, d)

            ps = {}
            env["ps"] = ps
            for nm in ["q0", "q1", "q2", "q3", "d0", "d1", "s"]:
                ps[nm] = stack.enter_context(
                    nc.psum_tensor("ps" + nm, [128, 512], F32))
            env["regs"] = {}
            for rv in sorted({256, 128, cfg.CAP, cfg.NSLOT, cfg.SHG}):
                env["regs"][rv] = stack.enter_context(
                    nc.gpsimd.register(f"rn{rv}"))

            phase_setup(env)
            if upto >= 1:
                with ExitStack() as stB:
                    ps["t"] = stB.enter_context(
                        nc.psum_tensor("pst", [128, 1024], BF16))
                    alloc_scopeB(env, stB)
                    with ExitStack() as stB1:
                        alloc_scopeB1(env, stB1)
                        phase_p0b(env)
                        phase_setup_bulk(env)
                        phase_attn_pass1(env)
                        if upto >= 2:
                            phase_rel(env)
                    if upto >= 3:
                        with ExitStack() as stB2:
                            alloc_scopeB2(env, stB2)
                            phase_attn_pass2(env)
                    del ps["t"]
            if upto >= 5:
                full_barrier(env)
                with ExitStack() as stC:
                    env["_stC"] = stC
                    alloc_scopeC(env, stC)
                    phase_wo_ln1_routing(env)
            if upto >= 6:
                full_barrier(env)
                with ExitStack() as stD:
                    ps["e"] = stD.enter_context(
                        nc.psum_tensor("pse", [128, 512], F32))
                    alloc_scopeD(env, stD)
                    phase_routing_finalize(env)
                    if upto >= 7:
                        with ExitStack() as stD1:
                            alloc_scopeD1(env, stD1)
                            phase_experts(env)
                    if upto >= 8:
                        with ExitStack() as stD2:
                            alloc_scopeD2(env, stD2)
                            phase_shared_epilogue(env)
            phase_finish(env, upto)

    lower_extended_insts(nc)
    return nc


def phase_setup(env):
    em, sb, cfg, cst = env["em"], env["sb"], env["cfg"], env["cst"]
    S, T = cfg.S, cfg.T

    @em.on("sync")
    def _(sy):
        for n, _s, _d in const_specs(cfg):
            sy.dma_start(out=sb["c_" + n][...], in_=cst[n]).then_inc(
                em.h["ld"], 16)
            em.inc("ld", 16)
        sy.dma_start(out=sb["nrmTf"][...],
                     in_=env["nrmT"].rearrange("h d m -> d h m")).then_inc(
            em.h["ld"], 16)
        em.inc("ld", 16)
        sy.dma_start(out=sb["nrmFs"][...], in_=env["nrmF"]).then_inc(
            em.h["ld"], 16)
        em.inc("ld", 16)
        sy.dma_start(out=sb["priorS"][...], in_=env["prior_sel"]).then_inc(
            em.h["ld"], 16)
        em.inc("ld", 16)
        sy.dma_start(out=sb["routerS"][...],
                     in_=env["routeri"].rearrange("(k p) e -> p k e", p=128)
                     ).then_inc(em.h["ld"], 16)
        em.inc("ld", 16)
        sy.dma_start(out=sb["tmp"][0:1, 0:H_PER], in_=env["gatei"]).then_inc(
            em.h["ld"], 16)
        em.inc("ld", 16)
        em.qw(sy, "ld")
    env["cld"] = em.v["ld"]

    @em.on("vector")
    def _(v):
        for nm in ["zero", "S", "Z", "Sb", "Zb", "relacc"]:
            v.memset(sb[nm][...], 0.0)
        v.memset(sb["tidx"][...], 0)
        v.memset(sb["rstash"][...], 0.0)
        v.nop().then_inc(em.h["vp"], 1)
        em.inc("vp", 1)
        em.W(v, "ld", env["cld"])
        v.tensor_copy(sb["nrmTb"][...], sb["nrmTf"][...]).then_inc(
            em.h["vp"], 1)
        em.inc("vp", 1)
    env["zeroed"] = em.v["vp"]

    @em.on("gpsimd")
    def _(gp):
        gp.load_library(library_config.mlp)
        for rv, rh in env["regs"].items():
            gp.reg_mov(rh, rv)

    @em.on("scalar")
    def _(s):
        em.W(s, "ld", env["cld"])
        s.activation(sb["grow"][...], sb["tmp"][0:1, 0:H_PER],
                     AF.Sigmoid).then_inc(em.h["sp"], 1)
        em.inc("sp", 1)
    gr = em.v["sp"]

    @em.on("tensor")
    def _(t):
        em.W(t, "sp", gr)
        t.matmul(env["ps"]["s"][:, 0:H_PER], sb["c_onesrow"][...],
                 sb["grow"][...], start=True, stop=True).then_inc(
            em.h["tp"], 1)
        em.inc("tp", 1)
    grep = em.v["tp"]

    @em.on("vector")
    def _(v):
        em.W(v, "tp", grep)
        v.tensor_scalar(sb["g1m"][...], env["ps"]["s"][:, 0:H_PER], -1.0, 1.0,
                        ALU.mult, ALU.add).then_inc(em.h["vp"], 1)
        em.inc("vp", 1)
    env["g1m_ready"] = em.v["vp"]


def phase_setup_bulk(env):
    em, sb, cfg = env["em"], env["sb"], env["cfg"]
    S, T = cfg.S, cfg.T
    ntblel = cfg.NTBL * cfg.TROW
    env["xbh"] = {}

    @em.on("gpsimd")
    def _(gp):
        em.W(gp, "vp", env["zeroed"])
        for i in range(S // 128):
            stg = sb["xstage"] if i % 2 == 0 else sb["xstage2"]
            if i >= 1:
                em.W(gp, "gs", env["xbh"][i - 1])
            gp.dma_start(out=stg[...],
                         in_=env["xb"][i * 128:(i + 1) * 128, :]).then_inc(
                em.h["gl"], 16)
            t = em.inc("gl", 16)
            gp.wait_ge(em.h["gl"], t)
            gp.dma_start(out=env["XBH"][i * 128:(i + 1) * 128, :],
                         in_=stg[...]).then_inc(em.h["gs"], 16)
            env["xbh"][i] = em.inc("gs", 16)

    @em.on("gpsimd")
    def _(gp):
        for i in range(T // 128):
            for n4 in range(4):
                em.W(gp, "gs", em.v["gs"])
                gp.dma_start(out=env["RACC"][i * 128:(i + 1) * 128,
                                             n4 * 512:(n4 + 1) * 512],
                             in_=sb["zero"][...]).then_inc(em.h["gs"], 16)
                em.inc("gs", 16)
        flat = env["TBL"].rearrange("r c -> (r c)")
        off = 0
        while off < ntblel:
            n = min(128 * 512, ntblel - off)
            fr = n // 128
            em.W(gp, "gs", em.v["gs"])
            gp.dma_start(out=flat[off:off + 128 * fr].rearrange(
                "(p f) -> p f", p=128),
                in_=sb["zero"][:, 0:fr]).then_inc(em.h["gs"], 16)
            em.inc("gs", 16)
            off += 128 * fr
    env["zeros_done"] = em.v["gs"]

    @em.on("gpsimd")
    def _(gp):
        for rb in range(16):
            for cb in range(6):
                w = min(2048, 2 * J_FF - cb * 2048)
                stg = sb["xstage"] if (rb * 6 + cb) % 2 == 0 else sb["xstage2"]
                em.W(gp, "gs", em.v["gs"])
                gp.dma_start(out=stg[:, 0:w],
                             in_=env["wsu"][rb * 128:(rb + 1) * 128,
                                            cb * 2048:cb * 2048 + w]
                             ).then_inc(em.h["gl"], 16)
                t = em.inc("gl", 16)
                gp.wait_ge(em.h["gl"], t)
                gp.dma_start(out=env["WSUB"][rb * 128:(rb + 1) * 128,
                                             cb * 2048:cb * 2048 + w],
                             in_=stg[:, 0:w]).then_inc(em.h["gs"], 16)
                em.inc("gs", 16)
        for rb in range(44):
            stg = sb["xstage"] if rb % 2 == 0 else sb["xstage2"]
            em.W(gp, "gs", em.v["gs"])
            gp.dma_start(out=stg[...],
                         in_=env["wsd"][rb * 128:(rb + 1) * 128, :]
                         ).then_inc(em.h["gl"], 16)
            t = em.inc("gl", 16)
            gp.wait_ge(em.h["gl"], t)
            gp.dma_start(out=env["WSDB"][rb * 128:(rb + 1) * 128, :],
                         in_=stg[...]).then_inc(em.h["gs"], 16)
            em.inc("gs", 16)
    env["wcast_done"] = em.v["gs"]

    @em.on("gpsimd")
    def _(gp):
        par = 0
        for e in range(E):
            for rb in range(16):
                for cb in range(2):
                    w = min(2048, 2 * I_FF - cb * 2048)
                    stg = sb["xstage"] if par % 2 == 0 else sb["xstage2"]
                    par += 1
                    em.W(gp, "gs", em.v["gs"])
                    gp.dma_start(out=stg[:, 0:w],
                                 in_=env["wug"][e, rb * 128:(rb + 1) * 128,
                                                cb * 2048:cb * 2048 + w]
                                 ).then_inc(em.h["gl"], 16)
                    t = em.inc("gl", 16)
                    gp.wait_ge(em.h["gl"], t)
                    gp.dma_start(out=env["WUGB"][e, rb * 128:(rb + 1) * 128,
                                                 cb * 2048:cb * 2048 + w],
                                 in_=stg[:, 0:w]).then_inc(em.h["gs"], 16)
                    em.inc("gs", 16)
            for rb in range(11):
                stg = sb["xstage"] if par % 2 == 0 else sb["xstage2"]
                par += 1
                em.W(gp, "gs", em.v["gs"])
                gp.dma_start(out=stg[...],
                             in_=env["wdn"][e, rb * 128:(rb + 1) * 128, :]
                             ).then_inc(em.h["gl"], 16)
                t = em.inc("gl", 16)
                gp.wait_ge(em.h["gl"], t)
                gp.dma_start(out=env["WDNB"][e, rb * 128:(rb + 1) * 128, :],
                             in_=stg[...]).then_inc(em.h["gs"], 16)
                em.inc("gs", 16)
    env["ecast_done"] = em.v["gs"]



def alloc_scopeB(env, st):
    sbuf = env["sbuf"]
    sbuf("memb", [128, M * H_PER, HD], BF16, st)
    sbuf("sqT", [128, H_PER, 512], BF16, st)
    sbuf("partS", [128, 4, HC], BF16, st)
    sbuf("scT", [128, H_PER * 128], BF16, st)


def alloc_scopeB1(env, st):
    sbuf = env["sbuf"]
    sbuf("wqTs", [128, 16, HC], BF16, st)
    sbuf("wkTs", [128, 16, HC], BF16, st)
    sbuf("wvTs", [128, 16, HC], BF16, st)
    sbuf("xT", [128, 16, 512], BF16, st)
    sbuf("xT2", [128, 16, 512], BF16, st)
    sbuf("kp", [128, 4, HC], BF16, st)
    sbuf("vc", [128, 4, HC], BF16, st)
    sbuf("kpT", [128, HC], BF16, st)
    for p in range(2):
        sbuf(f"sigF{p}", [128, 512], F32, st)
        sbuf(f"sigG{p}", [128, 512], F32, st)
        sbuf(f"sigH{p}", [128, 512], BF16, st)


def alloc_scopeB2(env, st):
    sbuf = env["sbuf"]
    sbuf("comb", [128, HC], BF16, st)
    sbuf("cm", [128, M * H_PER], F32, st)


def phase_p0b(env):
    em, sb = env["em"], env["sb"]

    @em.on("gpsimd")
    def _(gp):
        for nm, src in [("wqTs", env["wqT"]), ("wkTs", env["wkT"]),
                        ("wvTs", env["wvT"])]:
            gp.dma_start(out=sb[nm][...],
                         in_=src.rearrange("(k p) e -> p k e", p=128)
                         ).then_inc(em.h["wl"], 16)
            em.inc("wl", 16)
        gp.dma_start(out=sb["memb"][...],
                     in_=env["memi"].rearrange("mh (d e) -> d mh e", d=HD)
                     ).then_inc(em.h["wl"], 16)
        em.inc("wl", 16)
    env["wl_attn"] = em.v["wl"]
    env["wl_mem"] = em.v["wl"]




def phase_attn_pass1(env):
    for sc in range(env["cfg"].NSC):
        _p1_super(env, sc)
    env["p1_tdone"] = env["em"].v["tp"]


def _p1_super(env, sc):
    em, sb, ps, cfg = env["em"], env["sb"], env["ps"], env["cfg"]
    t0 = sc * 512
    xT = sb["xT"] if sc % 2 == 0 else sb["xT2"]

    @em.on("sync")
    def _(sy):
        em.W(sy, "gs", env["xbh"][(t0 + 512) // 128 - 1])
        if sc >= 2:
            em.W(sy, "tp", env[f"p1_projdone_{sc - 2}"])
        for k in range(16):
            sy.dma_start(out=xT[:, k, :],
                         in_=env["XBH"][t0:t0 + 512, k * 128:(k + 1) * 128],
                         transpose=True).then_inc(em.h["ld"], 16)
            em.inc("ld", 16)
        em.qw(sy, "ld")
    xT_ready = em.v["ld"]

    for et in range(H_PER):
        bank = ps["d0"] if et % 2 == 0 else ps["d1"]

        @em.on("tensor")
        def _(t, et=et, bank=bank):
            if et == 0:
                em.W(t, "ld", xT_ready)
                em.W(t, "wl", env["wl_attn"])
                em.W(t, "vp", env[f"p1_outev_{sc - 1}_3"] if sc > 0
                     else env["g1m_ready"])
            if et >= 2:
                em.W(t, "vp", env[f"p1_qev_{sc}_{et - 2}"])
            for k in range(16):
                mmi = t.matmul(bank[:, 0:512],
                               sb["wqTs"][:, k, et * 128:(et + 1) * 128],
                               xT[:, k, :], start=(k == 0), stop=(k == 15))
            mmi.then_inc(em.h["tp"], 1)
            env[f"p1_qps_{sc}_{et}"] = em.inc("tp", 1)

        @em.on("vector")
        def _(v, et=et, bank=bank):
            em.W(v, "tp", env[f"p1_qps_{sc}_{et}"])
            if sc > 0 and et == 0:
                em.W(v, "scr", env[f"p1_scr_{sc - 1}"])
            pr = et % 2
            v.reduce_sum(sb["tmp"][:, pr:pr + 1], bank[:, 0:512],
                         axis=AX.X).then_inc(em.h["vp"], 1)
            g0 = em.inc("vp", 1)
            v.tensor_scalar(sb[f"sigF{pr}"][...], bank[:, 0:512], 0.0, None,
                            ALU.min)
            v.tensor_scalar(sb[f"sigG{pr}"][...], bank[:, 0:512], 0.0, None,
                            ALU.max).then_inc(em.h["vp"], 1)
            env[f"p1_qsig_{sc}_{et}"] = em.inc("vp", 1)
            v.wait_ge(em.h["vp"], g0)
            v.tensor_tensor(sb["relacc"][:, et:et + 1],
                            sb["relacc"][:, et:et + 1],
                            sb["tmp"][:, pr:pr + 1], ALU.add)

        @em.on("scalar")
        def _(s, et=et):
            pr = et % 2
            em.W(s, "vp", env[f"p1_qsig_{sc}_{et}"])
            s.activation(sb[f"sigH{pr}"][...], sb[f"sigF{pr}"][...],
                         AF.Exp).then_inc(em.h["sp"], 1)
            env[f"p1_qexp_{sc}_{et}"] = em.inc("sp", 1)

        @em.on("vector")
        def _(v, et=et):
            pr = et % 2
            em.W(v, "sp", env[f"p1_qexp_{sc}_{et}"])
            v.tensor_tensor(sb["sqT"][:, et, :], sb[f"sigG{pr}"][...],
                            sb[f"sigH{pr}"][...], ALU.add).then_inc(
                em.h["vp"], 1)
            env[f"p1_qev_{sc}_{et}"] = em.inc("vp", 1)

    for kn in range(8):
        sub, n = kn // 2, kn % 2
        bank = ps[f"q{kn % 4}"]

        @em.on("tensor")
        def _(t, kn=kn, sub=sub, n=n, bank=bank):
            if kn >= 4:
                em.W(t, "vp", env[f"p1_kev_{sc}_{kn - 4}"])
            elif sc > 0:
                em.W(t, "vp", env[f"p1_outev_{sc - 1}_3"])
            for k in range(16):
                mmi = t.matmul(bank[:, 0:512],
                               xT[:, k, sub * 128:(sub + 1) * 128],
                               sb["wkTs"][:, k, n * 512:(n + 1) * 512],
                               start=(k == 0), stop=(k == 15))
            mmi.then_inc(em.h["tp"], 1)
            env[f"p1_kps_{sc}_{kn}"] = em.inc("tp", 1)

        @em.on("vector")
        def _(v, kn=kn, bank=bank):
            pr = kn % 2
            em.W(v, "tp", env[f"p1_kps_{sc}_{kn}"])
            v.tensor_scalar(sb[f"sigF{pr}"][...], bank[:, 0:512], 0.0, None,
                            ALU.min)
            v.tensor_scalar(sb[f"sigG{pr}"][...], bank[:, 0:512], 0.0, None,
                            ALU.max).then_inc(em.h["vp"], 1)
            env[f"p1_ksig_{sc}_{kn}"] = em.inc("vp", 1)

        @em.on("scalar")
        def _(s, kn=kn):
            pr = kn % 2
            em.W(s, "vp", env[f"p1_ksig_{sc}_{kn}"])
            s.activation(sb[f"sigH{pr}"][...], sb[f"sigF{pr}"][...],
                         AF.Exp).then_inc(em.h["sp"], 1)
            env[f"p1_kexp_{sc}_{kn}"] = em.inc("sp", 1)

        @em.on("vector")
        def _(v, kn=kn, sub=sub, n=n):
            pr = kn % 2
            em.W(v, "sp", env[f"p1_kexp_{sc}_{kn}"])
            v.tensor_tensor(sb["kp"][:, sub, n * 512:(n + 1) * 512],
                            sb[f"sigG{pr}"][...], sb[f"sigH{pr}"][...],
                            ALU.add).then_inc(em.h["vp"], 1)
            env[f"p1_kev_{sc}_{kn}"] = em.inc("vp", 1)

    for vn in range(8):
        sub, n = vn // 2, vn % 2
        bank = ps[f"q{vn % 4}"]

        @em.on("tensor")
        def _(t, vn=vn, sub=sub, n=n, bank=bank):
            em.W(t, "vp", env[f"p1_kev_{sc}_{4 + (vn % 4)}"] if vn < 4
                 else env[f"p1_vev_{sc}_{vn - 4}"])
            for k in range(16):
                mmi = t.matmul(bank[:, 0:512],
                               xT[:, k, sub * 128:(sub + 1) * 128],
                               sb["wvTs"][:, k, n * 512:(n + 1) * 512],
                               start=(k == 0), stop=(k == 15))
            mmi.then_inc(em.h["tp"], 1)
            env[f"p1_vps_{sc}_{vn}"] = em.inc("tp", 1)

        @em.on("vector")
        def _(v, vn=vn, sub=sub, n=n, bank=bank):
            em.W(v, "tp", env[f"p1_vps_{sc}_{vn}"])
            v.tensor_copy(sb["vc"][:, sub, n * 512:(n + 1) * 512],
                          bank[:, 0:512]).then_inc(em.h["vp"], 1)
            env[f"p1_vev_{sc}_{vn}"] = em.inc("vp", 1)
    env[f"p1_projdone_{sc}"] = em.v["tp"]

    for sub in range(4):
        _p1_chunk(env, sc, sub)

    @em.on("scalar")
    def _(s):
        em.W(s, "vp", env[f"p1_qev_{sc}_{H_PER - 1}"])
        for et in range(H_PER):
            s.dma_start(out=env["SQT"][et, :, t0:t0 + 512],
                        in_=sb["sqT"][:, et, :]).then_inc(em.h["scr"], 16)
            em.inc("scr", 16)
        em.W(s, "vp", env[f"p1_outev_{sc}_3"])
        s.dma_start(out=env["PART"][t0:t0 + 512, :].rearrange(
                        "(s p) e -> p s e", p=128),
                    in_=sb["partS"][...]).then_inc(em.h["scr"], 16)
        em.inc("scr", 16)
        em.qw(s, "scr")
    env[f"p1_scr_{sc}"] = em.v["scr"]


def _p1_chunk(env, sc, sub):
    em, sb, ps = env["em"], env["sb"], env["ps"]

    @em.on("vector")
    def _(v):
        v.tensor_copy(sb["Sb"][...], sb["S"][...])
        v.tensor_copy(sb["Zb"][...], sb["Z"][...]).then_inc(em.h["vp"], 1)
        env["p1_snap"] = em.inc("vp", 1)

    @em.on("tensor")
    def _(t):
        em.W(t, "vp", env[f"p1_kev_{sc}_{sub * 2 + 1}"])
        em.W(t, "vp", env[f"p1_vev_{sc}_{sub * 2 + 1}"])
        em.W(t, "vp", env[f"p1_qev_{sc}_{H_PER - 1}"])
        em.W(t, "vp", env["p1_snap"])
        if "p1_kpT_cp" in env:
            em.W(t, "sp", env["p1_kpT_cp"])
        for h in range(H_PER):
            tr = t.transpose(ps["t"][:, h * 128:(h + 1) * 128],
                             sb["kp"][:, sub, h * 128:(h + 1) * 128],
                             sb["c_ident"][...])
        tr.then_inc(em.h["tp"], 1)
        env["p1_kpT_ps"] = em.inc("tp", 1)

    @em.on("scalar")
    def _(s):
        em.W(s, "tp", env["p1_kpT_ps"])
        s.copy(sb["kpT"][...], ps["t"][:, 0:HC]).then_inc(em.h["sp"], 1)
        env["p1_kpT_cp"] = em.inc("sp", 1)

    @em.on("tensor")
    def _(t):
        em.W(t, "sp", env["p1_kpT_cp"])
        if "p1_scev" in env:
            em.W(t, "vp", env["p1_scev"])
        for h in range(H_PER):
            bank = ps["d0"] if h < 4 else ps["d1"]
            mmi = t.matmul(bank[:, (h % 4) * 128:(h % 4 + 1) * 128],
                           sb["kpT"][:, h * 128:(h + 1) * 128],
                           sb["sqT"][:, h, sub * 128:(sub + 1) * 128],
                           start=True, stop=True)
        mmi.then_inc(em.h["tp"], 1)
        env["p1_sc_ps"] = em.inc("tp", 1)

    @em.on("vector")
    def _(v):
        em.W(v, "tp", env["p1_sc_ps"])
        v.tensor_tensor(sb["scT"][:, 0:512], ps["d0"][:, 0:512],
                        sb["c_maskT"][:, 0:512], ALU.mult)
        v.tensor_tensor(sb["scT"][:, 512:1024], ps["d1"][:, 0:512],
                        sb["c_maskT"][:, 512:1024], ALU.mult).then_inc(
            em.h["vp"], 1)
        env["p1_scev"] = em.inc("vp", 1)

    @em.on("tensor")
    def _(t):
        em.W(t, "vp", env["p1_scev"])
        key = f"p1_outev_{sc}_{sub - 1}" if sub > 0 else f"p1_outev_{sc - 1}_3"
        if key in env:
            em.W(t, "vp", env[key])
        for h in range(H_PER):
            bank = ps["q0"] if h < 4 else ps["q1"]
            t.matmul(bank[:, (h % 4) * 128:(h % 4 + 1) * 128],
                     sb["kp"][:, sub, h * 128:(h + 1) * 128],
                     sb["vc"][:, sub, h * 128:(h + 1) * 128],
                     start=True, stop=True)
        for h in range(H_PER):
            t.matmul(ps["s"][:, h:h + 1],
                     sb["sqT"][:, h, sub * 128:(sub + 1) * 128],
                     sb["Zb"][:, h:h + 1], start=True, stop=False)
            t.matmul(ps["s"][:, h:h + 1], sb["scT"][:, h * 128:(h + 1) * 128],
                     sb["c_onesb"][...], start=False, stop=True)
        for h in range(H_PER):
            t.matmul(ps["s"][:, 16 + h:17 + h],
                     sb["kp"][:, sub, h * 128:(h + 1) * 128],
                     sb["c_onesb"][...], start=True, stop=True)
        for h in range(H_PER):
            bank = ps["q2"] if h < 4 else ps["q3"]
            t.matmul(bank[:, (h % 4) * 128:(h % 4 + 1) * 128],
                     sb["scT"][:, h * 128:(h + 1) * 128],
                     sb["vc"][:, sub, h * 128:(h + 1) * 128],
                     start=True, stop=False)
            mmi = t.matmul(bank[:, (h % 4) * 128:(h % 4 + 1) * 128],
                           sb["sqT"][:, h, sub * 128:(sub + 1) * 128],
                           sb["Sb"][:, h * 128:(h + 1) * 128],
                           start=False, stop=True)
        mmi.then_inc(em.h["tp"], 1)
        env["p1_att_ps"] = em.inc("tp", 1)

    @em.on("vector")
    def _(v):
        em.W(v, "tp", env["p1_att_ps"])
        if sub == 0 and sc > 0:
            em.W(v, "scr", env[f"p1_scr_{sc - 1}"])
        v.tensor_tensor(sb["S"][:, 0:512], sb["S"][:, 0:512],
                        ps["q0"][:, 0:512], ALU.add)
        v.tensor_tensor(sb["S"][:, 512:1024], sb["S"][:, 512:1024],
                        ps["q1"][:, 0:512], ALU.add)
        v.tensor_tensor(sb["Z"][...], sb["Z"][...],
                        ps["s"][:, 16:16 + H_PER], ALU.add)
        v.tensor_scalar(sb["tmp"][:, 0:H_PER], ps["s"][:, 0:H_PER], EPS,
                        None, ALU.max).then_inc(em.h["vp"], 1)
        v.wait_ge(em.h["vp"], em.inc("vp", 1))
        v.reciprocal(sb["tmp"][:, 8:8 + H_PER],
                     sb["tmp"][:, 0:H_PER]).then_inc(em.h["vp"], 1)
        v.wait_ge(em.h["vp"], em.inc("vp", 1))
        for h in range(H_PER):
            bank = ps["q2"] if h < 4 else ps["q3"]
            vi = v.tensor_scalar(sb["partS"][:, sub, h * 128:(h + 1) * 128],
                                 bank[:, (h % 4) * 128:(h % 4 + 1) * 128],
                                 sb["tmp"][:, 8 + h:9 + h],
                                 sb["g1m"][:, h:h + 1], ALU.mult, ALU.mult)
        vi.then_inc(em.h["vp"], 1)
        env[f"p1_outev_{sc}_{sub}"] = em.inc("vp", 1)


def phase_rel(env):
    em, sb, ps, cfg = env["em"], env["sb"], env["ps"], env["cfg"]

    @em.on("vector")
    def _(v):
        v.tensor_copy(sb["relaccb"][...], sb["relacc"][...])
        v.reduce_sum(sb["tmp"][0:M * 16, 0:1], sb["nrmFs"][...],
                     axis=AX.X).then_inc(em.h["vp"], 1)
        env["p2_v0"] = em.inc("vp", 1)

    @em.on("tensor")
    def _(t):
        em.W(t, "vp", env["p2_v0"])
        for h in range(H_PER):
            t.matmul(ps["s"][0:M, h:h + 1], sb["nrmTb"][:, h, :],
                     sb["relaccb"][:, h:h + 1], start=True, stop=True)
        mmi = t.matmul(ps["s"][0:M, 8:9], sb["c_esel"][...],
                       sb["tmp"][0:M * 16, 0:1], start=True, stop=True)
        mmi.then_inc(em.h["tp"], 1)
        env["p2_t0"] = em.inc("tp", 1)

    @em.on("vector")
    def _(v):
        em.W(v, "tp", env["p2_t0"])
        v.tensor_copy(sb["rel"][0:M, 0:H_PER], ps["s"][0:M, 0:H_PER])
        v.tensor_scalar(sb["rel"][0:M, 8:9], ps["s"][0:M, 8:9], 1e-6, None,
                        ALU.is_lt).then_inc(em.h["vp"], 1)
        env["p2_v1"] = em.inc("vp", 1)

    @em.on("tensor")
    def _(t):
        em.W(t, "vp", env["p2_v1"])
        t.matmul(ps["s"][0:H_PER, 16:16 + M], sb["rel"][0:M, 0:H_PER],
                 sb["c_identf"][0:M, 0:M], start=True, stop=True)
        mmi = t.matmul(ps["s"][0:1, 24:24 + M], sb["rel"][0:M, 8:9],
                       sb["c_identf"][0:M, 0:M], start=True, stop=True)
        mmi.then_inc(em.h["tp"], 1)
        env["p2_t1"] = em.inc("tp", 1)

    @em.on("vector")
    def _(v):
        em.W(v, "tp", env["p2_t1"])
        v.tensor_scalar(sb["rel"][0:H_PER, 0:M], ps["s"][0:H_PER, 16:16 + M],
                        SCALE / env["cfg"].S, None, ALU.mult)
        v.tensor_copy(sb["rel"][0:1, 12:12 + M],
                      ps["s"][0:1, 24:24 + M]).then_inc(em.h["vp"], 1)
        env["p2_v2"] = em.inc("vp", 1)

    @em.on("tensor")
    def _(t):
        em.W(t, "vp", env["p2_v2"])
        mmi = t.matmul(ps["s"][0:H_PER, 28:28 + M],
                       sb["c_onesrow"][0:1, 0:H_PER],
                       sb["rel"][0:1, 12:12 + M], start=True, stop=True)
        mmi.then_inc(em.h["tp"], 1)
        env["p2_t2"] = em.inc("tp", 1)

    @em.on("vector")
    def _(v):
        em.W(v, "tp", env["p2_t2"])
        v.scalar_tensor_tensor(sb["rel"][0:H_PER, 0:M],
                               ps["s"][0:H_PER, 28:28 + M], NEG_BIG,
                               sb["rel"][0:H_PER, 0:M], ALU.mult,
                               ALU.add).then_inc(em.h["vp"], 1)
        ga = em.inc("vp", 1)
        v.tensor_scalar(sb["rel"][0:H_PER, 4:4 + M],
                        ps["s"][0:H_PER, 28:28 + M], -1.0, 1.0, ALU.mult,
                        ALU.add)
        v.wait_ge(em.h["vp"], ga)
        v.reduce_max(sb["rel"][0:H_PER, 8:9], sb["rel"][0:H_PER, 0:M],
                     axis=AX.X).then_inc(em.h["vp"], 1)
        v.wait_ge(em.h["vp"], em.inc("vp", 1))
        v.tensor_scalar(sb["rel"][0:H_PER, 0:M], sb["rel"][0:H_PER, 0:M],
                        sb["rel"][0:H_PER, 8:9], None,
                        ALU.subtract).then_inc(em.h["vp"], 1)
        env["p2_v3"] = em.inc("vp", 1)

    @em.on("scalar")
    def _(s):
        em.W(s, "vp", env["p2_v3"])
        s.activation(sb["rel"][0:H_PER, 16:16 + M], sb["rel"][0:H_PER, 0:M],
                     AF.Exp).then_inc(em.h["sp"], 1)
        env["p2_s0"] = em.inc("sp", 1)

    @em.on("vector")
    def _(v):
        em.W(v, "sp", env["p2_s0"])
        v.reduce_sum(sb["rel"][0:H_PER, 9:10], sb["rel"][0:H_PER, 16:16 + M],
                     axis=AX.X).then_inc(em.h["vp"], 1)
        v.wait_ge(em.h["vp"], em.inc("vp", 1))
        v.reciprocal(sb["rel"][0:H_PER, 10:11],
                     sb["rel"][0:H_PER, 9:10]).then_inc(em.h["vp"], 1)
        v.wait_ge(em.h["vp"], em.inc("vp", 1))
        v.tensor_scalar(sb["rel"][0:H_PER, 16:16 + M],
                        sb["rel"][0:H_PER, 16:16 + M],
                        sb["rel"][0:H_PER, 10:11], None,
                        ALU.mult).then_inc(em.h["vp"], 1)
        v.wait_ge(em.h["vp"], em.inc("vp", 1))
        v.tensor_tensor(sb["rel"][0:H_PER, 16:16 + M],
                        sb["rel"][0:H_PER, 16:16 + M],
                        sb["rel"][0:H_PER, 4:4 + M], ALU.mult)
        v.reduce_sum(sb["rel"][0:1, 13:14], sb["rel"][0:1, 12:12 + M],
                     axis=AX.X).then_inc(em.h["vp"], 1)
        v.wait_ge(em.h["vp"], em.inc("vp", 1))
        v.tensor_scalar(sb["rel"][0:1, 14:15], sb["rel"][0:1, 13:14],
                        M - 0.5, None, ALU.is_lt).then_inc(em.h["vp"], 1)
        env["p2_v4"] = em.inc("vp", 1)

    @em.on("tensor")
    def _(t):
        em.W(t, "vp", env["p2_v4"])
        t.matmul(ps["s"][0:H_PER, 32:33], sb["c_onesrow"][0:1, 0:H_PER],
                 sb["rel"][0:1, 14:15], start=True, stop=True)
        mmi = t.matmul(ps["s"][0:H_PER, 33:34], sb["grow"][...],
                       sb["c_identf"][0:1, 0:1], start=True, stop=True)
        mmi.then_inc(em.h["tp"], 1)
        env["p2_t3"] = em.inc("tp", 1)

    @em.on("vector")
    def _(v):
        em.W(v, "tp", env["p2_t3"])
        v.tensor_scalar(sb["rel"][0:H_PER, 16:16 + M],
                        sb["rel"][0:H_PER, 16:16 + M],
                        ps["s"][0:H_PER, 32:33], ps["s"][0:H_PER, 33:34],
                        ALU.mult, ALU.mult).then_inc(em.h["vp"], 1)
        env["p2_v6"] = em.inc("vp", 1)

    @em.on("sync")
    def _(sy):
        em.W(sy, "vp", env["p2_v6"])
        sy.dma_start(out=sb["tmp"][0:1, 0:H_PER * M],
                     in_=sb["rel"][0:H_PER, 16:16 + M]).then_inc(
            em.h["ld"], 16)
        em.inc("ld", 16)
        em.qw(sy, "ld")
    env["p2_fl"] = em.v["ld"]

    @em.on("tensor")
    def _(t):
        em.W(t, "ld", env["p2_fl"])
        mmi = t.matmul(ps["s"][:, 34:34 + H_PER * M], sb["c_onesrow"][...],
                       sb["tmp"][0:1, 0:H_PER * M], start=True, stop=True)
        mmi.then_inc(em.h["tp"], 1)
        env["p2_t5"] = em.inc("tp", 1)

    @em.on("vector")
    def _(v):
        em.W(v, "tp", env["p2_t5"])
        v.tensor_copy(sb["weff"][...],
                      ps["s"][:, 34:34 + H_PER * M]).then_inc(em.h["vp"], 1)
        env["p2_done"] = em.inc("vp", 1)


def phase_attn_pass2(env):
    em = env["em"]
    for sc in range(env["cfg"].NSC):
        _p3_super(env, sc)
    env["p3_tdone"] = em.v["tp"]

    @em.on("gpsimd")
    def _(gp):
        em.W(gp, "scr", env[f"p3_scr_{env['cfg'].NSC - 1}"])
        gp.collective_compute(
            "AllGather", ALU.bypass,
            replica_groups=[[0, 1], [2, 3], [4, 5], [6, 7]],
            ins=[env["agin"].ap().opt()],
            outs=[env["agout"].ap().opt()],
        ).then_inc(em.h["cc"], 1)
        em.inc("cc", 1)
    env["a2a_done"] = em.v["cc"]


def _p3_super(env, sc):
    em, sb, cfg = env["em"], env["sb"], env["cfg"]
    t0 = sc * 512

    @em.on("scalar")
    def _(s):
        em.W(s, "scr", env[f"p1_scr_{cfg.NSC - 1}"])
        em.W(s, "vp", env["p2_done"])
        if sc > 0:
            em.W(s, "vp", env[f"p3_combev_{sc - 1}_3"])
        for et in range(H_PER):
            s.dma_start(out=sb["sqT"][:, et, :],
                        in_=env["SQT"][et, :, t0:t0 + 512]).then_inc(
                em.h["scr"], 16)
            em.inc("scr", 16)
        s.dma_start(out=sb["partS"][...],
                    in_=env["PART"][t0:t0 + 512, :].rearrange(
                        "(s p) e -> p s e", p=128)).then_inc(em.h["scr"], 16)
        em.inc("scr", 16)
        em.qw(s, "scr")
    env[f"p3_ld_{sc}"] = em.v["scr"]

    for sub in range(4):
        _p3_chunk(env, sc, sub)
    env[f"p3_scr_{sc}"] = em.v["scr"]


def _p3_chunk(env, sc, sub):
    em, sb, ps, cfg = env["em"], env["sb"], env["ps"], env["cfg"]
    ch = sc * 4 + sub

    @em.on("tensor")
    def _(t):
        em.W(t, "scr", env[f"p3_ld_{sc}"])
        em.W(t, "wl", env["wl_mem"])
        if "p3_cmev" in env:
            em.W(t, "vp", env["p3_cmev"])
        for h in range(H_PER):
            mmi = t.matmul(ps["s"][:, h * M:(h + 1) * M],
                           sb["sqT"][:, h, sub * 128:(sub + 1) * 128],
                           sb["nrmTb"][:, h, :], start=True, stop=True)
        mmi.then_inc(em.h["tp"], 1)
        env["p3_den_ps"] = em.inc("tp", 1)

    @em.on("vector")
    def _(v):
        em.W(v, "tp", env["p3_den_ps"])
        v.tensor_scalar(sb["cm"][...], ps["s"][:, 0:M * H_PER], EPS, None,
                        ALU.max).then_inc(em.h["vp"], 1)
        v.wait_ge(em.h["vp"], em.inc("vp", 1))
        v.reciprocal(sb["cm"][...], sb["cm"][...]).then_inc(em.h["vp"], 1)
        v.wait_ge(em.h["vp"], em.inc("vp", 1))
        v.tensor_tensor(sb["cm"][...], sb["cm"][...], sb["weff"][...],
                        ALU.mult).then_inc(em.h["vp"], 1)
        env["p3_cmev"] = em.inc("vp", 1)

    for m in range(M):
        bA = ps["q0"] if m % 2 == 0 else ps["q2"]
        bB = ps["q1"] if m % 2 == 0 else ps["q3"]

        @em.on("tensor")
        def _(t, m=m, bA=bA, bB=bB):
            guard = f"p3_mev_{m - 2}" if m >= 2 else f"p3_mev_{m + 2}"
            if guard in env:
                em.W(t, "vp", env[guard])
            if m == 0:
                em.W(t, "vp", env["p3_cmev"])
            for h in range(H_PER):
                bank = bA if h < 4 else bB
                mmi = t.matmul(bank[:, (h % 4) * 128:(h % 4 + 1) * 128],
                               sb["sqT"][:, h, sub * 128:(sub + 1) * 128],
                               sb["memb"][:, m * H_PER + h, :],
                               start=True, stop=True)
            mmi.then_inc(em.h["tp"], 1)
            env[f"p3_mps_{m}"] = em.inc("tp", 1)

        @em.on("vector")
        def _(v, m=m, bA=bA, bB=bB):
            em.W(v, "tp", env[f"p3_mps_{m}"])
            if m == 0:
                em.W(v, "scr", env.get("p3_aginst", 0))
            else:
                em.W(v, "vp", env[f"p3_mev_{m - 1}"])
            for h in range(H_PER):
                bank = bA if h < 4 else bB
                src1 = (sb["partS"][:, sub, h * 128:(h + 1) * 128] if m == 0
                        else sb["comb"][:, h * 128:(h + 1) * 128])
                vi = v.scalar_tensor_tensor(
                    sb["comb"][:, h * 128:(h + 1) * 128],
                    bank[:, (h % 4) * 128:(h % 4 + 1) * 128],
                    sb["cm"][:, h * M + m:h * M + m + 1],
                    src1, ALU.mult, ALU.add)
            vi.then_inc(em.h["vp"], 1)
            env[f"p3_mev_{m}"] = em.inc("vp", 1)

    @em.on("scalar")
    def _(s):
        em.W(s, "vp", env[f"p3_mev_{M - 1}"])
        s.dma_start(out=env["agin"].ap()[ch * 128:(ch + 1) * 128, :],
                    in_=sb["comb"][...]).then_inc(em.h["scr"], 16)
        env["p3_aginst"] = em.inc("scr", 16)
        em.qw(s, "scr")

    @em.on("vector")
    def _(v):
        em.W(v, "scr", env["p3_aginst"])
        v.nop().then_inc(em.h["vp"], 1)
        env[f"p3_combev_{sc}_{sub}"] = em.inc("vp", 1)


def alloc_scopeC(env, st):
    sbuf = env["sbuf"]
    sbuf("woTs", [128, 16, D], BF16, st)
    sbuf("wolh", [128, 8, 256], BF16, st)
    sbuf("wolh2", [128, 8, 256], BF16, st)
    sbuf("gwo", [128, 16], I16, st)
    sbuf("res", [128, D], F32, st)
    sbuf("res2", [128, D], F32, st)
    sbuf("s1", [128, D], F32, st)
    sbuf("scrA", [128, D], F32, st)
    sbuf("xp", [128, D], F32, st)
    sbuf("xphb", [128, D], BF16, st)
    sbuf("xpT", [128, 16, 128], F32, st)


def ln_stats_apply(env, src, dst, tc, scr_name="scrA", pre=None,
                   src_ready=None):
    em, sb = env["em"], env["sb"]

    @em.on("vector")
    def _(v):
        if src_ready is not None:
            em.W(v, src_ready[0], src_ready[1])
        v.reduce_sum(sb["tmp"][:, tc:tc + 1], src, axis=AX.X).then_inc(
            em.h["vp"], 1)
        env["ln_v0"] = em.inc("vp", 1)

    @em.on("scalar")
    def _(s):
        em.W(s, "vp", env["ln_v0"])
        s.activation(sb[scr_name][...], src, AF.Square,
                     accum_out=sb["tmp"][:, tc + 1:tc + 2]).then_inc(
            em.h["sp"], 1)
        env["ln_s0"] = em.inc("sp", 1)

    @em.on("vector")
    def _(v):
        em.W(v, "sp", env["ln_s0"])
        v.tensor_scalar(sb["tmp"][:, tc + 2:tc + 3], sb["tmp"][:, tc:tc + 1],
                        1.0 / D, None, ALU.mult).then_inc(em.h["vp"], 1)
        g1 = em.inc("vp", 1)
        v.tensor_scalar(sb["tmp"][:, tc + 3:tc + 4],
                        sb["tmp"][:, tc + 1:tc + 2], 1.0 / D, None,
                        ALU.mult).then_inc(em.h["vp"], 1)
        g2 = em.inc("vp", 1)
        v.wait_ge(em.h["vp"], g1)
        v.wait_ge(em.h["vp"], g2)
        v.tensor_tensor(sb["tmp"][:, tc + 6:tc + 7],
                        sb["tmp"][:, tc + 2:tc + 3],
                        sb["tmp"][:, tc + 2:tc + 3],
                        ALU.mult).then_inc(em.h["vp"], 1)
        g3 = em.inc("vp", 1)
        v.wait_ge(em.h["vp"], g3)
        v.tensor_tensor(sb["tmp"][:, tc + 3:tc + 4],
                        sb["tmp"][:, tc + 3:tc + 4],
                        sb["tmp"][:, tc + 6:tc + 7],
                        ALU.subtract).then_inc(em.h["vp"], 1)
        g4 = em.inc("vp", 1)
        v.wait_ge(em.h["vp"], g4)
        v.tensor_scalar(sb["tmp"][:, tc + 3:tc + 4],
                        sb["tmp"][:, tc + 3:tc + 4], 0.0, LN_EPS,
                        ALU.max, ALU.add).then_inc(em.h["vp"], 1)
        env["ln_v1"] = em.inc("vp", 1)

    @em.on("scalar")
    def _(s):
        em.W(s, "vp", env["ln_v1"])
        s.activation(sb["tmp"][:, tc + 6:tc + 7], sb["tmp"][:, tc + 3:tc + 4],
                     AF.Sqrt).then_inc(em.h["sp"], 1)
        env["ln_s1"] = em.inc("sp", 1)

    @em.on("vector")
    def _(v):
        em.W(v, "sp", env["ln_s1"])
        v.reciprocal(sb["tmp"][:, tc + 4:tc + 5],
                     sb["tmp"][:, tc + 6:tc + 7]).then_inc(em.h["vp"], 1)
        v.wait_ge(em.h["vp"], em.inc("vp", 1))
        v.tensor_tensor(sb["tmp"][:, tc + 7:tc + 8],
                        sb["tmp"][:, tc + 4:tc + 5],
                        sb["tmp"][:, tc + 4:tc + 5], ALU.mult).then_inc(
            em.h["vp"], 1)
        v.wait_ge(em.h["vp"], em.inc("vp", 1))
        v.tensor_tensor(sb["tmp"][:, tc + 7:tc + 8],
                        sb["tmp"][:, tc + 7:tc + 8],
                        sb["tmp"][:, tc + 3:tc + 4], ALU.mult).then_inc(
            em.h["vp"], 1)
        v.wait_ge(em.h["vp"], em.inc("vp", 1))
        v.tensor_scalar(sb["tmp"][:, tc + 7:tc + 8],
                        sb["tmp"][:, tc + 7:tc + 8], -0.5, 1.5, ALU.mult,
                        ALU.add).then_inc(em.h["vp"], 1)
        v.wait_ge(em.h["vp"], em.inc("vp", 1))
        v.tensor_tensor(sb["tmp"][:, tc + 4:tc + 5],
                        sb["tmp"][:, tc + 4:tc + 5],
                        sb["tmp"][:, tc + 7:tc + 8], ALU.mult).then_inc(
            em.h["vp"], 1)
        v.wait_ge(em.h["vp"], em.inc("vp", 1))
        v.tensor_scalar(sb["tmp"][:, tc + 5:tc + 6],
                        sb["tmp"][:, tc + 2:tc + 3], -1.0, None,
                        ALU.mult).then_inc(em.h["vp"], 1)
        v.wait_ge(em.h["vp"], em.inc("vp", 1))
        if pre is not None:
            em.W(v, pre[0], pre[1])
        v.tensor_scalar(dst, src, sb["tmp"][:, tc + 5:tc + 6],
                        sb["tmp"][:, tc + 4:tc + 5], ALU.add,
                        ALU.mult).then_inc(em.h["vp"], 1)
        env["ln_done"] = em.inc("vp", 1)


def phase_wo_ln1_routing(env):
    em, sb, cfg = env["em"], env["sb"], env["cfg"]
    env["sbuf"]("gwoall", [128, cfg.OCH * 16], I16, env["_stC"])

    @em.on("sync")
    def _(sy):
        sy.dma_start(out=sb["gwoall"][...],
                     in_=env["gidxwo"]).then_inc(em.h["ld"], 16)
        em.inc("ld", 16)
        em.qw(sy, "ld")
    env["p5_gwold"] = em.v["ld"]

    @em.on("gpsimd")
    def _(gp):
        em.W(gp, "tp", env["p3_tdone"])
        gp.dma_start(out=sb["woTs"][...],
                     in_=env["woT"].rearrange("(k p) e -> p k e", p=128)
                     ).then_inc(em.h["wl"], 16)
        em.inc("wl", 16)
        em.W(gp, "cc", env["a2a_done"])
        em.W(gp, "ld", env["p5_gwold"])
        gp.nop().then_inc(em.h["mk"], 1)
        em.inc("mk", 1)
    env["p5_w"] = em.v["wl"]
    env["p5_ag"] = em.v["mk"]

    for oc in range(cfg.OCH):
        _p5_chunk(env, oc)
    env["p5_tdone"] = em.v["tp"]


def _p5_chunk(env, oc):
    em, sb, ps, cfg = env["em"], env["sb"], env["ps"], env["cfg"]
    r0 = oc * 128
    lh = sb["wolh"] if oc % 2 == 0 else sb["wolh2"]
    res = sb["res"] if oc % 2 == 0 else sb["res2"]

    @em.on("sync")
    def _(sy):
        if oc >= 2:
            em.W(sy, "vp", env[f"p5_s1_{oc - 2}"])
        sy.dma_start(out=res[...],
                     in_=env["xown"][r0:r0 + 128, :]).then_inc(em.h["ld"], 16)
        em.inc("ld", 16)
        em.qw(sy, "ld")
    ldv = em.v["ld"]

    gsem = "qa" if oc % 2 == 0 else "qb"

    @em.on("gpsimd")
    def _(gp):
        em.W(gp, "mk", env["p5_ag"])
        if oc >= 2:
            em.W(gp, "tp", env[f"p5_wops_{oc - 2}"])
        gp.dma_gather(lh[...], env["agout"].ap(),
                      sb["gwoall"][:, oc * 16:(oc + 1) * 16],
                      num_idxs=256, num_idxs_reg=env["regs"][256], elem_size=HC,
                      elem_step=HC, transpose=True).then_inc(em.h[gsem], 16)
        em.inc(gsem, 16)
    ghv = (gsem, em.v[gsem])

    @em.on("tensor")
    def _(t):
        em.W(t, "ld", ldv)
        em.W(t, ghv[0], ghv[1])
        em.W(t, "wl", env["p5_w"])
        if oc > 0:
            em.W(t, "vp", env[f"p5_s1_{oc - 1}"])
        for n in range(4):
            for k in range(16):
                lhT = (lh[:, k, 0:128] if k < 8 else lh[:, k - 8, 128:256])
                mmi = t.matmul(ps[f"q{n}"][:, 0:512], lhT,
                               sb["woTs"][:, k, n * 512:(n + 1) * 512],
                               start=(k == 0), stop=(k == 15))
        mmi.then_inc(em.h["tp"], 1)
        env[f"p5_wops_{oc}"] = em.inc("tp", 1)

    @em.on("vector")
    def _(v):
        em.W(v, "tp", env[f"p5_wops_{oc}"])
        em.W(v, "sp", env.get("p5_sq_prev", 0))
        for n in range(4):
            vi = v.tensor_tensor(sb["s1"][:, n * 512:(n + 1) * 512],
                                 ps[f"q{n}"][:, 0:512],
                                 res[:, n * 512:(n + 1) * 512], ALU.add)
        vi.then_inc(em.h["vp"], 1)
        env[f"p5_s1_{oc}"] = em.inc("vp", 1)

    ln_stats_apply(env, sb["s1"][...], sb["xp"][...], 0,
                   pre=("scr", env.get(f"p5_xpst_{oc - 1}", 0)),
                   src_ready=("vp", env[f"p5_s1_{oc}"]))
    env["p5_sq_prev"] = env["ln_s0"]
    xp_done = env["ln_done"]

    @em.on("vector")
    def _(v):
        em.W(v, "scr", env.get(f"p5_xpst_{oc - 1}", 0))
        v.wait_ge(em.h["vp"], xp_done)
        v.tensor_copy(sb["xphb"][...], sb["xp"][...]).then_inc(em.h["vp"], 1)
        env["p5_xph"] = em.inc("vp", 1)

    @em.on("scalar")
    def _(s):
        em.W(s, "vp", env["p5_xph"])
        s.dma_start(out=env["XP"][r0:r0 + 128, :],
                    in_=sb["xp"][...]).then_inc(em.h["scr"], 16)
        em.inc("scr", 16)
        s.dma_start(out=env["XPH"][r0:r0 + 128, :],
                    in_=sb["xphb"][...]).then_inc(em.h["scr"], 16)
        em.inc("scr", 16)
        em.qw(s, "scr")
    env[f"p5_xpst_{oc}"] = em.v["scr"]

    for k in range(16):
        bank = ps["d0"] if k % 2 == 0 else ps["d1"]

        @em.on("tensor")
        def _(t, k=k, bank=bank):
            if k == 0:
                em.W(t, "vp", xp_done)
            if k >= 2:
                em.W(t, "sp", env[f"p5_xpTc_{k - 2}"])
            mmi = t.matmul(bank[:, 0:128], sb["xp"][:, k * 128:(k + 1) * 128],
                           sb["c_identf"][...], start=True, stop=True)
            mmi.then_inc(em.h["tp"], 1)
            env[f"p5_xpT_{k}"] = em.inc("tp", 1)

        @em.on("scalar")
        def _(s, k=k, bank=bank):
            em.W(s, "tp", env[f"p5_xpT_{k}"])
            s.copy(sb["xpT"][:, k, :], bank[:, 0:128]).then_inc(em.h["sp"], 1)
            env[f"p5_xpTc_{k}"] = em.inc("sp", 1)

    @em.on("tensor")
    def _(t):
        em.W(t, "sp", env["p5_xpTc_15"])
        if "p5_rtdone" in env:
            em.W(t, "vp", env["p5_rtdone"])
        for k in range(16):
            mmi = t.matmul(ps["s"][:, 0:E], sb["xpT"][:, k, :],
                           sb["routerS"][:, k, :], start=(k == 0),
                           stop=(k == 15))
        mmi.then_inc(em.h["tp"], 1)
        env["p5_lg"] = em.inc("tp", 1)

    _p5_routing(env, oc)


def _p5_routing(env, oc):
    em, sb, ps = env["em"], env["sb"], env["ps"]

    @em.on("vector")
    def _(v):
        def st(ins):
            ins.then_inc(em.h["vp"], 1)
            v.wait_ge(em.h["vp"], em.inc("vp", 1))

        em.W(v, "tp", env["p5_lg"])
        tm = sb["tmp"]
        rst = sb["rstash"]
        lg = ps["s"][:, 0:E]
        st(v.reduce_max(tm[:, 16:17], lg, axis=AX.X))
        st(v.tensor_scalar(tm[:, 17:17 + E], lg, tm[:, 16:17], None,
                           ALU.is_ge))
        st(v.tensor_scalar(tm[:, 25:25 + E], tm[:, 17:17 + E], -1e4, 1e4,
                           ALU.mult, ALU.add))
        st(v.tensor_tensor(tm[:, 25:25 + E], tm[:, 25:25 + E],
                           sb["c_iotae"][...], ALU.add))
        st(v.tensor_reduce(rst[:, oc, 0:1], tm[:, 25:25 + E], AX.X, ALU.min))
        st(v.tensor_scalar(tm[:, 17:17 + E], sb["c_iotae"][...],
                           rst[:, oc, 0:1], None, ALU.is_equal))
        st(v.scalar_tensor_tensor(tm[:, 33:33 + E], tm[:, 17:17 + E],
                                  NEG_BIG, lg, ALU.mult, ALU.add))
        st(v.reduce_max(tm[:, 41:42], tm[:, 33:33 + E], axis=AX.X))
        st(v.tensor_scalar(tm[:, 42:42 + E], tm[:, 33:33 + E], tm[:, 41:42],
                           None, ALU.is_ge))
        st(v.tensor_scalar(tm[:, 50:50 + E], tm[:, 42:42 + E], -1e4, 1e4,
                           ALU.mult, ALU.add))
        st(v.tensor_tensor(tm[:, 50:50 + E], tm[:, 50:50 + E],
                           sb["c_iotae"][...], ALU.add))
        st(v.tensor_reduce(rst[:, oc, 1:2], tm[:, 50:50 + E], AX.X, ALU.min))
        st(v.tensor_scalar(tm[:, 42:42 + E], sb["c_iotae"][...],
                           rst[:, oc, 1:2], None, ALU.is_equal))
        st(v.tensor_tensor(tm[:, 58:59], tm[:, 41:42], tm[:, 16:17],
                           ALU.subtract))
        env["p5_rt_d"] = em.v["vp"]

    @em.on("scalar")
    def _(s):
        em.W(s, "vp", env["p5_rt_d"])
        s.activation(sb["tmp"][:, 59:60], sb["tmp"][:, 58:59],
                     AF.Exp).then_inc(em.h["sp"], 1)
        env["p5_rt_e"] = em.inc("sp", 1)

    @em.on("vector")
    def _(v):
        def st(ins):
            ins.then_inc(em.h["vp"], 1)
            v.wait_ge(em.h["vp"], em.inc("vp", 1))

        em.W(v, "sp", env["p5_rt_e"])
        tm = sb["tmp"]
        rst = sb["rstash"]
        st(v.tensor_scalar(tm[:, 60:61], tm[:, 59:60], 1.0, None, ALU.add))
        st(v.reciprocal(rst[:, oc, 2:3], tm[:, 60:61]))
        st(v.tensor_scalar(rst[:, oc, 3:4], rst[:, oc, 2:3], -1.0, 1.0,
                           ALU.mult, ALU.add))
        st(v.tensor_tensor(tm[:, 25:25 + E], tm[:, 17:17 + E],
                           tm[:, 42:42 + E], ALU.add))
        st(v.tensor_copy(sb["tmpb"][:, 0:E], tm[:, 25:25 + E]))
        env["p5_oh"] = em.v["vp"]

    @em.on("tensor")
    def _(t):
        em.W(t, "vp", env["p5_oh"])
        t.matmul(ps["s"][:, 8:8 + E], sb["c_triu"][...], sb["tmpb"][:, 0:E],
                 start=True, stop=True)
        mmi = t.matmul(ps["s"][0:1, 24:24 + E], sb["c_onesb"][...],
                       sb["tmpb"][:, 0:E], start=True, stop=True)
        mmi.then_inc(em.h["tp"], 1)
        env["p5_Sx"] = em.inc("tp", 1)

    @em.on("vector")
    def _(v):
        def st(ins):
            ins.then_inc(em.h["vp"], 1)
            v.wait_ge(em.h["vp"], em.inc("vp", 1))

        em.W(v, "tp", env["p5_Sx"])
        tm = sb["tmp"]
        rst = sb["rstash"]
        st(v.tensor_tensor(tm[:, 33:33 + E], ps["s"][:, 8:8 + E],
                           tm[:, 17:17 + E], ALU.mult))
        st(v.reduce_sum(rst[:, oc, 4:5], tm[:, 33:33 + E], axis=AX.X))
        st(v.tensor_tensor(tm[:, 50:50 + E], ps["s"][:, 8:8 + E],
                           tm[:, 42:42 + E], ALU.mult))
        st(v.reduce_sum(rst[:, oc, 5:6], tm[:, 50:50 + E], axis=AX.X))
        st(v.tensor_copy(sb["ctotrow"][0:1, oc * E:(oc + 1) * E],
                         ps["s"][0:1, 24:24 + E]))
        env["p5_rtdone"] = em.v["vp"]
    env[f"p5_rt_{oc}"] = env["p5_rtdone"]


def alloc_scopeD(env, st):
    sbuf, cfg = env["sbuf"], env["cfg"]
    sbuf("tb", [128, E * cfg.CC, 64], F32, st)
    sbuf("gidxall", [128, E, cfg.CAP // 16], I16, st)
    sbuf("wupg", [128, 16, 256], BF16, st)
    sbuf("wupg2", [128, 16, 256], BF16, st)
    sbuf("wdg", [128, 512], BF16, st)
    sbuf("wdg2", [128, 512], BF16, st)
    sbuf("fS", [128, 1024], F32, st)
    sbuf("fS2", [128, 2048], F32, st)
    sbuf("xprow", [128, 512], F32, st)
    sbuf("raccrow", [128, 512], F32, st)
    sbuf("orow", [128, D], F32, st)
    sbuf("scrD", [128, D], F32, st)


def alloc_scopeD1(env, st):
    sbuf, cfg = env["sbuf"], env["cfg"]
    CAP = cfg.CAP
    sbuf("xeT", [128, 16, CAP], BF16, st)
    sbuf("xeT2", [128, 16, CAP], BF16, st)
    sbuf("hT", [128, 11, CAP], BF16, st)
    sbuf("wdq", [128, 11, 512], BF16, st)
    sbuf("wdq2", [128, 11, 512], BF16, st)
    sbuf("stg", [128, cfg.CC, D], F32, st)


def alloc_scopeD2(env, st):
    sbuf, cfg = env["sbuf"], env["cfg"]
    sbuf("shxT", [128, 16, cfg.SHG], BF16, st)
    sbuf("shT", [128, 44, cfg.SHG], BF16, st)
    sbuf("sacc", [128, 4, D], F32, st)


def phase_routing_finalize(env):
    em, sb, ps, cfg = env["em"], env["sb"], env["ps"], env["cfg"]
    OCH = cfg.OCH

    @em.on("sync")
    def _(sy):
        sy.dma_start(out=sb["ctot"][...],
                     in_=sb["ctotrow"][0:1, :]).then_inc(em.h["ld"], 16)
        em.inc("ld", 16)
        em.qw(sy, "ld")
    env["p6_ctl"] = em.v["ld"]

    @em.on("vector")
    def _(v):
        em.W(v, "ld", env["p6_ctl"])
        v.tensor_copy(sb["ctotb"][...], sb["ctot"][...]).then_inc(
            em.h["vp"], 1)
        env["p6_ctb"] = em.inc("vp", 1)

    @em.on("tensor")
    def _(t):
        em.W(t, "vp", env["p6_ctb"])
        t.matmul(ps["s"][0:1, 0:E], sb["c_onesb"][0:OCH, :],
                 sb["ctotb"][...], start=True, stop=True)
        mmi = t.matmul(ps["s"][0:OCH, 8:8 + E], sb["c_triu"][0:OCH, 0:OCH],
                       sb["ctotb"][...], start=True, stop=True)
        mmi.then_inc(em.h["tp"], 1)
        env["p6_t0"] = em.inc("tp", 1)

    @em.on("vector")
    def _(v):
        em.W(v, "tp", env["p6_t0"])
        v.tensor_copy(sb["tmp"][0:1, 0:E], ps["s"][0:1, 0:E])
        v.tensor_copy(sb["ctot"][...], ps["s"][0:OCH, 8:8 + E]).then_inc(
            em.h["vp"], 1)
        env["p6_v0"] = em.inc("vp", 1)

    @em.on("sync")
    def _(sy):
        em.W(sy, "vp", env["p6_v0"])
        sy.dma_start(out=env["cnt_in"].ap(),
                     in_=sb["tmp"][0:1, 0:E]).then_inc(em.h["ld"], 16)
        em.inc("ld", 16)
        sy.dma_start(out=sb["cprerow"][0:1, 0:OCH * E],
                     in_=sb["ctot"][...]).then_inc(em.h["ld"], 16)
        em.inc("ld", 16)
        em.qw(sy, "ld")
    env["p6_cst"] = em.v["ld"]
    env["p6_fl"] = em.v["ld"]

    @em.on("gpsimd")
    def _(gp):
        em.W(gp, "ld", env["p6_cst"])
        gp.collective_compute(
            "AllGather", ALU.bypass,
            replica_groups=[[0, 1, 2, 3, 4, 5, 6, 7]],
            ins=[env["cnt_in"].ap().opt()],
            outs=[env["cnt_out"].ap().opt()],
        ).then_inc(em.h["cc"], 1)
        em.inc("cc", 1)
    env["p6_ag"] = em.v["cc"]

    @em.on("sync")
    def _(sy):
        em.W(sy, "cc", env["p6_ag"])
        sy.dma_start(out=sb["cnt"][...],
                     in_=env["cnt_out"].ap()).then_inc(em.h["ld"], 16)
        em.inc("ld", 16)
        em.qw(sy, "ld")
    env["p6_cl"] = em.v["ld"]

    @em.on("tensor")
    def _(t):
        em.W(t, "ld", env["p6_cl"])
        mmi = t.matmul(ps["s"][0:1, 16:16 + E], sb["priorS"][...],
                       sb["cnt"][...], start=True, stop=True)
        mmi.then_inc(em.h["tp"], 1)
        env["p6_t1"] = em.inc("tp", 1)

    @em.on("vector")
    def _(v):
        em.W(v, "tp", env["p6_t1"])
        v.tensor_copy(sb["priorrow"][...],
                      ps["s"][0:1, 16:16 + E]).then_inc(em.h["vp"], 1)
        env["p6_v1"] = em.inc("vp", 1)

    @em.on("tensor")
    def _(t):
        em.W(t, "vp", env["p6_v1"])
        em.W(t, "ld", env["p6_fl"])
        t.matmul(ps["s"][:, 24:24 + E], sb["c_onesrow"][...],
                 sb["priorrow"][...], start=True, stop=True)
        nrep = (OCH * E + 63) // 64
        for i in range(nrep):
            w = min(64, OCH * E - i * 64)
            mmi = t.matmul(ps["d0"][:, i * 64:i * 64 + w],
                           sb["c_onesrow"][...],
                           sb["cprerow"][0:1, i * 64:i * 64 + w],
                           start=True, stop=True)
        mmi.then_inc(em.h["tp"], 1)
        env["p6_t2"] = em.inc("tp", 1)

    @em.on("vector")
    def _(v):
        em.W(v, "tp", env["p6_t2"])
        v.tensor_copy(sb["priorrep"][...], ps["s"][:, 24:24 + E])
        v.tensor_copy(sb["fS"][:, 0:OCH * E],
                      ps["d0"][:, 0:OCH * E]).then_inc(em.h["vp"], 1)
        env["p6_v3"] = em.inc("vp", 1)

    for oc in range(OCH):
        for k in range(2):
            @em.on("vector")
            def _(v, oc=oc, k=k):
                def st(ins):
                    ins.then_inc(em.h["vp"], 1)
                    v.wait_ge(em.h["vp"], em.inc("vp", 1))

                if oc == 0 and k == 0:
                    em.W(v, "vp", env["p6_v3"])
                tm = sb["tmp"]
                rst = sb["rstash"]
                j = k * OCH + oc
                st(v.tensor_scalar(tm[:, 0:E], sb["c_iotae"][...],
                                   rst[:, oc, k:k + 1], None, ALU.is_equal))
                st(v.tensor_tensor(tm[:, 8:8 + E], tm[:, 0:E],
                                   sb["priorrep"][...], ALU.mult))
                st(v.reduce_sum(tm[:, 16:17], tm[:, 8:8 + E], axis=AX.X))
                st(v.tensor_tensor(tm[:, 24:24 + E], tm[:, 0:E],
                                   sb["fS"][:, oc * E:(oc + 1) * E],
                                   ALU.mult))
                st(v.reduce_sum(tm[:, 32:33], tm[:, 24:24 + E], axis=AX.X))
                st(v.tensor_tensor(tm[:, 33:34], rst[:, oc, 4 + k:5 + k],
                                   tm[:, 32:33], ALU.add))
                st(v.tensor_tensor(tm[:, 34:35], tm[:, 33:34], tm[:, 16:17],
                                   ALU.add))
                st(v.tensor_scalar(tm[:, 35:36], tm[:, 34:35],
                                   float(cfg.CREF), None, ALU.is_lt))
                st(v.tensor_tensor(sb["pay"][:, j, 1:2],
                                   rst[:, oc, 2 + k:3 + k], tm[:, 35:36],
                                   ALU.mult))
                st(v.tensor_copy(sb["pay"][:, j, 0:1],
                                 sb["c_iotat"][:, oc:oc + 1]))
                st(v.tensor_scalar(tm[:, 36:37], tm[:, 33:34],
                                   float(cfg.CAP), None, ALU.is_ge))
                st(v.tensor_scalar(tm[:, 37:38], rst[:, oc, k:k + 1],
                                   float(cfg.CAP), None, ALU.mult))
                st(v.tensor_tensor(tm[:, 37:38], tm[:, 37:38], tm[:, 33:34],
                                   ALU.add))
                st(v.tensor_scalar(tm[:, 38:39], sb["c_iotas"][:, j:j + 1],
                                   1.0, float(E * cfg.CAP), ALU.mult,
                                   ALU.add))
                st(v.tensor_tensor(tm[:, 39:40], tm[:, 38:39], tm[:, 37:38],
                                   ALU.subtract))
                st(v.tensor_tensor(tm[:, 39:40], tm[:, 39:40], tm[:, 36:37],
                                   ALU.mult))
                st(v.tensor_tensor(sb["gdst"][:, j:j + 1], tm[:, 37:38],
                                   tm[:, 39:40], ALU.add))
    env["p6_slots"] = em.v["vp"]

    @em.on("tensor")
    def _(t):
        em.W(t, "vp", env["p6_slots"])
        for a in range(8):
            mmi = t.matmul(ps["s"][0:16, a * 2 * OCH:(a + 1) * 2 * OCH],
                           sb["c_pa"][:, a, :], sb["gdst"][...],
                           start=True, stop=True)
        mmi.then_inc(em.h["tp"], 1)
        env["p6_fold"] = em.inc("tp", 1)

    @em.on("vector")
    def _(v):
        em.W(v, "tp", env["p6_fold"])
        for a in range(8):
            vi = v.tensor_copy(
                sb["tidx"][0:16, :].rearrange("b (g a) -> b g a", a=8)[:, :, a],
                ps["s"][0:16, a * 2 * OCH:(a + 1) * 2 * OCH])
        vi.then_inc(em.h["vp"], 1)
        env["p6_tidx"] = em.inc("vp", 1)

    @em.on("sync")
    def _(sy):
        em.W(sy, "vp", env["p6_tidx"])
        sy.dma_start(out=sb["tidx"][16:32, :],
                     in_=sb["tidx"][0:16, :]).then_inc(em.h["ld"], 16)
        em.inc("ld", 16)
        em.qw(sy, "ld")
    env["p6_trep"] = em.v["ld"]

    @em.on("gpsimd")
    def _(gp):
        em.W(gp, "ld", env["p6_trep"])
        em.W(gp, "gs", env["zeros_done"])
        gp.dma_scatter_add(
            env["TBL"][:, 0:2], sb["pay"][...], sb["tidx"][...],
            num_idxs=cfg.NSLOT, num_idxs_reg=env["regs"][cfg.NSLOT],
            elem_size=2, elem_step=cfg.TROW).then_inc(em.h["gq"], 16)
        em.inc("gq", 16)
    env["p6_scat"] = em.v["gq"]

    @em.on("sync")
    def _(sy):
        em.W(sy, "gq", env["p6_scat"])
        sy.dma_start(out=sb["tb"][...],
                     in_=env["TBL"][0:E * cfg.CAP, :].rearrange(
                         "(g p) c -> p g c", p=128)).then_inc(em.h["ld"], 16)
        em.inc("ld", 16)
        em.qw(sy, "ld")
    env["p6_tb"] = em.v["ld"]

    CC = cfg.CC

    @em.on("vector")
    def _(v):
        v.memset(sb["gidxall"][...], 0).then_inc(em.h["vp"], 1)
        env["p6_giz"] = em.inc("vp", 1)

    for eh in range(0, E, 2):
        @em.on("tensor")
        def _(t, eh=eh):
            em.W(t, "ld", env["p6_tb"])
            if eh >= 2:
                em.W(t, "vp", env[f"p6_gcp_{eh - 2}"])
            for e2 in (eh, eh + 1):
                for a in range(8):
                    mmi = t.matmul(
                        ps["s"][0:16, (e2 % 2) * 256 + a * CC:
                                (e2 % 2) * 256 + (a + 1) * CC],
                        sb["c_pa"][:, a, :],
                        sb["tb"][:, e2 * CC:(e2 + 1) * CC, 0:1].rearrange(
                            "p g c -> p (g c)"),
                        start=True, stop=True)
            mmi.then_inc(em.h["tp"], 1)
            env[f"p6_gf_{eh}"] = em.inc("tp", 1)

        @em.on("vector")
        def _(v, eh=eh):
            em.W(v, "tp", env[f"p6_gf_{eh}"])
            if eh == 0:
                em.W(v, "vp", env["p6_giz"])
            for e2 in (eh, eh + 1):
                for a in range(8):
                    vi = v.tensor_copy(
                        sb["gidxall"][0:16, e2, :].rearrange(
                            "b (g a) -> b g a", a=8)[:, :, a],
                        ps["s"][0:16, (e2 % 2) * 256 + a * CC:
                                (e2 % 2) * 256 + (a + 1) * CC])
            vi.then_inc(em.h["vp"], 1)
            env[f"p6_gcp_{eh}"] = em.inc("vp", 1)

    @em.on("sync")
    def _(sy):
        em.W(sy, "vp", env[f"p6_gcp_{E - 2}"])
        sy.dma_start(out=sb["gidxall"][16:32, :, :],
                     in_=sb["gidxall"][0:16, :, :]).then_inc(em.h["ld"], 16)
        em.inc("ld", 16)
        em.qw(sy, "ld")
    env["p6_gall"] = em.v["ld"]


def phase_experts(env):
    for e in range(E):
        _p7_expert(env, e)


def _p7_expert(env, e):
    em, sb, ps, cfg = env["em"], env["sb"], env["ps"], env["cfg"]
    CAP, CC = cfg.CAP, cfg.CC
    xeT = sb["xeT"] if e % 2 == 0 else sb["xeT2"]
    gidx = sb["gidxall"][:, e, :]
    gsem = "qa" if e % 2 == 0 else "qb"

    @em.on("gpsimd")
    def _(gp):
        em.W(gp, "ld", env["p6_gall"])
        em.W(gp, "scr", env[f"p5_xpst_{cfg.OCH - 1}"])
        if e >= 2:
            em.W(gp, "tp", env[f"p7_updone_{e - 2}"])
        gp.dma_gather(xeT[...], env["XPH"], gidx,
                      num_idxs=CAP, num_idxs_reg=env["regs"][CAP], elem_size=D,
                      elem_step=D, transpose=True).then_inc(em.h[gsem], 16)
        env[f"p7_gath_{e}"] = (gsem, em.inc(gsem, 16))

    NP = 11
    for pi in range(NP):
        wub = sb["wupg"] if pi % 2 == 0 else sb["wupg2"]

        wsem = "wsa" if pi % 2 == 0 else "wsb"

        @em.on("sync")
        def _(sy, pi=pi, wub=wub, wsem=wsem):
            if e == 0 and pi < 2:
                em.W(sy, "gs", env["ecast_done"])
            guard = env.get(f"p7_up_{e}_{pi - 2}",
                            env.get(f"p7_updone_{e - 1}", 0))
            em.W(sy, "tp", guard)
            sy.dma_start(out=wub[:, :, 0:128],
                         in_=env["WUGB"][e, :, pi * 128:(pi + 1) * 128]
                         .rearrange("(k p) m -> p k m", p=128)).then_inc(
                em.h[wsem], 16)
            em.inc(wsem, 16)
            sy.dma_start(out=wub[:, :, 128:256],
                         in_=env["WUGB"][e, :,
                                         (NP + pi) * 128:(NP + pi + 1) * 128]
                         .rearrange("(k p) m -> p k m", p=128)).then_inc(
                em.h[wsem], 16)
            em.inc(wsem, 16)
        wld = (wsem, em.v[wsem])

        bset = (["q0", "q1", "q2", "q3"] if pi % 2 == 0
                else ["d0", "d1", "s", "e"])

        @em.on("tensor")
        def _(t, pi=pi, wub=wub, bset=bset):
            em.W(t, wld[0], wld[1])
            if pi == 0:
                em.W(t, env[f"p7_gath_{e}"][0], env[f"p7_gath_{e}"][1])
            if pi < 2:
                if e >= 1:
                    em.W(t, "vp", env[f"p7_dnev_{e - 1}_3_{CC - 1}"])
            if pi >= 2:
                em.W(t, "vp", env[f"p7_upev_{e}_{pi - 2}"])
            slc = ([(0, 512), (512, CAP)] if CAP > 512 else [(0, CAP)])
            for si, (a, b) in enumerate(slc):
                for k in range(16):
                    t.matmul(ps[bset[si]][:, 0:b - a], wub[:, k, 0:128],
                             xeT[:, k, a:b], start=(k == 0), stop=(k == 15))
            for si, (a, b) in enumerate(slc):
                for k in range(16):
                    mmi = t.matmul(ps[bset[2 + si]][:, 0:b - a],
                                   wub[:, k, 128:256], xeT[:, k, a:b],
                                   start=(k == 0), stop=(k == 15))
            mmi.then_inc(em.h["tp"], 1)
            env[f"p7_up_{e}_{pi}"] = em.inc("tp", 1)

        @em.on("scalar")
        def _(s, pi=pi, bset=bset):
            em.W(s, "tp", env[f"p7_up_{e}_{pi}"])
            fofs = 0 if pi % 2 == 0 else 1024
            slc = ([(0, 512), (512, CAP)] if CAP > 512 else [(0, CAP)])
            for si, (a, b) in enumerate(slc):
                si_i = s.activation(sb["fS2"][:, fofs + a:fofs + b],
                                    ps[bset[si]][:, 0:b - a], AF.Silu)
            si_i.then_inc(em.h["sp"], 1)
            env[f"p7_silu_{e}_{pi}"] = em.inc("sp", 1)

        @em.on("vector")
        def _(v, pi=pi, bset=bset):
            em.W(v, "sp", env[f"p7_silu_{e}_{pi}"])
            fofs = 0 if pi % 2 == 0 else 1024
            slc = ([(0, 512), (512, CAP)] if CAP > 512 else [(0, CAP)])
            for si, (a, b) in enumerate(slc):
                vi = v.tensor_tensor(sb["hT"][:, pi, a:b],
                                     sb["fS2"][:, fofs + a:fofs + b],
                                     ps[bset[2 + si]][:, 0:b - a], ALU.mult)
            vi.then_inc(em.h["vp"], 1)
            env[f"p7_upev_{e}_{pi}"] = em.inc("vp", 1)
    env[f"p7_updone_{e}"] = em.v["tp"]

    dbanks = ["q0", "q1", "q2", "q3", "d0"]
    for n in range(4):
        wdb = sb["wdq"] if n % 2 == 0 else sb["wdq2"]
        dsem = "wla" if n % 2 == 0 else "wlb"

        @em.on("sync")
        def _(sy, n=n, wdb=wdb, dsem=dsem):
            if e == 0 and n < 2:
                em.W(sy, "gs", env["ecast_done"])
            guard = env.get(f"p7_dnmm_{e}_{n - 2}",
                            env.get(f"p7_dndone_{e - 1}", 0))
            em.W(sy, "tp", guard)
            sy.dma_start(out=wdb[...],
                         in_=env["WDNB"][e, :, n * 512:(n + 1) * 512]
                         .rearrange("(j p) d -> p j d", p=128)).then_inc(
                em.h[dsem], 16)
            em.inc(dsem, 16)
        dld = (dsem, em.v[dsem])

        @em.on("tensor")
        def _(t, n=n, wdb=wdb, dld=dld):
            em.W(t, dld[0], dld[1])
            if n == 0:
                em.W(t, "vp", env[f"p7_upev_{e}_{10}"])
            else:
                em.W(t, "vp", env[f"p7_dnev_{e}_{n - 1}_{CC - 1}"])
            for j in range(11):
                for cc in range(CC):
                    mmi = t.matmul(ps[dbanks[cc]][:, 0:512],
                                   sb["hT"][:, j, cc * 128:(cc + 1) * 128],
                                   wdb[:, j, :],
                                   start=(j == 0), stop=(j == 10))
            mmi.then_inc(em.h["tp"], 1)
            env[f"p7_dnmm_{e}_{n}"] = em.inc("tp", 1)

        @em.on("vector")
        def _(v, n=n):
            em.W(v, "tp", env[f"p7_dnmm_{e}_{n}"])
            if n == 0:
                prev = env.get(f"p7_scat_{e - 1}_{CC - 1}")
                if prev is not None:
                    em.W(v, "sca", prev)
            for cc in range(CC):
                vi = v.tensor_scalar(sb["stg"][:, cc, n * 512:(n + 1) * 512],
                                     ps[dbanks[cc]][:, 0:512],
                                     sb["tb"][:, e * CC + cc, 1:2], None,
                                     ALU.mult)
                vi.then_inc(em.h["vp"], 1)
                env[f"p7_dnev_{e}_{n}_{cc}"] = em.inc("vp", 1)

    for cc in range(CC):
        @em.on("gpsimd")
        def _(gp, cc=cc):
            em.W(gp, "vp", env[f"p7_dnev_{e}_3_{cc}"])
            em.W(gp, "sca", em.v["sca"])
            gp.dma_scatter_add(
                env["RACC"],
                sb["stg"][:, cc, :].rearrange("p (g d) -> p g d", g=1),
                gidx[:, cc * 8:(cc + 1) * 8],
                num_idxs=128,
                num_idxs_reg=env["regs"][128],
                elem_size=D,
                elem_step=D).then_inc(em.h["sca"], 16)
            env[f"p7_scat_{e}_{cc}"] = em.inc("sca", 16)
    env[f"p7_dndone_{e}"] = em.v["tp"]
    env["p7_allscat"] = [("sca", em.v["sca"])]


def phase_shared_epilogue(env):
    for sg in range(env["cfg"].NSHG):
        _p8_group(env, sg)


def _p8_group(env, sg):
    em, sb, ps, cfg = env["em"], env["sb"], env["ps"], env["cfg"]
    SHG = cfg.SHG
    g0 = sg * SHG
    NS = SHG // 512

    @em.on("gpsimd")
    def _(gp):
        em.W(gp, "scr", env[f"p5_xpst_{cfg.OCH - 1}"])
        if sg == 0:
            for sn, sv in env["p7_allscat"]:
                em.W(gp, sn, sv)
            em.W(gp, "tp", env[f"p7_dndone_{E - 1}"])
        else:
            em.W(gp, "tp", env[f"p8_updone_{sg - 1}"])
        gp.dma_gather(sb["shxT"][...], env["XPH"],
                      sb["c_gidxsh"][:, g0 // 16:(g0 + SHG) // 16],
                      num_idxs=SHG, num_idxs_reg=env["regs"][SHG],
                      elem_size=D,
                      elem_step=D, transpose=True).then_inc(em.h["gq"], 16)
        env[f"p8_gath_{sg}"] = em.inc("gq", 16)

    for pi in range(44):
        wub = sb["wupg"] if pi % 2 == 0 else sb["wupg2"]

        wsem = "wsa" if pi % 2 == 0 else "wsb"

        @em.on("sync")
        def _(sy, pi=pi, wub=wub, wsem=wsem):
            if pi < 2:
                em.W(sy, "gs", env["wcast_done"])
                em.W(sy, "tp", env[f"p7_updone_{E - 1}"] if sg == 0
                     else env[f"p8_dndone_{sg - 1}"])
            else:
                em.W(sy, "tp", env[f"p8_up_{sg}_{pi - 2}"])
            sy.dma_start(out=wub[:, :, 0:128],
                         in_=env["WSUB"][:, pi * 128:(pi + 1) * 128]
                         .rearrange("(k p) m -> p k m", p=128)).then_inc(
                em.h[wsem], 16)
            em.inc(wsem, 16)
            sy.dma_start(out=wub[:, :, 128:256],
                         in_=env["WSUB"][:, (44 + pi) * 128:(45 + pi) * 128]
                         .rearrange("(k p) m -> p k m", p=128)).then_inc(
                em.h[wsem], 16)
            em.inc(wsem, 16)
        wld = (wsem, em.v[wsem])

        bset = (["q0", "q1", "q2", "q3"] if pi % 2 == 0
                else ["d0", "d1", "s", "e"])

        @em.on("tensor")
        def _(t, pi=pi, wub=wub, bset=bset):
            em.W(t, wld[0], wld[1])
            if pi == 0:
                em.W(t, "gq", env[f"p8_gath_{sg}"])
            if pi < 2:
                em.W(t, "vp", env.get("p8_evlast",
                                      env[f"p7_dnev_{E - 1}_3_{env["cfg"].CC - 1}"]))
            if pi >= 2:
                em.W(t, "vp", env[f"p8_upev_{sg}_{pi - 2}"])
            for ns in range(NS):
                for k in range(16):
                    t.matmul(ps[bset[0] if ns == 0 else bset[1]][:, 0:512],
                             wub[:, k, 0:128],
                             sb["shxT"][:, k, ns * 512:(ns + 1) * 512],
                             start=(k == 0), stop=(k == 15))
            for ns in range(NS):
                for k in range(16):
                    mmi = t.matmul(ps[bset[2] if ns == 0 else bset[3]][:, 0:512],
                                   wub[:, k, 128:256],
                                   sb["shxT"][:, k, ns * 512:(ns + 1) * 512],
                                   start=(k == 0), stop=(k == 15))
            mmi.then_inc(em.h["tp"], 1)
            env[f"p8_up_{sg}_{pi}"] = em.inc("tp", 1)

        @em.on("scalar")
        def _(s, pi=pi, bset=bset):
            em.W(s, "tp", env[f"p8_up_{sg}_{pi}"])
            fofs = 0 if pi % 2 == 0 else 1024
            si_i = s.activation(sb["fS2"][:, fofs:fofs + 512],
                                ps[bset[0]][:, 0:512], AF.Silu)
            if NS > 1:
                si_i = s.activation(sb["fS2"][:, fofs + 512:fofs + 1024],
                                    ps[bset[1]][:, 0:512], AF.Silu)
            si_i.then_inc(em.h["sp"], 1)
            env[f"p8_silu_{sg}_{pi}"] = em.inc("sp", 1)

        @em.on("vector")
        def _(v, pi=pi, bset=bset):
            em.W(v, "sp", env[f"p8_silu_{sg}_{pi}"])
            fofs = 0 if pi % 2 == 0 else 1024
            vi = v.tensor_tensor(sb["shT"][:, pi, 0:512],
                                 sb["fS2"][:, fofs:fofs + 512],
                                 ps[bset[2]][:, 0:512], ALU.mult)
            if NS > 1:
                vi = v.tensor_tensor(sb["shT"][:, pi, 512:1024],
                                     sb["fS2"][:, fofs + 512:fofs + 1024],
                                     ps[bset[3]][:, 0:512], ALU.mult)
            vi.then_inc(em.h["vp"], 1)
            env[f"p8_upev_{sg}_{pi}"] = em.inc("vp", 1)
    env[f"p8_updone_{sg}"] = em.v["tp"]

    banks = ["q0", "q1", "q2", "q3"]
    NCB = (SHG // 128) // 4
    for ccb in range(NCB):
        for dq in range(4):
            for j in range(44):
                gi = (ccb * 4 + dq) * 44 + j
                wdb = sb["wdg"] if gi % 2 == 0 else sb["wdg2"]

                wsem = "wsa" if gi % 2 == 0 else "wsb"

                @em.on("sync")
                def _(sy, gi=gi, dq=dq, j=j, wdb=wdb, wsem=wsem):
                    if gi < 2:
                        em.W(sy, "tp", env[f"p8_updone_{sg}"])
                    else:
                        em.W(sy, "tp", env[f"p8_dn_{sg}_{gi - 2}"])
                    sy.dma_start(out=wdb[...],
                                 in_=env["WSDB"][j * 128:(j + 1) * 128,
                                                 dq * 512:(dq + 1) * 512]
                                 ).then_inc(em.h[wsem], 16)
                    em.inc(wsem, 16)
                wld = (wsem, em.v[wsem])

                @em.on("tensor")
                def _(t, gi=gi, ccb=ccb, dq=dq, j=j, wdb=wdb):
                    em.W(t, wld[0], wld[1])
                    if j == 0:
                        if dq == 0 and ccb == 0:
                            em.W(t, "vp", env[f"p8_upev_{sg}_{43}"])
                        if "p8_evlast" in env:
                            em.W(t, "vp", env["p8_evlast"])
                    for cc4 in range(4):
                        cc = ccb * 4 + cc4
                        mmi = t.matmul(ps[banks[cc4]][:, 0:512],
                                       sb["shT"][:, j, cc * 128:(cc + 1) * 128],
                                       wdb[...], start=(j == 0),
                                       stop=(j == 43))
                    mmi.then_inc(em.h["tp"], 1)
                    env[f"p8_dn_{sg}_{gi}"] = em.inc("tp", 1)

            for cc4 in range(4):
                cc = ccb * 4 + cc4
                rows = g0 + cc * 128

                @em.on("sync")
                def _(sy, cc4=cc4, rows=rows, dq=dq):
                    if "p8_rowused" not in env:
                        for sn, sv in env["p7_allscat"]:
                            em.W(sy, sn, sv)
                        em.W(sy, "scr", env[f"p5_xpst_{cfg.OCH - 1}"])
                    em.W(sy, "vp", env.get("p8_rowused", 0))
                    sy.dma_start(out=sb["xprow"][...],
                                 in_=env["XP"][rows:rows + 128,
                                               dq * 512:(dq + 1) * 512]
                                 ).then_inc(em.h["ld"], 16)
                    em.inc("ld", 16)
                    sy.dma_start(out=sb["raccrow"][...],
                                 in_=env["RACC"][rows:rows + 128,
                                                 dq * 512:(dq + 1) * 512]
                                 ).then_inc(em.h["ld"], 16)
                    em.inc("ld", 16)
                    em.qw(sy, "ld")
                rowld = em.v["ld"]

                @em.on("vector")
                def _(v, cc4=cc4, dq=dq, ccb=ccb, rowld=rowld):
                    em.W(v, "ld", rowld)
                    em.W(v, "tp", env[f"p8_dn_{sg}_{(ccb * 4 + dq) * 44 + 43}"])
                    em.W(v, "sp", env.get("p8_lnsq", 0))
                    v.tensor_tensor(sb["xprow"][...], sb["xprow"][...],
                                    sb["raccrow"][...],
                                    ALU.add).then_inc(em.h["vp"], 1)
                    v.wait_ge(em.h["vp"], em.inc("vp", 1))
                    vi = v.tensor_tensor(
                        sb["sacc"][:, cc4, dq * 512:(dq + 1) * 512],
                        ps[banks[cc4]][:, 0:512], sb["xprow"][...], ALU.add)
                    vi.then_inc(em.h["vp"], 1)
                    env["p8_rowused"] = em.inc("vp", 1)
                    if cc4 == 3:
                        env["p8_evlast"] = env["p8_rowused"]

        for cc4 in range(4):
            cc = ccb * 4 + cc4
            rows = g0 + cc * 128
            ln_stats_apply(env, sb["sacc"][:, cc4, :], sb["orow"][...], 16,
                           scr_name="scrD",
                           pre=("ost", env.get("p8_ost", 0)),
                           src_ready=("vp", env["p8_rowused"]))
            env["p8_lnsq"] = env["ln_s0"]

            @em.on("sync")
            def _(sy, rows=rows, od=env["ln_done"]):
                em.W(sy, "vp", od)
                sy.dma_start(out=env["out"][rows:rows + 128, :],
                             in_=sb["orow"][...]).then_inc(em.h["ost"], 16)
                env["p8_ost"] = em.inc("ost", 16)
                em.qw(sy, "ost")
    env[f"p8_dndone_{sg}"] = em.v["tp"]


def phase_finish(env, upto):
    em, cfg, sb = env["em"], env["cfg"], env["sb"]
    dbg = env["dbg"]

    @em.on("sync")
    def _(sy):
        if "x1" in dbg:
            em.W(sy, "scr", env[f"p5_xpst_{cfg.OCH - 1}"])
            for i in range(cfg.T // 128):
                sy.dma_start(out=dbg["x1"][i * 128:(i + 1) * 128, :],
                             in_=env["XP"][i * 128:(i + 1) * 128, :]
                             ).then_inc(em.h["ost"], 16)
                em.inc("ost", 16)
        if "part" in dbg:
            em.W(sy, "scr", env[f"p1_scr_{cfg.NSC - 1}"])
            for i in range(cfg.S // 128):
                sy.dma_start(out=dbg["part"][i * 128:(i + 1) * 128, :],
                             in_=env["PART"][i * 128:(i + 1) * 128, :]
                             ).then_inc(em.h["ost"], 16)
                em.inc("ost", 16)
        if "sqt" in dbg:
            em.W(sy, "scr", env[f"p1_scr_{cfg.NSC - 1}"])
            for h in range(H_PER):
                sy.dma_start(out=dbg["sqt"][h * 128:(h + 1) * 128, :],
                             in_=env["SQT"][h, :, :]).then_inc(
                    em.h["ost"], 16)
                em.inc("ost", 16)
        if "agout" in dbg:
            em.W(sy, "cc", env["a2a_done"])
            for i in range(2 * cfg.S // 128):
                sy.dma_start(out=dbg["agout"][i * 128:(i + 1) * 128, :],
                             in_=env["agout"].ap()[i * 128:(i + 1) * 128, :]
                             ).then_inc(em.h["ost"], 16)
                em.inc("ost", 16)
        if "rstash" in dbg:
            em.W(sy, "vp", env.get("p5_rtdone", 0))
            sy.dma_start(out=dbg["rstash"][...],
                         in_=sb["rstash"][...].rearrange("p o c -> p (o c)")
                         ).then_inc(em.h["ost"], 16)
            em.inc("ost", 16)
        if "racc" in dbg:
            for sn, sv in env.get("p7_allscat", []):
                em.W(sy, sn, sv)
            for i in range(cfg.T // 128):
                sy.dma_start(out=dbg["racc"][i * 128:(i + 1) * 128, :],
                             in_=env["RACC"][i * 128:(i + 1) * 128, :]
                             ).then_inc(em.h["ost"], 16)
                em.inc("ost", 16)
        if "tblv" in dbg:
            em.W(sy, "gq", env.get("p6_scat", 0))
            n = dbg["tblv"].shape[0]
            sy.dma_start(out=dbg["tblv"],
                         in_=env["TBL"][0:n, :]).then_inc(em.h["ost"], 16)
            em.inc("ost", 16)
        em.W(sy, "ost", em.v["ost"])
        em.W(sy, "scr", em.v["scr"])

    @em.on("gpsimd")
    def _(gp):
        em.W(gp, "gq", em.v["gq"])
        em.W(gp, "cc", em.v["cc"])


def _gidxwo(cfg, hh):
    cols = np.zeros((128, cfg.OCH * 16), np.int16)
    for oc in range(cfg.OCH):
        ids = []
        for p in range(128):
            ids.append(hh * cfg.T + oc * 128 + p)
        for p in range(128):
            ids.append(cfg.S + hh * cfg.T + oc * 128 + p)
        cols[:, oc * 16:(oc + 1) * 16] = _wrap16(ids)[0:128]
    return cols


def _prep_inputs(cfg, inputs, core, consts):
    b, hh = core // 2, core % 2
    h0 = hh * H_PER
    x = np.asarray(inputs["L0_attn_hidden_states"])
    wq = np.asarray(inputs["L0_attn_w_q"])
    wk = np.asarray(inputs["L0_attn_w_k"])
    wv = np.asarray(inputs["L0_attn_w_v"])
    wo = np.asarray(inputs["L0_attn_w_o"])
    mems = np.asarray(inputs["L0_attn_memories"])
    nrms = np.asarray(inputs["L0_attn_memory_norms"])
    im = {
        "xb": np.ascontiguousarray(x[b]),
        "xown": np.ascontiguousarray(x[b, hh * cfg.T:(hh + 1) * cfg.T]),
        "wqT": np.ascontiguousarray(wq[h0 * HD:(h0 + H_PER) * HD, :].T),
        "wkT": np.ascontiguousarray(wk[h0 * HD:(h0 + H_PER) * HD, :].T),
        "wvT": np.ascontiguousarray(wv[h0 * HD:(h0 + H_PER) * HD, :].T),
        "woT": np.ascontiguousarray(wo.T),
        "memi": np.ascontiguousarray(
            mems[:, h0:h0 + H_PER].reshape(M * H_PER, HD * HD)),
        "nrmT": np.ascontiguousarray(
            nrms[:, h0:h0 + H_PER].transpose(1, 2, 0)),
        "nrmF": np.ascontiguousarray(nrms.reshape(M * 16, HD)),
        "gatei": np.ascontiguousarray(
            np.asarray(inputs["L0_attn_gate"])[h0:h0 + H_PER][None, :]),
        "routeri": np.ascontiguousarray(inputs["L0_ffn_router"]),
        "wug": np.asarray(inputs["L0_ffn_w_up_gate"]),
        "wdn": np.asarray(inputs["L0_ffn_w_down"]),
        "wsu": np.asarray(inputs["L0_ffn_shared_w_up_gate"]),
        "wsd": np.asarray(inputs["L0_ffn_shared_w_down"]),
        "prior_sel": (np.arange(8) < core).astype(np.float32)[:, None],
        "gidxwo": _gidxwo(cfg, hh),
    }
    for n, v in consts.items():
        im["cin_" + n] = v
    return im


def kernel(**inputs):
    cfg = Cfg()
    nc = build_program(cfg)
    consts = make_consts(cfg)
    in_maps = [_prep_inputs(cfg, inputs, c, consts) for c in range(8)]
    res = run_bass_kernel_spmd(nc, in_maps, core_ids=list(range(8)))
    full = np.empty((4, 4096, 2048), np.float32)
    for c in range(8):
        b, hh = c // 2, c % 2
        full[b, hh * cfg.T:(hh + 1) * cfg.T, :] = res.results[c]["out"]
    return full


# version 04
# speedup vs baseline: 1.2977x; 1.2977x over previous
import math
from contextlib import ExitStack

import numpy as np
import ml_dtypes

import concourse.bass as bass
import concourse.mybir as mybir
from concourse import library_config
from concourse.library_overlay import lower_extended_insts
from concourse.aot_env import allow_ici_all_to_all
from concourse.bass_utils import run_bass_kernel_spmd

F32 = mybir.dt.float32
BF16 = mybir.dt.bfloat16
I16 = mybir.dt.int16
AF = mybir.ActivationFunctionType
ALU = mybir.AluOpType
AX = mybir.AxisListType
BF = ml_dtypes.bfloat16

D = 2048
H_PER = 8
HD = 128
HC = H_PER * HD
M = 4
SCALE = HD ** (-0.5)
EPS = 1e-6
LN_EPS = 1e-5
E = 8
I_FF = 1408
J_FF = 5632
NEG_BIG = -1.0e30


class Cfg:
    def __init__(self, seq=4096):
        self.S = seq
        self.T = seq // 2
        self.NCH = self.S // 128
        self.NSC = self.S // 512
        self.OCH = self.T // 128
        cap = int(math.ceil(self.T * 2 * 1.25 / E))
        self.CAP = ((cap + 127) // 128) * 128
        self.CC = self.CAP // 128
        self.CREF = int(math.ceil(8 * self.T * 2 / E * 1.25))
        self.NSLOT = self.T * 2
        self.TROW = 64
        self.NTBL = E * self.CAP + self.NSLOT
        self.SHG = min(512, self.T)
        self.NSHG = self.T // self.SHG


def _wrap16(vals, rows=128):
    n = len(vals)
    cols = (n + 15) // 16
    out = np.zeros((rows, cols), np.int16)
    for i, v in enumerate(vals):
        out[i % 16, i // 16] = v
    out[16:32] = out[0:16]
    return out


def make_consts(cfg):
    c = {}
    c["ident"] = np.eye(128, dtype=BF)
    c["identf"] = np.eye(128, dtype=np.float32)
    c["triu"] = np.triu(np.ones((128, 128), np.float32), 1).astype(BF)
    mt = np.triu(np.ones((128, 128), np.float32), 0)
    c["maskT"] = np.tile(mt, (1, H_PER)).astype(BF)
    c["onesb"] = np.ones((128, 1), dtype=BF)
    c["onesrow"] = np.ones((1, 128), np.float32)
    pa = np.zeros((128, 8, 16), np.float32)
    for a in range(8):
        for b in range(16):
            pa[16 * a + b, a, b] = 1.0
    c["pa"] = pa
    c["iotae"] = np.tile(np.arange(8, dtype=np.float32)[None, :], (128, 1))
    c["iotat"] = (np.arange(cfg.OCH, dtype=np.float32)[None, :] * 128
                  + np.arange(128, dtype=np.float32)[:, None])
    c["iotas"] = (np.arange(2 * cfg.OCH, dtype=np.float32)[None, :] * 128
                  + np.arange(128, dtype=np.float32)[:, None])
    c["gidxsh"] = _wrap16(np.arange(cfg.T, dtype=np.int16))
    sel = np.zeros((M * 16, M), np.float32)
    for m in range(M):
        sel[m * 16:(m + 1) * 16, m] = 1.0
    c["esel"] = sel
    return c


def const_specs(c):
    return [
        ("ident", [128, 128], BF16), ("identf", [128, 128], F32),
        ("triu", [128, 128], BF16),
        ("maskT", [128, H_PER * 128], BF16), ("onesb", [128, 1], BF16),
        ("onesrow", [1, 128], F32), ("pa", [128, 8, 16], F32),
        ("iotae", [128, 8], F32), ("iotat", [128, c.OCH], F32),
        ("iotas", [128, 2 * c.OCH], F32), ("gidxsh", [128, c.T // 16], I16),
        ("esel", [M * 16, M], F32),
    ]


class EM:

    def __init__(self, nc, block, stack, names):
        self.nc = nc
        self.block = block
        self.h = {}
        self.v = {}
        for n in names:
            self.h[n] = stack.enter_context(nc.semaphore("s_" + n))
            self.v[n] = 0

    def inc(self, n, by=1):
        self.v[n] += by
        return self.v[n]

    def W(self, eng, n, val):
        assert val <= self.v[n], f"wait on future {n}: {val} > {self.v[n]}"
        if val > 0:
            eng.wait_ge(self.h[n], val)

    def on(self, engine):
        def deco(fn):
            getattr(self.block, engine)(fn)
        return deco

    def qw(self, eng, n):
        if self.v[n] > 0:
            eng.wait_ge(self.h[n], self.v[n])


SEMS = ["ld", "wl", "scr", "tp", "vp", "sp", "gq", "cc", "ost", "bar",
        "gl", "gs", "wla", "wlb", "qa", "qb", "sca", "scb", "mk",
        "wsa", "wsb"]


def full_barrier(env):
    em = env["em"]
    tgt = em.v["bar"] + 5
    dmas = {k: em.v[k] for k in ["scr", "ld", "wl", "ost", "gq", "cc", "gl",
                                 "gs", "wla", "wlb", "qa", "qb", "sca",
                                 "scb", "wsa", "wsb"]}
    prods = {k: em.v[k] for k in ["tp", "vp", "sp"]}

    @em.on("sync")
    def _(sy):
        for k, v in dmas.items():
            em.W(sy, k, v)
        for k, v in prods.items():
            em.W(sy, k, v)
        sy.sem_inc(em.h["bar"], 1)
        sy.wait_ge(em.h["bar"], tgt)

    for name in ["gpsimd", "tensor", "vector", "scalar"]:
        @em.on(name)
        def _(eng):
            eng.sem_inc(em.h["bar"], 1)
            eng.wait_ge(em.h["bar"], tgt)
    em.v["bar"] = tgt


def build_program(cfg, dbg=(), upto=99):
    nc = bass.Bass("TRN2", target_bir_lowering=False)
    S, T, OCH, NSC, CAP, CC = cfg.S, cfg.T, cfg.OCH, cfg.NSC, cfg.CAP, cfg.CC

    ein = lambda n, s, d: nc.dram_tensor(n, s, d, kind="ExternalInput").ap()
    env = {"cfg": cfg, "nc": nc}
    env["xb"] = ein("xb", [S, D], F32)
    env["xown"] = ein("xown", [T, D], F32)
    env["wqT"] = ein("wqT", [D, HC], F32)
    env["wkT"] = ein("wkT", [D, HC], F32)
    env["wvT"] = ein("wvT", [D, HC], F32)
    env["woT"] = ein("woT", [D, D], F32)
    env["memi"] = ein("memi", [M * H_PER, HD * HD], F32)
    env["nrmT"] = ein("nrmT", [H_PER, HD, M], F32)
    env["nrmF"] = ein("nrmF", [M * 16, HD], F32)
    env["gatei"] = ein("gatei", [1, H_PER], F32)
    env["routeri"] = ein("routeri", [D, E], F32)
    env["wug"] = ein("wug", [E, D, 2 * I_FF], F32)
    env["wdn"] = ein("wdn", [E, I_FF, D], F32)
    env["wsu"] = ein("wsu", [D, 2 * J_FF], F32)
    env["wsd"] = ein("wsd", [J_FF, D], F32)
    env["prior_sel"] = ein("prior_sel", [8, 1], F32)
    env["gidxwo"] = ein("gidxwo", [128, cfg.OCH * 16], I16)
    cst_sp = const_specs(cfg)
    env["cst"] = {n: ein("cin_" + n, s, d) for n, s, d in cst_sp}
    env["out"] = nc.dram_tensor("out", [T, D], F32, kind="ExternalOutput").ap()
    env["dbg"] = {}
    for name, shape, dt in dbg:
        env["dbg"][name] = nc.dram_tensor("dbg_" + name, shape, dt,
                                          kind="ExternalOutput").ap()

    env["XBH"] = nc.dram_tensor("XBH", [S, D], BF16).ap()
    env["SQT"] = nc.dram_tensor("SQT", [H_PER, HD, S], BF16).ap()
    env["PART"] = nc.dram_tensor("PART", [S, HC], BF16).ap()
    env["XP"] = nc.dram_tensor("XP", [T, D], F32).ap()
    env["XPH"] = nc.dram_tensor("XPH", [T, D], BF16).ap()
    env["TBL"] = nc.dram_tensor("TBL", [cfg.NTBL, cfg.TROW], F32).ap()
    env["RACC"] = nc.dram_tensor("RACC", [T, D], F32).ap()
    env["WSUB"] = nc.dram_tensor("WSUB", [D, 2 * J_FF], BF16).ap()
    env["WSDB"] = nc.dram_tensor("WSDB", [J_FF, D], BF16).ap()
    env["agin"] = nc.dram_tensor("agin", [S, HC], BF16)
    env["agout"] = nc.dram_tensor("agout", [2 * S, HC], BF16)
    env["cnt_in"] = nc.dram_tensor("cnt_in", [1, E], F32)
    env["cnt_out"] = nc.dram_tensor("cnt_out", [8, E], F32)

    with ExitStack() as stack:
        with nc.Block() as block:
            em = EM(nc, block, stack, SEMS)
            env["em"] = em
            sb = {}
            env["sb"] = sb

            def sbuf(name, shape, dt, st=stack):
                t = st.enter_context(nc.sbuf_tensor(name, shape, dt))
                sb[name] = t
                return t
            env["sbuf"] = sbuf

            for n, s, d in cst_sp:
                sbuf("c_" + n, s, d)
            for n, s, d in [
                ("nrmTf", [128, H_PER, M], F32), ("nrmTb", [128, H_PER, M], BF16),
                ("nrmFs", [M * 16, HD], F32), ("grow", [1, H_PER], F32),
                ("g1m", [128, H_PER], F32), ("weff", [128, H_PER * M], F32),
                ("relacc", [128, H_PER], F32), ("relaccb", [128, H_PER], BF16),
                ("priorS", [8, 1], F32), ("priorrow", [1, E], F32),
                ("priorrep", [128, E], F32), ("routerS", [128, 16, E], F32),
                ("zero", [128, 512], BF16),
                ("S", [128, HC], F32), ("Sb", [128, HC], BF16),
                ("Z", [128, H_PER], F32), ("Zb", [128, H_PER], BF16),
                ("rstash", [128, OCH, 8], F32),
                ("ctot", [OCH, E], F32), ("ctotb", [OCH, E], BF16),
                ("ctotrow", [1, OCH * E], F32),
                ("cprerow", [1, 128], F32), ("cnt", [8, E], F32),
                ("tmp", [128, 64], F32), ("tmpb", [128, 64], BF16),
                ("rel", [16, 40], F32),
                ("pay", [128, 2 * OCH, 2], F32), ("gdst", [128, 2 * OCH], F32),
                ("tidx", [128, (2 * OCH * 128) // 16], I16),
                ("xstage", [128, 2048], BF16), ("xstage2", [128, 2048], BF16),
            ]:
                sbuf(n, s, d)

            ps = {}
            env["ps"] = ps
            for nm in ["q0", "q1", "q2", "q3", "d0", "d1", "s"]:
                ps[nm] = stack.enter_context(
                    nc.psum_tensor("ps" + nm, [128, 512], F32))
            env["regs"] = {}
            for rv in sorted({256, 128, cfg.CAP, cfg.NSLOT, cfg.SHG}):
                env["regs"][rv] = stack.enter_context(
                    nc.gpsimd.register(f"rn{rv}"))

            phase_setup(env)
            if upto >= 1:
                with ExitStack() as stB:
                    ps["t"] = stB.enter_context(
                        nc.psum_tensor("pst", [128, 1024], BF16))
                    alloc_scopeB(env, stB)
                    with ExitStack() as stB1:
                        alloc_scopeB1(env, stB1)
                        phase_p0b(env)
                        phase_setup_bulk(env)
                        phase_attn_pass1(env)
                        if upto >= 2:
                            phase_rel(env)
                    if upto >= 3:
                        with ExitStack() as stB2:
                            alloc_scopeB2(env, stB2)
                            phase_attn_pass2(env)
                    del ps["t"]
            if upto >= 5:
                full_barrier(env)
                with ExitStack() as stC:
                    env["_stC"] = stC
                    alloc_scopeC(env, stC)
                    phase_wo_ln1_routing(env)
            if upto >= 6:
                full_barrier(env)
                with ExitStack() as stD:
                    ps["e"] = stD.enter_context(
                        nc.psum_tensor("pse", [128, 512], F32))
                    alloc_scopeD(env, stD)
                    phase_routing_finalize(env)
                    if upto >= 7:
                        with ExitStack() as stD1:
                            alloc_scopeD1(env, stD1)
                            phase_experts(env)
                    if upto >= 8:
                        with ExitStack() as stD2:
                            alloc_scopeD2(env, stD2)
                            phase_shared_epilogue(env)
            phase_finish(env, upto)

    lower_extended_insts(nc)
    return nc


def phase_setup(env):
    em, sb, cfg, cst = env["em"], env["sb"], env["cfg"], env["cst"]
    S, T = cfg.S, cfg.T

    @em.on("sync")
    def _(sy):
        for n, _s, _d in const_specs(cfg):
            sy.dma_start(out=sb["c_" + n][...], in_=cst[n]).then_inc(
                em.h["ld"], 16)
            em.inc("ld", 16)
        sy.dma_start(out=sb["nrmTf"][...],
                     in_=env["nrmT"].rearrange("h d m -> d h m")).then_inc(
            em.h["ld"], 16)
        em.inc("ld", 16)
        sy.dma_start(out=sb["nrmFs"][...], in_=env["nrmF"]).then_inc(
            em.h["ld"], 16)
        em.inc("ld", 16)
        sy.dma_start(out=sb["priorS"][...], in_=env["prior_sel"]).then_inc(
            em.h["ld"], 16)
        em.inc("ld", 16)
        sy.dma_start(out=sb["routerS"][...],
                     in_=env["routeri"].rearrange("(k p) e -> p k e", p=128)
                     ).then_inc(em.h["ld"], 16)
        em.inc("ld", 16)
        sy.dma_start(out=sb["tmp"][0:1, 0:H_PER], in_=env["gatei"]).then_inc(
            em.h["ld"], 16)
        em.inc("ld", 16)
        em.qw(sy, "ld")
    env["cld"] = em.v["ld"]

    @em.on("vector")
    def _(v):
        for nm in ["zero", "S", "Z", "Sb", "Zb", "relacc"]:
            v.memset(sb[nm][...], 0.0)
        v.memset(sb["tidx"][...], 0)
        v.memset(sb["rstash"][...], 0.0)
        v.nop().then_inc(em.h["vp"], 1)
        em.inc("vp", 1)
        em.W(v, "ld", env["cld"])
        v.tensor_copy(sb["nrmTb"][...], sb["nrmTf"][...]).then_inc(
            em.h["vp"], 1)
        em.inc("vp", 1)
    env["zeroed"] = em.v["vp"]

    @em.on("gpsimd")
    def _(gp):
        gp.load_library(library_config.mlp)
        for rv, rh in env["regs"].items():
            gp.reg_mov(rh, rv)

    @em.on("scalar")
    def _(s):
        em.W(s, "ld", env["cld"])
        s.activation(sb["grow"][...], sb["tmp"][0:1, 0:H_PER],
                     AF.Sigmoid).then_inc(em.h["sp"], 1)
        em.inc("sp", 1)
    gr = em.v["sp"]

    @em.on("tensor")
    def _(t):
        em.W(t, "sp", gr)
        t.matmul(env["ps"]["s"][:, 0:H_PER], sb["c_onesrow"][...],
                 sb["grow"][...], start=True, stop=True).then_inc(
            em.h["tp"], 1)
        em.inc("tp", 1)
    grep = em.v["tp"]

    @em.on("vector")
    def _(v):
        em.W(v, "tp", grep)
        v.tensor_scalar(sb["g1m"][...], env["ps"]["s"][:, 0:H_PER], -1.0, 1.0,
                        ALU.mult, ALU.add).then_inc(em.h["vp"], 1)
        em.inc("vp", 1)
    env["g1m_ready"] = em.v["vp"]


def phase_setup_bulk(env):
    em, sb, cfg = env["em"], env["sb"], env["cfg"]
    S, T = cfg.S, cfg.T
    ntblel = cfg.NTBL * cfg.TROW
    env["xbh"] = {}

    @em.on("gpsimd")
    def _(gp):
        em.W(gp, "vp", env["zeroed"])
        for i in range(S // 128):
            stg = sb["xstage"] if i % 2 == 0 else sb["xstage2"]
            if i >= 1:
                em.W(gp, "gs", env["xbh"][i - 1])
            gp.dma_start(out=stg[...],
                         in_=env["xb"][i * 128:(i + 1) * 128, :]).then_inc(
                em.h["gl"], 16)
            t = em.inc("gl", 16)
            gp.wait_ge(em.h["gl"], t)
            gp.dma_start(out=env["XBH"][i * 128:(i + 1) * 128, :],
                         in_=stg[...]).then_inc(em.h["gs"], 16)
            env["xbh"][i] = em.inc("gs", 16)

    @em.on("gpsimd")
    def _(gp):
        for i in range(T // 128):
            for n4 in range(4):
                em.W(gp, "gs", em.v["gs"])
                gp.dma_start(out=env["RACC"][i * 128:(i + 1) * 128,
                                             n4 * 512:(n4 + 1) * 512],
                             in_=sb["zero"][...]).then_inc(em.h["gs"], 16)
                em.inc("gs", 16)
        flat = env["TBL"].rearrange("r c -> (r c)")
        off = 0
        while off < ntblel:
            n = min(128 * 512, ntblel - off)
            fr = n // 128
            em.W(gp, "gs", em.v["gs"])
            gp.dma_start(out=flat[off:off + 128 * fr].rearrange(
                "(p f) -> p f", p=128),
                in_=sb["zero"][:, 0:fr]).then_inc(em.h["gs"], 16)
            em.inc("gs", 16)
            off += 128 * fr
    env["zeros_done"] = em.v["gs"]

    @em.on("gpsimd")
    def _(gp):
        for rb in range(16):
            for cb in range(6):
                w = min(2048, 2 * J_FF - cb * 2048)
                stg = sb["xstage"] if (rb * 6 + cb) % 2 == 0 else sb["xstage2"]
                em.W(gp, "gs", em.v["gs"])
                gp.dma_start(out=stg[:, 0:w],
                             in_=env["wsu"][rb * 128:(rb + 1) * 128,
                                            cb * 2048:cb * 2048 + w]
                             ).then_inc(em.h["gl"], 16)
                t = em.inc("gl", 16)
                gp.wait_ge(em.h["gl"], t)
                gp.dma_start(out=env["WSUB"][rb * 128:(rb + 1) * 128,
                                             cb * 2048:cb * 2048 + w],
                             in_=stg[:, 0:w]).then_inc(em.h["gs"], 16)
                em.inc("gs", 16)
        for rb in range(44):
            stg = sb["xstage"] if rb % 2 == 0 else sb["xstage2"]
            em.W(gp, "gs", em.v["gs"])
            gp.dma_start(out=stg[...],
                         in_=env["wsd"][rb * 128:(rb + 1) * 128, :]
                         ).then_inc(em.h["gl"], 16)
            t = em.inc("gl", 16)
            gp.wait_ge(em.h["gl"], t)
            gp.dma_start(out=env["WSDB"][rb * 128:(rb + 1) * 128, :],
                         in_=stg[...]).then_inc(em.h["gs"], 16)
            em.inc("gs", 16)
    env["wcast_done"] = em.v["gs"]




def alloc_scopeB(env, st):
    sbuf = env["sbuf"]
    sbuf("memb", [128, M * H_PER, HD], BF16, st)
    sbuf("sqT", [128, H_PER, 512], BF16, st)
    sbuf("partS", [128, 4, HC], BF16, st)
    sbuf("scT", [128, H_PER * 128], BF16, st)


def alloc_scopeB1(env, st):
    sbuf = env["sbuf"]
    sbuf("wqTs", [128, 16, HC], BF16, st)
    sbuf("wkTs", [128, 16, HC], BF16, st)
    sbuf("wvTs", [128, 16, HC], BF16, st)
    sbuf("xT", [128, 16, 512], BF16, st)
    sbuf("xT2", [128, 16, 512], BF16, st)
    sbuf("kp", [128, 4, HC], BF16, st)
    sbuf("vc", [128, 4, HC], BF16, st)
    sbuf("kpT", [128, HC], BF16, st)
    for p in range(2):
        sbuf(f"sigF{p}", [128, 512], F32, st)
        sbuf(f"sigG{p}", [128, 512], F32, st)
        sbuf(f"sigH{p}", [128, 512], BF16, st)


def alloc_scopeB2(env, st):
    sbuf = env["sbuf"]
    sbuf("comb", [128, HC], BF16, st)
    sbuf("cm", [128, M * H_PER], F32, st)


def phase_p0b(env):
    em, sb = env["em"], env["sb"]

    @em.on("gpsimd")
    def _(gp):
        for nm, src in [("wqTs", env["wqT"]), ("wkTs", env["wkT"]),
                        ("wvTs", env["wvT"])]:
            gp.dma_start(out=sb[nm][...],
                         in_=src.rearrange("(k p) e -> p k e", p=128)
                         ).then_inc(em.h["wl"], 16)
            em.inc("wl", 16)
        gp.dma_start(out=sb["memb"][...],
                     in_=env["memi"].rearrange("mh (d e) -> d mh e", d=HD)
                     ).then_inc(em.h["wl"], 16)
        em.inc("wl", 16)
    env["wl_attn"] = em.v["wl"]
    env["wl_mem"] = em.v["wl"]




def phase_attn_pass1(env):
    for sc in range(env["cfg"].NSC):
        _p1_super(env, sc)
    env["p1_tdone"] = env["em"].v["tp"]


def _p1_super(env, sc):
    em, sb, ps, cfg = env["em"], env["sb"], env["ps"], env["cfg"]
    t0 = sc * 512
    xT = sb["xT"] if sc % 2 == 0 else sb["xT2"]

    @em.on("sync")
    def _(sy):
        em.W(sy, "gs", env["xbh"][(t0 + 512) // 128 - 1])
        if sc >= 2:
            em.W(sy, "tp", env[f"p1_projdone_{sc - 2}"])
        for k in range(16):
            sy.dma_start(out=xT[:, k, :],
                         in_=env["XBH"][t0:t0 + 512, k * 128:(k + 1) * 128],
                         transpose=True).then_inc(em.h["ld"], 16)
            em.inc("ld", 16)
        em.qw(sy, "ld")
    xT_ready = em.v["ld"]

    for et in range(H_PER):
        bank = ps["d0"] if et % 2 == 0 else ps["d1"]

        @em.on("tensor")
        def _(t, et=et, bank=bank):
            if et == 0:
                em.W(t, "ld", xT_ready)
                em.W(t, "wl", env["wl_attn"])
                em.W(t, "vp", env[f"p1_outev_{sc - 1}_3"] if sc > 0
                     else env["g1m_ready"])
            if et >= 2:
                em.W(t, "vp", env[f"p1_qev_{sc}_{et - 2}"])
            for k in range(16):
                mmi = t.matmul(bank[:, 0:512],
                               sb["wqTs"][:, k, et * 128:(et + 1) * 128],
                               xT[:, k, :], start=(k == 0), stop=(k == 15))
            mmi.then_inc(em.h["tp"], 1)
            env[f"p1_qps_{sc}_{et}"] = em.inc("tp", 1)

        @em.on("vector")
        def _(v, et=et, bank=bank):
            em.W(v, "tp", env[f"p1_qps_{sc}_{et}"])
            if sc > 0 and et == 0:
                em.W(v, "scr", env[f"p1_scr_{sc - 1}"])
            pr = et % 2
            v.reduce_sum(sb["tmp"][:, pr:pr + 1], bank[:, 0:512],
                         axis=AX.X).then_inc(em.h["vp"], 1)
            g0 = em.inc("vp", 1)
            v.tensor_scalar(sb[f"sigF{pr}"][...], bank[:, 0:512], 0.0, None,
                            ALU.min)
            v.tensor_scalar(sb[f"sigG{pr}"][...], bank[:, 0:512], 0.0, None,
                            ALU.max).then_inc(em.h["vp"], 1)
            env[f"p1_qsig_{sc}_{et}"] = em.inc("vp", 1)
            v.wait_ge(em.h["vp"], g0)
            v.tensor_tensor(sb["relacc"][:, et:et + 1],
                            sb["relacc"][:, et:et + 1],
                            sb["tmp"][:, pr:pr + 1], ALU.add)

        @em.on("scalar")
        def _(s, et=et):
            pr = et % 2
            em.W(s, "vp", env[f"p1_qsig_{sc}_{et}"])
            s.activation(sb[f"sigH{pr}"][...], sb[f"sigF{pr}"][...],
                         AF.Exp).then_inc(em.h["sp"], 1)
            env[f"p1_qexp_{sc}_{et}"] = em.inc("sp", 1)

        @em.on("vector")
        def _(v, et=et):
            pr = et % 2
            em.W(v, "sp", env[f"p1_qexp_{sc}_{et}"])
            v.tensor_tensor(sb["sqT"][:, et, :], sb[f"sigG{pr}"][...],
                            sb[f"sigH{pr}"][...], ALU.add).then_inc(
                em.h["vp"], 1)
            env[f"p1_qev_{sc}_{et}"] = em.inc("vp", 1)

    for kn in range(8):
        sub, n = kn // 2, kn % 2
        bank = ps[f"q{kn % 4}"]

        @em.on("tensor")
        def _(t, kn=kn, sub=sub, n=n, bank=bank):
            if kn >= 4:
                em.W(t, "vp", env[f"p1_kev_{sc}_{kn - 4}"])
            elif sc > 0:
                em.W(t, "vp", env[f"p1_outev_{sc - 1}_3"])
            for k in range(16):
                mmi = t.matmul(bank[:, 0:512],
                               xT[:, k, sub * 128:(sub + 1) * 128],
                               sb["wkTs"][:, k, n * 512:(n + 1) * 512],
                               start=(k == 0), stop=(k == 15))
            mmi.then_inc(em.h["tp"], 1)
            env[f"p1_kps_{sc}_{kn}"] = em.inc("tp", 1)

        @em.on("vector")
        def _(v, kn=kn, bank=bank):
            pr = kn % 2
            em.W(v, "tp", env[f"p1_kps_{sc}_{kn}"])
            v.tensor_scalar(sb[f"sigF{pr}"][...], bank[:, 0:512], 0.0, None,
                            ALU.min)
            v.tensor_scalar(sb[f"sigG{pr}"][...], bank[:, 0:512], 0.0, None,
                            ALU.max).then_inc(em.h["vp"], 1)
            env[f"p1_ksig_{sc}_{kn}"] = em.inc("vp", 1)

        @em.on("scalar")
        def _(s, kn=kn):
            pr = kn % 2
            em.W(s, "vp", env[f"p1_ksig_{sc}_{kn}"])
            s.activation(sb[f"sigH{pr}"][...], sb[f"sigF{pr}"][...],
                         AF.Exp).then_inc(em.h["sp"], 1)
            env[f"p1_kexp_{sc}_{kn}"] = em.inc("sp", 1)

        @em.on("vector")
        def _(v, kn=kn, sub=sub, n=n):
            pr = kn % 2
            em.W(v, "sp", env[f"p1_kexp_{sc}_{kn}"])
            v.tensor_tensor(sb["kp"][:, sub, n * 512:(n + 1) * 512],
                            sb[f"sigG{pr}"][...], sb[f"sigH{pr}"][...],
                            ALU.add).then_inc(em.h["vp"], 1)
            env[f"p1_kev_{sc}_{kn}"] = em.inc("vp", 1)

    for vn in range(8):
        sub, n = vn // 2, vn % 2
        bank = ps[f"q{vn % 4}"]

        @em.on("tensor")
        def _(t, vn=vn, sub=sub, n=n, bank=bank):
            em.W(t, "vp", env[f"p1_kev_{sc}_{4 + (vn % 4)}"] if vn < 4
                 else env[f"p1_vev_{sc}_{vn - 4}"])
            for k in range(16):
                mmi = t.matmul(bank[:, 0:512],
                               xT[:, k, sub * 128:(sub + 1) * 128],
                               sb["wvTs"][:, k, n * 512:(n + 1) * 512],
                               start=(k == 0), stop=(k == 15))
            mmi.then_inc(em.h["tp"], 1)
            env[f"p1_vps_{sc}_{vn}"] = em.inc("tp", 1)

        @em.on("vector")
        def _(v, vn=vn, sub=sub, n=n, bank=bank):
            em.W(v, "tp", env[f"p1_vps_{sc}_{vn}"])
            v.tensor_copy(sb["vc"][:, sub, n * 512:(n + 1) * 512],
                          bank[:, 0:512]).then_inc(em.h["vp"], 1)
            env[f"p1_vev_{sc}_{vn}"] = em.inc("vp", 1)
    env[f"p1_projdone_{sc}"] = em.v["tp"]

    for sub in range(4):
        _p1_chunk(env, sc, sub)

    @em.on("scalar")
    def _(s):
        em.W(s, "vp", env[f"p1_qev_{sc}_{H_PER - 1}"])
        for et in range(H_PER):
            s.dma_start(out=env["SQT"][et, :, t0:t0 + 512],
                        in_=sb["sqT"][:, et, :]).then_inc(em.h["scr"], 16)
            em.inc("scr", 16)
        em.W(s, "vp", env[f"p1_outev_{sc}_3"])
        s.dma_start(out=env["PART"][t0:t0 + 512, :].rearrange(
                        "(s p) e -> p s e", p=128),
                    in_=sb["partS"][...]).then_inc(em.h["scr"], 16)
        em.inc("scr", 16)
        em.qw(s, "scr")
    env[f"p1_scr_{sc}"] = em.v["scr"]


def _p1_chunk(env, sc, sub):
    em, sb, ps = env["em"], env["sb"], env["ps"]

    @em.on("vector")
    def _(v):
        v.tensor_copy(sb["Sb"][...], sb["S"][...])
        v.tensor_copy(sb["Zb"][...], sb["Z"][...]).then_inc(em.h["vp"], 1)
        env["p1_snap"] = em.inc("vp", 1)

    @em.on("tensor")
    def _(t):
        em.W(t, "vp", env[f"p1_kev_{sc}_{sub * 2 + 1}"])
        em.W(t, "vp", env[f"p1_vev_{sc}_{sub * 2 + 1}"])
        em.W(t, "vp", env[f"p1_qev_{sc}_{H_PER - 1}"])
        em.W(t, "vp", env["p1_snap"])
        if "p1_kpT_cp" in env:
            em.W(t, "sp", env["p1_kpT_cp"])
        for h in range(H_PER):
            tr = t.transpose(ps["t"][:, h * 128:(h + 1) * 128],
                             sb["kp"][:, sub, h * 128:(h + 1) * 128],
                             sb["c_ident"][...])
        tr.then_inc(em.h["tp"], 1)
        env["p1_kpT_ps"] = em.inc("tp", 1)

    @em.on("scalar")
    def _(s):
        em.W(s, "tp", env["p1_kpT_ps"])
        s.copy(sb["kpT"][...], ps["t"][:, 0:HC]).then_inc(em.h["sp"], 1)
        env["p1_kpT_cp"] = em.inc("sp", 1)

    @em.on("tensor")
    def _(t):
        em.W(t, "sp", env["p1_kpT_cp"])
        if "p1_scev" in env:
            em.W(t, "vp", env["p1_scev"])
        for h in range(H_PER):
            bank = ps["d0"] if h < 4 else ps["d1"]
            mmi = t.matmul(bank[:, (h % 4) * 128:(h % 4 + 1) * 128],
                           sb["kpT"][:, h * 128:(h + 1) * 128],
                           sb["sqT"][:, h, sub * 128:(sub + 1) * 128],
                           start=True, stop=True)
        mmi.then_inc(em.h["tp"], 1)
        env["p1_sc_ps"] = em.inc("tp", 1)

    @em.on("vector")
    def _(v):
        em.W(v, "tp", env["p1_sc_ps"])
        v.tensor_tensor(sb["scT"][:, 0:512], ps["d0"][:, 0:512],
                        sb["c_maskT"][:, 0:512], ALU.mult)
        v.tensor_tensor(sb["scT"][:, 512:1024], ps["d1"][:, 0:512],
                        sb["c_maskT"][:, 512:1024], ALU.mult).then_inc(
            em.h["vp"], 1)
        env["p1_scev"] = em.inc("vp", 1)

    @em.on("tensor")
    def _(t):
        em.W(t, "vp", env["p1_scev"])
        key = f"p1_outev_{sc}_{sub - 1}" if sub > 0 else f"p1_outev_{sc - 1}_3"
        if key in env:
            em.W(t, "vp", env[key])
        for h in range(H_PER):
            bank = ps["q0"] if h < 4 else ps["q1"]
            t.matmul(bank[:, (h % 4) * 128:(h % 4 + 1) * 128],
                     sb["kp"][:, sub, h * 128:(h + 1) * 128],
                     sb["vc"][:, sub, h * 128:(h + 1) * 128],
                     start=True, stop=True)
        for h in range(H_PER):
            t.matmul(ps["s"][:, h:h + 1],
                     sb["sqT"][:, h, sub * 128:(sub + 1) * 128],
                     sb["Zb"][:, h:h + 1], start=True, stop=False)
            t.matmul(ps["s"][:, h:h + 1], sb["scT"][:, h * 128:(h + 1) * 128],
                     sb["c_onesb"][...], start=False, stop=True)
        for h in range(H_PER):
            t.matmul(ps["s"][:, 16 + h:17 + h],
                     sb["kp"][:, sub, h * 128:(h + 1) * 128],
                     sb["c_onesb"][...], start=True, stop=True)
        for h in range(H_PER):
            bank = ps["q2"] if h < 4 else ps["q3"]
            t.matmul(bank[:, (h % 4) * 128:(h % 4 + 1) * 128],
                     sb["scT"][:, h * 128:(h + 1) * 128],
                     sb["vc"][:, sub, h * 128:(h + 1) * 128],
                     start=True, stop=False)
            mmi = t.matmul(bank[:, (h % 4) * 128:(h % 4 + 1) * 128],
                           sb["sqT"][:, h, sub * 128:(sub + 1) * 128],
                           sb["Sb"][:, h * 128:(h + 1) * 128],
                           start=False, stop=True)
        mmi.then_inc(em.h["tp"], 1)
        env["p1_att_ps"] = em.inc("tp", 1)

    @em.on("vector")
    def _(v):
        em.W(v, "tp", env["p1_att_ps"])
        if sub == 0 and sc > 0:
            em.W(v, "scr", env[f"p1_scr_{sc - 1}"])
        v.tensor_tensor(sb["S"][:, 0:512], sb["S"][:, 0:512],
                        ps["q0"][:, 0:512], ALU.add)
        v.tensor_tensor(sb["S"][:, 512:1024], sb["S"][:, 512:1024],
                        ps["q1"][:, 0:512], ALU.add)
        v.tensor_tensor(sb["Z"][...], sb["Z"][...],
                        ps["s"][:, 16:16 + H_PER], ALU.add)
        v.tensor_scalar(sb["tmp"][:, 0:H_PER], ps["s"][:, 0:H_PER], EPS,
                        None, ALU.max).then_inc(em.h["vp"], 1)
        v.wait_ge(em.h["vp"], em.inc("vp", 1))
        v.reciprocal(sb["tmp"][:, 8:8 + H_PER],
                     sb["tmp"][:, 0:H_PER]).then_inc(em.h["vp"], 1)
        v.wait_ge(em.h["vp"], em.inc("vp", 1))
        for h in range(H_PER):
            bank = ps["q2"] if h < 4 else ps["q3"]
            vi = v.tensor_scalar(sb["partS"][:, sub, h * 128:(h + 1) * 128],
                                 bank[:, (h % 4) * 128:(h % 4 + 1) * 128],
                                 sb["tmp"][:, 8 + h:9 + h],
                                 sb["g1m"][:, h:h + 1], ALU.mult, ALU.mult)
        vi.then_inc(em.h["vp"], 1)
        env[f"p1_outev_{sc}_{sub}"] = em.inc("vp", 1)


def phase_rel(env):
    em, sb, ps, cfg = env["em"], env["sb"], env["ps"], env["cfg"]

    @em.on("vector")
    def _(v):
        v.tensor_copy(sb["relaccb"][...], sb["relacc"][...])
        v.reduce_sum(sb["tmp"][0:M * 16, 0:1], sb["nrmFs"][...],
                     axis=AX.X).then_inc(em.h["vp"], 1)
        env["p2_v0"] = em.inc("vp", 1)

    @em.on("tensor")
    def _(t):
        em.W(t, "vp", env["p2_v0"])
        for h in range(H_PER):
            t.matmul(ps["s"][0:M, h:h + 1], sb["nrmTb"][:, h, :],
                     sb["relaccb"][:, h:h + 1], start=True, stop=True)
        mmi = t.matmul(ps["s"][0:M, 8:9], sb["c_esel"][...],
                       sb["tmp"][0:M * 16, 0:1], start=True, stop=True)
        mmi.then_inc(em.h["tp"], 1)
        env["p2_t0"] = em.inc("tp", 1)

    @em.on("vector")
    def _(v):
        em.W(v, "tp", env["p2_t0"])
        v.tensor_copy(sb["rel"][0:M, 0:H_PER], ps["s"][0:M, 0:H_PER])
        v.tensor_scalar(sb["rel"][0:M, 8:9], ps["s"][0:M, 8:9], 1e-6, None,
                        ALU.is_lt).then_inc(em.h["vp"], 1)
        env["p2_v1"] = em.inc("vp", 1)

    @em.on("tensor")
    def _(t):
        em.W(t, "vp", env["p2_v1"])
        t.matmul(ps["s"][0:H_PER, 16:16 + M], sb["rel"][0:M, 0:H_PER],
                 sb["c_identf"][0:M, 0:M], start=True, stop=True)
        mmi = t.matmul(ps["s"][0:1, 24:24 + M], sb["rel"][0:M, 8:9],
                       sb["c_identf"][0:M, 0:M], start=True, stop=True)
        mmi.then_inc(em.h["tp"], 1)
        env["p2_t1"] = em.inc("tp", 1)

    @em.on("vector")
    def _(v):
        em.W(v, "tp", env["p2_t1"])
        v.tensor_scalar(sb["rel"][0:H_PER, 0:M], ps["s"][0:H_PER, 16:16 + M],
                        SCALE / env["cfg"].S, None, ALU.mult)
        v.tensor_copy(sb["rel"][0:1, 12:12 + M],
                      ps["s"][0:1, 24:24 + M]).then_inc(em.h["vp"], 1)
        env["p2_v2"] = em.inc("vp", 1)

    @em.on("tensor")
    def _(t):
        em.W(t, "vp", env["p2_v2"])
        mmi = t.matmul(ps["s"][0:H_PER, 28:28 + M],
                       sb["c_onesrow"][0:1, 0:H_PER],
                       sb["rel"][0:1, 12:12 + M], start=True, stop=True)
        mmi.then_inc(em.h["tp"], 1)
        env["p2_t2"] = em.inc("tp", 1)

    @em.on("vector")
    def _(v):
        em.W(v, "tp", env["p2_t2"])
        v.scalar_tensor_tensor(sb["rel"][0:H_PER, 0:M],
                               ps["s"][0:H_PER, 28:28 + M], NEG_BIG,
                               sb["rel"][0:H_PER, 0:M], ALU.mult,
                               ALU.add).then_inc(em.h["vp"], 1)
        ga = em.inc("vp", 1)
        v.tensor_scalar(sb["rel"][0:H_PER, 4:4 + M],
                        ps["s"][0:H_PER, 28:28 + M], -1.0, 1.0, ALU.mult,
                        ALU.add)
        v.wait_ge(em.h["vp"], ga)
        v.reduce_max(sb["rel"][0:H_PER, 8:9], sb["rel"][0:H_PER, 0:M],
                     axis=AX.X).then_inc(em.h["vp"], 1)
        v.wait_ge(em.h["vp"], em.inc("vp", 1))
        v.tensor_scalar(sb["rel"][0:H_PER, 0:M], sb["rel"][0:H_PER, 0:M],
                        sb["rel"][0:H_PER, 8:9], None,
                        ALU.subtract).then_inc(em.h["vp"], 1)
        env["p2_v3"] = em.inc("vp", 1)

    @em.on("scalar")
    def _(s):
        em.W(s, "vp", env["p2_v3"])
        s.activation(sb["rel"][0:H_PER, 16:16 + M], sb["rel"][0:H_PER, 0:M],
                     AF.Exp).then_inc(em.h["sp"], 1)
        env["p2_s0"] = em.inc("sp", 1)

    @em.on("vector")
    def _(v):
        em.W(v, "sp", env["p2_s0"])
        v.reduce_sum(sb["rel"][0:H_PER, 9:10], sb["rel"][0:H_PER, 16:16 + M],
                     axis=AX.X).then_inc(em.h["vp"], 1)
        v.wait_ge(em.h["vp"], em.inc("vp", 1))
        v.reciprocal(sb["rel"][0:H_PER, 10:11],
                     sb["rel"][0:H_PER, 9:10]).then_inc(em.h["vp"], 1)
        v.wait_ge(em.h["vp"], em.inc("vp", 1))
        v.tensor_scalar(sb["rel"][0:H_PER, 16:16 + M],
                        sb["rel"][0:H_PER, 16:16 + M],
                        sb["rel"][0:H_PER, 10:11], None,
                        ALU.mult).then_inc(em.h["vp"], 1)
        v.wait_ge(em.h["vp"], em.inc("vp", 1))
        v.tensor_tensor(sb["rel"][0:H_PER, 16:16 + M],
                        sb["rel"][0:H_PER, 16:16 + M],
                        sb["rel"][0:H_PER, 4:4 + M], ALU.mult)
        v.reduce_sum(sb["rel"][0:1, 13:14], sb["rel"][0:1, 12:12 + M],
                     axis=AX.X).then_inc(em.h["vp"], 1)
        v.wait_ge(em.h["vp"], em.inc("vp", 1))
        v.tensor_scalar(sb["rel"][0:1, 14:15], sb["rel"][0:1, 13:14],
                        M - 0.5, None, ALU.is_lt).then_inc(em.h["vp"], 1)
        env["p2_v4"] = em.inc("vp", 1)

    @em.on("tensor")
    def _(t):
        em.W(t, "vp", env["p2_v4"])
        t.matmul(ps["s"][0:H_PER, 32:33], sb["c_onesrow"][0:1, 0:H_PER],
                 sb["rel"][0:1, 14:15], start=True, stop=True)
        mmi = t.matmul(ps["s"][0:H_PER, 33:34], sb["grow"][...],
                       sb["c_identf"][0:1, 0:1], start=True, stop=True)
        mmi.then_inc(em.h["tp"], 1)
        env["p2_t3"] = em.inc("tp", 1)

    @em.on("vector")
    def _(v):
        em.W(v, "tp", env["p2_t3"])
        v.tensor_scalar(sb["rel"][0:H_PER, 16:16 + M],
                        sb["rel"][0:H_PER, 16:16 + M],
                        ps["s"][0:H_PER, 32:33], ps["s"][0:H_PER, 33:34],
                        ALU.mult, ALU.mult).then_inc(em.h["vp"], 1)
        env["p2_v6"] = em.inc("vp", 1)

    @em.on("sync")
    def _(sy):
        em.W(sy, "vp", env["p2_v6"])
        sy.dma_start(out=sb["tmp"][0:1, 0:H_PER * M],
                     in_=sb["rel"][0:H_PER, 16:16 + M]).then_inc(
            em.h["ld"], 16)
        em.inc("ld", 16)
        em.qw(sy, "ld")
    env["p2_fl"] = em.v["ld"]

    @em.on("tensor")
    def _(t):
        em.W(t, "ld", env["p2_fl"])
        mmi = t.matmul(ps["s"][:, 34:34 + H_PER * M], sb["c_onesrow"][...],
                       sb["tmp"][0:1, 0:H_PER * M], start=True, stop=True)
        mmi.then_inc(em.h["tp"], 1)
        env["p2_t5"] = em.inc("tp", 1)

    @em.on("vector")
    def _(v):
        em.W(v, "tp", env["p2_t5"])
        v.tensor_copy(sb["weff"][...],
                      ps["s"][:, 34:34 + H_PER * M]).then_inc(em.h["vp"], 1)
        env["p2_done"] = em.inc("vp", 1)


def phase_attn_pass2(env):
    em = env["em"]
    for sc in range(env["cfg"].NSC):
        _p3_super(env, sc)
    env["p3_tdone"] = em.v["tp"]

    @em.on("gpsimd")
    def _(gp):
        em.W(gp, "scr", env[f"p3_scr_{env['cfg'].NSC - 1}"])
        gp.collective_compute(
            "AllGather", ALU.bypass,
            replica_groups=[[0, 1], [2, 3], [4, 5], [6, 7]],
            ins=[env["agin"].ap().opt()],
            outs=[env["agout"].ap().opt()],
        ).then_inc(em.h["cc"], 1)
        em.inc("cc", 1)
    env["a2a_done"] = em.v["cc"]


def _p3_super(env, sc):
    em, sb, cfg = env["em"], env["sb"], env["cfg"]
    t0 = sc * 512

    @em.on("scalar")
    def _(s):
        em.W(s, "scr", env[f"p1_scr_{cfg.NSC - 1}"])
        em.W(s, "vp", env["p2_done"])
        if sc > 0:
            em.W(s, "vp", env[f"p3_combev_{sc - 1}_3"])
        for et in range(H_PER):
            s.dma_start(out=sb["sqT"][:, et, :],
                        in_=env["SQT"][et, :, t0:t0 + 512]).then_inc(
                em.h["scr"], 16)
            em.inc("scr", 16)
        s.dma_start(out=sb["partS"][...],
                    in_=env["PART"][t0:t0 + 512, :].rearrange(
                        "(s p) e -> p s e", p=128)).then_inc(em.h["scr"], 16)
        em.inc("scr", 16)
        em.qw(s, "scr")
    env[f"p3_ld_{sc}"] = em.v["scr"]

    for sub in range(4):
        _p3_chunk(env, sc, sub)
    env[f"p3_scr_{sc}"] = em.v["scr"]


def _p3_chunk(env, sc, sub):
    em, sb, ps, cfg = env["em"], env["sb"], env["ps"], env["cfg"]
    ch = sc * 4 + sub

    @em.on("tensor")
    def _(t):
        em.W(t, "scr", env[f"p3_ld_{sc}"])
        em.W(t, "wl", env["wl_mem"])
        if "p3_cmev" in env:
            em.W(t, "vp", env["p3_cmev"])
        for h in range(H_PER):
            mmi = t.matmul(ps["s"][:, h * M:(h + 1) * M],
                           sb["sqT"][:, h, sub * 128:(sub + 1) * 128],
                           sb["nrmTb"][:, h, :], start=True, stop=True)
        mmi.then_inc(em.h["tp"], 1)
        env["p3_den_ps"] = em.inc("tp", 1)

    @em.on("vector")
    def _(v):
        em.W(v, "tp", env["p3_den_ps"])
        v.tensor_scalar(sb["cm"][...], ps["s"][:, 0:M * H_PER], EPS, None,
                        ALU.max).then_inc(em.h["vp"], 1)
        v.wait_ge(em.h["vp"], em.inc("vp", 1))
        v.reciprocal(sb["cm"][...], sb["cm"][...]).then_inc(em.h["vp"], 1)
        v.wait_ge(em.h["vp"], em.inc("vp", 1))
        v.tensor_tensor(sb["cm"][...], sb["cm"][...], sb["weff"][...],
                        ALU.mult).then_inc(em.h["vp"], 1)
        env["p3_cmev"] = em.inc("vp", 1)

    for m in range(M):
        bA = ps["q0"] if m % 2 == 0 else ps["q2"]
        bB = ps["q1"] if m % 2 == 0 else ps["q3"]

        @em.on("tensor")
        def _(t, m=m, bA=bA, bB=bB):
            guard = f"p3_mev_{m - 2}" if m >= 2 else f"p3_mev_{m + 2}"
            if guard in env:
                em.W(t, "vp", env[guard])
            if m == 0:
                em.W(t, "vp", env["p3_cmev"])
            for h in range(H_PER):
                bank = bA if h < 4 else bB
                mmi = t.matmul(bank[:, (h % 4) * 128:(h % 4 + 1) * 128],
                               sb["sqT"][:, h, sub * 128:(sub + 1) * 128],
                               sb["memb"][:, m * H_PER + h, :],
                               start=True, stop=True)
            mmi.then_inc(em.h["tp"], 1)
            env[f"p3_mps_{m}"] = em.inc("tp", 1)

        @em.on("vector")
        def _(v, m=m, bA=bA, bB=bB):
            em.W(v, "tp", env[f"p3_mps_{m}"])
            if m == 0:
                em.W(v, "scr", env.get("p3_aginst", 0))
            else:
                em.W(v, "vp", env[f"p3_mev_{m - 1}"])
            for h in range(H_PER):
                bank = bA if h < 4 else bB
                src1 = (sb["partS"][:, sub, h * 128:(h + 1) * 128] if m == 0
                        else sb["comb"][:, h * 128:(h + 1) * 128])
                vi = v.scalar_tensor_tensor(
                    sb["comb"][:, h * 128:(h + 1) * 128],
                    bank[:, (h % 4) * 128:(h % 4 + 1) * 128],
                    sb["cm"][:, h * M + m:h * M + m + 1],
                    src1, ALU.mult, ALU.add)
            vi.then_inc(em.h["vp"], 1)
            env[f"p3_mev_{m}"] = em.inc("vp", 1)

    @em.on("scalar")
    def _(s):
        em.W(s, "vp", env[f"p3_mev_{M - 1}"])
        s.dma_start(out=env["agin"].ap()[ch * 128:(ch + 1) * 128, :],
                    in_=sb["comb"][...]).then_inc(em.h["scr"], 16)
        env["p3_aginst"] = em.inc("scr", 16)
        em.qw(s, "scr")

    @em.on("vector")
    def _(v):
        em.W(v, "scr", env["p3_aginst"])
        v.nop().then_inc(em.h["vp"], 1)
        env[f"p3_combev_{sc}_{sub}"] = em.inc("vp", 1)


def alloc_scopeC(env, st):
    sbuf = env["sbuf"]
    sbuf("woTs", [128, 16, D], BF16, st)
    sbuf("wolh", [128, 8, 256], BF16, st)
    sbuf("wolh2", [128, 8, 256], BF16, st)
    sbuf("gwo", [128, 16], I16, st)
    sbuf("res", [128, D], F32, st)
    sbuf("res2", [128, D], F32, st)
    sbuf("s1", [128, D], F32, st)
    sbuf("scrA", [128, D], F32, st)
    sbuf("xp", [128, D], F32, st)
    sbuf("xphb", [128, D], BF16, st)
    sbuf("xpT", [128, 16, 128], F32, st)


def ln_stats_apply(env, src, dst, tc, scr_name="scrA", pre=None,
                   src_ready=None):
    em, sb = env["em"], env["sb"]

    @em.on("vector")
    def _(v):
        if src_ready is not None:
            em.W(v, src_ready[0], src_ready[1])
        v.reduce_sum(sb["tmp"][:, tc:tc + 1], src, axis=AX.X).then_inc(
            em.h["vp"], 1)
        env["ln_v0"] = em.inc("vp", 1)

    @em.on("scalar")
    def _(s):
        em.W(s, "vp", env["ln_v0"])
        s.activation(sb[scr_name][...], src, AF.Square,
                     accum_out=sb["tmp"][:, tc + 1:tc + 2]).then_inc(
            em.h["sp"], 1)
        env["ln_s0"] = em.inc("sp", 1)

    @em.on("vector")
    def _(v):
        em.W(v, "sp", env["ln_s0"])
        v.tensor_scalar(sb["tmp"][:, tc + 2:tc + 3], sb["tmp"][:, tc:tc + 1],
                        1.0 / D, None, ALU.mult).then_inc(em.h["vp"], 1)
        g1 = em.inc("vp", 1)
        v.tensor_scalar(sb["tmp"][:, tc + 3:tc + 4],
                        sb["tmp"][:, tc + 1:tc + 2], 1.0 / D, None,
                        ALU.mult).then_inc(em.h["vp"], 1)
        g2 = em.inc("vp", 1)
        v.wait_ge(em.h["vp"], g1)
        v.wait_ge(em.h["vp"], g2)
        v.tensor_tensor(sb["tmp"][:, tc + 6:tc + 7],
                        sb["tmp"][:, tc + 2:tc + 3],
                        sb["tmp"][:, tc + 2:tc + 3],
                        ALU.mult).then_inc(em.h["vp"], 1)
        g3 = em.inc("vp", 1)
        v.wait_ge(em.h["vp"], g3)
        v.tensor_tensor(sb["tmp"][:, tc + 3:tc + 4],
                        sb["tmp"][:, tc + 3:tc + 4],
                        sb["tmp"][:, tc + 6:tc + 7],
                        ALU.subtract).then_inc(em.h["vp"], 1)
        g4 = em.inc("vp", 1)
        v.wait_ge(em.h["vp"], g4)
        v.tensor_scalar(sb["tmp"][:, tc + 3:tc + 4],
                        sb["tmp"][:, tc + 3:tc + 4], 0.0, LN_EPS,
                        ALU.max, ALU.add).then_inc(em.h["vp"], 1)
        env["ln_v1"] = em.inc("vp", 1)

    @em.on("scalar")
    def _(s):
        em.W(s, "vp", env["ln_v1"])
        s.activation(sb["tmp"][:, tc + 6:tc + 7], sb["tmp"][:, tc + 3:tc + 4],
                     AF.Sqrt).then_inc(em.h["sp"], 1)
        env["ln_s1"] = em.inc("sp", 1)

    @em.on("vector")
    def _(v):
        em.W(v, "sp", env["ln_s1"])
        v.reciprocal(sb["tmp"][:, tc + 4:tc + 5],
                     sb["tmp"][:, tc + 6:tc + 7]).then_inc(em.h["vp"], 1)
        v.wait_ge(em.h["vp"], em.inc("vp", 1))
        v.tensor_tensor(sb["tmp"][:, tc + 7:tc + 8],
                        sb["tmp"][:, tc + 4:tc + 5],
                        sb["tmp"][:, tc + 4:tc + 5], ALU.mult).then_inc(
            em.h["vp"], 1)
        v.wait_ge(em.h["vp"], em.inc("vp", 1))
        v.tensor_tensor(sb["tmp"][:, tc + 7:tc + 8],
                        sb["tmp"][:, tc + 7:tc + 8],
                        sb["tmp"][:, tc + 3:tc + 4], ALU.mult).then_inc(
            em.h["vp"], 1)
        v.wait_ge(em.h["vp"], em.inc("vp", 1))
        v.tensor_scalar(sb["tmp"][:, tc + 7:tc + 8],
                        sb["tmp"][:, tc + 7:tc + 8], -0.5, 1.5, ALU.mult,
                        ALU.add).then_inc(em.h["vp"], 1)
        v.wait_ge(em.h["vp"], em.inc("vp", 1))
        v.tensor_tensor(sb["tmp"][:, tc + 4:tc + 5],
                        sb["tmp"][:, tc + 4:tc + 5],
                        sb["tmp"][:, tc + 7:tc + 8], ALU.mult).then_inc(
            em.h["vp"], 1)
        v.wait_ge(em.h["vp"], em.inc("vp", 1))
        v.tensor_scalar(sb["tmp"][:, tc + 5:tc + 6],
                        sb["tmp"][:, tc + 2:tc + 3], -1.0, None,
                        ALU.mult).then_inc(em.h["vp"], 1)
        v.wait_ge(em.h["vp"], em.inc("vp", 1))
        if pre is not None:
            em.W(v, pre[0], pre[1])
        v.tensor_scalar(dst, src, sb["tmp"][:, tc + 5:tc + 6],
                        sb["tmp"][:, tc + 4:tc + 5], ALU.add,
                        ALU.mult).then_inc(em.h["vp"], 1)
        env["ln_done"] = em.inc("vp", 1)


def phase_wo_ln1_routing(env):
    em, sb, cfg = env["em"], env["sb"], env["cfg"]
    env["sbuf"]("gwoall", [128, cfg.OCH * 16], I16, env["_stC"])

    @em.on("sync")
    def _(sy):
        sy.dma_start(out=sb["gwoall"][...],
                     in_=env["gidxwo"]).then_inc(em.h["ld"], 16)
        em.inc("ld", 16)
        em.qw(sy, "ld")
    env["p5_gwold"] = em.v["ld"]

    @em.on("gpsimd")
    def _(gp):
        em.W(gp, "tp", env["p3_tdone"])
        gp.dma_start(out=sb["woTs"][...],
                     in_=env["woT"].rearrange("(k p) e -> p k e", p=128)
                     ).then_inc(em.h["wl"], 16)
        em.inc("wl", 16)
        em.W(gp, "cc", env["a2a_done"])
        em.W(gp, "ld", env["p5_gwold"])
        gp.nop().then_inc(em.h["mk"], 1)
        em.inc("mk", 1)
    env["p5_w"] = em.v["wl"]
    env["p5_ag"] = em.v["mk"]

    for oc in range(cfg.OCH):
        _p5_chunk(env, oc)
    env["p5_tdone"] = em.v["tp"]


def _p5_chunk(env, oc):
    em, sb, ps, cfg = env["em"], env["sb"], env["ps"], env["cfg"]
    r0 = oc * 128
    lh = sb["wolh"] if oc % 2 == 0 else sb["wolh2"]
    res = sb["res"] if oc % 2 == 0 else sb["res2"]

    @em.on("sync")
    def _(sy):
        if oc >= 2:
            em.W(sy, "vp", env[f"p5_s1_{oc - 2}"])
        sy.dma_start(out=res[...],
                     in_=env["xown"][r0:r0 + 128, :]).then_inc(em.h["ld"], 16)
        em.inc("ld", 16)
        em.qw(sy, "ld")
    ldv = em.v["ld"]

    gsem = "qa" if oc % 2 == 0 else "qb"

    @em.on("gpsimd")
    def _(gp):
        em.W(gp, "mk", env["p5_ag"])
        if oc >= 2:
            em.W(gp, "tp", env[f"p5_wops_{oc - 2}"])
        gp.dma_gather(lh[...], env["agout"].ap(),
                      sb["gwoall"][:, oc * 16:(oc + 1) * 16],
                      num_idxs=256, num_idxs_reg=env["regs"][256], elem_size=HC,
                      elem_step=HC, transpose=True).then_inc(em.h[gsem], 16)
        em.inc(gsem, 16)
    ghv = (gsem, em.v[gsem])

    @em.on("tensor")
    def _(t):
        em.W(t, "ld", ldv)
        em.W(t, ghv[0], ghv[1])
        em.W(t, "wl", env["p5_w"])
        if oc > 0:
            em.W(t, "vp", env[f"p5_s1_{oc - 1}"])
        for n in range(4):
            for k in range(16):
                lhT = (lh[:, k, 0:128] if k < 8 else lh[:, k - 8, 128:256])
                mmi = t.matmul(ps[f"q{n}"][:, 0:512], lhT,
                               sb["woTs"][:, k, n * 512:(n + 1) * 512],
                               start=(k == 0), stop=(k == 15))
        mmi.then_inc(em.h["tp"], 1)
        env[f"p5_wops_{oc}"] = em.inc("tp", 1)

    @em.on("vector")
    def _(v):
        em.W(v, "tp", env[f"p5_wops_{oc}"])
        em.W(v, "sp", env.get("p5_sq_prev", 0))
        for n in range(4):
            vi = v.tensor_tensor(sb["s1"][:, n * 512:(n + 1) * 512],
                                 ps[f"q{n}"][:, 0:512],
                                 res[:, n * 512:(n + 1) * 512], ALU.add)
        vi.then_inc(em.h["vp"], 1)
        env[f"p5_s1_{oc}"] = em.inc("vp", 1)

    ln_stats_apply(env, sb["s1"][...], sb["xp"][...], 0,
                   pre=("scr", env.get(f"p5_xpst_{oc - 1}", 0)),
                   src_ready=("vp", env[f"p5_s1_{oc}"]))
    env["p5_sq_prev"] = env["ln_s0"]
    xp_done = env["ln_done"]

    @em.on("vector")
    def _(v):
        em.W(v, "scr", env.get(f"p5_xpst_{oc - 1}", 0))
        v.wait_ge(em.h["vp"], xp_done)
        v.tensor_copy(sb["xphb"][...], sb["xp"][...]).then_inc(em.h["vp"], 1)
        env["p5_xph"] = em.inc("vp", 1)

    @em.on("scalar")
    def _(s):
        em.W(s, "vp", env["p5_xph"])
        s.dma_start(out=env["XP"][r0:r0 + 128, :],
                    in_=sb["xp"][...]).then_inc(em.h["scr"], 16)
        em.inc("scr", 16)
        s.dma_start(out=env["XPH"][r0:r0 + 128, :],
                    in_=sb["xphb"][...]).then_inc(em.h["scr"], 16)
        em.inc("scr", 16)
        em.qw(s, "scr")
    env[f"p5_xpst_{oc}"] = em.v["scr"]

    for k in range(16):
        bank = ps["d0"] if k % 2 == 0 else ps["d1"]

        @em.on("tensor")
        def _(t, k=k, bank=bank):
            if k == 0:
                em.W(t, "vp", xp_done)
            if k >= 2:
                em.W(t, "sp", env[f"p5_xpTc_{k - 2}"])
            mmi = t.matmul(bank[:, 0:128], sb["xp"][:, k * 128:(k + 1) * 128],
                           sb["c_identf"][...], start=True, stop=True)
            mmi.then_inc(em.h["tp"], 1)
            env[f"p5_xpT_{k}"] = em.inc("tp", 1)

        @em.on("scalar")
        def _(s, k=k, bank=bank):
            em.W(s, "tp", env[f"p5_xpT_{k}"])
            s.copy(sb["xpT"][:, k, :], bank[:, 0:128]).then_inc(em.h["sp"], 1)
            env[f"p5_xpTc_{k}"] = em.inc("sp", 1)

    @em.on("tensor")
    def _(t):
        em.W(t, "sp", env["p5_xpTc_15"])
        if "p5_rtdone" in env:
            em.W(t, "vp", env["p5_rtdone"])
        for k in range(16):
            mmi = t.matmul(ps["s"][:, 0:E], sb["xpT"][:, k, :],
                           sb["routerS"][:, k, :], start=(k == 0),
                           stop=(k == 15))
        mmi.then_inc(em.h["tp"], 1)
        env["p5_lg"] = em.inc("tp", 1)

    _p5_routing(env, oc)


def _p5_routing(env, oc):
    em, sb, ps = env["em"], env["sb"], env["ps"]

    @em.on("vector")
    def _(v):
        def st(ins):
            ins.then_inc(em.h["vp"], 1)
            v.wait_ge(em.h["vp"], em.inc("vp", 1))

        em.W(v, "tp", env["p5_lg"])
        tm = sb["tmp"]
        rst = sb["rstash"]
        lg = ps["s"][:, 0:E]
        st(v.reduce_max(tm[:, 16:17], lg, axis=AX.X))
        st(v.tensor_scalar(tm[:, 17:17 + E], lg, tm[:, 16:17], None,
                           ALU.is_ge))
        st(v.tensor_scalar(tm[:, 25:25 + E], tm[:, 17:17 + E], -1e4, 1e4,
                           ALU.mult, ALU.add))
        st(v.tensor_tensor(tm[:, 25:25 + E], tm[:, 25:25 + E],
                           sb["c_iotae"][...], ALU.add))
        st(v.tensor_reduce(rst[:, oc, 0:1], tm[:, 25:25 + E], AX.X, ALU.min))
        st(v.tensor_scalar(tm[:, 17:17 + E], sb["c_iotae"][...],
                           rst[:, oc, 0:1], None, ALU.is_equal))
        st(v.scalar_tensor_tensor(tm[:, 33:33 + E], tm[:, 17:17 + E],
                                  NEG_BIG, lg, ALU.mult, ALU.add))
        st(v.reduce_max(tm[:, 41:42], tm[:, 33:33 + E], axis=AX.X))
        st(v.tensor_scalar(tm[:, 42:42 + E], tm[:, 33:33 + E], tm[:, 41:42],
                           None, ALU.is_ge))
        st(v.tensor_scalar(tm[:, 50:50 + E], tm[:, 42:42 + E], -1e4, 1e4,
                           ALU.mult, ALU.add))
        st(v.tensor_tensor(tm[:, 50:50 + E], tm[:, 50:50 + E],
                           sb["c_iotae"][...], ALU.add))
        st(v.tensor_reduce(rst[:, oc, 1:2], tm[:, 50:50 + E], AX.X, ALU.min))
        st(v.tensor_scalar(tm[:, 42:42 + E], sb["c_iotae"][...],
                           rst[:, oc, 1:2], None, ALU.is_equal))
        st(v.tensor_tensor(tm[:, 58:59], tm[:, 41:42], tm[:, 16:17],
                           ALU.subtract))
        env["p5_rt_d"] = em.v["vp"]

    @em.on("scalar")
    def _(s):
        em.W(s, "vp", env["p5_rt_d"])
        s.activation(sb["tmp"][:, 59:60], sb["tmp"][:, 58:59],
                     AF.Exp).then_inc(em.h["sp"], 1)
        env["p5_rt_e"] = em.inc("sp", 1)

    @em.on("vector")
    def _(v):
        def st(ins):
            ins.then_inc(em.h["vp"], 1)
            v.wait_ge(em.h["vp"], em.inc("vp", 1))

        em.W(v, "sp", env["p5_rt_e"])
        tm = sb["tmp"]
        rst = sb["rstash"]
        st(v.tensor_scalar(tm[:, 60:61], tm[:, 59:60], 1.0, None, ALU.add))
        st(v.reciprocal(rst[:, oc, 2:3], tm[:, 60:61]))
        st(v.tensor_scalar(rst[:, oc, 3:4], rst[:, oc, 2:3], -1.0, 1.0,
                           ALU.mult, ALU.add))
        st(v.tensor_tensor(tm[:, 25:25 + E], tm[:, 17:17 + E],
                           tm[:, 42:42 + E], ALU.add))
        st(v.tensor_copy(sb["tmpb"][:, 0:E], tm[:, 25:25 + E]))
        env["p5_oh"] = em.v["vp"]

    @em.on("tensor")
    def _(t):
        em.W(t, "vp", env["p5_oh"])
        t.matmul(ps["s"][:, 8:8 + E], sb["c_triu"][...], sb["tmpb"][:, 0:E],
                 start=True, stop=True)
        mmi = t.matmul(ps["s"][0:1, 24:24 + E], sb["c_onesb"][...],
                       sb["tmpb"][:, 0:E], start=True, stop=True)
        mmi.then_inc(em.h["tp"], 1)
        env["p5_Sx"] = em.inc("tp", 1)

    @em.on("vector")
    def _(v):
        def st(ins):
            ins.then_inc(em.h["vp"], 1)
            v.wait_ge(em.h["vp"], em.inc("vp", 1))

        em.W(v, "tp", env["p5_Sx"])
        tm = sb["tmp"]
        rst = sb["rstash"]
        st(v.tensor_tensor(tm[:, 33:33 + E], ps["s"][:, 8:8 + E],
                           tm[:, 17:17 + E], ALU.mult))
        st(v.reduce_sum(rst[:, oc, 4:5], tm[:, 33:33 + E], axis=AX.X))
        st(v.tensor_tensor(tm[:, 50:50 + E], ps["s"][:, 8:8 + E],
                           tm[:, 42:42 + E], ALU.mult))
        st(v.reduce_sum(rst[:, oc, 5:6], tm[:, 50:50 + E], axis=AX.X))
        st(v.tensor_copy(sb["ctotrow"][0:1, oc * E:(oc + 1) * E],
                         ps["s"][0:1, 24:24 + E]))
        env["p5_rtdone"] = em.v["vp"]
    env[f"p5_rt_{oc}"] = env["p5_rtdone"]


def alloc_scopeD(env, st):
    sbuf, cfg = env["sbuf"], env["cfg"]
    sbuf("tb", [128, E * cfg.CC, 64], F32, st)
    sbuf("gidxall", [128, E, cfg.CAP // 16], I16, st)
    sbuf("wupg", [128, 16, 256], BF16, st)
    sbuf("wupg2", [128, 16, 256], BF16, st)
    sbuf("wdg", [128, 512], BF16, st)
    sbuf("wdg2", [128, 512], BF16, st)
    sbuf("fS", [128, 1024], F32, st)
    sbuf("fS2", [128, 2048], F32, st)
    sbuf("xprow", [128, 512], F32, st)
    sbuf("raccrow", [128, 512], F32, st)
    sbuf("orow", [128, D], F32, st)
    sbuf("scrD", [128, D], F32, st)


def alloc_scopeD1(env, st):
    sbuf, cfg = env["sbuf"], env["cfg"]
    CAP = cfg.CAP
    sbuf("xeT", [128, 16, CAP], BF16, st)
    sbuf("xeT2", [128, 16, CAP], BF16, st)
    sbuf("hT", [128, 11, CAP], BF16, st)
    sbuf("wdq", [128, 11, 512], BF16, st)
    sbuf("wdq2", [128, 11, 512], BF16, st)
    sbuf("stg", [128, cfg.CC, D], F32, st)


def alloc_scopeD2(env, st):
    sbuf, cfg = env["sbuf"], env["cfg"]
    sbuf("shxT", [128, 16, cfg.SHG], BF16, st)
    sbuf("shT", [128, 44, cfg.SHG], BF16, st)
    sbuf("sacc", [128, 4, D], F32, st)


def phase_routing_finalize(env):
    em, sb, ps, cfg = env["em"], env["sb"], env["ps"], env["cfg"]
    OCH = cfg.OCH

    @em.on("sync")
    def _(sy):
        sy.dma_start(out=sb["ctot"][...],
                     in_=sb["ctotrow"][0:1, :]).then_inc(em.h["ld"], 16)
        em.inc("ld", 16)
        em.qw(sy, "ld")
    env["p6_ctl"] = em.v["ld"]

    @em.on("vector")
    def _(v):
        em.W(v, "ld", env["p6_ctl"])
        v.tensor_copy(sb["ctotb"][...], sb["ctot"][...]).then_inc(
            em.h["vp"], 1)
        env["p6_ctb"] = em.inc("vp", 1)

    @em.on("tensor")
    def _(t):
        em.W(t, "vp", env["p6_ctb"])
        t.matmul(ps["s"][0:1, 0:E], sb["c_onesb"][0:OCH, :],
                 sb["ctotb"][...], start=True, stop=True)
        mmi = t.matmul(ps["s"][0:OCH, 8:8 + E], sb["c_triu"][0:OCH, 0:OCH],
                       sb["ctotb"][...], start=True, stop=True)
        mmi.then_inc(em.h["tp"], 1)
        env["p6_t0"] = em.inc("tp", 1)

    @em.on("vector")
    def _(v):
        em.W(v, "tp", env["p6_t0"])
        v.tensor_copy(sb["tmp"][0:1, 0:E], ps["s"][0:1, 0:E])
        v.tensor_copy(sb["ctot"][...], ps["s"][0:OCH, 8:8 + E]).then_inc(
            em.h["vp"], 1)
        env["p6_v0"] = em.inc("vp", 1)

    @em.on("sync")
    def _(sy):
        em.W(sy, "vp", env["p6_v0"])
        sy.dma_start(out=env["cnt_in"].ap(),
                     in_=sb["tmp"][0:1, 0:E]).then_inc(em.h["ld"], 16)
        em.inc("ld", 16)
        sy.dma_start(out=sb["cprerow"][0:1, 0:OCH * E],
                     in_=sb["ctot"][...]).then_inc(em.h["ld"], 16)
        em.inc("ld", 16)
        em.qw(sy, "ld")
    env["p6_cst"] = em.v["ld"]
    env["p6_fl"] = em.v["ld"]

    @em.on("gpsimd")
    def _(gp):
        em.W(gp, "ld", env["p6_cst"])
        gp.collective_compute(
            "AllGather", ALU.bypass,
            replica_groups=[[0, 1, 2, 3, 4, 5, 6, 7]],
            ins=[env["cnt_in"].ap().opt()],
            outs=[env["cnt_out"].ap().opt()],
        ).then_inc(em.h["cc"], 1)
        em.inc("cc", 1)
    env["p6_ag"] = em.v["cc"]

    @em.on("sync")
    def _(sy):
        em.W(sy, "cc", env["p6_ag"])
        sy.dma_start(out=sb["cnt"][...],
                     in_=env["cnt_out"].ap()).then_inc(em.h["ld"], 16)
        em.inc("ld", 16)
        em.qw(sy, "ld")
    env["p6_cl"] = em.v["ld"]

    @em.on("tensor")
    def _(t):
        em.W(t, "ld", env["p6_cl"])
        mmi = t.matmul(ps["s"][0:1, 16:16 + E], sb["priorS"][...],
                       sb["cnt"][...], start=True, stop=True)
        mmi.then_inc(em.h["tp"], 1)
        env["p6_t1"] = em.inc("tp", 1)

    @em.on("vector")
    def _(v):
        em.W(v, "tp", env["p6_t1"])
        v.tensor_copy(sb["priorrow"][...],
                      ps["s"][0:1, 16:16 + E]).then_inc(em.h["vp"], 1)
        env["p6_v1"] = em.inc("vp", 1)

    @em.on("tensor")
    def _(t):
        em.W(t, "vp", env["p6_v1"])
        em.W(t, "ld", env["p6_fl"])
        t.matmul(ps["s"][:, 24:24 + E], sb["c_onesrow"][...],
                 sb["priorrow"][...], start=True, stop=True)
        nrep = (OCH * E + 63) // 64
        for i in range(nrep):
            w = min(64, OCH * E - i * 64)
            mmi = t.matmul(ps["d0"][:, i * 64:i * 64 + w],
                           sb["c_onesrow"][...],
                           sb["cprerow"][0:1, i * 64:i * 64 + w],
                           start=True, stop=True)
        mmi.then_inc(em.h["tp"], 1)
        env["p6_t2"] = em.inc("tp", 1)

    @em.on("vector")
    def _(v):
        em.W(v, "tp", env["p6_t2"])
        v.tensor_copy(sb["priorrep"][...], ps["s"][:, 24:24 + E])
        v.tensor_copy(sb["fS"][:, 0:OCH * E],
                      ps["d0"][:, 0:OCH * E]).then_inc(em.h["vp"], 1)
        env["p6_v3"] = em.inc("vp", 1)

    for oc in range(OCH):
        for k in range(2):
            @em.on("vector")
            def _(v, oc=oc, k=k):
                def st(ins):
                    ins.then_inc(em.h["vp"], 1)
                    v.wait_ge(em.h["vp"], em.inc("vp", 1))

                if oc == 0 and k == 0:
                    em.W(v, "vp", env["p6_v3"])
                tm = sb["tmp"]
                rst = sb["rstash"]
                j = k * OCH + oc
                st(v.tensor_scalar(tm[:, 0:E], sb["c_iotae"][...],
                                   rst[:, oc, k:k + 1], None, ALU.is_equal))
                st(v.tensor_tensor(tm[:, 8:8 + E], tm[:, 0:E],
                                   sb["priorrep"][...], ALU.mult))
                st(v.reduce_sum(tm[:, 16:17], tm[:, 8:8 + E], axis=AX.X))
                st(v.tensor_tensor(tm[:, 24:24 + E], tm[:, 0:E],
                                   sb["fS"][:, oc * E:(oc + 1) * E],
                                   ALU.mult))
                st(v.reduce_sum(tm[:, 32:33], tm[:, 24:24 + E], axis=AX.X))
                st(v.tensor_tensor(tm[:, 33:34], rst[:, oc, 4 + k:5 + k],
                                   tm[:, 32:33], ALU.add))
                st(v.tensor_tensor(tm[:, 34:35], tm[:, 33:34], tm[:, 16:17],
                                   ALU.add))
                st(v.tensor_scalar(tm[:, 35:36], tm[:, 34:35],
                                   float(cfg.CREF), None, ALU.is_lt))
                st(v.tensor_tensor(sb["pay"][:, j, 1:2],
                                   rst[:, oc, 2 + k:3 + k], tm[:, 35:36],
                                   ALU.mult))
                st(v.tensor_copy(sb["pay"][:, j, 0:1],
                                 sb["c_iotat"][:, oc:oc + 1]))
                st(v.tensor_scalar(tm[:, 36:37], tm[:, 33:34],
                                   float(cfg.CAP), None, ALU.is_ge))
                st(v.tensor_scalar(tm[:, 37:38], rst[:, oc, k:k + 1],
                                   float(cfg.CAP), None, ALU.mult))
                st(v.tensor_tensor(tm[:, 37:38], tm[:, 37:38], tm[:, 33:34],
                                   ALU.add))
                st(v.tensor_scalar(tm[:, 38:39], sb["c_iotas"][:, j:j + 1],
                                   1.0, float(E * cfg.CAP), ALU.mult,
                                   ALU.add))
                st(v.tensor_tensor(tm[:, 39:40], tm[:, 38:39], tm[:, 37:38],
                                   ALU.subtract))
                st(v.tensor_tensor(tm[:, 39:40], tm[:, 39:40], tm[:, 36:37],
                                   ALU.mult))
                st(v.tensor_tensor(sb["gdst"][:, j:j + 1], tm[:, 37:38],
                                   tm[:, 39:40], ALU.add))
    env["p6_slots"] = em.v["vp"]

    @em.on("tensor")
    def _(t):
        em.W(t, "vp", env["p6_slots"])
        for a in range(8):
            mmi = t.matmul(ps["s"][0:16, a * 2 * OCH:(a + 1) * 2 * OCH],
                           sb["c_pa"][:, a, :], sb["gdst"][...],
                           start=True, stop=True)
        mmi.then_inc(em.h["tp"], 1)
        env["p6_fold"] = em.inc("tp", 1)

    @em.on("vector")
    def _(v):
        em.W(v, "tp", env["p6_fold"])
        for a in range(8):
            vi = v.tensor_copy(
                sb["tidx"][0:16, :].rearrange("b (g a) -> b g a", a=8)[:, :, a],
                ps["s"][0:16, a * 2 * OCH:(a + 1) * 2 * OCH])
        vi.then_inc(em.h["vp"], 1)
        env["p6_tidx"] = em.inc("vp", 1)

    @em.on("sync")
    def _(sy):
        em.W(sy, "vp", env["p6_tidx"])
        sy.dma_start(out=sb["tidx"][16:32, :],
                     in_=sb["tidx"][0:16, :]).then_inc(em.h["ld"], 16)
        em.inc("ld", 16)
        em.qw(sy, "ld")
    env["p6_trep"] = em.v["ld"]

    @em.on("gpsimd")
    def _(gp):
        em.W(gp, "ld", env["p6_trep"])
        em.W(gp, "gs", env["zeros_done"])
        gp.dma_scatter_add(
            env["TBL"][:, 0:2], sb["pay"][...], sb["tidx"][...],
            num_idxs=cfg.NSLOT, num_idxs_reg=env["regs"][cfg.NSLOT],
            elem_size=2, elem_step=cfg.TROW).then_inc(em.h["gq"], 16)
        em.inc("gq", 16)
    env["p6_scat"] = em.v["gq"]

    @em.on("sync")
    def _(sy):
        em.W(sy, "gq", env["p6_scat"])
        sy.dma_start(out=sb["tb"][...],
                     in_=env["TBL"][0:E * cfg.CAP, :].rearrange(
                         "(g p) c -> p g c", p=128)).then_inc(em.h["ld"], 16)
        em.inc("ld", 16)
        em.qw(sy, "ld")
    env["p6_tb"] = em.v["ld"]

    CC = cfg.CC

    @em.on("vector")
    def _(v):
        v.memset(sb["gidxall"][...], 0).then_inc(em.h["vp"], 1)
        env["p6_giz"] = em.inc("vp", 1)

    for eh in range(0, E, 2):
        @em.on("tensor")
        def _(t, eh=eh):
            em.W(t, "ld", env["p6_tb"])
            if eh >= 2:
                em.W(t, "vp", env[f"p6_gcp_{eh - 2}"])
            for e2 in (eh, eh + 1):
                for a in range(8):
                    mmi = t.matmul(
                        ps["s"][0:16, (e2 % 2) * 256 + a * CC:
                                (e2 % 2) * 256 + (a + 1) * CC],
                        sb["c_pa"][:, a, :],
                        sb["tb"][:, e2 * CC:(e2 + 1) * CC, 0:1].rearrange(
                            "p g c -> p (g c)"),
                        start=True, stop=True)
            mmi.then_inc(em.h["tp"], 1)
            env[f"p6_gf_{eh}"] = em.inc("tp", 1)

        @em.on("vector")
        def _(v, eh=eh):
            em.W(v, "tp", env[f"p6_gf_{eh}"])
            if eh == 0:
                em.W(v, "vp", env["p6_giz"])
            for e2 in (eh, eh + 1):
                for a in range(8):
                    vi = v.tensor_copy(
                        sb["gidxall"][0:16, e2, :].rearrange(
                            "b (g a) -> b g a", a=8)[:, :, a],
                        ps["s"][0:16, (e2 % 2) * 256 + a * CC:
                                (e2 % 2) * 256 + (a + 1) * CC])
            vi.then_inc(em.h["vp"], 1)
            env[f"p6_gcp_{eh}"] = em.inc("vp", 1)

    @em.on("sync")
    def _(sy):
        em.W(sy, "vp", env[f"p6_gcp_{E - 2}"])
        sy.dma_start(out=sb["gidxall"][16:32, :, :],
                     in_=sb["gidxall"][0:16, :, :]).then_inc(em.h["ld"], 16)
        em.inc("ld", 16)
        em.qw(sy, "ld")
    env["p6_gall"] = em.v["ld"]


def phase_experts(env):
    for e in range(E):
        _p7_expert(env, e)


def _p7_expert(env, e):
    em, sb, ps, cfg = env["em"], env["sb"], env["ps"], env["cfg"]
    CAP, CC = cfg.CAP, cfg.CC
    xeT = sb["xeT"] if e % 2 == 0 else sb["xeT2"]
    gidx = sb["gidxall"][:, e, :]
    gsem = "qa" if e % 2 == 0 else "qb"

    @em.on("gpsimd")
    def _(gp):
        em.W(gp, "ld", env["p6_gall"])
        em.W(gp, "scr", env[f"p5_xpst_{cfg.OCH - 1}"])
        if e >= 2:
            em.W(gp, "tp", env[f"p7_updone_{e - 2}"])
        gp.dma_gather(xeT[...], env["XPH"], gidx,
                      num_idxs=CAP, num_idxs_reg=env["regs"][CAP], elem_size=D,
                      elem_step=D, transpose=True).then_inc(em.h[gsem], 16)
        env[f"p7_gath_{e}"] = (gsem, em.inc(gsem, 16))

    NP = 11
    for pi in range(NP):
        wub = sb["wupg"] if pi % 2 == 0 else sb["wupg2"]

        wsem = "wla" if pi % 2 == 0 else "wlb"

        @em.on("gpsimd")
        def _(gp, pi=pi, wub=wub, wsem=wsem):
            guard = env.get(f"p7_up_{e}_{pi - 2}",
                            env.get(f"p7_updone_{e - 1}", 0))
            em.W(gp, "tp", guard)
            gp.dma_start(out=wub[:, :, 0:128],
                         in_=env["wug"][e, :, pi * 128:(pi + 1) * 128]
                         .rearrange("(k p) m -> p k m", p=128)).then_inc(
                em.h[wsem], 16)
            em.inc(wsem, 16)
            gp.dma_start(out=wub[:, :, 128:256],
                         in_=env["wug"][e, :,
                                        (NP + pi) * 128:(NP + pi + 1) * 128]
                         .rearrange("(k p) m -> p k m", p=128)).then_inc(
                em.h[wsem], 16)
            em.inc(wsem, 16)
        wld = (wsem, em.v[wsem])

        bset = (["q0", "q1", "q2", "q3"] if pi % 2 == 0
                else ["d0", "d1", "s", "e"])

        @em.on("tensor")
        def _(t, pi=pi, wub=wub, bset=bset):
            em.W(t, wld[0], wld[1])
            if pi == 0:
                em.W(t, env[f"p7_gath_{e}"][0], env[f"p7_gath_{e}"][1])
            if pi < 2:
                if e >= 1:
                    em.W(t, "vp", env[f"p7_dnev_{e - 1}_3_{CC - 1}"])
            if pi >= 2:
                em.W(t, "vp", env[f"p7_upev_{e}_{pi - 2}"])
            slc = ([(0, 512), (512, CAP)] if CAP > 512 else [(0, CAP)])
            for si, (a, b) in enumerate(slc):
                for k in range(16):
                    t.matmul(ps[bset[si]][:, 0:b - a], wub[:, k, 0:128],
                             xeT[:, k, a:b], start=(k == 0), stop=(k == 15))
            for si, (a, b) in enumerate(slc):
                for k in range(16):
                    mmi = t.matmul(ps[bset[2 + si]][:, 0:b - a],
                                   wub[:, k, 128:256], xeT[:, k, a:b],
                                   start=(k == 0), stop=(k == 15))
            mmi.then_inc(em.h["tp"], 1)
            env[f"p7_up_{e}_{pi}"] = em.inc("tp", 1)

        @em.on("scalar")
        def _(s, pi=pi, bset=bset):
            em.W(s, "tp", env[f"p7_up_{e}_{pi}"])
            fofs = 0 if pi % 2 == 0 else 1024
            slc = ([(0, 512), (512, CAP)] if CAP > 512 else [(0, CAP)])
            for si, (a, b) in enumerate(slc):
                si_i = s.activation(sb["fS2"][:, fofs + a:fofs + b],
                                    ps[bset[si]][:, 0:b - a], AF.Silu)
            si_i.then_inc(em.h["sp"], 1)
            env[f"p7_silu_{e}_{pi}"] = em.inc("sp", 1)

        @em.on("vector")
        def _(v, pi=pi, bset=bset):
            em.W(v, "sp", env[f"p7_silu_{e}_{pi}"])
            fofs = 0 if pi % 2 == 0 else 1024
            slc = ([(0, 512), (512, CAP)] if CAP > 512 else [(0, CAP)])
            for si, (a, b) in enumerate(slc):
                vi = v.tensor_tensor(sb["hT"][:, pi, a:b],
                                     sb["fS2"][:, fofs + a:fofs + b],
                                     ps[bset[2 + si]][:, 0:b - a], ALU.mult)
            vi.then_inc(em.h["vp"], 1)
            env[f"p7_upev_{e}_{pi}"] = em.inc("vp", 1)
    env[f"p7_updone_{e}"] = em.v["tp"]

    dbanks = ["q0", "q1", "q2", "q3", "d0"]
    for n in range(4):
        wdb = sb["wdq"] if n % 2 == 0 else sb["wdq2"]
        dsem = "scb" if n % 2 == 0 else "gl"

        @em.on("gpsimd")
        def _(gp, n=n, wdb=wdb, dsem=dsem):
            guard = env.get(f"p7_dnmm_{e}_{n - 2}",
                            env.get(f"p7_dndone_{e - 1}", 0))
            em.W(gp, "tp", guard)
            gp.dma_start(out=wdb[...],
                         in_=env["wdn"][e, :, n * 512:(n + 1) * 512]
                         .rearrange("(j p) d -> p j d", p=128)).then_inc(
                em.h[dsem], 16)
            em.inc(dsem, 16)
        dld = (dsem, em.v[dsem])

        @em.on("tensor")
        def _(t, n=n, wdb=wdb, dld=dld):
            em.W(t, dld[0], dld[1])
            if n == 0:
                em.W(t, "vp", env[f"p7_upev_{e}_{10}"])
            else:
                em.W(t, "vp", env[f"p7_dnev_{e}_{n - 1}_{CC - 1}"])
            for j in range(11):
                for cc in range(CC):
                    mmi = t.matmul(ps[dbanks[cc]][:, 0:512],
                                   sb["hT"][:, j, cc * 128:(cc + 1) * 128],
                                   wdb[:, j, :],
                                   start=(j == 0), stop=(j == 10))
            mmi.then_inc(em.h["tp"], 1)
            env[f"p7_dnmm_{e}_{n}"] = em.inc("tp", 1)

        @em.on("vector")
        def _(v, n=n):
            em.W(v, "tp", env[f"p7_dnmm_{e}_{n}"])
            if n == 0:
                prev = env.get(f"p7_scat_{e - 1}_{CC - 1}")
                if prev is not None:
                    em.W(v, "sca", prev)
            for cc in range(CC):
                vi = v.tensor_scalar(sb["stg"][:, cc, n * 512:(n + 1) * 512],
                                     ps[dbanks[cc]][:, 0:512],
                                     sb["tb"][:, e * CC + cc, 1:2], None,
                                     ALU.mult)
                vi.then_inc(em.h["vp"], 1)
                env[f"p7_dnev_{e}_{n}_{cc}"] = em.inc("vp", 1)

    for cc in range(CC):
        @em.on("gpsimd")
        def _(gp, cc=cc):
            em.W(gp, "vp", env[f"p7_dnev_{e}_3_{cc}"])
            em.W(gp, "sca", em.v["sca"])
            gp.dma_scatter_add(
                env["RACC"],
                sb["stg"][:, cc, :].rearrange("p (g d) -> p g d", g=1),
                gidx[:, cc * 8:(cc + 1) * 8],
                num_idxs=128,
                num_idxs_reg=env["regs"][128],
                elem_size=D,
                elem_step=D).then_inc(em.h["sca"], 16)
            env[f"p7_scat_{e}_{cc}"] = em.inc("sca", 16)
    env[f"p7_dndone_{e}"] = em.v["tp"]
    env["p7_allscat"] = [("sca", em.v["sca"])]


def phase_shared_epilogue(env):
    for sg in range(env["cfg"].NSHG):
        _p8_group(env, sg)


def _p8_group(env, sg):
    em, sb, ps, cfg = env["em"], env["sb"], env["ps"], env["cfg"]
    SHG = cfg.SHG
    g0 = sg * SHG
    NS = SHG // 512

    @em.on("gpsimd")
    def _(gp):
        em.W(gp, "scr", env[f"p5_xpst_{cfg.OCH - 1}"])
        if sg == 0:
            for sn, sv in env["p7_allscat"]:
                em.W(gp, sn, sv)
            em.W(gp, "tp", env[f"p7_dndone_{E - 1}"])
        else:
            em.W(gp, "tp", env[f"p8_updone_{sg - 1}"])
        gp.dma_gather(sb["shxT"][...], env["XPH"],
                      sb["c_gidxsh"][:, g0 // 16:(g0 + SHG) // 16],
                      num_idxs=SHG, num_idxs_reg=env["regs"][SHG],
                      elem_size=D,
                      elem_step=D, transpose=True).then_inc(em.h["gq"], 16)
        env[f"p8_gath_{sg}"] = em.inc("gq", 16)

    for pi in range(44):
        wub = sb["wupg"] if pi % 2 == 0 else sb["wupg2"]

        wsem = "wsa" if pi % 2 == 0 else "wsb"

        @em.on("sync")
        def _(sy, pi=pi, wub=wub, wsem=wsem):
            if pi < 2:
                em.W(sy, "gs", env["wcast_done"])
                em.W(sy, "tp", env[f"p7_updone_{E - 1}"] if sg == 0
                     else env[f"p8_dndone_{sg - 1}"])
            else:
                em.W(sy, "tp", env[f"p8_up_{sg}_{pi - 2}"])
            sy.dma_start(out=wub[:, :, 0:128],
                         in_=env["WSUB"][:, pi * 128:(pi + 1) * 128]
                         .rearrange("(k p) m -> p k m", p=128)).then_inc(
                em.h[wsem], 16)
            em.inc(wsem, 16)
            sy.dma_start(out=wub[:, :, 128:256],
                         in_=env["WSUB"][:, (44 + pi) * 128:(45 + pi) * 128]
                         .rearrange("(k p) m -> p k m", p=128)).then_inc(
                em.h[wsem], 16)
            em.inc(wsem, 16)
        wld = (wsem, em.v[wsem])

        bset = (["q0", "q1", "q2", "q3"] if pi % 2 == 0
                else ["d0", "d1", "s", "e"])

        @em.on("tensor")
        def _(t, pi=pi, wub=wub, bset=bset):
            em.W(t, wld[0], wld[1])
            if pi == 0:
                em.W(t, "gq", env[f"p8_gath_{sg}"])
            if pi < 2:
                em.W(t, "vp", env.get("p8_evlast",
                                      env[f"p7_dnev_{E - 1}_3_{env["cfg"].CC - 1}"]))
            if pi >= 2:
                em.W(t, "vp", env[f"p8_upev_{sg}_{pi - 2}"])
            for ns in range(NS):
                for k in range(16):
                    t.matmul(ps[bset[0] if ns == 0 else bset[1]][:, 0:512],
                             wub[:, k, 0:128],
                             sb["shxT"][:, k, ns * 512:(ns + 1) * 512],
                             start=(k == 0), stop=(k == 15))
            for ns in range(NS):
                for k in range(16):
                    mmi = t.matmul(ps[bset[2] if ns == 0 else bset[3]][:, 0:512],
                                   wub[:, k, 128:256],
                                   sb["shxT"][:, k, ns * 512:(ns + 1) * 512],
                                   start=(k == 0), stop=(k == 15))
            mmi.then_inc(em.h["tp"], 1)
            env[f"p8_up_{sg}_{pi}"] = em.inc("tp", 1)

        @em.on("scalar")
        def _(s, pi=pi, bset=bset):
            em.W(s, "tp", env[f"p8_up_{sg}_{pi}"])
            fofs = 0 if pi % 2 == 0 else 1024
            si_i = s.activation(sb["fS2"][:, fofs:fofs + 512],
                                ps[bset[0]][:, 0:512], AF.Silu)
            if NS > 1:
                si_i = s.activation(sb["fS2"][:, fofs + 512:fofs + 1024],
                                    ps[bset[1]][:, 0:512], AF.Silu)
            si_i.then_inc(em.h["sp"], 1)
            env[f"p8_silu_{sg}_{pi}"] = em.inc("sp", 1)

        @em.on("vector")
        def _(v, pi=pi, bset=bset):
            em.W(v, "sp", env[f"p8_silu_{sg}_{pi}"])
            fofs = 0 if pi % 2 == 0 else 1024
            vi = v.tensor_tensor(sb["shT"][:, pi, 0:512],
                                 sb["fS2"][:, fofs:fofs + 512],
                                 ps[bset[2]][:, 0:512], ALU.mult)
            if NS > 1:
                vi = v.tensor_tensor(sb["shT"][:, pi, 512:1024],
                                     sb["fS2"][:, fofs + 512:fofs + 1024],
                                     ps[bset[3]][:, 0:512], ALU.mult)
            vi.then_inc(em.h["vp"], 1)
            env[f"p8_upev_{sg}_{pi}"] = em.inc("vp", 1)
    env[f"p8_updone_{sg}"] = em.v["tp"]

    banks = ["q0", "q1", "q2", "q3"]
    NCB = (SHG // 128) // 4
    for ccb in range(NCB):
        for dq in range(4):
            for j in range(44):
                gi = (ccb * 4 + dq) * 44 + j
                wdb = sb["wdg"] if gi % 2 == 0 else sb["wdg2"]

                wsem = "wsa" if gi % 2 == 0 else "wsb"

                @em.on("sync")
                def _(sy, gi=gi, dq=dq, j=j, wdb=wdb, wsem=wsem):
                    if gi < 2:
                        em.W(sy, "tp", env[f"p8_updone_{sg}"])
                    else:
                        em.W(sy, "tp", env[f"p8_dn_{sg}_{gi - 2}"])
                    sy.dma_start(out=wdb[...],
                                 in_=env["WSDB"][j * 128:(j + 1) * 128,
                                                 dq * 512:(dq + 1) * 512]
                                 ).then_inc(em.h[wsem], 16)
                    em.inc(wsem, 16)
                wld = (wsem, em.v[wsem])

                @em.on("tensor")
                def _(t, gi=gi, ccb=ccb, dq=dq, j=j, wdb=wdb):
                    em.W(t, wld[0], wld[1])
                    if j == 0:
                        if dq == 0 and ccb == 0:
                            em.W(t, "vp", env[f"p8_upev_{sg}_{43}"])
                        if "p8_evlast" in env:
                            em.W(t, "vp", env["p8_evlast"])
                    for cc4 in range(4):
                        cc = ccb * 4 + cc4
                        mmi = t.matmul(ps[banks[cc4]][:, 0:512],
                                       sb["shT"][:, j, cc * 128:(cc + 1) * 128],
                                       wdb[...], start=(j == 0),
                                       stop=(j == 43))
                    mmi.then_inc(em.h["tp"], 1)
                    env[f"p8_dn_{sg}_{gi}"] = em.inc("tp", 1)

            for cc4 in range(4):
                cc = ccb * 4 + cc4
                rows = g0 + cc * 128

                @em.on("sync")
                def _(sy, cc4=cc4, rows=rows, dq=dq):
                    if "p8_rowused" not in env:
                        for sn, sv in env["p7_allscat"]:
                            em.W(sy, sn, sv)
                        em.W(sy, "scr", env[f"p5_xpst_{cfg.OCH - 1}"])
                    em.W(sy, "vp", env.get("p8_rowused", 0))
                    sy.dma_start(out=sb["xprow"][...],
                                 in_=env["XP"][rows:rows + 128,
                                               dq * 512:(dq + 1) * 512]
                                 ).then_inc(em.h["ld"], 16)
                    em.inc("ld", 16)
                    sy.dma_start(out=sb["raccrow"][...],
                                 in_=env["RACC"][rows:rows + 128,
                                                 dq * 512:(dq + 1) * 512]
                                 ).then_inc(em.h["ld"], 16)
                    em.inc("ld", 16)
                    em.qw(sy, "ld")
                rowld = em.v["ld"]

                @em.on("vector")
                def _(v, cc4=cc4, dq=dq, ccb=ccb, rowld=rowld):
                    em.W(v, "ld", rowld)
                    em.W(v, "tp", env[f"p8_dn_{sg}_{(ccb * 4 + dq) * 44 + 43}"])
                    em.W(v, "sp", env.get("p8_lnsq", 0))
                    v.tensor_tensor(sb["xprow"][...], sb["xprow"][...],
                                    sb["raccrow"][...],
                                    ALU.add).then_inc(em.h["vp"], 1)
                    v.wait_ge(em.h["vp"], em.inc("vp", 1))
                    vi = v.tensor_tensor(
                        sb["sacc"][:, cc4, dq * 512:(dq + 1) * 512],
                        ps[banks[cc4]][:, 0:512], sb["xprow"][...], ALU.add)
                    vi.then_inc(em.h["vp"], 1)
                    env["p8_rowused"] = em.inc("vp", 1)
                    if cc4 == 3:
                        env["p8_evlast"] = env["p8_rowused"]

        for cc4 in range(4):
            cc = ccb * 4 + cc4
            rows = g0 + cc * 128
            ln_stats_apply(env, sb["sacc"][:, cc4, :], sb["orow"][...], 16,
                           scr_name="scrD",
                           pre=("ost", env.get("p8_ost", 0)),
                           src_ready=("vp", env["p8_rowused"]))
            env["p8_lnsq"] = env["ln_s0"]

            @em.on("sync")
            def _(sy, rows=rows, od=env["ln_done"]):
                em.W(sy, "vp", od)
                sy.dma_start(out=env["out"][rows:rows + 128, :],
                             in_=sb["orow"][...]).then_inc(em.h["ost"], 16)
                env["p8_ost"] = em.inc("ost", 16)
                em.qw(sy, "ost")
    env[f"p8_dndone_{sg}"] = em.v["tp"]


def phase_finish(env, upto):
    em, cfg, sb = env["em"], env["cfg"], env["sb"]
    dbg = env["dbg"]

    @em.on("sync")
    def _(sy):
        if "x1" in dbg:
            em.W(sy, "scr", env[f"p5_xpst_{cfg.OCH - 1}"])
            for i in range(cfg.T // 128):
                sy.dma_start(out=dbg["x1"][i * 128:(i + 1) * 128, :],
                             in_=env["XP"][i * 128:(i + 1) * 128, :]
                             ).then_inc(em.h["ost"], 16)
                em.inc("ost", 16)
        if "part" in dbg:
            em.W(sy, "scr", env[f"p1_scr_{cfg.NSC - 1}"])
            for i in range(cfg.S // 128):
                sy.dma_start(out=dbg["part"][i * 128:(i + 1) * 128, :],
                             in_=env["PART"][i * 128:(i + 1) * 128, :]
                             ).then_inc(em.h["ost"], 16)
                em.inc("ost", 16)
        if "sqt" in dbg:
            em.W(sy, "scr", env[f"p1_scr_{cfg.NSC - 1}"])
            for h in range(H_PER):
                sy.dma_start(out=dbg["sqt"][h * 128:(h + 1) * 128, :],
                             in_=env["SQT"][h, :, :]).then_inc(
                    em.h["ost"], 16)
                em.inc("ost", 16)
        if "agout" in dbg:
            em.W(sy, "cc", env["a2a_done"])
            for i in range(2 * cfg.S // 128):
                sy.dma_start(out=dbg["agout"][i * 128:(i + 1) * 128, :],
                             in_=env["agout"].ap()[i * 128:(i + 1) * 128, :]
                             ).then_inc(em.h["ost"], 16)
                em.inc("ost", 16)
        if "rstash" in dbg:
            em.W(sy, "vp", env.get("p5_rtdone", 0))
            sy.dma_start(out=dbg["rstash"][...],
                         in_=sb["rstash"][...].rearrange("p o c -> p (o c)")
                         ).then_inc(em.h["ost"], 16)
            em.inc("ost", 16)
        if "racc" in dbg:
            for sn, sv in env.get("p7_allscat", []):
                em.W(sy, sn, sv)
            for i in range(cfg.T // 128):
                sy.dma_start(out=dbg["racc"][i * 128:(i + 1) * 128, :],
                             in_=env["RACC"][i * 128:(i + 1) * 128, :]
                             ).then_inc(em.h["ost"], 16)
                em.inc("ost", 16)
        if "tblv" in dbg:
            em.W(sy, "gq", env.get("p6_scat", 0))
            n = dbg["tblv"].shape[0]
            sy.dma_start(out=dbg["tblv"],
                         in_=env["TBL"][0:n, :]).then_inc(em.h["ost"], 16)
            em.inc("ost", 16)
        em.W(sy, "ost", em.v["ost"])
        em.W(sy, "scr", em.v["scr"])

    @em.on("gpsimd")
    def _(gp):
        em.W(gp, "gq", em.v["gq"])
        em.W(gp, "cc", em.v["cc"])


def _gidxwo(cfg, hh):
    cols = np.zeros((128, cfg.OCH * 16), np.int16)
    for oc in range(cfg.OCH):
        ids = []
        for p in range(128):
            ids.append(hh * cfg.T + oc * 128 + p)
        for p in range(128):
            ids.append(cfg.S + hh * cfg.T + oc * 128 + p)
        cols[:, oc * 16:(oc + 1) * 16] = _wrap16(ids)[0:128]
    return cols


def _prep_inputs(cfg, inputs, core, consts):
    b, hh = core // 2, core % 2
    h0 = hh * H_PER
    x = np.asarray(inputs["L0_attn_hidden_states"])
    wq = np.asarray(inputs["L0_attn_w_q"])
    wk = np.asarray(inputs["L0_attn_w_k"])
    wv = np.asarray(inputs["L0_attn_w_v"])
    wo = np.asarray(inputs["L0_attn_w_o"])
    mems = np.asarray(inputs["L0_attn_memories"])
    nrms = np.asarray(inputs["L0_attn_memory_norms"])
    im = {
        "xb": np.ascontiguousarray(x[b]),
        "xown": np.ascontiguousarray(x[b, hh * cfg.T:(hh + 1) * cfg.T]),
        "wqT": np.ascontiguousarray(wq[h0 * HD:(h0 + H_PER) * HD, :].T),
        "wkT": np.ascontiguousarray(wk[h0 * HD:(h0 + H_PER) * HD, :].T),
        "wvT": np.ascontiguousarray(wv[h0 * HD:(h0 + H_PER) * HD, :].T),
        "woT": np.ascontiguousarray(wo.T),
        "memi": np.ascontiguousarray(
            mems[:, h0:h0 + H_PER].reshape(M * H_PER, HD * HD)),
        "nrmT": np.ascontiguousarray(
            nrms[:, h0:h0 + H_PER].transpose(1, 2, 0)),
        "nrmF": np.ascontiguousarray(nrms.reshape(M * 16, HD)),
        "gatei": np.ascontiguousarray(
            np.asarray(inputs["L0_attn_gate"])[h0:h0 + H_PER][None, :]),
        "routeri": np.ascontiguousarray(inputs["L0_ffn_router"]),
        "wug": np.asarray(inputs["L0_ffn_w_up_gate"]),
        "wdn": np.asarray(inputs["L0_ffn_w_down"]),
        "wsu": np.asarray(inputs["L0_ffn_shared_w_up_gate"]),
        "wsd": np.asarray(inputs["L0_ffn_shared_w_down"]),
        "prior_sel": (np.arange(8) < core).astype(np.float32)[:, None],
        "gidxwo": _gidxwo(cfg, hh),
    }
    for n, v in consts.items():
        im["cin_" + n] = v
    return im


def kernel(**inputs):
    cfg = Cfg()
    nc = build_program(cfg)
    consts = make_consts(cfg)
    in_maps = [_prep_inputs(cfg, inputs, c, consts) for c in range(8)]
    res = run_bass_kernel_spmd(nc, in_maps, core_ids=list(range(8)))
    full = np.empty((4, 4096, 2048), np.float32)
    for c in range(8):
        b, hh = c // 2, c % 2
        full[b, hh * cfg.T:(hh + 1) * cfg.T, :] = res.results[c]["out"]
    return full


# version 05
# speedup vs baseline: 1.4239x; 1.0973x over previous
import math
from contextlib import ExitStack

import numpy as np
import ml_dtypes

import concourse.bass as bass
import concourse.mybir as mybir
from concourse import library_config
from concourse.library_overlay import lower_extended_insts
from concourse.aot_env import allow_ici_all_to_all
from concourse.bass_utils import run_bass_kernel_spmd

F32 = mybir.dt.float32
BF16 = mybir.dt.bfloat16
I16 = mybir.dt.int16
AF = mybir.ActivationFunctionType
ALU = mybir.AluOpType
AX = mybir.AxisListType
BF = ml_dtypes.bfloat16

D = 2048
H_PER = 8
HD = 128
HC = H_PER * HD
M = 4
SCALE = HD ** (-0.5)
EPS = 1e-6
LN_EPS = 1e-5
E = 8
I_FF = 1408
J_FF = 5632
NEG_BIG = -1.0e30


class Cfg:
    def __init__(self, seq=4096):
        self.S = seq
        self.T = seq // 2
        self.NCH = self.S // 128
        self.NSC = self.S // 512
        self.OCH = self.T // 128
        cap = int(math.ceil(self.T * 2 * 1.25 / E))
        self.CAP = ((cap + 127) // 128) * 128
        self.CC = self.CAP // 128
        self.CREF = int(math.ceil(8 * self.T * 2 / E * 1.25))
        self.NSLOT = self.T * 2
        self.TROW = 64
        self.NTBL = E * self.CAP + self.NSLOT
        self.SHG = min(512, self.T)
        self.NSHG = self.T // self.SHG


def _wrap16(vals, rows=128):
    n = len(vals)
    cols = (n + 15) // 16
    out = np.zeros((rows, cols), np.int16)
    for i, v in enumerate(vals):
        out[i % 16, i // 16] = v
    out[16:32] = out[0:16]
    return out


def make_consts(cfg):
    c = {}
    c["ident"] = np.eye(128, dtype=BF)
    c["identf"] = np.eye(128, dtype=np.float32)
    c["triu"] = np.triu(np.ones((128, 128), np.float32), 1).astype(BF)
    mt = np.triu(np.ones((128, 128), np.float32), 0)
    c["maskT"] = np.tile(mt, (1, H_PER)).astype(BF)
    c["onesb"] = np.ones((128, 1), dtype=BF)
    c["onesrow"] = np.ones((1, 128), np.float32)
    pa = np.zeros((128, 8, 16), np.float32)
    for a in range(8):
        for b in range(16):
            pa[16 * a + b, a, b] = 1.0
    c["pa"] = pa
    c["iotae"] = np.tile(np.arange(8, dtype=np.float32)[None, :], (128, 1))
    c["iotat"] = (np.arange(cfg.OCH, dtype=np.float32)[None, :] * 128
                  + np.arange(128, dtype=np.float32)[:, None])
    c["iotas"] = (np.arange(2 * cfg.OCH, dtype=np.float32)[None, :] * 128
                  + np.arange(128, dtype=np.float32)[:, None])
    c["gidxsh"] = _wrap16(np.arange(cfg.T, dtype=np.int16))
    sel = np.zeros((M * 16, M), np.float32)
    for m in range(M):
        sel[m * 16:(m + 1) * 16, m] = 1.0
    c["esel"] = sel
    return c


def const_specs(c):
    return [
        ("ident", [128, 128], BF16), ("identf", [128, 128], F32),
        ("triu", [128, 128], BF16),
        ("maskT", [128, H_PER * 128], BF16), ("onesb", [128, 1], BF16),
        ("onesrow", [1, 128], F32), ("pa", [128, 8, 16], F32),
        ("iotae", [128, 8], F32), ("iotat", [128, c.OCH], F32),
        ("iotas", [128, 2 * c.OCH], F32), ("gidxsh", [128, c.T // 16], I16),
        ("esel", [M * 16, M], F32),
    ]


class EM:

    def __init__(self, nc, block, stack, names):
        self.nc = nc
        self.block = block
        self.h = {}
        self.v = {}
        for n in names:
            self.h[n] = stack.enter_context(nc.semaphore("s_" + n))
            self.v[n] = 0

    def inc(self, n, by=1):
        self.v[n] += by
        return self.v[n]

    def W(self, eng, n, val):
        assert val <= self.v[n], f"wait on future {n}: {val} > {self.v[n]}"
        if val > 0:
            eng.wait_ge(self.h[n], val)

    def on(self, engine):
        def deco(fn):
            getattr(self.block, engine)(fn)
        return deco

    def qw(self, eng, n):
        if self.v[n] > 0:
            eng.wait_ge(self.h[n], self.v[n])


SEMS = ["ld", "wl", "scr", "tp", "vp", "sp", "gq", "cc", "ost", "bar",
        "gl", "gs", "wla", "wlb", "qa", "qb", "sca", "scb", "mk",
        "mk2", "mk3", "mk4", "wsa", "wsb"]


def full_barrier(env):
    em = env["em"]
    tgt = em.v["bar"] + 5
    dmas = {k: em.v[k] for k in ["scr", "ld", "wl", "ost", "gq", "cc", "gl",
                                 "gs", "wla", "wlb", "qa", "qb", "sca",
                                 "scb", "mk", "mk2", "mk3", "mk4",
                                 "wsa", "wsb"]}
    prods = {k: em.v[k] for k in ["tp", "vp", "sp"]}

    @em.on("sync")
    def _(sy):
        for k, v in dmas.items():
            em.W(sy, k, v)
        for k, v in prods.items():
            em.W(sy, k, v)
        sy.sem_inc(em.h["bar"], 1)
        sy.wait_ge(em.h["bar"], tgt)

    for name in ["gpsimd", "tensor", "vector", "scalar"]:
        @em.on(name)
        def _(eng):
            eng.sem_inc(em.h["bar"], 1)
            eng.wait_ge(em.h["bar"], tgt)
    em.v["bar"] = tgt


def build_program(cfg, dbg=(), upto=99):
    nc = bass.Bass("TRN2", target_bir_lowering=False)
    S, T, OCH, NSC, CAP, CC = cfg.S, cfg.T, cfg.OCH, cfg.NSC, cfg.CAP, cfg.CC

    ein = lambda n, s, d: nc.dram_tensor(n, s, d, kind="ExternalInput").ap()
    env = {"cfg": cfg, "nc": nc}
    env["xb"] = ein("xb", [S, D], F32)
    env["xown"] = ein("xown", [T, D], F32)
    env["wqT"] = ein("wqT", [D, HC], F32)
    env["wkT"] = ein("wkT", [D, HC], F32)
    env["wvT"] = ein("wvT", [D, HC], F32)
    env["woT"] = ein("woT", [D, D], F32)
    env["memi"] = ein("memi", [M * H_PER, HD * HD], F32)
    env["nrmT"] = ein("nrmT", [H_PER, HD, M], F32)
    env["nrmF"] = ein("nrmF", [M * 16, HD], F32)
    env["gatei"] = ein("gatei", [1, H_PER], F32)
    env["routeri"] = ein("routeri", [D, E], F32)
    env["wug"] = ein("wug", [E, D, 2 * I_FF], F32)
    env["wdn"] = ein("wdn", [E, I_FF, D], F32)
    env["wsu"] = ein("wsu", [D, 2 * J_FF], F32)
    env["wsd"] = ein("wsd", [J_FF, D], F32)
    env["prior_sel"] = ein("prior_sel", [8, 1], F32)
    env["gidxwo"] = ein("gidxwo", [128, cfg.OCH * 16], I16)
    cst_sp = const_specs(cfg)
    env["cst"] = {n: ein("cin_" + n, s, d) for n, s, d in cst_sp}
    env["out"] = nc.dram_tensor("out", [T, D], F32, kind="ExternalOutput").ap()
    env["dbg"] = {}
    for name, shape, dt in dbg:
        env["dbg"][name] = nc.dram_tensor("dbg_" + name, shape, dt,
                                          kind="ExternalOutput").ap()

    env["XBH"] = nc.dram_tensor("XBH", [S, D], BF16).ap()
    env["SQT"] = nc.dram_tensor("SQT", [H_PER, HD, S], BF16).ap()
    env["PART"] = nc.dram_tensor("PART", [S, HC], BF16).ap()
    env["XP"] = nc.dram_tensor("XP", [T, D], F32).ap()
    env["XPH"] = nc.dram_tensor("XPH", [T, D], BF16).ap()
    env["TBL"] = nc.dram_tensor("TBL", [cfg.NTBL, cfg.TROW], F32).ap()
    env["RACC"] = nc.dram_tensor("RACC", [T, D], F32).ap()
    env["WSUB"] = nc.dram_tensor("WSUB", [D, 2 * J_FF], BF16).ap()
    env["WSDB"] = nc.dram_tensor("WSDB", [J_FF, D], BF16).ap()
    env["agin"] = nc.dram_tensor("agin", [S, HC], BF16)
    env["agout"] = nc.dram_tensor("agout", [2 * S, HC], BF16)
    env["cnt_in"] = nc.dram_tensor("cnt_in", [1, E], F32)
    env["cnt_out"] = nc.dram_tensor("cnt_out", [8, E], F32)

    with ExitStack() as stack:
        with nc.Block() as block:
            em = EM(nc, block, stack, SEMS)
            env["em"] = em
            sb = {}
            env["sb"] = sb

            def sbuf(name, shape, dt, st=stack):
                t = st.enter_context(nc.sbuf_tensor(name, shape, dt))
                sb[name] = t
                return t
            env["sbuf"] = sbuf

            for n, s, d in cst_sp:
                sbuf("c_" + n, s, d)
            for n, s, d in [
                ("nrmTf", [128, H_PER, M], F32), ("nrmTb", [128, H_PER, M], BF16),
                ("nrmFs", [M * 16, HD], F32), ("grow", [1, H_PER], F32),
                ("g1m", [128, H_PER], F32), ("weff", [128, H_PER * M], F32),
                ("relacc", [128, H_PER], F32), ("relaccb", [128, H_PER], BF16),
                ("priorS", [8, 1], F32), ("priorrow", [1, E], F32),
                ("priorrep", [128, E], F32), ("routerS", [128, 16, E], F32),
                ("zero", [128, 512], BF16),
                ("S", [128, HC], F32), ("Sb", [128, HC], BF16),
                ("Z", [128, H_PER], F32), ("Zb", [128, H_PER], BF16),
                ("rstash", [128, OCH, 8], F32),
                ("ctot", [OCH, E], F32), ("ctotb", [OCH, E], BF16),
                ("ctotrow", [1, OCH * E], F32),
                ("cprerow", [1, 128], F32), ("cnt", [8, E], F32),
                ("tmp", [128, 64], F32), ("tmpb", [128, 64], BF16),
                ("rel", [16, 40], F32),
                ("pay", [128, 2 * OCH, 2], F32), ("gdst", [128, 2 * OCH], F32),
                ("tidx", [128, (2 * OCH * 128) // 16], I16),
                ("xstage", [128, 2048], BF16), ("xstage2", [128, 2048], BF16),
            ]:
                sbuf(n, s, d)

            ps = {}
            env["ps"] = ps
            for nm in ["q0", "q1", "q2", "q3", "d0", "d1", "s"]:
                ps[nm] = stack.enter_context(
                    nc.psum_tensor("ps" + nm, [128, 512], F32))
            env["regs"] = {}
            for rv in sorted({256, 128, cfg.CAP, cfg.NSLOT, cfg.SHG}):
                env["regs"][rv] = stack.enter_context(
                    nc.gpsimd.register(f"rn{rv}"))

            phase_setup(env)
            if upto >= 1:
                with ExitStack() as stB:
                    ps["t"] = stB.enter_context(
                        nc.psum_tensor("pst", [128, 1024], BF16))
                    alloc_scopeB(env, stB)
                    with ExitStack() as stB1:
                        alloc_scopeB1(env, stB1)
                        phase_p0b(env)
                        phase_setup_bulk(env)
                        phase_attn_pass1(env)
                        if upto >= 2:
                            phase_rel(env)
                    if upto >= 3:
                        with ExitStack() as stB2:
                            alloc_scopeB2(env, stB2)
                            phase_attn_pass2(env)
                    del ps["t"]
            if upto >= 5:
                full_barrier(env)
                with ExitStack() as stC:
                    env["_stC"] = stC
                    alloc_scopeC(env, stC)
                    phase_wo_ln1_routing(env)
            if upto >= 6:
                full_barrier(env)
                with ExitStack() as stD:
                    ps["e"] = stD.enter_context(
                        nc.psum_tensor("pse", [128, 512], F32))
                    alloc_scopeD(env, stD)
                    phase_routing_finalize(env)
                    if upto >= 7:
                        with ExitStack() as stD1:
                            alloc_scopeD1(env, stD1)
                            phase_experts(env)
                    if upto >= 8:
                        with ExitStack() as stD2:
                            alloc_scopeD2(env, stD2)
                            phase_shared_epilogue(env)
            phase_finish(env, upto)

    lower_extended_insts(nc)
    return nc


def phase_setup(env):
    em, sb, cfg, cst = env["em"], env["sb"], env["cfg"], env["cst"]
    S, T = cfg.S, cfg.T

    @em.on("sync")
    def _(sy):
        for n, _s, _d in const_specs(cfg):
            sy.dma_start(out=sb["c_" + n][...], in_=cst[n]).then_inc(
                em.h["ld"], 16)
            em.inc("ld", 16)
        sy.dma_start(out=sb["nrmTf"][...],
                     in_=env["nrmT"].rearrange("h d m -> d h m")).then_inc(
            em.h["ld"], 16)
        em.inc("ld", 16)
        sy.dma_start(out=sb["nrmFs"][...], in_=env["nrmF"]).then_inc(
            em.h["ld"], 16)
        em.inc("ld", 16)
        sy.dma_start(out=sb["priorS"][...], in_=env["prior_sel"]).then_inc(
            em.h["ld"], 16)
        em.inc("ld", 16)
        sy.dma_start(out=sb["routerS"][...],
                     in_=env["routeri"].rearrange("(k p) e -> p k e", p=128)
                     ).then_inc(em.h["ld"], 16)
        em.inc("ld", 16)
        sy.dma_start(out=sb["tmp"][0:1, 0:H_PER], in_=env["gatei"]).then_inc(
            em.h["ld"], 16)
        em.inc("ld", 16)
        em.qw(sy, "ld")
    env["cld"] = em.v["ld"]

    @em.on("vector")
    def _(v):
        for nm in ["zero", "S", "Z", "Sb", "Zb", "relacc"]:
            v.memset(sb[nm][...], 0.0)
        v.memset(sb["tidx"][...], 0)
        v.memset(sb["rstash"][...], 0.0)
        v.nop().then_inc(em.h["vp"], 1)
        em.inc("vp", 1)
        em.W(v, "ld", env["cld"])
        v.tensor_copy(sb["nrmTb"][...], sb["nrmTf"][...]).then_inc(
            em.h["vp"], 1)
        em.inc("vp", 1)
    env["zeroed"] = em.v["vp"]

    @em.on("gpsimd")
    def _(gp):
        gp.load_library(library_config.mlp)
        for rv, rh in env["regs"].items():
            gp.reg_mov(rh, rv)

    @em.on("scalar")
    def _(s):
        em.W(s, "ld", env["cld"])
        s.activation(sb["grow"][...], sb["tmp"][0:1, 0:H_PER],
                     AF.Sigmoid).then_inc(em.h["sp"], 1)
        em.inc("sp", 1)
    gr = em.v["sp"]

    @em.on("tensor")
    def _(t):
        em.W(t, "sp", gr)
        t.matmul(env["ps"]["s"][:, 0:H_PER], sb["c_onesrow"][...],
                 sb["grow"][...], start=True, stop=True).then_inc(
            em.h["tp"], 1)
        em.inc("tp", 1)
    grep = em.v["tp"]

    @em.on("vector")
    def _(v):
        em.W(v, "tp", grep)
        v.tensor_scalar(sb["g1m"][...], env["ps"]["s"][:, 0:H_PER], -1.0, 1.0,
                        ALU.mult, ALU.add).then_inc(em.h["vp"], 1)
        em.inc("vp", 1)
    env["g1m_ready"] = em.v["vp"]


def phase_setup_bulk(env):
    em, sb, cfg = env["em"], env["sb"], env["cfg"]
    S, T = cfg.S, cfg.T
    ntblel = cfg.NTBL * cfg.TROW
    env["xbh"] = {}

    @em.on("gpsimd")
    def _(gp):
        em.W(gp, "vp", env["zeroed"])
        for i in range(S // 128):
            stg = sb["xstage"] if i % 2 == 0 else sb["xstage2"]
            if i >= 1:
                em.W(gp, "gs", env["xbh"][i - 1])
            gp.dma_start(out=stg[...],
                         in_=env["xb"][i * 128:(i + 1) * 128, :]).then_inc(
                em.h["gl"], 16)
            t = em.inc("gl", 16)
            gp.wait_ge(em.h["gl"], t)
            gp.dma_start(out=env["XBH"][i * 128:(i + 1) * 128, :],
                         in_=stg[...]).then_inc(em.h["gs"], 16)
            env["xbh"][i] = em.inc("gs", 16)

    @em.on("gpsimd")
    def _(gp):
        for i in range(T // 128):
            for n4 in range(4):
                em.W(gp, "gs", em.v["gs"])
                gp.dma_start(out=env["RACC"][i * 128:(i + 1) * 128,
                                             n4 * 512:(n4 + 1) * 512],
                             in_=sb["zero"][...]).then_inc(em.h["gs"], 16)
                em.inc("gs", 16)
        flat = env["TBL"].rearrange("r c -> (r c)")
        off = 0
        while off < ntblel:
            n = min(128 * 512, ntblel - off)
            fr = n // 128
            em.W(gp, "gs", em.v["gs"])
            gp.dma_start(out=flat[off:off + 128 * fr].rearrange(
                "(p f) -> p f", p=128),
                in_=sb["zero"][:, 0:fr]).then_inc(em.h["gs"], 16)
            em.inc("gs", 16)
            off += 128 * fr
    env["zeros_done"] = em.v["gs"]

    @em.on("gpsimd")
    def _(gp):
        for rb in range(16):
            for cb in range(6):
                w = min(2048, 2 * J_FF - cb * 2048)
                stg = sb["xstage"] if (rb * 6 + cb) % 2 == 0 else sb["xstage2"]
                em.W(gp, "gs", em.v["gs"])
                gp.dma_start(out=stg[:, 0:w],
                             in_=env["wsu"][rb * 128:(rb + 1) * 128,
                                            cb * 2048:cb * 2048 + w]
                             ).then_inc(em.h["gl"], 16)
                t = em.inc("gl", 16)
                gp.wait_ge(em.h["gl"], t)
                gp.dma_start(out=env["WSUB"][rb * 128:(rb + 1) * 128,
                                             cb * 2048:cb * 2048 + w],
                             in_=stg[:, 0:w]).then_inc(em.h["gs"], 16)
                em.inc("gs", 16)
        for rb in range(44):
            stg = sb["xstage"] if rb % 2 == 0 else sb["xstage2"]
            em.W(gp, "gs", em.v["gs"])
            gp.dma_start(out=stg[...],
                         in_=env["wsd"][rb * 128:(rb + 1) * 128, :]
                         ).then_inc(em.h["gl"], 16)
            t = em.inc("gl", 16)
            gp.wait_ge(em.h["gl"], t)
            gp.dma_start(out=env["WSDB"][rb * 128:(rb + 1) * 128, :],
                         in_=stg[...]).then_inc(em.h["gs"], 16)
            em.inc("gs", 16)
    env["wcast_done"] = em.v["gs"]




def alloc_scopeB(env, st):
    sbuf = env["sbuf"]
    sbuf("memb", [128, M * H_PER, HD], BF16, st)
    sbuf("sqT", [128, H_PER, 512], BF16, st)
    sbuf("partS", [128, 4, HC], BF16, st)
    sbuf("scT", [128, H_PER * 128], BF16, st)


def alloc_scopeB1(env, st):
    sbuf = env["sbuf"]
    sbuf("wqTs", [128, 16, HC], BF16, st)
    sbuf("wkTs", [128, 16, HC], BF16, st)
    sbuf("wvTs", [128, 16, HC], BF16, st)
    sbuf("xT", [128, 16, 512], BF16, st)
    sbuf("xT2", [128, 16, 512], BF16, st)
    sbuf("kp", [128, 4, HC], BF16, st)
    sbuf("vc", [128, 4, HC], BF16, st)
    sbuf("kpT", [128, HC], BF16, st)
    for p in range(2):
        sbuf(f"sigF{p}", [128, 512], F32, st)
        sbuf(f"sigG{p}", [128, 512], F32, st)
        sbuf(f"sigH{p}", [128, 512], BF16, st)


def alloc_scopeB2(env, st):
    sbuf = env["sbuf"]
    sbuf("comb", [128, HC], BF16, st)
    sbuf("cm", [128, M * H_PER], F32, st)


def phase_p0b(env):
    em, sb = env["em"], env["sb"]

    @em.on("gpsimd")
    def _(gp):
        for nm, src in [("wqTs", env["wqT"]), ("wkTs", env["wkT"]),
                        ("wvTs", env["wvT"])]:
            gp.dma_start(out=sb[nm][...],
                         in_=src.rearrange("(k p) e -> p k e", p=128)
                         ).then_inc(em.h["wl"], 16)
            em.inc("wl", 16)
        gp.dma_start(out=sb["memb"][...],
                     in_=env["memi"].rearrange("mh (d e) -> d mh e", d=HD)
                     ).then_inc(em.h["wl"], 16)
        em.inc("wl", 16)
    env["wl_attn"] = em.v["wl"]
    env["wl_mem"] = em.v["wl"]




def phase_attn_pass1(env):
    for sc in range(env["cfg"].NSC):
        _p1_super(env, sc)
    env["p1_tdone"] = env["em"].v["tp"]


def _p1_super(env, sc):
    em, sb, ps, cfg = env["em"], env["sb"], env["ps"], env["cfg"]
    t0 = sc * 512
    xT = sb["xT"] if sc % 2 == 0 else sb["xT2"]

    @em.on("sync")
    def _(sy):
        em.W(sy, "gs", env["xbh"][(t0 + 512) // 128 - 1])
        if sc >= 2:
            em.W(sy, "tp", env[f"p1_projdone_{sc - 2}"])
        for k in range(16):
            sy.dma_start(out=xT[:, k, :],
                         in_=env["XBH"][t0:t0 + 512, k * 128:(k + 1) * 128],
                         transpose=True).then_inc(em.h["ld"], 16)
            em.inc("ld", 16)
        em.qw(sy, "ld")
    xT_ready = em.v["ld"]

    for et in range(H_PER):
        bank = ps["d0"] if et % 2 == 0 else ps["d1"]

        @em.on("tensor")
        def _(t, et=et, bank=bank):
            if et == 0:
                em.W(t, "ld", xT_ready)
                em.W(t, "wl", env["wl_attn"])
                em.W(t, "vp", env[f"p1_outev_{sc - 1}_3"] if sc > 0
                     else env["g1m_ready"])
            if et >= 2:
                em.W(t, "vp", env[f"p1_qev_{sc}_{et - 2}"])
            for k in range(16):
                mmi = t.matmul(bank[:, 0:512],
                               sb["wqTs"][:, k, et * 128:(et + 1) * 128],
                               xT[:, k, :], start=(k == 0), stop=(k == 15))
            mmi.then_inc(em.h["tp"], 1)
            env[f"p1_qps_{sc}_{et}"] = em.inc("tp", 1)

        @em.on("vector")
        def _(v, et=et, bank=bank):
            em.W(v, "tp", env[f"p1_qps_{sc}_{et}"])
            if sc > 0 and et == 0:
                em.W(v, "scr", env[f"p1_scr_{sc - 1}"])
            pr = et % 2
            v.reduce_sum(sb["tmp"][:, pr:pr + 1], bank[:, 0:512],
                         axis=AX.X).then_inc(em.h["vp"], 1)
            g0 = em.inc("vp", 1)
            v.tensor_scalar(sb[f"sigF{pr}"][...], bank[:, 0:512], 0.0, None,
                            ALU.min)
            v.tensor_scalar(sb[f"sigG{pr}"][...], bank[:, 0:512], 0.0, None,
                            ALU.max).then_inc(em.h["vp"], 1)
            env[f"p1_qsig_{sc}_{et}"] = em.inc("vp", 1)
            v.wait_ge(em.h["vp"], g0)
            v.tensor_tensor(sb["relacc"][:, et:et + 1],
                            sb["relacc"][:, et:et + 1],
                            sb["tmp"][:, pr:pr + 1], ALU.add)

        @em.on("scalar")
        def _(s, et=et):
            pr = et % 2
            em.W(s, "vp", env[f"p1_qsig_{sc}_{et}"])
            s.activation(sb[f"sigH{pr}"][...], sb[f"sigF{pr}"][...],
                         AF.Exp).then_inc(em.h["sp"], 1)
            env[f"p1_qexp_{sc}_{et}"] = em.inc("sp", 1)

        @em.on("vector")
        def _(v, et=et):
            pr = et % 2
            em.W(v, "sp", env[f"p1_qexp_{sc}_{et}"])
            v.tensor_tensor(sb["sqT"][:, et, :], sb[f"sigG{pr}"][...],
                            sb[f"sigH{pr}"][...], ALU.add).then_inc(
                em.h["vp"], 1)
            env[f"p1_qev_{sc}_{et}"] = em.inc("vp", 1)

    for kn in range(8):
        sub, n = kn // 2, kn % 2
        bank = ps[f"q{kn % 4}"]

        @em.on("tensor")
        def _(t, kn=kn, sub=sub, n=n, bank=bank):
            if kn >= 4:
                em.W(t, "vp", env[f"p1_kev_{sc}_{kn - 4}"])
            elif sc > 0:
                em.W(t, "vp", env[f"p1_outev_{sc - 1}_3"])
            for k in range(16):
                mmi = t.matmul(bank[:, 0:512],
                               xT[:, k, sub * 128:(sub + 1) * 128],
                               sb["wkTs"][:, k, n * 512:(n + 1) * 512],
                               start=(k == 0), stop=(k == 15))
            mmi.then_inc(em.h["tp"], 1)
            env[f"p1_kps_{sc}_{kn}"] = em.inc("tp", 1)

        @em.on("vector")
        def _(v, kn=kn, bank=bank):
            pr = kn % 2
            em.W(v, "tp", env[f"p1_kps_{sc}_{kn}"])
            v.tensor_scalar(sb[f"sigF{pr}"][...], bank[:, 0:512], 0.0, None,
                            ALU.min)
            v.tensor_scalar(sb[f"sigG{pr}"][...], bank[:, 0:512], 0.0, None,
                            ALU.max).then_inc(em.h["vp"], 1)
            env[f"p1_ksig_{sc}_{kn}"] = em.inc("vp", 1)

        @em.on("scalar")
        def _(s, kn=kn):
            pr = kn % 2
            em.W(s, "vp", env[f"p1_ksig_{sc}_{kn}"])
            s.activation(sb[f"sigH{pr}"][...], sb[f"sigF{pr}"][...],
                         AF.Exp).then_inc(em.h["sp"], 1)
            env[f"p1_kexp_{sc}_{kn}"] = em.inc("sp", 1)

        @em.on("vector")
        def _(v, kn=kn, sub=sub, n=n):
            pr = kn % 2
            em.W(v, "sp", env[f"p1_kexp_{sc}_{kn}"])
            v.tensor_tensor(sb["kp"][:, sub, n * 512:(n + 1) * 512],
                            sb[f"sigG{pr}"][...], sb[f"sigH{pr}"][...],
                            ALU.add).then_inc(em.h["vp"], 1)
            env[f"p1_kev_{sc}_{kn}"] = em.inc("vp", 1)

    for vn in range(8):
        sub, n = vn // 2, vn % 2
        bank = ps[f"q{vn % 4}"]

        @em.on("tensor")
        def _(t, vn=vn, sub=sub, n=n, bank=bank):
            em.W(t, "vp", env[f"p1_kev_{sc}_{4 + (vn % 4)}"] if vn < 4
                 else env[f"p1_vev_{sc}_{vn - 4}"])
            for k in range(16):
                mmi = t.matmul(bank[:, 0:512],
                               xT[:, k, sub * 128:(sub + 1) * 128],
                               sb["wvTs"][:, k, n * 512:(n + 1) * 512],
                               start=(k == 0), stop=(k == 15))
            mmi.then_inc(em.h["tp"], 1)
            env[f"p1_vps_{sc}_{vn}"] = em.inc("tp", 1)

        @em.on("vector")
        def _(v, vn=vn, sub=sub, n=n, bank=bank):
            em.W(v, "tp", env[f"p1_vps_{sc}_{vn}"])
            v.tensor_copy(sb["vc"][:, sub, n * 512:(n + 1) * 512],
                          bank[:, 0:512]).then_inc(em.h["vp"], 1)
            env[f"p1_vev_{sc}_{vn}"] = em.inc("vp", 1)
    env[f"p1_projdone_{sc}"] = em.v["tp"]

    for sub in range(4):
        _p1_chunk(env, sc, sub)

    @em.on("scalar")
    def _(s):
        em.W(s, "vp", env[f"p1_qev_{sc}_{H_PER - 1}"])
        for et in range(H_PER):
            s.dma_start(out=env["SQT"][et, :, t0:t0 + 512],
                        in_=sb["sqT"][:, et, :]).then_inc(em.h["scr"], 16)
            em.inc("scr", 16)
        em.W(s, "vp", env[f"p1_outev_{sc}_3"])
        s.dma_start(out=env["PART"][t0:t0 + 512, :].rearrange(
                        "(s p) e -> p s e", p=128),
                    in_=sb["partS"][...]).then_inc(em.h["scr"], 16)
        em.inc("scr", 16)
        em.qw(s, "scr")
    env[f"p1_scr_{sc}"] = em.v["scr"]


def _p1_chunk(env, sc, sub):
    em, sb, ps = env["em"], env["sb"], env["ps"]

    @em.on("vector")
    def _(v):
        v.tensor_copy(sb["Sb"][...], sb["S"][...])
        v.tensor_copy(sb["Zb"][...], sb["Z"][...]).then_inc(em.h["vp"], 1)
        env["p1_snap"] = em.inc("vp", 1)

    @em.on("tensor")
    def _(t):
        em.W(t, "vp", env[f"p1_kev_{sc}_{sub * 2 + 1}"])
        em.W(t, "vp", env[f"p1_vev_{sc}_{sub * 2 + 1}"])
        em.W(t, "vp", env[f"p1_qev_{sc}_{H_PER - 1}"])
        em.W(t, "vp", env["p1_snap"])
        if "p1_kpT_cp" in env:
            em.W(t, "sp", env["p1_kpT_cp"])
        for h in range(H_PER):
            tr = t.transpose(ps["t"][:, h * 128:(h + 1) * 128],
                             sb["kp"][:, sub, h * 128:(h + 1) * 128],
                             sb["c_ident"][...])
        tr.then_inc(em.h["tp"], 1)
        env["p1_kpT_ps"] = em.inc("tp", 1)

    @em.on("scalar")
    def _(s):
        em.W(s, "tp", env["p1_kpT_ps"])
        s.copy(sb["kpT"][...], ps["t"][:, 0:HC]).then_inc(em.h["sp"], 1)
        env["p1_kpT_cp"] = em.inc("sp", 1)

    @em.on("tensor")
    def _(t):
        em.W(t, "sp", env["p1_kpT_cp"])
        if "p1_scev" in env:
            em.W(t, "vp", env["p1_scev"])
        for h in range(H_PER):
            bank = ps["d0"] if h < 4 else ps["d1"]
            mmi = t.matmul(bank[:, (h % 4) * 128:(h % 4 + 1) * 128],
                           sb["kpT"][:, h * 128:(h + 1) * 128],
                           sb["sqT"][:, h, sub * 128:(sub + 1) * 128],
                           start=True, stop=True)
        mmi.then_inc(em.h["tp"], 1)
        env["p1_sc_ps"] = em.inc("tp", 1)

    @em.on("vector")
    def _(v):
        em.W(v, "tp", env["p1_sc_ps"])
        v.tensor_tensor(sb["scT"][:, 0:512], ps["d0"][:, 0:512],
                        sb["c_maskT"][:, 0:512], ALU.mult)
        v.tensor_tensor(sb["scT"][:, 512:1024], ps["d1"][:, 0:512],
                        sb["c_maskT"][:, 512:1024], ALU.mult).then_inc(
            em.h["vp"], 1)
        env["p1_scev"] = em.inc("vp", 1)

    @em.on("tensor")
    def _(t):
        em.W(t, "vp", env["p1_scev"])
        key = f"p1_outev_{sc}_{sub - 1}" if sub > 0 else f"p1_outev_{sc - 1}_3"
        if key in env:
            em.W(t, "vp", env[key])
        for h in range(H_PER):
            bank = ps["q0"] if h < 4 else ps["q1"]
            t.matmul(bank[:, (h % 4) * 128:(h % 4 + 1) * 128],
                     sb["kp"][:, sub, h * 128:(h + 1) * 128],
                     sb["vc"][:, sub, h * 128:(h + 1) * 128],
                     start=True, stop=True)
        for h in range(H_PER):
            t.matmul(ps["s"][:, h:h + 1],
                     sb["sqT"][:, h, sub * 128:(sub + 1) * 128],
                     sb["Zb"][:, h:h + 1], start=True, stop=False)
            t.matmul(ps["s"][:, h:h + 1], sb["scT"][:, h * 128:(h + 1) * 128],
                     sb["c_onesb"][...], start=False, stop=True)
        for h in range(H_PER):
            t.matmul(ps["s"][:, 16 + h:17 + h],
                     sb["kp"][:, sub, h * 128:(h + 1) * 128],
                     sb["c_onesb"][...], start=True, stop=True)
        for h in range(H_PER):
            bank = ps["q2"] if h < 4 else ps["q3"]
            t.matmul(bank[:, (h % 4) * 128:(h % 4 + 1) * 128],
                     sb["scT"][:, h * 128:(h + 1) * 128],
                     sb["vc"][:, sub, h * 128:(h + 1) * 128],
                     start=True, stop=False)
            mmi = t.matmul(bank[:, (h % 4) * 128:(h % 4 + 1) * 128],
                           sb["sqT"][:, h, sub * 128:(sub + 1) * 128],
                           sb["Sb"][:, h * 128:(h + 1) * 128],
                           start=False, stop=True)
        mmi.then_inc(em.h["tp"], 1)
        env["p1_att_ps"] = em.inc("tp", 1)

    @em.on("vector")
    def _(v):
        em.W(v, "tp", env["p1_att_ps"])
        if sub == 0 and sc > 0:
            em.W(v, "scr", env[f"p1_scr_{sc - 1}"])
        v.tensor_tensor(sb["S"][:, 0:512], sb["S"][:, 0:512],
                        ps["q0"][:, 0:512], ALU.add)
        v.tensor_tensor(sb["S"][:, 512:1024], sb["S"][:, 512:1024],
                        ps["q1"][:, 0:512], ALU.add)
        v.tensor_tensor(sb["Z"][...], sb["Z"][...],
                        ps["s"][:, 16:16 + H_PER], ALU.add)
        v.tensor_scalar(sb["tmp"][:, 0:H_PER], ps["s"][:, 0:H_PER], EPS,
                        None, ALU.max).then_inc(em.h["vp"], 1)
        v.wait_ge(em.h["vp"], em.inc("vp", 1))
        v.reciprocal(sb["tmp"][:, 8:8 + H_PER],
                     sb["tmp"][:, 0:H_PER]).then_inc(em.h["vp"], 1)
        v.wait_ge(em.h["vp"], em.inc("vp", 1))
        for h in range(H_PER):
            bank = ps["q2"] if h < 4 else ps["q3"]
            vi = v.tensor_scalar(sb["partS"][:, sub, h * 128:(h + 1) * 128],
                                 bank[:, (h % 4) * 128:(h % 4 + 1) * 128],
                                 sb["tmp"][:, 8 + h:9 + h],
                                 sb["g1m"][:, h:h + 1], ALU.mult, ALU.mult)
        vi.then_inc(em.h["vp"], 1)
        env[f"p1_outev_{sc}_{sub}"] = em.inc("vp", 1)


def phase_rel(env):
    em, sb, ps, cfg = env["em"], env["sb"], env["ps"], env["cfg"]

    @em.on("vector")
    def _(v):
        v.tensor_copy(sb["relaccb"][...], sb["relacc"][...])
        v.reduce_sum(sb["tmp"][0:M * 16, 0:1], sb["nrmFs"][...],
                     axis=AX.X).then_inc(em.h["vp"], 1)
        env["p2_v0"] = em.inc("vp", 1)

    @em.on("tensor")
    def _(t):
        em.W(t, "vp", env["p2_v0"])
        for h in range(H_PER):
            t.matmul(ps["s"][0:M, h:h + 1], sb["nrmTb"][:, h, :],
                     sb["relaccb"][:, h:h + 1], start=True, stop=True)
        mmi = t.matmul(ps["s"][0:M, 8:9], sb["c_esel"][...],
                       sb["tmp"][0:M * 16, 0:1], start=True, stop=True)
        mmi.then_inc(em.h["tp"], 1)
        env["p2_t0"] = em.inc("tp", 1)

    @em.on("vector")
    def _(v):
        em.W(v, "tp", env["p2_t0"])
        v.tensor_copy(sb["rel"][0:M, 0:H_PER], ps["s"][0:M, 0:H_PER])
        v.tensor_scalar(sb["rel"][0:M, 8:9], ps["s"][0:M, 8:9], 1e-6, None,
                        ALU.is_lt).then_inc(em.h["vp"], 1)
        env["p2_v1"] = em.inc("vp", 1)

    @em.on("tensor")
    def _(t):
        em.W(t, "vp", env["p2_v1"])
        t.matmul(ps["s"][0:H_PER, 16:16 + M], sb["rel"][0:M, 0:H_PER],
                 sb["c_identf"][0:M, 0:M], start=True, stop=True)
        mmi = t.matmul(ps["s"][0:1, 24:24 + M], sb["rel"][0:M, 8:9],
                       sb["c_identf"][0:M, 0:M], start=True, stop=True)
        mmi.then_inc(em.h["tp"], 1)
        env["p2_t1"] = em.inc("tp", 1)

    @em.on("vector")
    def _(v):
        em.W(v, "tp", env["p2_t1"])
        v.tensor_scalar(sb["rel"][0:H_PER, 0:M], ps["s"][0:H_PER, 16:16 + M],
                        SCALE / env["cfg"].S, None, ALU.mult)
        v.tensor_copy(sb["rel"][0:1, 12:12 + M],
                      ps["s"][0:1, 24:24 + M]).then_inc(em.h["vp"], 1)
        env["p2_v2"] = em.inc("vp", 1)

    @em.on("tensor")
    def _(t):
        em.W(t, "vp", env["p2_v2"])
        mmi = t.matmul(ps["s"][0:H_PER, 28:28 + M],
                       sb["c_onesrow"][0:1, 0:H_PER],
                       sb["rel"][0:1, 12:12 + M], start=True, stop=True)
        mmi.then_inc(em.h["tp"], 1)
        env["p2_t2"] = em.inc("tp", 1)

    @em.on("vector")
    def _(v):
        em.W(v, "tp", env["p2_t2"])
        v.scalar_tensor_tensor(sb["rel"][0:H_PER, 0:M],
                               ps["s"][0:H_PER, 28:28 + M], NEG_BIG,
                               sb["rel"][0:H_PER, 0:M], ALU.mult,
                               ALU.add).then_inc(em.h["vp"], 1)
        ga = em.inc("vp", 1)
        v.tensor_scalar(sb["rel"][0:H_PER, 4:4 + M],
                        ps["s"][0:H_PER, 28:28 + M], -1.0, 1.0, ALU.mult,
                        ALU.add)
        v.wait_ge(em.h["vp"], ga)
        v.reduce_max(sb["rel"][0:H_PER, 8:9], sb["rel"][0:H_PER, 0:M],
                     axis=AX.X).then_inc(em.h["vp"], 1)
        v.wait_ge(em.h["vp"], em.inc("vp", 1))
        v.tensor_scalar(sb["rel"][0:H_PER, 0:M], sb["rel"][0:H_PER, 0:M],
                        sb["rel"][0:H_PER, 8:9], None,
                        ALU.subtract).then_inc(em.h["vp"], 1)
        env["p2_v3"] = em.inc("vp", 1)

    @em.on("scalar")
    def _(s):
        em.W(s, "vp", env["p2_v3"])
        s.activation(sb["rel"][0:H_PER, 16:16 + M], sb["rel"][0:H_PER, 0:M],
                     AF.Exp).then_inc(em.h["sp"], 1)
        env["p2_s0"] = em.inc("sp", 1)

    @em.on("vector")
    def _(v):
        em.W(v, "sp", env["p2_s0"])
        v.reduce_sum(sb["rel"][0:H_PER, 9:10], sb["rel"][0:H_PER, 16:16 + M],
                     axis=AX.X).then_inc(em.h["vp"], 1)
        v.wait_ge(em.h["vp"], em.inc("vp", 1))
        v.reciprocal(sb["rel"][0:H_PER, 10:11],
                     sb["rel"][0:H_PER, 9:10]).then_inc(em.h["vp"], 1)
        v.wait_ge(em.h["vp"], em.inc("vp", 1))
        v.tensor_scalar(sb["rel"][0:H_PER, 16:16 + M],
                        sb["rel"][0:H_PER, 16:16 + M],
                        sb["rel"][0:H_PER, 10:11], None,
                        ALU.mult).then_inc(em.h["vp"], 1)
        v.wait_ge(em.h["vp"], em.inc("vp", 1))
        v.tensor_tensor(sb["rel"][0:H_PER, 16:16 + M],
                        sb["rel"][0:H_PER, 16:16 + M],
                        sb["rel"][0:H_PER, 4:4 + M], ALU.mult)
        v.reduce_sum(sb["rel"][0:1, 13:14], sb["rel"][0:1, 12:12 + M],
                     axis=AX.X).then_inc(em.h["vp"], 1)
        v.wait_ge(em.h["vp"], em.inc("vp", 1))
        v.tensor_scalar(sb["rel"][0:1, 14:15], sb["rel"][0:1, 13:14],
                        M - 0.5, None, ALU.is_lt).then_inc(em.h["vp"], 1)
        env["p2_v4"] = em.inc("vp", 1)

    @em.on("tensor")
    def _(t):
        em.W(t, "vp", env["p2_v4"])
        t.matmul(ps["s"][0:H_PER, 32:33], sb["c_onesrow"][0:1, 0:H_PER],
                 sb["rel"][0:1, 14:15], start=True, stop=True)
        mmi = t.matmul(ps["s"][0:H_PER, 33:34], sb["grow"][...],
                       sb["c_identf"][0:1, 0:1], start=True, stop=True)
        mmi.then_inc(em.h["tp"], 1)
        env["p2_t3"] = em.inc("tp", 1)

    @em.on("vector")
    def _(v):
        em.W(v, "tp", env["p2_t3"])
        v.tensor_scalar(sb["rel"][0:H_PER, 16:16 + M],
                        sb["rel"][0:H_PER, 16:16 + M],
                        ps["s"][0:H_PER, 32:33], ps["s"][0:H_PER, 33:34],
                        ALU.mult, ALU.mult).then_inc(em.h["vp"], 1)
        env["p2_v6"] = em.inc("vp", 1)

    @em.on("sync")
    def _(sy):
        em.W(sy, "vp", env["p2_v6"])
        sy.dma_start(out=sb["tmp"][0:1, 0:H_PER * M],
                     in_=sb["rel"][0:H_PER, 16:16 + M]).then_inc(
            em.h["ld"], 16)
        em.inc("ld", 16)
        em.qw(sy, "ld")
    env["p2_fl"] = em.v["ld"]

    @em.on("tensor")
    def _(t):
        em.W(t, "ld", env["p2_fl"])
        mmi = t.matmul(ps["s"][:, 34:34 + H_PER * M], sb["c_onesrow"][...],
                       sb["tmp"][0:1, 0:H_PER * M], start=True, stop=True)
        mmi.then_inc(em.h["tp"], 1)
        env["p2_t5"] = em.inc("tp", 1)

    @em.on("vector")
    def _(v):
        em.W(v, "tp", env["p2_t5"])
        v.tensor_copy(sb["weff"][...],
                      ps["s"][:, 34:34 + H_PER * M]).then_inc(em.h["vp"], 1)
        env["p2_done"] = em.inc("vp", 1)


def phase_attn_pass2(env):
    em = env["em"]
    for sc in range(env["cfg"].NSC):
        _p3_super(env, sc)
    env["p3_tdone"] = em.v["tp"]

    @em.on("gpsimd")
    def _(gp):
        em.W(gp, "scr", env[f"p3_scr_{env['cfg'].NSC - 1}"])
        gp.collective_compute(
            "AllGather", ALU.bypass,
            replica_groups=[[0, 1], [2, 3], [4, 5], [6, 7]],
            ins=[env["agin"].ap().opt()],
            outs=[env["agout"].ap().opt()],
        ).then_inc(em.h["cc"], 1)
        em.inc("cc", 1)
    env["a2a_done"] = em.v["cc"]


def _p3_super(env, sc):
    em, sb, cfg = env["em"], env["sb"], env["cfg"]
    t0 = sc * 512

    @em.on("scalar")
    def _(s):
        em.W(s, "scr", env[f"p1_scr_{cfg.NSC - 1}"])
        em.W(s, "vp", env["p2_done"])
        if sc > 0:
            em.W(s, "vp", env[f"p3_combev_{sc - 1}_3"])
        for et in range(H_PER):
            s.dma_start(out=sb["sqT"][:, et, :],
                        in_=env["SQT"][et, :, t0:t0 + 512]).then_inc(
                em.h["scr"], 16)
            em.inc("scr", 16)
        s.dma_start(out=sb["partS"][...],
                    in_=env["PART"][t0:t0 + 512, :].rearrange(
                        "(s p) e -> p s e", p=128)).then_inc(em.h["scr"], 16)
        em.inc("scr", 16)
        em.qw(s, "scr")
    env[f"p3_ld_{sc}"] = em.v["scr"]

    for sub in range(4):
        _p3_chunk(env, sc, sub)
    env[f"p3_scr_{sc}"] = em.v["scr"]


def _p3_chunk(env, sc, sub):
    em, sb, ps, cfg = env["em"], env["sb"], env["ps"], env["cfg"]
    ch = sc * 4 + sub

    @em.on("tensor")
    def _(t):
        em.W(t, "scr", env[f"p3_ld_{sc}"])
        em.W(t, "wl", env["wl_mem"])
        if "p3_cmev" in env:
            em.W(t, "vp", env["p3_cmev"])
        for h in range(H_PER):
            mmi = t.matmul(ps["s"][:, h * M:(h + 1) * M],
                           sb["sqT"][:, h, sub * 128:(sub + 1) * 128],
                           sb["nrmTb"][:, h, :], start=True, stop=True)
        mmi.then_inc(em.h["tp"], 1)
        env["p3_den_ps"] = em.inc("tp", 1)

    @em.on("vector")
    def _(v):
        em.W(v, "tp", env["p3_den_ps"])
        v.tensor_scalar(sb["cm"][...], ps["s"][:, 0:M * H_PER], EPS, None,
                        ALU.max).then_inc(em.h["vp"], 1)
        v.wait_ge(em.h["vp"], em.inc("vp", 1))
        v.reciprocal(sb["cm"][...], sb["cm"][...]).then_inc(em.h["vp"], 1)
        v.wait_ge(em.h["vp"], em.inc("vp", 1))
        v.tensor_tensor(sb["cm"][...], sb["cm"][...], sb["weff"][...],
                        ALU.mult).then_inc(em.h["vp"], 1)
        env["p3_cmev"] = em.inc("vp", 1)

    for m in range(M):
        bA = ps["q0"] if m % 2 == 0 else ps["q2"]
        bB = ps["q1"] if m % 2 == 0 else ps["q3"]

        @em.on("tensor")
        def _(t, m=m, bA=bA, bB=bB):
            guard = f"p3_mev_{m - 2}" if m >= 2 else f"p3_mev_{m + 2}"
            if guard in env:
                em.W(t, "vp", env[guard])
            if m == 0:
                em.W(t, "vp", env["p3_cmev"])
            for h in range(H_PER):
                bank = bA if h < 4 else bB
                mmi = t.matmul(bank[:, (h % 4) * 128:(h % 4 + 1) * 128],
                               sb["sqT"][:, h, sub * 128:(sub + 1) * 128],
                               sb["memb"][:, m * H_PER + h, :],
                               start=True, stop=True)
            mmi.then_inc(em.h["tp"], 1)
            env[f"p3_mps_{m}"] = em.inc("tp", 1)

        @em.on("vector")
        def _(v, m=m, bA=bA, bB=bB):
            em.W(v, "tp", env[f"p3_mps_{m}"])
            if m == 0:
                em.W(v, "scr", env.get("p3_aginst", 0))
            else:
                em.W(v, "vp", env[f"p3_mev_{m - 1}"])
            for h in range(H_PER):
                bank = bA if h < 4 else bB
                src1 = (sb["partS"][:, sub, h * 128:(h + 1) * 128] if m == 0
                        else sb["comb"][:, h * 128:(h + 1) * 128])
                vi = v.scalar_tensor_tensor(
                    sb["comb"][:, h * 128:(h + 1) * 128],
                    bank[:, (h % 4) * 128:(h % 4 + 1) * 128],
                    sb["cm"][:, h * M + m:h * M + m + 1],
                    src1, ALU.mult, ALU.add)
            vi.then_inc(em.h["vp"], 1)
            env[f"p3_mev_{m}"] = em.inc("vp", 1)

    @em.on("scalar")
    def _(s):
        em.W(s, "vp", env[f"p3_mev_{M - 1}"])
        s.dma_start(out=env["agin"].ap()[ch * 128:(ch + 1) * 128, :],
                    in_=sb["comb"][...]).then_inc(em.h["scr"], 16)
        env["p3_aginst"] = em.inc("scr", 16)
        em.qw(s, "scr")

    @em.on("vector")
    def _(v):
        em.W(v, "scr", env["p3_aginst"])
        v.nop().then_inc(em.h["vp"], 1)
        env[f"p3_combev_{sc}_{sub}"] = em.inc("vp", 1)


def alloc_scopeC(env, st):
    sbuf = env["sbuf"]
    sbuf("woTs", [128, 16, D], BF16, st)
    sbuf("wolh", [128, 8, 256], BF16, st)
    sbuf("wolh2", [128, 8, 256], BF16, st)
    sbuf("gwo", [128, 16], I16, st)
    sbuf("res", [128, D], F32, st)
    sbuf("res2", [128, D], F32, st)
    sbuf("s1", [128, D], F32, st)
    sbuf("scrA", [128, D], F32, st)
    sbuf("xp", [128, D], F32, st)
    sbuf("xphb", [128, D], BF16, st)
    sbuf("xpT", [128, 16, 128], F32, st)


def ln_stats_apply(env, src, dst, tc, scr_name="scrA", pre=None,
                   src_ready=None):
    em, sb = env["em"], env["sb"]

    @em.on("vector")
    def _(v):
        if src_ready is not None:
            em.W(v, src_ready[0], src_ready[1])
        v.reduce_sum(sb["tmp"][:, tc:tc + 1], src, axis=AX.X).then_inc(
            em.h["vp"], 1)
        env["ln_v0"] = em.inc("vp", 1)

    @em.on("scalar")
    def _(s):
        em.W(s, "vp", env["ln_v0"])
        s.activation(sb[scr_name][...], src, AF.Square,
                     accum_out=sb["tmp"][:, tc + 1:tc + 2]).then_inc(
            em.h["sp"], 1)
        env["ln_s0"] = em.inc("sp", 1)

    @em.on("vector")
    def _(v):
        em.W(v, "sp", env["ln_s0"])
        v.tensor_scalar(sb["tmp"][:, tc + 2:tc + 3], sb["tmp"][:, tc:tc + 1],
                        1.0 / D, None, ALU.mult).then_inc(em.h["vp"], 1)
        g1 = em.inc("vp", 1)
        v.tensor_scalar(sb["tmp"][:, tc + 3:tc + 4],
                        sb["tmp"][:, tc + 1:tc + 2], 1.0 / D, None,
                        ALU.mult).then_inc(em.h["vp"], 1)
        g2 = em.inc("vp", 1)
        v.wait_ge(em.h["vp"], g1)
        v.wait_ge(em.h["vp"], g2)
        v.tensor_tensor(sb["tmp"][:, tc + 6:tc + 7],
                        sb["tmp"][:, tc + 2:tc + 3],
                        sb["tmp"][:, tc + 2:tc + 3],
                        ALU.mult).then_inc(em.h["vp"], 1)
        g3 = em.inc("vp", 1)
        v.wait_ge(em.h["vp"], g3)
        v.tensor_tensor(sb["tmp"][:, tc + 3:tc + 4],
                        sb["tmp"][:, tc + 3:tc + 4],
                        sb["tmp"][:, tc + 6:tc + 7],
                        ALU.subtract).then_inc(em.h["vp"], 1)
        g4 = em.inc("vp", 1)
        v.wait_ge(em.h["vp"], g4)
        v.tensor_scalar(sb["tmp"][:, tc + 3:tc + 4],
                        sb["tmp"][:, tc + 3:tc + 4], 0.0, LN_EPS,
                        ALU.max, ALU.add).then_inc(em.h["vp"], 1)
        env["ln_v1"] = em.inc("vp", 1)

    @em.on("scalar")
    def _(s):
        em.W(s, "vp", env["ln_v1"])
        s.activation(sb["tmp"][:, tc + 6:tc + 7], sb["tmp"][:, tc + 3:tc + 4],
                     AF.Sqrt).then_inc(em.h["sp"], 1)
        env["ln_s1"] = em.inc("sp", 1)

    @em.on("vector")
    def _(v):
        em.W(v, "sp", env["ln_s1"])
        v.reciprocal(sb["tmp"][:, tc + 4:tc + 5],
                     sb["tmp"][:, tc + 6:tc + 7]).then_inc(em.h["vp"], 1)
        v.wait_ge(em.h["vp"], em.inc("vp", 1))
        v.tensor_tensor(sb["tmp"][:, tc + 7:tc + 8],
                        sb["tmp"][:, tc + 4:tc + 5],
                        sb["tmp"][:, tc + 4:tc + 5], ALU.mult).then_inc(
            em.h["vp"], 1)
        v.wait_ge(em.h["vp"], em.inc("vp", 1))
        v.tensor_tensor(sb["tmp"][:, tc + 7:tc + 8],
                        sb["tmp"][:, tc + 7:tc + 8],
                        sb["tmp"][:, tc + 3:tc + 4], ALU.mult).then_inc(
            em.h["vp"], 1)
        v.wait_ge(em.h["vp"], em.inc("vp", 1))
        v.tensor_scalar(sb["tmp"][:, tc + 7:tc + 8],
                        sb["tmp"][:, tc + 7:tc + 8], -0.5, 1.5, ALU.mult,
                        ALU.add).then_inc(em.h["vp"], 1)
        v.wait_ge(em.h["vp"], em.inc("vp", 1))
        v.tensor_tensor(sb["tmp"][:, tc + 4:tc + 5],
                        sb["tmp"][:, tc + 4:tc + 5],
                        sb["tmp"][:, tc + 7:tc + 8], ALU.mult).then_inc(
            em.h["vp"], 1)
        v.wait_ge(em.h["vp"], em.inc("vp", 1))
        v.tensor_scalar(sb["tmp"][:, tc + 5:tc + 6],
                        sb["tmp"][:, tc + 2:tc + 3], -1.0, None,
                        ALU.mult).then_inc(em.h["vp"], 1)
        v.wait_ge(em.h["vp"], em.inc("vp", 1))
        if pre is not None:
            em.W(v, pre[0], pre[1])
        v.tensor_scalar(dst, src, sb["tmp"][:, tc + 5:tc + 6],
                        sb["tmp"][:, tc + 4:tc + 5], ALU.add,
                        ALU.mult).then_inc(em.h["vp"], 1)
        env["ln_done"] = em.inc("vp", 1)


def phase_wo_ln1_routing(env):
    em, sb, cfg = env["em"], env["sb"], env["cfg"]
    env["sbuf"]("gwoall", [128, cfg.OCH * 16], I16, env["_stC"])

    @em.on("sync")
    def _(sy):
        sy.dma_start(out=sb["gwoall"][...],
                     in_=env["gidxwo"]).then_inc(em.h["ld"], 16)
        em.inc("ld", 16)
        em.qw(sy, "ld")
    env["p5_gwold"] = em.v["ld"]

    @em.on("gpsimd")
    def _(gp):
        em.W(gp, "tp", env["p3_tdone"])
        gp.dma_start(out=sb["woTs"][...],
                     in_=env["woT"].rearrange("(k p) e -> p k e", p=128)
                     ).then_inc(em.h["wl"], 16)
        em.inc("wl", 16)
        em.W(gp, "cc", env["a2a_done"])
        em.W(gp, "ld", env["p5_gwold"])
        gp.nop().then_inc(em.h["mk"], 1)
        em.inc("mk", 1)
    env["p5_w"] = em.v["wl"]
    env["p5_ag"] = em.v["mk"]

    for oc in range(cfg.OCH):
        _p5_chunk(env, oc)
    env["p5_tdone"] = em.v["tp"]


def _p5_chunk(env, oc):
    em, sb, ps, cfg = env["em"], env["sb"], env["ps"], env["cfg"]
    r0 = oc * 128
    lh = sb["wolh"] if oc % 2 == 0 else sb["wolh2"]
    res = sb["res"] if oc % 2 == 0 else sb["res2"]

    @em.on("sync")
    def _(sy):
        if oc >= 2:
            em.W(sy, "vp", env[f"p5_s1_{oc - 2}"])
        sy.dma_start(out=res[...],
                     in_=env["xown"][r0:r0 + 128, :]).then_inc(em.h["ld"], 16)
        em.inc("ld", 16)
        em.qw(sy, "ld")
    ldv = em.v["ld"]

    gsem = "qa" if oc % 2 == 0 else "qb"

    @em.on("gpsimd")
    def _(gp):
        em.W(gp, "mk", env["p5_ag"])
        if oc >= 2:
            em.W(gp, "tp", env[f"p5_wops_{oc - 2}"])
        gp.dma_gather(lh[...], env["agout"].ap(),
                      sb["gwoall"][:, oc * 16:(oc + 1) * 16],
                      num_idxs=256, num_idxs_reg=env["regs"][256], elem_size=HC,
                      elem_step=HC, transpose=True).then_inc(em.h[gsem], 16)
        em.inc(gsem, 16)
    ghv = (gsem, em.v[gsem])

    @em.on("tensor")
    def _(t):
        em.W(t, "ld", ldv)
        em.W(t, ghv[0], ghv[1])
        em.W(t, "wl", env["p5_w"])
        if oc > 0:
            em.W(t, "vp", env[f"p5_s1_{oc - 1}"])
        for n in range(4):
            for k in range(16):
                lhT = (lh[:, k, 0:128] if k < 8 else lh[:, k - 8, 128:256])
                mmi = t.matmul(ps[f"q{n}"][:, 0:512], lhT,
                               sb["woTs"][:, k, n * 512:(n + 1) * 512],
                               start=(k == 0), stop=(k == 15))
        mmi.then_inc(em.h["tp"], 1)
        env[f"p5_wops_{oc}"] = em.inc("tp", 1)

    @em.on("vector")
    def _(v):
        em.W(v, "tp", env[f"p5_wops_{oc}"])
        em.W(v, "sp", env.get("p5_sq_prev", 0))
        for n in range(4):
            vi = v.tensor_tensor(sb["s1"][:, n * 512:(n + 1) * 512],
                                 ps[f"q{n}"][:, 0:512],
                                 res[:, n * 512:(n + 1) * 512], ALU.add)
        vi.then_inc(em.h["vp"], 1)
        env[f"p5_s1_{oc}"] = em.inc("vp", 1)

    ln_stats_apply(env, sb["s1"][...], sb["xp"][...], 0,
                   pre=("scr", env.get(f"p5_xpst_{oc - 1}", 0)),
                   src_ready=("vp", env[f"p5_s1_{oc}"]))
    env["p5_sq_prev"] = env["ln_s0"]
    xp_done = env["ln_done"]

    @em.on("vector")
    def _(v):
        em.W(v, "scr", env.get(f"p5_xpst_{oc - 1}", 0))
        v.wait_ge(em.h["vp"], xp_done)
        v.tensor_copy(sb["xphb"][...], sb["xp"][...]).then_inc(em.h["vp"], 1)
        env["p5_xph"] = em.inc("vp", 1)

    @em.on("scalar")
    def _(s):
        em.W(s, "vp", env["p5_xph"])
        s.dma_start(out=env["XP"][r0:r0 + 128, :],
                    in_=sb["xp"][...]).then_inc(em.h["scr"], 16)
        em.inc("scr", 16)
        s.dma_start(out=env["XPH"][r0:r0 + 128, :],
                    in_=sb["xphb"][...]).then_inc(em.h["scr"], 16)
        em.inc("scr", 16)
        em.qw(s, "scr")
    env[f"p5_xpst_{oc}"] = em.v["scr"]

    for k in range(16):
        bank = ps["d0"] if k % 2 == 0 else ps["d1"]

        @em.on("tensor")
        def _(t, k=k, bank=bank):
            if k == 0:
                em.W(t, "vp", xp_done)
            if k >= 2:
                em.W(t, "sp", env[f"p5_xpTc_{k - 2}"])
            mmi = t.matmul(bank[:, 0:128], sb["xp"][:, k * 128:(k + 1) * 128],
                           sb["c_identf"][...], start=True, stop=True)
            mmi.then_inc(em.h["tp"], 1)
            env[f"p5_xpT_{k}"] = em.inc("tp", 1)

        @em.on("scalar")
        def _(s, k=k, bank=bank):
            em.W(s, "tp", env[f"p5_xpT_{k}"])
            s.copy(sb["xpT"][:, k, :], bank[:, 0:128]).then_inc(em.h["sp"], 1)
            env[f"p5_xpTc_{k}"] = em.inc("sp", 1)

    @em.on("tensor")
    def _(t):
        em.W(t, "sp", env["p5_xpTc_15"])
        if "p5_rtdone" in env:
            em.W(t, "vp", env["p5_rtdone"])
        for k in range(16):
            mmi = t.matmul(ps["s"][:, 0:E], sb["xpT"][:, k, :],
                           sb["routerS"][:, k, :], start=(k == 0),
                           stop=(k == 15))
        mmi.then_inc(em.h["tp"], 1)
        env["p5_lg"] = em.inc("tp", 1)

    _p5_routing(env, oc)


def _p5_routing(env, oc):
    em, sb, ps = env["em"], env["sb"], env["ps"]

    @em.on("vector")
    def _(v):
        def st(ins):
            ins.then_inc(em.h["vp"], 1)
            v.wait_ge(em.h["vp"], em.inc("vp", 1))

        em.W(v, "tp", env["p5_lg"])
        tm = sb["tmp"]
        rst = sb["rstash"]
        lg = ps["s"][:, 0:E]
        st(v.reduce_max(tm[:, 16:17], lg, axis=AX.X))
        st(v.tensor_scalar(tm[:, 17:17 + E], lg, tm[:, 16:17], None,
                           ALU.is_ge))
        st(v.tensor_scalar(tm[:, 25:25 + E], tm[:, 17:17 + E], -1e4, 1e4,
                           ALU.mult, ALU.add))
        st(v.tensor_tensor(tm[:, 25:25 + E], tm[:, 25:25 + E],
                           sb["c_iotae"][...], ALU.add))
        st(v.tensor_reduce(rst[:, oc, 0:1], tm[:, 25:25 + E], AX.X, ALU.min))
        st(v.tensor_scalar(tm[:, 17:17 + E], sb["c_iotae"][...],
                           rst[:, oc, 0:1], None, ALU.is_equal))
        st(v.scalar_tensor_tensor(tm[:, 33:33 + E], tm[:, 17:17 + E],
                                  NEG_BIG, lg, ALU.mult, ALU.add))
        st(v.reduce_max(tm[:, 41:42], tm[:, 33:33 + E], axis=AX.X))
        st(v.tensor_scalar(tm[:, 42:42 + E], tm[:, 33:33 + E], tm[:, 41:42],
                           None, ALU.is_ge))
        st(v.tensor_scalar(tm[:, 50:50 + E], tm[:, 42:42 + E], -1e4, 1e4,
                           ALU.mult, ALU.add))
        st(v.tensor_tensor(tm[:, 50:50 + E], tm[:, 50:50 + E],
                           sb["c_iotae"][...], ALU.add))
        st(v.tensor_reduce(rst[:, oc, 1:2], tm[:, 50:50 + E], AX.X, ALU.min))
        st(v.tensor_scalar(tm[:, 42:42 + E], sb["c_iotae"][...],
                           rst[:, oc, 1:2], None, ALU.is_equal))
        st(v.tensor_tensor(tm[:, 58:59], tm[:, 41:42], tm[:, 16:17],
                           ALU.subtract))
        env["p5_rt_d"] = em.v["vp"]

    @em.on("scalar")
    def _(s):
        em.W(s, "vp", env["p5_rt_d"])
        s.activation(sb["tmp"][:, 59:60], sb["tmp"][:, 58:59],
                     AF.Exp).then_inc(em.h["sp"], 1)
        env["p5_rt_e"] = em.inc("sp", 1)

    @em.on("vector")
    def _(v):
        def st(ins):
            ins.then_inc(em.h["vp"], 1)
            v.wait_ge(em.h["vp"], em.inc("vp", 1))

        em.W(v, "sp", env["p5_rt_e"])
        tm = sb["tmp"]
        rst = sb["rstash"]
        st(v.tensor_scalar(tm[:, 60:61], tm[:, 59:60], 1.0, None, ALU.add))
        st(v.reciprocal(rst[:, oc, 2:3], tm[:, 60:61]))
        st(v.tensor_scalar(rst[:, oc, 3:4], rst[:, oc, 2:3], -1.0, 1.0,
                           ALU.mult, ALU.add))
        st(v.tensor_tensor(tm[:, 25:25 + E], tm[:, 17:17 + E],
                           tm[:, 42:42 + E], ALU.add))
        st(v.tensor_copy(sb["tmpb"][:, 0:E], tm[:, 25:25 + E]))
        env["p5_oh"] = em.v["vp"]

    @em.on("tensor")
    def _(t):
        em.W(t, "vp", env["p5_oh"])
        t.matmul(ps["s"][:, 8:8 + E], sb["c_triu"][...], sb["tmpb"][:, 0:E],
                 start=True, stop=True)
        mmi = t.matmul(ps["s"][0:1, 24:24 + E], sb["c_onesb"][...],
                       sb["tmpb"][:, 0:E], start=True, stop=True)
        mmi.then_inc(em.h["tp"], 1)
        env["p5_Sx"] = em.inc("tp", 1)

    @em.on("vector")
    def _(v):
        def st(ins):
            ins.then_inc(em.h["vp"], 1)
            v.wait_ge(em.h["vp"], em.inc("vp", 1))

        em.W(v, "tp", env["p5_Sx"])
        tm = sb["tmp"]
        rst = sb["rstash"]
        st(v.tensor_tensor(tm[:, 33:33 + E], ps["s"][:, 8:8 + E],
                           tm[:, 17:17 + E], ALU.mult))
        st(v.reduce_sum(rst[:, oc, 4:5], tm[:, 33:33 + E], axis=AX.X))
        st(v.tensor_tensor(tm[:, 50:50 + E], ps["s"][:, 8:8 + E],
                           tm[:, 42:42 + E], ALU.mult))
        st(v.reduce_sum(rst[:, oc, 5:6], tm[:, 50:50 + E], axis=AX.X))
        st(v.tensor_copy(sb["ctotrow"][0:1, oc * E:(oc + 1) * E],
                         ps["s"][0:1, 24:24 + E]))
        env["p5_rtdone"] = em.v["vp"]
    env[f"p5_rt_{oc}"] = env["p5_rtdone"]


def alloc_scopeD(env, st):
    sbuf, cfg = env["sbuf"], env["cfg"]
    sbuf("tb", [128, E * cfg.CC, 64], F32, st)
    sbuf("gidxall", [128, E, cfg.CAP // 16], I16, st)
    sbuf("wupg", [128, 16, 256], BF16, st)
    sbuf("wupg2", [128, 16, 256], BF16, st)
    sbuf("wupg3", [128, 16, 256], BF16, st)
    sbuf("wdg", [128, 512], BF16, st)
    sbuf("wdg2", [128, 512], BF16, st)
    sbuf("wdg3", [128, 512], BF16, st)
    sbuf("wdg4", [128, 512], BF16, st)
    sbuf("fS", [128, 1024], F32, st)
    sbuf("fS2", [128, 2048], F32, st)
    sbuf("xprow", [128, 512], F32, st)
    sbuf("raccrow", [128, 512], F32, st)
    sbuf("orow", [128, D], F32, st)


def alloc_scopeD1(env, st):
    sbuf, cfg = env["sbuf"], env["cfg"]
    CAP = cfg.CAP
    sbuf("xeT", [128, 16, CAP], BF16, st)
    sbuf("xeT2", [128, 16, CAP], BF16, st)
    sbuf("hT", [128, 11, CAP], BF16, st)
    sbuf("wdq", [128, 11, 512], BF16, st)
    sbuf("wdq2", [128, 11, 512], BF16, st)
    sbuf("stg", [128, cfg.CC, D], F32, st)


def alloc_scopeD2(env, st):
    sbuf, cfg = env["sbuf"], env["cfg"]
    sbuf("scrD", [128, D], F32, st)
    sbuf("shxT", [128, 16, cfg.SHG], BF16, st)
    sbuf("shT", [128, 44, cfg.SHG], BF16, st)
    sbuf("sacc", [128, 4, D], F32, st)


def phase_routing_finalize(env):
    em, sb, ps, cfg = env["em"], env["sb"], env["ps"], env["cfg"]
    OCH = cfg.OCH

    @em.on("sync")
    def _(sy):
        sy.dma_start(out=sb["ctot"][...],
                     in_=sb["ctotrow"][0:1, :]).then_inc(em.h["ld"], 16)
        em.inc("ld", 16)
        em.qw(sy, "ld")
    env["p6_ctl"] = em.v["ld"]

    @em.on("vector")
    def _(v):
        em.W(v, "ld", env["p6_ctl"])
        v.tensor_copy(sb["ctotb"][...], sb["ctot"][...]).then_inc(
            em.h["vp"], 1)
        env["p6_ctb"] = em.inc("vp", 1)

    @em.on("tensor")
    def _(t):
        em.W(t, "vp", env["p6_ctb"])
        t.matmul(ps["s"][0:1, 0:E], sb["c_onesb"][0:OCH, :],
                 sb["ctotb"][...], start=True, stop=True)
        mmi = t.matmul(ps["s"][0:OCH, 8:8 + E], sb["c_triu"][0:OCH, 0:OCH],
                       sb["ctotb"][...], start=True, stop=True)
        mmi.then_inc(em.h["tp"], 1)
        env["p6_t0"] = em.inc("tp", 1)

    @em.on("vector")
    def _(v):
        em.W(v, "tp", env["p6_t0"])
        v.tensor_copy(sb["tmp"][0:1, 0:E], ps["s"][0:1, 0:E])
        v.tensor_copy(sb["ctot"][...], ps["s"][0:OCH, 8:8 + E]).then_inc(
            em.h["vp"], 1)
        env["p6_v0"] = em.inc("vp", 1)

    @em.on("sync")
    def _(sy):
        em.W(sy, "vp", env["p6_v0"])
        sy.dma_start(out=env["cnt_in"].ap(),
                     in_=sb["tmp"][0:1, 0:E]).then_inc(em.h["ld"], 16)
        em.inc("ld", 16)
        sy.dma_start(out=sb["cprerow"][0:1, 0:OCH * E],
                     in_=sb["ctot"][...]).then_inc(em.h["ld"], 16)
        em.inc("ld", 16)
        em.qw(sy, "ld")
    env["p6_cst"] = em.v["ld"]
    env["p6_fl"] = em.v["ld"]

    @em.on("gpsimd")
    def _(gp):
        em.W(gp, "ld", env["p6_cst"])
        gp.collective_compute(
            "AllGather", ALU.bypass,
            replica_groups=[[0, 1, 2, 3, 4, 5, 6, 7]],
            ins=[env["cnt_in"].ap().opt()],
            outs=[env["cnt_out"].ap().opt()],
        ).then_inc(em.h["cc"], 1)
        em.inc("cc", 1)
    env["p6_ag"] = em.v["cc"]

    @em.on("sync")
    def _(sy):
        em.W(sy, "cc", env["p6_ag"])
        sy.dma_start(out=sb["cnt"][...],
                     in_=env["cnt_out"].ap()).then_inc(em.h["ld"], 16)
        em.inc("ld", 16)
        em.qw(sy, "ld")
    env["p6_cl"] = em.v["ld"]

    @em.on("tensor")
    def _(t):
        em.W(t, "ld", env["p6_cl"])
        mmi = t.matmul(ps["s"][0:1, 16:16 + E], sb["priorS"][...],
                       sb["cnt"][...], start=True, stop=True)
        mmi.then_inc(em.h["tp"], 1)
        env["p6_t1"] = em.inc("tp", 1)

    @em.on("vector")
    def _(v):
        em.W(v, "tp", env["p6_t1"])
        v.tensor_copy(sb["priorrow"][...],
                      ps["s"][0:1, 16:16 + E]).then_inc(em.h["vp"], 1)
        env["p6_v1"] = em.inc("vp", 1)

    @em.on("tensor")
    def _(t):
        em.W(t, "vp", env["p6_v1"])
        em.W(t, "ld", env["p6_fl"])
        t.matmul(ps["s"][:, 24:24 + E], sb["c_onesrow"][...],
                 sb["priorrow"][...], start=True, stop=True)
        nrep = (OCH * E + 63) // 64
        for i in range(nrep):
            w = min(64, OCH * E - i * 64)
            mmi = t.matmul(ps["d0"][:, i * 64:i * 64 + w],
                           sb["c_onesrow"][...],
                           sb["cprerow"][0:1, i * 64:i * 64 + w],
                           start=True, stop=True)
        mmi.then_inc(em.h["tp"], 1)
        env["p6_t2"] = em.inc("tp", 1)

    @em.on("vector")
    def _(v):
        em.W(v, "tp", env["p6_t2"])
        v.tensor_copy(sb["priorrep"][...], ps["s"][:, 24:24 + E])
        v.tensor_copy(sb["fS"][:, 0:OCH * E],
                      ps["d0"][:, 0:OCH * E]).then_inc(em.h["vp"], 1)
        env["p6_v3"] = em.inc("vp", 1)

    for oc in range(OCH):
        for k in range(2):
            @em.on("vector")
            def _(v, oc=oc, k=k):
                def st(ins):
                    ins.then_inc(em.h["vp"], 1)
                    v.wait_ge(em.h["vp"], em.inc("vp", 1))

                if oc == 0 and k == 0:
                    em.W(v, "vp", env["p6_v3"])
                tm = sb["tmp"]
                rst = sb["rstash"]
                j = k * OCH + oc
                st(v.tensor_scalar(tm[:, 0:E], sb["c_iotae"][...],
                                   rst[:, oc, k:k + 1], None, ALU.is_equal))
                st(v.tensor_tensor(tm[:, 8:8 + E], tm[:, 0:E],
                                   sb["priorrep"][...], ALU.mult))
                st(v.reduce_sum(tm[:, 16:17], tm[:, 8:8 + E], axis=AX.X))
                st(v.tensor_tensor(tm[:, 24:24 + E], tm[:, 0:E],
                                   sb["fS"][:, oc * E:(oc + 1) * E],
                                   ALU.mult))
                st(v.reduce_sum(tm[:, 32:33], tm[:, 24:24 + E], axis=AX.X))
                st(v.tensor_tensor(tm[:, 33:34], rst[:, oc, 4 + k:5 + k],
                                   tm[:, 32:33], ALU.add))
                st(v.tensor_tensor(tm[:, 34:35], tm[:, 33:34], tm[:, 16:17],
                                   ALU.add))
                st(v.tensor_scalar(tm[:, 35:36], tm[:, 34:35],
                                   float(cfg.CREF), None, ALU.is_lt))
                st(v.tensor_tensor(sb["pay"][:, j, 1:2],
                                   rst[:, oc, 2 + k:3 + k], tm[:, 35:36],
                                   ALU.mult))
                st(v.tensor_copy(sb["pay"][:, j, 0:1],
                                 sb["c_iotat"][:, oc:oc + 1]))
                st(v.tensor_scalar(tm[:, 36:37], tm[:, 33:34],
                                   float(cfg.CAP), None, ALU.is_ge))
                st(v.tensor_scalar(tm[:, 37:38], rst[:, oc, k:k + 1],
                                   float(cfg.CAP), None, ALU.mult))
                st(v.tensor_tensor(tm[:, 37:38], tm[:, 37:38], tm[:, 33:34],
                                   ALU.add))
                st(v.tensor_scalar(tm[:, 38:39], sb["c_iotas"][:, j:j + 1],
                                   1.0, float(E * cfg.CAP), ALU.mult,
                                   ALU.add))
                st(v.tensor_tensor(tm[:, 39:40], tm[:, 38:39], tm[:, 37:38],
                                   ALU.subtract))
                st(v.tensor_tensor(tm[:, 39:40], tm[:, 39:40], tm[:, 36:37],
                                   ALU.mult))
                st(v.tensor_tensor(sb["gdst"][:, j:j + 1], tm[:, 37:38],
                                   tm[:, 39:40], ALU.add))
    env["p6_slots"] = em.v["vp"]

    @em.on("tensor")
    def _(t):
        em.W(t, "vp", env["p6_slots"])
        for a in range(8):
            mmi = t.matmul(ps["s"][0:16, a * 2 * OCH:(a + 1) * 2 * OCH],
                           sb["c_pa"][:, a, :], sb["gdst"][...],
                           start=True, stop=True)
        mmi.then_inc(em.h["tp"], 1)
        env["p6_fold"] = em.inc("tp", 1)

    @em.on("vector")
    def _(v):
        em.W(v, "tp", env["p6_fold"])
        for a in range(8):
            vi = v.tensor_copy(
                sb["tidx"][0:16, :].rearrange("b (g a) -> b g a", a=8)[:, :, a],
                ps["s"][0:16, a * 2 * OCH:(a + 1) * 2 * OCH])
        vi.then_inc(em.h["vp"], 1)
        env["p6_tidx"] = em.inc("vp", 1)

    @em.on("sync")
    def _(sy):
        em.W(sy, "vp", env["p6_tidx"])
        sy.dma_start(out=sb["tidx"][16:32, :],
                     in_=sb["tidx"][0:16, :]).then_inc(em.h["ld"], 16)
        em.inc("ld", 16)
        em.qw(sy, "ld")
    env["p6_trep"] = em.v["ld"]

    @em.on("gpsimd")
    def _(gp):
        em.W(gp, "ld", env["p6_trep"])
        em.W(gp, "gs", env["zeros_done"])
        gp.dma_scatter_add(
            env["TBL"][:, 0:2], sb["pay"][...], sb["tidx"][...],
            num_idxs=cfg.NSLOT, num_idxs_reg=env["regs"][cfg.NSLOT],
            elem_size=2, elem_step=cfg.TROW).then_inc(em.h["gq"], 16)
        em.inc("gq", 16)
    env["p6_scat"] = em.v["gq"]

    @em.on("sync")
    def _(sy):
        em.W(sy, "gq", env["p6_scat"])
        sy.dma_start(out=sb["tb"][...],
                     in_=env["TBL"][0:E * cfg.CAP, :].rearrange(
                         "(g p) c -> p g c", p=128)).then_inc(em.h["ld"], 16)
        em.inc("ld", 16)
        em.qw(sy, "ld")
    env["p6_tb"] = em.v["ld"]

    CC = cfg.CC

    @em.on("vector")
    def _(v):
        v.memset(sb["gidxall"][...], 0).then_inc(em.h["vp"], 1)
        env["p6_giz"] = em.inc("vp", 1)

    for eh in range(0, E, 2):
        @em.on("tensor")
        def _(t, eh=eh):
            em.W(t, "ld", env["p6_tb"])
            if eh >= 2:
                em.W(t, "vp", env[f"p6_gcp_{eh - 2}"])
            for e2 in (eh, eh + 1):
                for a in range(8):
                    mmi = t.matmul(
                        ps["s"][0:16, (e2 % 2) * 256 + a * CC:
                                (e2 % 2) * 256 + (a + 1) * CC],
                        sb["c_pa"][:, a, :],
                        sb["tb"][:, e2 * CC:(e2 + 1) * CC, 0:1].rearrange(
                            "p g c -> p (g c)"),
                        start=True, stop=True)
            mmi.then_inc(em.h["tp"], 1)
            env[f"p6_gf_{eh}"] = em.inc("tp", 1)

        @em.on("vector")
        def _(v, eh=eh):
            em.W(v, "tp", env[f"p6_gf_{eh}"])
            if eh == 0:
                em.W(v, "vp", env["p6_giz"])
            for e2 in (eh, eh + 1):
                for a in range(8):
                    vi = v.tensor_copy(
                        sb["gidxall"][0:16, e2, :].rearrange(
                            "b (g a) -> b g a", a=8)[:, :, a],
                        ps["s"][0:16, (e2 % 2) * 256 + a * CC:
                                (e2 % 2) * 256 + (a + 1) * CC])
            vi.then_inc(em.h["vp"], 1)
            env[f"p6_gcp_{eh}"] = em.inc("vp", 1)

    @em.on("sync")
    def _(sy):
        em.W(sy, "vp", env[f"p6_gcp_{E - 2}"])
        sy.dma_start(out=sb["gidxall"][16:32, :, :],
                     in_=sb["gidxall"][0:16, :, :]).then_inc(em.h["ld"], 16)
        em.inc("ld", 16)
        em.qw(sy, "ld")
    env["p6_gall"] = em.v["ld"]


def phase_experts(env):
    for e in range(E):
        _p7_expert(env, e)


def _p7_expert(env, e):
    em, sb, ps, cfg = env["em"], env["sb"], env["ps"], env["cfg"]
    CAP, CC = cfg.CAP, cfg.CC
    xeT = sb["xeT"] if e % 2 == 0 else sb["xeT2"]
    gidx = sb["gidxall"][:, e, :]
    gsem = "qa" if e % 2 == 0 else "qb"

    @em.on("gpsimd")
    def _(gp):
        em.W(gp, "ld", env["p6_gall"])
        em.W(gp, "scr", env[f"p5_xpst_{cfg.OCH - 1}"])
        if e >= 2:
            em.W(gp, "tp", env[f"p7_updone_{e - 2}"])
        gp.dma_gather(xeT[...], env["XPH"], gidx,
                      num_idxs=CAP, num_idxs_reg=env["regs"][CAP], elem_size=D,
                      elem_step=D, transpose=True).then_inc(em.h[gsem], 16)
        env[f"p7_gath_{e}"] = (gsem, em.inc(gsem, 16))

    NP = 11
    for pi in range(NP):
        g = e * NP + pi
        wub = [sb["wupg"], sb["wupg2"], sb["wupg3"]][g % 3]
        wsem = ["wla", "wlb", "wl"][g % 3]

        @em.on("gpsimd")
        def _(gp, pi=pi, wub=wub, wsem=wsem, g=g):
            if g >= 3:
                ep, pp = divmod(g - 3, NP)
                em.W(gp, "tp", env[f"p7_up_{ep}_{pp}"])
            gp.dma_start(out=wub[:, :, 0:128],
                         in_=env["wug"][e, :, pi * 128:(pi + 1) * 128]
                         .rearrange("(k p) m -> p k m", p=128)).then_inc(
                em.h[wsem], 16)
            em.inc(wsem, 16)
            gp.dma_start(out=wub[:, :, 128:256],
                         in_=env["wug"][e, :,
                                        (NP + pi) * 128:(NP + pi + 1) * 128]
                         .rearrange("(k p) m -> p k m", p=128)).then_inc(
                em.h[wsem], 16)
            em.inc(wsem, 16)
        wld = (wsem, em.v[wsem])

        bset = (["q0", "q1", "q2", "q3"] if pi % 2 == 0
                else ["d0", "d1", "s", "e"])

        @em.on("tensor")
        def _(t, pi=pi, wub=wub, bset=bset):
            em.W(t, wld[0], wld[1])
            if pi == 0:
                em.W(t, env[f"p7_gath_{e}"][0], env[f"p7_gath_{e}"][1])
            if pi < 2:
                if e >= 1:
                    em.W(t, "vp", env[f"p7_dnev_{e - 1}_3_{CC - 1}"])
            if pi >= 2:
                em.W(t, "vp", env[f"p7_upev_{e}_{pi - 2}"])
            slc = ([(0, 512), (512, CAP)] if CAP > 512 else [(0, CAP)])
            for si, (a, b) in enumerate(slc):
                for k in range(16):
                    t.matmul(ps[bset[si]][:, 0:b - a], wub[:, k, 0:128],
                             xeT[:, k, a:b], start=(k == 0), stop=(k == 15))
            for si, (a, b) in enumerate(slc):
                for k in range(16):
                    mmi = t.matmul(ps[bset[2 + si]][:, 0:b - a],
                                   wub[:, k, 128:256], xeT[:, k, a:b],
                                   start=(k == 0), stop=(k == 15))
            mmi.then_inc(em.h["tp"], 1)
            env[f"p7_up_{e}_{pi}"] = em.inc("tp", 1)

        @em.on("scalar")
        def _(s, pi=pi, bset=bset):
            em.W(s, "tp", env[f"p7_up_{e}_{pi}"])
            fofs = 0 if pi % 2 == 0 else 1024
            slc = ([(0, 512), (512, CAP)] if CAP > 512 else [(0, CAP)])
            for si, (a, b) in enumerate(slc):
                si_i = s.activation(sb["fS2"][:, fofs + a:fofs + b],
                                    ps[bset[si]][:, 0:b - a], AF.Silu)
            si_i.then_inc(em.h["sp"], 1)
            env[f"p7_silu_{e}_{pi}"] = em.inc("sp", 1)

        @em.on("vector")
        def _(v, pi=pi, bset=bset):
            em.W(v, "sp", env[f"p7_silu_{e}_{pi}"])
            fofs = 0 if pi % 2 == 0 else 1024
            slc = ([(0, 512), (512, CAP)] if CAP > 512 else [(0, CAP)])
            for si, (a, b) in enumerate(slc):
                vi = v.tensor_tensor(sb["hT"][:, pi, a:b],
                                     sb["fS2"][:, fofs + a:fofs + b],
                                     ps[bset[2 + si]][:, 0:b - a], ALU.mult)
            vi.then_inc(em.h["vp"], 1)
            env[f"p7_upev_{e}_{pi}"] = em.inc("vp", 1)
    env[f"p7_updone_{e}"] = em.v["tp"]

    dbanks = ["q0", "q1", "q2", "q3", "d0"]
    for n in range(4):
        wdb = sb["wdq"] if n % 2 == 0 else sb["wdq2"]
        dsem = "scb" if n % 2 == 0 else "gl"

        @em.on("gpsimd")
        def _(gp, n=n, wdb=wdb, dsem=dsem):
            guard = env.get(f"p7_dnmm_{e}_{n - 2}",
                            env.get(f"p7_dndone_{e - 1}", 0))
            em.W(gp, "tp", guard)
            gp.dma_start(out=wdb[...],
                         in_=env["wdn"][e, :, n * 512:(n + 1) * 512]
                         .rearrange("(j p) d -> p j d", p=128)).then_inc(
                em.h[dsem], 16)
            em.inc(dsem, 16)
        dld = (dsem, em.v[dsem])

        @em.on("tensor")
        def _(t, n=n, wdb=wdb, dld=dld):
            em.W(t, dld[0], dld[1])
            if n == 0:
                em.W(t, "vp", env[f"p7_upev_{e}_{10}"])
            for cc in range(CC):
                if n > 0:
                    em.W(t, "vp", env[f"p7_dnev_{e}_{n - 1}_{cc}"])
                for j in range(11):
                    mmi = t.matmul(ps[dbanks[cc]][:, 0:512],
                                   sb["hT"][:, j, cc * 128:(cc + 1) * 128],
                                   wdb[:, j, :],
                                   start=(j == 0), stop=(j == 10))
                mmi.then_inc(em.h["tp"], 1)
                env[f"p7_dnmmc_{e}_{n}_{cc}"] = em.inc("tp", 1)
            env[f"p7_dnmm_{e}_{n}"] = em.v["tp"]

        @em.on("vector")
        def _(v, n=n):
            if n == 0:
                prev = env.get(f"p7_scat_{e - 1}_{CC - 1}")
                if prev is not None:
                    em.W(v, "sca", prev)
            for cc in range(CC):
                em.W(v, "tp", env[f"p7_dnmmc_{e}_{n}_{cc}"])
                vi = v.tensor_scalar(sb["stg"][:, cc, n * 512:(n + 1) * 512],
                                     ps[dbanks[cc]][:, 0:512],
                                     sb["tb"][:, e * CC + cc, 1:2], None,
                                     ALU.mult)
                vi.then_inc(em.h["vp"], 1)
                env[f"p7_dnev_{e}_{n}_{cc}"] = em.inc("vp", 1)

    for cc in range(CC):
        @em.on("gpsimd")
        def _(gp, cc=cc):
            em.W(gp, "vp", env[f"p7_dnev_{e}_3_{cc}"])
            em.W(gp, "sca", em.v["sca"])
            gp.dma_scatter_add(
                env["RACC"],
                sb["stg"][:, cc, :].rearrange("p (g d) -> p g d", g=1),
                gidx[:, cc * 8:(cc + 1) * 8],
                num_idxs=128,
                num_idxs_reg=env["regs"][128],
                elem_size=D,
                elem_step=D).then_inc(em.h["sca"], 16)
            env[f"p7_scat_{e}_{cc}"] = em.inc("sca", 16)
    env[f"p7_dndone_{e}"] = em.v["tp"]
    env["p7_allscat"] = [("sca", em.v["sca"])]


def phase_shared_epilogue(env):
    for sg in range(env["cfg"].NSHG):
        _p8_group(env, sg)


def _p8_group(env, sg):
    em, sb, ps, cfg = env["em"], env["sb"], env["ps"], env["cfg"]
    SHG = cfg.SHG
    g0 = sg * SHG
    NS = SHG // 512

    @em.on("gpsimd")
    def _(gp):
        em.W(gp, "scr", env[f"p5_xpst_{cfg.OCH - 1}"])
        if sg == 0:
            for sn, sv in env["p7_allscat"]:
                em.W(gp, sn, sv)
            em.W(gp, "tp", env[f"p7_dndone_{E - 1}"])
        else:
            em.W(gp, "tp", env[f"p8_updone_{sg - 1}"])
        gp.dma_gather(sb["shxT"][...], env["XPH"],
                      sb["c_gidxsh"][:, g0 // 16:(g0 + SHG) // 16],
                      num_idxs=SHG, num_idxs_reg=env["regs"][SHG],
                      elem_size=D,
                      elem_step=D, transpose=True).then_inc(em.h["gq"], 16)
        env[f"p8_gath_{sg}"] = em.inc("gq", 16)

    for pi in range(44):
        wub = sb["wupg"] if pi % 2 == 0 else sb["wupg2"]

        wsem = "wsa" if pi % 2 == 0 else "wsb"

        @em.on("sync")
        def _(sy, pi=pi, wub=wub, wsem=wsem):
            if pi < 2:
                em.W(sy, "gs", env["wcast_done"])
                em.W(sy, "tp", env[f"p7_updone_{E - 1}"] if sg == 0
                     else env[f"p8_up_{sg - 1}_{42 + pi}"])
            else:
                em.W(sy, "tp", env[f"p8_up_{sg}_{pi - 2}"])
            sy.dma_start(out=wub[:, :, 0:128],
                         in_=env["WSUB"][:, pi * 128:(pi + 1) * 128]
                         .rearrange("(k p) m -> p k m", p=128)).then_inc(
                em.h[wsem], 16)
            em.inc(wsem, 16)
            sy.dma_start(out=wub[:, :, 128:256],
                         in_=env["WSUB"][:, (44 + pi) * 128:(45 + pi) * 128]
                         .rearrange("(k p) m -> p k m", p=128)).then_inc(
                em.h[wsem], 16)
            em.inc(wsem, 16)
        wld = (wsem, em.v[wsem])

        bset = (["q0", "q1", "q2", "q3"] if pi % 2 == 0
                else ["d0", "d1", "s", "e"])

        @em.on("tensor")
        def _(t, pi=pi, wub=wub, bset=bset):
            em.W(t, wld[0], wld[1])
            if pi == 0:
                em.W(t, "gq", env[f"p8_gath_{sg}"])
            if pi < 2:
                em.W(t, "vp", env.get("p8_evlast",
                                      env[f"p7_dnev_{E - 1}_3_{env["cfg"].CC - 1}"]))
            if pi >= 2:
                em.W(t, "vp", env[f"p8_upev_{sg}_{pi - 2}"])
            for ns in range(NS):
                for k in range(16):
                    t.matmul(ps[bset[0] if ns == 0 else bset[1]][:, 0:512],
                             wub[:, k, 0:128],
                             sb["shxT"][:, k, ns * 512:(ns + 1) * 512],
                             start=(k == 0), stop=(k == 15))
            for ns in range(NS):
                for k in range(16):
                    mmi = t.matmul(ps[bset[2] if ns == 0 else bset[3]][:, 0:512],
                                   wub[:, k, 128:256],
                                   sb["shxT"][:, k, ns * 512:(ns + 1) * 512],
                                   start=(k == 0), stop=(k == 15))
            mmi.then_inc(em.h["tp"], 1)
            env[f"p8_up_{sg}_{pi}"] = em.inc("tp", 1)

        @em.on("scalar")
        def _(s, pi=pi, bset=bset):
            em.W(s, "tp", env[f"p8_up_{sg}_{pi}"])
            fofs = 0 if pi % 2 == 0 else 1024
            si_i = s.activation(sb["fS2"][:, fofs:fofs + 512],
                                ps[bset[0]][:, 0:512], AF.Silu)
            if NS > 1:
                si_i = s.activation(sb["fS2"][:, fofs + 512:fofs + 1024],
                                    ps[bset[1]][:, 0:512], AF.Silu)
            si_i.then_inc(em.h["sp"], 1)
            env[f"p8_silu_{sg}_{pi}"] = em.inc("sp", 1)

        @em.on("vector")
        def _(v, pi=pi, bset=bset):
            em.W(v, "sp", env[f"p8_silu_{sg}_{pi}"])
            fofs = 0 if pi % 2 == 0 else 1024
            vi = v.tensor_tensor(sb["shT"][:, pi, 0:512],
                                 sb["fS2"][:, fofs:fofs + 512],
                                 ps[bset[2]][:, 0:512], ALU.mult)
            if NS > 1:
                vi = v.tensor_tensor(sb["shT"][:, pi, 512:1024],
                                     sb["fS2"][:, fofs + 512:fofs + 1024],
                                     ps[bset[3]][:, 0:512], ALU.mult)
            vi.then_inc(em.h["vp"], 1)
            env[f"p8_upev_{sg}_{pi}"] = em.inc("vp", 1)
    env[f"p8_updone_{sg}"] = em.v["tp"]

    banks = ["q0", "q1", "q2", "q3"]
    NCB = (SHG // 128) // 4
    for ccb in range(NCB):
        for dq in range(4):
            for j in range(44):
                gi = (ccb * 4 + dq) * 44 + j
                wdb = [sb["wdg"], sb["wdg2"], sb["wdg3"],
                       sb["wdg4"]][gi % 4]
                wsem = ["mk", "mk2", "mk3", "mk4"][gi % 4]

                @em.on("scalar")
                def _(s, gi=gi, dq=dq, j=j, wdb=wdb, wsem=wsem):
                    if gi < 4:
                        if sg == 0:
                            em.W(s, "gs", env["wcast_done"])
                        else:
                            em.W(s, "tp", env[f"p8_dn_{sg - 1}_{gi + 172}"])
                    else:
                        em.W(s, "tp", env[f"p8_dn_{sg}_{gi - 4}"])
                    s.dma_start(out=wdb[...],
                                in_=env["WSDB"][j * 128:(j + 1) * 128,
                                                dq * 512:(dq + 1) * 512]
                                ).then_inc(em.h[wsem], 16)
                    em.inc(wsem, 16)
                wld = (wsem, em.v[wsem])

                @em.on("tensor")
                def _(t, gi=gi, ccb=ccb, dq=dq, j=j, wdb=wdb):
                    em.W(t, wld[0], wld[1])
                    if j == 0:
                        if dq == 0 and ccb == 0:
                            em.W(t, "vp", env[f"p8_upev_{sg}_{43}"])
                        if "p8_evlast" in env:
                            em.W(t, "vp", env["p8_evlast"])
                    for cc4 in range(4):
                        cc = ccb * 4 + cc4
                        mmi = t.matmul(ps[banks[cc4]][:, 0:512],
                                       sb["shT"][:, j, cc * 128:(cc + 1) * 128],
                                       wdb[...], start=(j == 0),
                                       stop=(j == 43))
                    mmi.then_inc(em.h["tp"], 1)
                    env[f"p8_dn_{sg}_{gi}"] = em.inc("tp", 1)

            for cc4 in range(4):
                cc = ccb * 4 + cc4
                rows = g0 + cc * 128

                @em.on("scalar")
                def _(s, cc4=cc4, rows=rows, dq=dq):
                    if "p8_rowused" not in env:
                        for sn, sv in env["p7_allscat"]:
                            em.W(s, sn, sv)
                        em.W(s, "scr", env[f"p5_xpst_{cfg.OCH - 1}"])
                    em.W(s, "vp", env.get("p8_rowused", 0))
                    s.dma_start(out=sb["xprow"][...],
                                in_=env["XP"][rows:rows + 128,
                                              dq * 512:(dq + 1) * 512]
                                ).then_inc(em.h["ld"], 16)
                    em.inc("ld", 16)
                    s.dma_start(out=sb["raccrow"][...],
                                in_=env["RACC"][rows:rows + 128,
                                                dq * 512:(dq + 1) * 512]
                                ).then_inc(em.h["ld"], 16)
                    em.inc("ld", 16)
                    em.qw(s, "ld")
                rowld = em.v["ld"]

                @em.on("vector")
                def _(v, cc4=cc4, dq=dq, ccb=ccb, rowld=rowld):
                    em.W(v, "ld", rowld)
                    em.W(v, "tp", env[f"p8_dn_{sg}_{(ccb * 4 + dq) * 44 + 43}"])
                    em.W(v, "sp", env.get("p8_lnsq", 0))
                    v.tensor_tensor(sb["xprow"][...], sb["xprow"][...],
                                    sb["raccrow"][...],
                                    ALU.add).then_inc(em.h["vp"], 1)
                    v.wait_ge(em.h["vp"], em.inc("vp", 1))
                    vi = v.tensor_tensor(
                        sb["sacc"][:, cc4, dq * 512:(dq + 1) * 512],
                        ps[banks[cc4]][:, 0:512], sb["xprow"][...], ALU.add)
                    vi.then_inc(em.h["vp"], 1)
                    env["p8_rowused"] = em.inc("vp", 1)
                    if cc4 == 3:
                        env["p8_evlast"] = env["p8_rowused"]

        for cc4 in range(4):
            cc = ccb * 4 + cc4
            rows = g0 + cc * 128
            ln_stats_apply(env, sb["sacc"][:, cc4, :], sb["orow"][...], 16,
                           scr_name="scrD",
                           pre=("ost", env.get("p8_ost", 0)),
                           src_ready=("vp", env["p8_rowused"]))
            env["p8_lnsq"] = env["ln_s0"]

            @em.on("scalar")
            def _(s, rows=rows, od=env["ln_done"]):
                em.W(s, "vp", od)
                s.dma_start(out=env["out"][rows:rows + 128, :],
                            in_=sb["orow"][...]).then_inc(em.h["ost"], 16)
                env["p8_ost"] = em.inc("ost", 16)
                em.qw(s, "ost")
    env[f"p8_dndone_{sg}"] = em.v["tp"]


def phase_finish(env, upto):
    em, cfg, sb = env["em"], env["cfg"], env["sb"]
    dbg = env["dbg"]

    @em.on("sync")
    def _(sy):
        if "x1" in dbg:
            em.W(sy, "scr", env[f"p5_xpst_{cfg.OCH - 1}"])
            for i in range(cfg.T // 128):
                sy.dma_start(out=dbg["x1"][i * 128:(i + 1) * 128, :],
                             in_=env["XP"][i * 128:(i + 1) * 128, :]
                             ).then_inc(em.h["scr"], 16)
                em.inc("scr", 16)
        if "part" in dbg:
            em.W(sy, "scr", env[f"p1_scr_{cfg.NSC - 1}"])
            for i in range(cfg.S // 128):
                sy.dma_start(out=dbg["part"][i * 128:(i + 1) * 128, :],
                             in_=env["PART"][i * 128:(i + 1) * 128, :]
                             ).then_inc(em.h["scr"], 16)
                em.inc("scr", 16)
        if "sqt" in dbg:
            em.W(sy, "scr", env[f"p1_scr_{cfg.NSC - 1}"])
            for h in range(H_PER):
                sy.dma_start(out=dbg["sqt"][h * 128:(h + 1) * 128, :],
                             in_=env["SQT"][h, :, :]).then_inc(
                    em.h["ost"], 16)
                em.inc("scr", 16)
        if "agout" in dbg:
            em.W(sy, "cc", env["a2a_done"])
            for i in range(2 * cfg.S // 128):
                sy.dma_start(out=dbg["agout"][i * 128:(i + 1) * 128, :],
                             in_=env["agout"].ap()[i * 128:(i + 1) * 128, :]
                             ).then_inc(em.h["scr"], 16)
                em.inc("scr", 16)
        if "rstash" in dbg:
            em.W(sy, "vp", env.get("p5_rtdone", 0))
            sy.dma_start(out=dbg["rstash"][...],
                         in_=sb["rstash"][...].rearrange("p o c -> p (o c)")
                         ).then_inc(em.h["scr"], 16)
            em.inc("scr", 16)
        if "racc" in dbg:
            for sn, sv in env.get("p7_allscat", []):
                em.W(sy, sn, sv)
            for i in range(cfg.T // 128):
                sy.dma_start(out=dbg["racc"][i * 128:(i + 1) * 128, :],
                             in_=env["RACC"][i * 128:(i + 1) * 128, :]
                             ).then_inc(em.h["scr"], 16)
                em.inc("scr", 16)
        if "tblv" in dbg:
            em.W(sy, "gq", env.get("p6_scat", 0))
            n = dbg["tblv"].shape[0]
            sy.dma_start(out=dbg["tblv"],
                         in_=env["TBL"][0:n, :]).then_inc(em.h["scr"], 16)
            em.inc("scr", 16)
        em.W(sy, "ost", em.v["ost"])
        em.W(sy, "scr", em.v["scr"])

    @em.on("gpsimd")
    def _(gp):
        em.W(gp, "gq", em.v["gq"])
        em.W(gp, "cc", em.v["cc"])


def _gidxwo(cfg, hh):
    cols = np.zeros((128, cfg.OCH * 16), np.int16)
    for oc in range(cfg.OCH):
        ids = []
        for p in range(128):
            ids.append(hh * cfg.T + oc * 128 + p)
        for p in range(128):
            ids.append(cfg.S + hh * cfg.T + oc * 128 + p)
        cols[:, oc * 16:(oc + 1) * 16] = _wrap16(ids)[0:128]
    return cols


def _prep_inputs(cfg, inputs, core, consts):
    b, hh = core // 2, core % 2
    h0 = hh * H_PER
    x = np.asarray(inputs["L0_attn_hidden_states"])
    wq = np.asarray(inputs["L0_attn_w_q"])
    wk = np.asarray(inputs["L0_attn_w_k"])
    wv = np.asarray(inputs["L0_attn_w_v"])
    wo = np.asarray(inputs["L0_attn_w_o"])
    mems = np.asarray(inputs["L0_attn_memories"])
    nrms = np.asarray(inputs["L0_attn_memory_norms"])
    im = {
        "xb": np.ascontiguousarray(x[b]),
        "xown": np.ascontiguousarray(x[b, hh * cfg.T:(hh + 1) * cfg.T]),
        "wqT": np.ascontiguousarray(wq[h0 * HD:(h0 + H_PER) * HD, :].T),
        "wkT": np.ascontiguousarray(wk[h0 * HD:(h0 + H_PER) * HD, :].T),
        "wvT": np.ascontiguousarray(wv[h0 * HD:(h0 + H_PER) * HD, :].T),
        "woT": np.ascontiguousarray(wo.T),
        "memi": np.ascontiguousarray(
            mems[:, h0:h0 + H_PER].reshape(M * H_PER, HD * HD)),
        "nrmT": np.ascontiguousarray(
            nrms[:, h0:h0 + H_PER].transpose(1, 2, 0)),
        "nrmF": np.ascontiguousarray(nrms.reshape(M * 16, HD)),
        "gatei": np.ascontiguousarray(
            np.asarray(inputs["L0_attn_gate"])[h0:h0 + H_PER][None, :]),
        "routeri": np.ascontiguousarray(inputs["L0_ffn_router"]),
        "wug": np.asarray(inputs["L0_ffn_w_up_gate"]),
        "wdn": np.asarray(inputs["L0_ffn_w_down"]),
        "wsu": np.asarray(inputs["L0_ffn_shared_w_up_gate"]),
        "wsd": np.asarray(inputs["L0_ffn_shared_w_down"]),
        "prior_sel": (np.arange(8) < core).astype(np.float32)[:, None],
        "gidxwo": _gidxwo(cfg, hh),
    }
    for n, v in consts.items():
        im["cin_" + n] = v
    return im


def kernel(**inputs):
    cfg = Cfg()
    nc = build_program(cfg)
    consts = make_consts(cfg)
    in_maps = [_prep_inputs(cfg, inputs, c, consts) for c in range(8)]
    res = run_bass_kernel_spmd(nc, in_maps, core_ids=list(range(8)))
    full = np.empty((4, 4096, 2048), np.float32)
    for c in range(8):
        b, hh = c // 2, c % 2
        full[b, hh * cfg.T:(hh + 1) * cfg.T, :] = res.results[c]["out"]
    return full


# version 06
# speedup vs baseline: 1.4537x; 1.0209x over previous
import math
from contextlib import ExitStack

import numpy as np
import ml_dtypes

import concourse.bass as bass
import concourse.mybir as mybir
from concourse import library_config
from concourse.library_overlay import lower_extended_insts
from concourse.aot_env import allow_ici_all_to_all
from concourse.bass_utils import run_bass_kernel_spmd

F32 = mybir.dt.float32
BF16 = mybir.dt.bfloat16
I16 = mybir.dt.int16
AF = mybir.ActivationFunctionType
ALU = mybir.AluOpType
AX = mybir.AxisListType
BF = ml_dtypes.bfloat16

D = 2048
H_PER = 8
HD = 128
HC = H_PER * HD
M = 4
SCALE = HD ** (-0.5)
EPS = 1e-6
LN_EPS = 1e-5
E = 8
I_FF = 1408
J_FF = 5632
NEG_BIG = -1.0e30


class Cfg:
    def __init__(self, seq=4096):
        self.S = seq
        self.T = seq // 2
        self.NCH = self.S // 128
        self.NSC = self.S // 512
        self.OCH = self.T // 128
        cap = int(math.ceil(self.T * 2 * 1.25 / E))
        self.CAP = ((cap + 127) // 128) * 128
        self.CC = self.CAP // 128
        self.CREF = int(math.ceil(8 * self.T * 2 / E * 1.25))
        self.NSLOT = self.T * 2
        self.TROW = 64
        self.NTBL = E * self.CAP + self.NSLOT
        self.SHG = min(512, self.T)
        self.NSHG = self.T // self.SHG


def _wrap16(vals, rows=128):
    n = len(vals)
    cols = (n + 15) // 16
    out = np.zeros((rows, cols), np.int16)
    for i, v in enumerate(vals):
        out[i % 16, i // 16] = v
    out[16:32] = out[0:16]
    return out


def make_consts(cfg):
    c = {}
    c["ident"] = np.eye(128, dtype=BF)
    c["identf"] = np.eye(128, dtype=np.float32)
    c["triu"] = np.triu(np.ones((128, 128), np.float32), 1).astype(BF)
    mt = np.triu(np.ones((128, 128), np.float32), 0)
    c["maskT"] = np.tile(mt, (1, H_PER)).astype(BF)
    c["onesb"] = np.ones((128, 1), dtype=BF)
    c["onesrow"] = np.ones((1, 128), np.float32)
    pa = np.zeros((128, 8, 16), np.float32)
    for a in range(8):
        for b in range(16):
            pa[16 * a + b, a, b] = 1.0
    c["pa"] = pa
    c["iotae"] = np.tile(np.arange(8, dtype=np.float32)[None, :], (128, 1))
    c["iotat"] = (np.arange(cfg.OCH, dtype=np.float32)[None, :] * 128
                  + np.arange(128, dtype=np.float32)[:, None])
    c["iotas"] = (np.arange(2 * cfg.OCH, dtype=np.float32)[None, :] * 128
                  + np.arange(128, dtype=np.float32)[:, None])
    c["gidxsh"] = _wrap16(np.arange(cfg.T, dtype=np.int16))
    c["dmp"] = (cfg.T + (np.arange(E * cfg.CC) % cfg.CC)[None, :] * 128
                + np.arange(128)[:, None]).astype(np.float32)
    sel = np.zeros((M * 16, M), np.float32)
    for m in range(M):
        sel[m * 16:(m + 1) * 16, m] = 1.0
    c["esel"] = sel
    return c


def const_specs(c):
    return [
        ("ident", [128, 128], BF16), ("identf", [128, 128], F32),
        ("triu", [128, 128], BF16),
        ("maskT", [128, H_PER * 128], BF16), ("onesb", [128, 1], BF16),
        ("onesrow", [1, 128], F32), ("pa", [128, 8, 16], F32),
        ("iotae", [128, 8], F32), ("iotat", [128, c.OCH], F32),
        ("iotas", [128, 2 * c.OCH], F32), ("gidxsh", [128, c.T // 16], I16),
        ("esel", [M * 16, M], F32), ("dmp", [128, E * c.CC], F32),
    ]


class EM:

    def __init__(self, nc, block, stack, names):
        self.nc = nc
        self.block = block
        self.h = {}
        self.v = {}
        for n in names:
            self.h[n] = stack.enter_context(nc.semaphore("s_" + n))
            self.v[n] = 0

    def inc(self, n, by=1):
        self.v[n] += by
        return self.v[n]

    def W(self, eng, n, val):
        assert val <= self.v[n], f"wait on future {n}: {val} > {self.v[n]}"
        if val > 0:
            eng.wait_ge(self.h[n], val)

    def on(self, engine):
        def deco(fn):
            getattr(self.block, engine)(fn)
        return deco

    def qw(self, eng, n):
        if self.v[n] > 0:
            eng.wait_ge(self.h[n], self.v[n])


SEMS = ["ld", "wl", "scr", "tp", "vp", "sp", "gq", "cc", "ost", "bar",
        "gl", "gs", "wla", "wlb", "qa", "qb", "sca", "scb", "mk",
        "mk2", "mk3", "mk4", "wsa", "wsb"]


def full_barrier(env):
    em = env["em"]
    tgt = em.v["bar"] + 5
    dmas = {k: em.v[k] for k in ["scr", "ld", "wl", "ost", "gq", "cc", "gl",
                                 "gs", "wla", "wlb", "qa", "qb", "sca",
                                 "scb", "mk", "mk2", "mk3", "mk4",
                                 "wsa", "wsb"]}
    prods = {k: em.v[k] for k in ["tp", "vp", "sp"]}

    @em.on("sync")
    def _(sy):
        for k, v in dmas.items():
            em.W(sy, k, v)
        for k, v in prods.items():
            em.W(sy, k, v)
        sy.sem_inc(em.h["bar"], 1)
        sy.wait_ge(em.h["bar"], tgt)

    for name in ["gpsimd", "tensor", "vector", "scalar"]:
        @em.on(name)
        def _(eng):
            eng.sem_inc(em.h["bar"], 1)
            eng.wait_ge(em.h["bar"], tgt)
    em.v["bar"] = tgt


def build_program(cfg, dbg=(), upto=99):
    nc = bass.Bass("TRN2", target_bir_lowering=False)
    S, T, OCH, NSC, CAP, CC = cfg.S, cfg.T, cfg.OCH, cfg.NSC, cfg.CAP, cfg.CC

    ein = lambda n, s, d: nc.dram_tensor(n, s, d, kind="ExternalInput").ap()
    env = {"cfg": cfg, "nc": nc}
    env["xb"] = ein("xb", [S, D], F32)
    env["xown"] = ein("xown", [T, D], F32)
    env["wqT"] = ein("wqT", [D, HC], F32)
    env["wkT"] = ein("wkT", [D, HC], F32)
    env["wvT"] = ein("wvT", [D, HC], F32)
    env["woT"] = ein("woT", [D, D], F32)
    env["memi"] = ein("memi", [M * H_PER, HD * HD], F32)
    env["nrmT"] = ein("nrmT", [H_PER, HD, M], F32)
    env["nrmF"] = ein("nrmF", [M * 16, HD], F32)
    env["gatei"] = ein("gatei", [1, H_PER], F32)
    env["routeri"] = ein("routeri", [D, E], F32)
    env["wug"] = ein("wug", [E, D, 2 * I_FF], F32)
    env["wdn"] = ein("wdn", [E, I_FF, D], F32)
    env["wsu"] = ein("wsu", [D, 2 * J_FF], F32)
    env["wsd"] = ein("wsd", [J_FF, D], F32)
    env["prior_sel"] = ein("prior_sel", [8, 1], F32)
    env["gidxwo"] = ein("gidxwo", [128, cfg.OCH * 16], I16)
    cst_sp = const_specs(cfg)
    env["cst"] = {n: ein("cin_" + n, s, d) for n, s, d in cst_sp}
    env["out"] = nc.dram_tensor("out", [T, D], F32, kind="ExternalOutput").ap()
    env["dbg"] = {}
    for name, shape, dt in dbg:
        env["dbg"][name] = nc.dram_tensor("dbg_" + name, shape, dt,
                                          kind="ExternalOutput").ap()

    env["XBH"] = nc.dram_tensor("XBH", [S, D], BF16).ap()
    env["SQT"] = nc.dram_tensor("SQT", [H_PER, HD, S], BF16).ap()
    env["PART"] = nc.dram_tensor("PART", [S, HC], BF16).ap()
    env["XP"] = nc.dram_tensor("XP", [T, D], F32).ap()
    env["XPH"] = nc.dram_tensor("XPH", [T + cfg.CAP, D], BF16).ap()
    env["TBL"] = nc.dram_tensor("TBL", [cfg.NTBL, cfg.TROW], F32).ap()
    env["RACC"] = nc.dram_tensor("RACC", [T + cfg.CAP, D], F32).ap()
    env["WSUB"] = nc.dram_tensor("WSUB", [D, 2 * J_FF], BF16).ap()
    env["WSDB"] = nc.dram_tensor("WSDB", [J_FF, D], BF16).ap()
    env["agin"] = nc.dram_tensor("agin", [S, HC], BF16)
    env["agout"] = nc.dram_tensor("agout", [2 * S, HC], BF16)
    env["cnt_in"] = nc.dram_tensor("cnt_in", [1, E], F32)
    env["cnt_out"] = nc.dram_tensor("cnt_out", [8, E], F32)

    with ExitStack() as stack:
        with nc.Block() as block:
            em = EM(nc, block, stack, SEMS)
            env["em"] = em
            sb = {}
            env["sb"] = sb

            def sbuf(name, shape, dt, st=stack):
                t = st.enter_context(nc.sbuf_tensor(name, shape, dt))
                sb[name] = t
                return t
            env["sbuf"] = sbuf

            for n, s, d in cst_sp:
                sbuf("c_" + n, s, d)
            for n, s, d in [
                ("nrmTf", [128, H_PER, M], F32), ("nrmTb", [128, H_PER, M], BF16),
                ("nrmFs", [M * 16, HD], F32), ("grow", [1, H_PER], F32),
                ("g1m", [128, H_PER], F32), ("weff", [128, H_PER * M], F32),
                ("relacc", [128, H_PER], F32), ("relaccb", [128, H_PER], BF16),
                ("priorS", [8, 1], F32), ("priorrow", [1, E], F32),
                ("priorrep", [128, E], F32), ("routerS", [128, 16, E], F32),
                ("zero", [128, 512], BF16),
                ("S", [128, HC], F32), ("Sb", [128, HC], BF16),
                ("Z", [128, H_PER], F32), ("Zb", [128, H_PER], BF16),
                ("rstash", [128, OCH, 8], F32),
                ("ctot", [OCH, E], F32), ("ctotb", [OCH, E], BF16),
                ("ctotrow", [1, OCH * E], F32),
                ("cprerow", [1, 128], F32), ("cnt", [8, E], F32),
                ("tmp", [128, 64], F32), ("tmpb", [128, 64], BF16),
                ("rel", [16, 40], F32),
                ("pay", [128, 2 * OCH, 2], F32), ("gdst", [128, 2 * OCH], F32),
                ("tidx", [128, (2 * OCH * 128) // 16], I16),
                ("xstage", [128, 2048], BF16), ("xstage2", [128, 2048], BF16),
            ]:
                sbuf(n, s, d)

            ps = {}
            env["ps"] = ps
            for nm in ["q0", "q1", "q2", "q3", "d0", "d1", "s"]:
                ps[nm] = stack.enter_context(
                    nc.psum_tensor("ps" + nm, [128, 512], F32))
            env["regs"] = {}
            for rv in sorted({256, 128, cfg.CAP, cfg.NSLOT, cfg.SHG}):
                env["regs"][rv] = stack.enter_context(
                    nc.gpsimd.register(f"rn{rv}"))

            phase_setup(env)
            if upto >= 1:
                with ExitStack() as stB:
                    ps["t"] = stB.enter_context(
                        nc.psum_tensor("pst", [128, 1024], BF16))
                    alloc_scopeB(env, stB)
                    with ExitStack() as stB1:
                        alloc_scopeB1(env, stB1)
                        phase_p0b(env)
                        phase_setup_bulk(env)
                        phase_attn_pass1(env)
                        if upto >= 2:
                            phase_rel(env)
                    if upto >= 3:
                        with ExitStack() as stB2:
                            alloc_scopeB2(env, stB2)
                            phase_attn_pass2(env)
                    del ps["t"]
            if upto >= 5:
                full_barrier(env)
                with ExitStack() as stC:
                    env["_stC"] = stC
                    alloc_scopeC(env, stC)
                    phase_wo_ln1_routing(env)
            if upto >= 6:
                full_barrier(env)
                with ExitStack() as stD:
                    ps["e"] = stD.enter_context(
                        nc.psum_tensor("pse", [128, 512], F32))
                    alloc_scopeD(env, stD)
                    phase_routing_finalize(env)
                    if upto >= 7:
                        with ExitStack() as stD1:
                            alloc_scopeD1(env, stD1)
                            phase_experts(env)
                    if upto >= 8:
                        with ExitStack() as stD2:
                            alloc_scopeD2(env, stD2)
                            phase_shared_epilogue(env)
            phase_finish(env, upto)

    lower_extended_insts(nc)
    return nc


def phase_setup(env):
    em, sb, cfg, cst = env["em"], env["sb"], env["cfg"], env["cst"]
    S, T = cfg.S, cfg.T

    @em.on("sync")
    def _(sy):
        for n, _s, _d in const_specs(cfg):
            sy.dma_start(out=sb["c_" + n][...], in_=cst[n]).then_inc(
                em.h["ld"], 16)
            em.inc("ld", 16)
        sy.dma_start(out=sb["nrmTf"][...],
                     in_=env["nrmT"].rearrange("h d m -> d h m")).then_inc(
            em.h["ld"], 16)
        em.inc("ld", 16)
        sy.dma_start(out=sb["nrmFs"][...], in_=env["nrmF"]).then_inc(
            em.h["ld"], 16)
        em.inc("ld", 16)
        sy.dma_start(out=sb["priorS"][...], in_=env["prior_sel"]).then_inc(
            em.h["ld"], 16)
        em.inc("ld", 16)
        sy.dma_start(out=sb["routerS"][...],
                     in_=env["routeri"].rearrange("(k p) e -> p k e", p=128)
                     ).then_inc(em.h["ld"], 16)
        em.inc("ld", 16)
        sy.dma_start(out=sb["tmp"][0:1, 0:H_PER], in_=env["gatei"]).then_inc(
            em.h["ld"], 16)
        em.inc("ld", 16)
        em.qw(sy, "ld")
    env["cld"] = em.v["ld"]

    @em.on("vector")
    def _(v):
        for nm in ["zero", "S", "Z", "Sb", "Zb", "relacc"]:
            v.memset(sb[nm][...], 0.0)
        v.memset(sb["tidx"][...], 0)
        v.memset(sb["rstash"][...], 0.0)
        v.nop().then_inc(em.h["vp"], 1)
        em.inc("vp", 1)
        em.W(v, "ld", env["cld"])
        v.tensor_copy(sb["nrmTb"][...], sb["nrmTf"][...]).then_inc(
            em.h["vp"], 1)
        em.inc("vp", 1)
    env["zeroed"] = em.v["vp"]

    @em.on("gpsimd")
    def _(gp):
        gp.load_library(library_config.mlp)
        for rv, rh in env["regs"].items():
            gp.reg_mov(rh, rv)

    @em.on("scalar")
    def _(s):
        em.W(s, "ld", env["cld"])
        s.activation(sb["grow"][...], sb["tmp"][0:1, 0:H_PER],
                     AF.Sigmoid).then_inc(em.h["sp"], 1)
        em.inc("sp", 1)
    gr = em.v["sp"]

    @em.on("tensor")
    def _(t):
        em.W(t, "sp", gr)
        t.matmul(env["ps"]["s"][:, 0:H_PER], sb["c_onesrow"][...],
                 sb["grow"][...], start=True, stop=True).then_inc(
            em.h["tp"], 1)
        em.inc("tp", 1)
    grep = em.v["tp"]

    @em.on("vector")
    def _(v):
        em.W(v, "tp", grep)
        v.tensor_scalar(sb["g1m"][...], env["ps"]["s"][:, 0:H_PER], -1.0, 1.0,
                        ALU.mult, ALU.add).then_inc(em.h["vp"], 1)
        em.inc("vp", 1)
    env["g1m_ready"] = em.v["vp"]


def phase_setup_bulk(env):
    em, sb, cfg = env["em"], env["sb"], env["cfg"]
    S, T = cfg.S, cfg.T
    ntblel = cfg.NTBL * cfg.TROW
    env["xbh"] = {}

    @em.on("gpsimd")
    def _(gp):
        em.W(gp, "vp", env["zeroed"])
        for i in range(S // 128):
            stg = sb["xstage"] if i % 2 == 0 else sb["xstage2"]
            if i >= 1:
                em.W(gp, "gs", env["xbh"][i - 1])
            gp.dma_start(out=stg[...],
                         in_=env["xb"][i * 128:(i + 1) * 128, :]).then_inc(
                em.h["gl"], 16)
            t = em.inc("gl", 16)
            gp.wait_ge(em.h["gl"], t)
            gp.dma_start(out=env["XBH"][i * 128:(i + 1) * 128, :],
                         in_=stg[...]).then_inc(em.h["gs"], 16)
            env["xbh"][i] = em.inc("gs", 16)

    @em.on("gpsimd")
    def _(gp):
        for i in range((T + cfg.CAP) // 128):
            for n4 in range(4):
                em.W(gp, "gs", em.v["gs"])
                gp.dma_start(out=env["RACC"][i * 128:(i + 1) * 128,
                                             n4 * 512:(n4 + 1) * 512],
                             in_=sb["zero"][...]).then_inc(em.h["gs"], 16)
                em.inc("gs", 16)
        for i in range(cfg.CAP // 128):
            for n4 in range(4):
                em.W(gp, "gs", em.v["gs"])
                gp.dma_start(out=env["XPH"][T + i * 128:T + (i + 1) * 128,
                                            n4 * 512:(n4 + 1) * 512],
                             in_=sb["zero"][...]).then_inc(em.h["gs"], 16)
                em.inc("gs", 16)
        flat = env["TBL"].rearrange("r c -> (r c)")
        off = 0
        while off < ntblel:
            n = min(128 * 512, ntblel - off)
            fr = n // 128
            em.W(gp, "gs", em.v["gs"])
            gp.dma_start(out=flat[off:off + 128 * fr].rearrange(
                "(p f) -> p f", p=128),
                in_=sb["zero"][:, 0:fr]).then_inc(em.h["gs"], 16)
            em.inc("gs", 16)
            off += 128 * fr
    env["zeros_done"] = em.v["gs"]

    @em.on("gpsimd")
    def _(gp):
        for rb in range(16):
            for cb in range(6):
                w = min(2048, 2 * J_FF - cb * 2048)
                stg = sb["xstage"] if (rb * 6 + cb) % 2 == 0 else sb["xstage2"]
                em.W(gp, "gs", em.v["gs"])
                gp.dma_start(out=stg[:, 0:w],
                             in_=env["wsu"][rb * 128:(rb + 1) * 128,
                                            cb * 2048:cb * 2048 + w]
                             ).then_inc(em.h["gl"], 16)
                t = em.inc("gl", 16)
                gp.wait_ge(em.h["gl"], t)
                gp.dma_start(out=env["WSUB"][rb * 128:(rb + 1) * 128,
                                             cb * 2048:cb * 2048 + w],
                             in_=stg[:, 0:w]).then_inc(em.h["gs"], 16)
                em.inc("gs", 16)
        for rb in range(44):
            stg = sb["xstage"] if rb % 2 == 0 else sb["xstage2"]
            em.W(gp, "gs", em.v["gs"])
            gp.dma_start(out=stg[...],
                         in_=env["wsd"][rb * 128:(rb + 1) * 128, :]
                         ).then_inc(em.h["gl"], 16)
            t = em.inc("gl", 16)
            gp.wait_ge(em.h["gl"], t)
            gp.dma_start(out=env["WSDB"][rb * 128:(rb + 1) * 128, :],
                         in_=stg[...]).then_inc(em.h["gs"], 16)
            em.inc("gs", 16)
    env["wcast_done"] = em.v["gs"]




def alloc_scopeB(env, st):
    sbuf = env["sbuf"]
    sbuf("memb", [128, M * H_PER, HD], BF16, st)
    sbuf("sqT", [128, H_PER, 512], BF16, st)
    sbuf("partS", [128, 4, HC], BF16, st)
    sbuf("scT", [128, H_PER * 128], BF16, st)


def alloc_scopeB1(env, st):
    sbuf = env["sbuf"]
    sbuf("wqTs", [128, 16, HC], BF16, st)
    sbuf("wkTs", [128, 16, HC], BF16, st)
    sbuf("wvTs", [128, 16, HC], BF16, st)
    sbuf("xT", [128, 16, 512], BF16, st)
    sbuf("xT2", [128, 16, 512], BF16, st)
    sbuf("kp", [128, 4, HC], BF16, st)
    sbuf("vc", [128, 4, HC], BF16, st)
    sbuf("kpT", [128, HC], BF16, st)
    for p in range(2):
        sbuf(f"sigF{p}", [128, 512], F32, st)
        sbuf(f"sigG{p}", [128, 512], F32, st)
        sbuf(f"sigH{p}", [128, 512], BF16, st)


def alloc_scopeB2(env, st):
    sbuf = env["sbuf"]
    sbuf("comb", [128, HC], BF16, st)
    sbuf("cm", [128, M * H_PER], F32, st)


def phase_p0b(env):
    em, sb = env["em"], env["sb"]

    @em.on("gpsimd")
    def _(gp):
        for nm, src in [("wqTs", env["wqT"]), ("wkTs", env["wkT"]),
                        ("wvTs", env["wvT"])]:
            gp.dma_start(out=sb[nm][...],
                         in_=src.rearrange("(k p) e -> p k e", p=128)
                         ).then_inc(em.h["wl"], 16)
            em.inc("wl", 16)
        gp.dma_start(out=sb["memb"][...],
                     in_=env["memi"].rearrange("mh (d e) -> d mh e", d=HD)
                     ).then_inc(em.h["wl"], 16)
        em.inc("wl", 16)
    env["wl_attn"] = em.v["wl"]
    env["wl_mem"] = em.v["wl"]




def phase_attn_pass1(env):
    for sc in range(env["cfg"].NSC):
        _p1_super(env, sc)
    env["p1_tdone"] = env["em"].v["tp"]


def _p1_super(env, sc):
    em, sb, ps, cfg = env["em"], env["sb"], env["ps"], env["cfg"]
    t0 = sc * 512
    xT = sb["xT"] if sc % 2 == 0 else sb["xT2"]

    @em.on("sync")
    def _(sy):
        em.W(sy, "gs", env["xbh"][(t0 + 512) // 128 - 1])
        if sc >= 2:
            em.W(sy, "tp", env[f"p1_projdone_{sc - 2}"])
        for k in range(16):
            sy.dma_start(out=xT[:, k, :],
                         in_=env["XBH"][t0:t0 + 512, k * 128:(k + 1) * 128],
                         transpose=True).then_inc(em.h["ld"], 16)
            em.inc("ld", 16)
        em.qw(sy, "ld")
    xT_ready = em.v["ld"]

    for et in range(H_PER):
        bank = ps["d0"] if et % 2 == 0 else ps["d1"]

        @em.on("tensor")
        def _(t, et=et, bank=bank):
            if et == 0:
                em.W(t, "ld", xT_ready)
                em.W(t, "wl", env["wl_attn"])
                em.W(t, "vp", env[f"p1_outev_{sc - 1}_3"] if sc > 0
                     else env["g1m_ready"])
            if et >= 2:
                em.W(t, "vp", env[f"p1_qev_{sc}_{et - 2}"])
            for k in range(16):
                mmi = t.matmul(bank[:, 0:512],
                               sb["wqTs"][:, k, et * 128:(et + 1) * 128],
                               xT[:, k, :], start=(k == 0), stop=(k == 15))
            mmi.then_inc(em.h["tp"], 1)
            env[f"p1_qps_{sc}_{et}"] = em.inc("tp", 1)

        @em.on("vector")
        def _(v, et=et, bank=bank):
            em.W(v, "tp", env[f"p1_qps_{sc}_{et}"])
            if sc > 0 and et == 0:
                em.W(v, "scr", env[f"p1_scr_{sc - 1}"])
            pr = et % 2
            v.reduce_sum(sb["tmp"][:, pr:pr + 1], bank[:, 0:512],
                         axis=AX.X).then_inc(em.h["vp"], 1)
            g0 = em.inc("vp", 1)
            v.tensor_scalar(sb[f"sigF{pr}"][...], bank[:, 0:512], 0.0, None,
                            ALU.min)
            v.tensor_scalar(sb[f"sigG{pr}"][...], bank[:, 0:512], 0.0, None,
                            ALU.max).then_inc(em.h["vp"], 1)
            env[f"p1_qsig_{sc}_{et}"] = em.inc("vp", 1)
            v.wait_ge(em.h["vp"], g0)
            v.tensor_tensor(sb["relacc"][:, et:et + 1],
                            sb["relacc"][:, et:et + 1],
                            sb["tmp"][:, pr:pr + 1], ALU.add)

        @em.on("scalar")
        def _(s, et=et):
            pr = et % 2
            em.W(s, "vp", env[f"p1_qsig_{sc}_{et}"])
            s.activation(sb[f"sigH{pr}"][...], sb[f"sigF{pr}"][...],
                         AF.Exp).then_inc(em.h["sp"], 1)
            env[f"p1_qexp_{sc}_{et}"] = em.inc("sp", 1)

        @em.on("vector")
        def _(v, et=et):
            pr = et % 2
            em.W(v, "sp", env[f"p1_qexp_{sc}_{et}"])
            v.tensor_tensor(sb["sqT"][:, et, :], sb[f"sigG{pr}"][...],
                            sb[f"sigH{pr}"][...], ALU.add).then_inc(
                em.h["vp"], 1)
            env[f"p1_qev_{sc}_{et}"] = em.inc("vp", 1)

    for kn in range(8):
        sub, n = kn // 2, kn % 2
        bank = ps[f"q{kn % 4}"]

        @em.on("tensor")
        def _(t, kn=kn, sub=sub, n=n, bank=bank):
            if kn >= 4:
                em.W(t, "vp", env[f"p1_kev_{sc}_{kn - 4}"])
            elif sc > 0:
                em.W(t, "vp", env[f"p1_outev_{sc - 1}_3"])
            for k in range(16):
                mmi = t.matmul(bank[:, 0:512],
                               xT[:, k, sub * 128:(sub + 1) * 128],
                               sb["wkTs"][:, k, n * 512:(n + 1) * 512],
                               start=(k == 0), stop=(k == 15))
            mmi.then_inc(em.h["tp"], 1)
            env[f"p1_kps_{sc}_{kn}"] = em.inc("tp", 1)

        @em.on("vector")
        def _(v, kn=kn, bank=bank):
            pr = kn % 2
            em.W(v, "tp", env[f"p1_kps_{sc}_{kn}"])
            v.tensor_scalar(sb[f"sigF{pr}"][...], bank[:, 0:512], 0.0, None,
                            ALU.min)
            v.tensor_scalar(sb[f"sigG{pr}"][...], bank[:, 0:512], 0.0, None,
                            ALU.max).then_inc(em.h["vp"], 1)
            env[f"p1_ksig_{sc}_{kn}"] = em.inc("vp", 1)

        @em.on("scalar")
        def _(s, kn=kn):
            pr = kn % 2
            em.W(s, "vp", env[f"p1_ksig_{sc}_{kn}"])
            s.activation(sb[f"sigH{pr}"][...], sb[f"sigF{pr}"][...],
                         AF.Exp).then_inc(em.h["sp"], 1)
            env[f"p1_kexp_{sc}_{kn}"] = em.inc("sp", 1)

        @em.on("vector")
        def _(v, kn=kn, sub=sub, n=n):
            pr = kn % 2
            em.W(v, "sp", env[f"p1_kexp_{sc}_{kn}"])
            v.tensor_tensor(sb["kp"][:, sub, n * 512:(n + 1) * 512],
                            sb[f"sigG{pr}"][...], sb[f"sigH{pr}"][...],
                            ALU.add).then_inc(em.h["vp"], 1)
            env[f"p1_kev_{sc}_{kn}"] = em.inc("vp", 1)

    for vn in range(8):
        sub, n = vn // 2, vn % 2
        bank = ps[f"q{vn % 4}"]

        @em.on("tensor")
        def _(t, vn=vn, sub=sub, n=n, bank=bank):
            em.W(t, "vp", env[f"p1_kev_{sc}_{4 + (vn % 4)}"] if vn < 4
                 else env[f"p1_vev_{sc}_{vn - 4}"])
            for k in range(16):
                mmi = t.matmul(bank[:, 0:512],
                               xT[:, k, sub * 128:(sub + 1) * 128],
                               sb["wvTs"][:, k, n * 512:(n + 1) * 512],
                               start=(k == 0), stop=(k == 15))
            mmi.then_inc(em.h["tp"], 1)
            env[f"p1_vps_{sc}_{vn}"] = em.inc("tp", 1)

        @em.on("vector")
        def _(v, vn=vn, sub=sub, n=n, bank=bank):
            em.W(v, "tp", env[f"p1_vps_{sc}_{vn}"])
            v.tensor_copy(sb["vc"][:, sub, n * 512:(n + 1) * 512],
                          bank[:, 0:512]).then_inc(em.h["vp"], 1)
            env[f"p1_vev_{sc}_{vn}"] = em.inc("vp", 1)
    env[f"p1_projdone_{sc}"] = em.v["tp"]

    for sub in range(4):
        _p1_chunk(env, sc, sub)

    @em.on("scalar")
    def _(s):
        em.W(s, "vp", env[f"p1_qev_{sc}_{H_PER - 1}"])
        for et in range(H_PER):
            s.dma_start(out=env["SQT"][et, :, t0:t0 + 512],
                        in_=sb["sqT"][:, et, :]).then_inc(em.h["scr"], 16)
            em.inc("scr", 16)
        em.W(s, "vp", env[f"p1_outev_{sc}_3"])
        s.dma_start(out=env["PART"][t0:t0 + 512, :].rearrange(
                        "(s p) e -> p s e", p=128),
                    in_=sb["partS"][...]).then_inc(em.h["scr"], 16)
        em.inc("scr", 16)
        em.qw(s, "scr")
    env[f"p1_scr_{sc}"] = em.v["scr"]


def _p1_chunk(env, sc, sub):
    em, sb, ps = env["em"], env["sb"], env["ps"]

    @em.on("vector")
    def _(v):
        v.tensor_copy(sb["Sb"][...], sb["S"][...])
        v.tensor_copy(sb["Zb"][...], sb["Z"][...]).then_inc(em.h["vp"], 1)
        env["p1_snap"] = em.inc("vp", 1)

    @em.on("tensor")
    def _(t):
        em.W(t, "vp", env[f"p1_kev_{sc}_{sub * 2 + 1}"])
        em.W(t, "vp", env[f"p1_vev_{sc}_{sub * 2 + 1}"])
        em.W(t, "vp", env[f"p1_qev_{sc}_{H_PER - 1}"])
        em.W(t, "vp", env["p1_snap"])
        if "p1_kpT_cp" in env:
            em.W(t, "sp", env["p1_kpT_cp"])
        for h in range(H_PER):
            tr = t.transpose(ps["t"][:, h * 128:(h + 1) * 128],
                             sb["kp"][:, sub, h * 128:(h + 1) * 128],
                             sb["c_ident"][...])
        tr.then_inc(em.h["tp"], 1)
        env["p1_kpT_ps"] = em.inc("tp", 1)

    @em.on("scalar")
    def _(s):
        em.W(s, "tp", env["p1_kpT_ps"])
        s.copy(sb["kpT"][...], ps["t"][:, 0:HC]).then_inc(em.h["sp"], 1)
        env["p1_kpT_cp"] = em.inc("sp", 1)

    @em.on("tensor")
    def _(t):
        em.W(t, "sp", env["p1_kpT_cp"])
        if "p1_scev" in env:
            em.W(t, "vp", env["p1_scev"])
        for h in range(H_PER):
            bank = ps["d0"] if h < 4 else ps["d1"]
            mmi = t.matmul(bank[:, (h % 4) * 128:(h % 4 + 1) * 128],
                           sb["kpT"][:, h * 128:(h + 1) * 128],
                           sb["sqT"][:, h, sub * 128:(sub + 1) * 128],
                           start=True, stop=True)
        mmi.then_inc(em.h["tp"], 1)
        env["p1_sc_ps"] = em.inc("tp", 1)

    @em.on("vector")
    def _(v):
        em.W(v, "tp", env["p1_sc_ps"])
        v.tensor_tensor(sb["scT"][:, 0:512], ps["d0"][:, 0:512],
                        sb["c_maskT"][:, 0:512], ALU.mult)
        v.tensor_tensor(sb["scT"][:, 512:1024], ps["d1"][:, 0:512],
                        sb["c_maskT"][:, 512:1024], ALU.mult).then_inc(
            em.h["vp"], 1)
        env["p1_scev"] = em.inc("vp", 1)

    @em.on("tensor")
    def _(t):
        em.W(t, "vp", env["p1_scev"])
        key = f"p1_outev_{sc}_{sub - 1}" if sub > 0 else f"p1_outev_{sc - 1}_3"
        if key in env:
            em.W(t, "vp", env[key])
        for h in range(H_PER):
            bank = ps["q0"] if h < 4 else ps["q1"]
            t.matmul(bank[:, (h % 4) * 128:(h % 4 + 1) * 128],
                     sb["kp"][:, sub, h * 128:(h + 1) * 128],
                     sb["vc"][:, sub, h * 128:(h + 1) * 128],
                     start=True, stop=True)
        for h in range(H_PER):
            t.matmul(ps["s"][:, h:h + 1],
                     sb["sqT"][:, h, sub * 128:(sub + 1) * 128],
                     sb["Zb"][:, h:h + 1], start=True, stop=False)
            t.matmul(ps["s"][:, h:h + 1], sb["scT"][:, h * 128:(h + 1) * 128],
                     sb["c_onesb"][...], start=False, stop=True)
        for h in range(H_PER):
            t.matmul(ps["s"][:, 16 + h:17 + h],
                     sb["kp"][:, sub, h * 128:(h + 1) * 128],
                     sb["c_onesb"][...], start=True, stop=True)
        for h in range(H_PER):
            bank = ps["q2"] if h < 4 else ps["q3"]
            t.matmul(bank[:, (h % 4) * 128:(h % 4 + 1) * 128],
                     sb["scT"][:, h * 128:(h + 1) * 128],
                     sb["vc"][:, sub, h * 128:(h + 1) * 128],
                     start=True, stop=False)
            mmi = t.matmul(bank[:, (h % 4) * 128:(h % 4 + 1) * 128],
                           sb["sqT"][:, h, sub * 128:(sub + 1) * 128],
                           sb["Sb"][:, h * 128:(h + 1) * 128],
                           start=False, stop=True)
        mmi.then_inc(em.h["tp"], 1)
        env["p1_att_ps"] = em.inc("tp", 1)

    @em.on("vector")
    def _(v):
        em.W(v, "tp", env["p1_att_ps"])
        if sub == 0 and sc > 0:
            em.W(v, "scr", env[f"p1_scr_{sc - 1}"])
        v.tensor_tensor(sb["S"][:, 0:512], sb["S"][:, 0:512],
                        ps["q0"][:, 0:512], ALU.add)
        v.tensor_tensor(sb["S"][:, 512:1024], sb["S"][:, 512:1024],
                        ps["q1"][:, 0:512], ALU.add)
        v.tensor_tensor(sb["Z"][...], sb["Z"][...],
                        ps["s"][:, 16:16 + H_PER], ALU.add)
        v.tensor_scalar(sb["tmp"][:, 0:H_PER], ps["s"][:, 0:H_PER], EPS,
                        None, ALU.max).then_inc(em.h["vp"], 1)
        v.wait_ge(em.h["vp"], em.inc("vp", 1))
        v.reciprocal(sb["tmp"][:, 8:8 + H_PER],
                     sb["tmp"][:, 0:H_PER]).then_inc(em.h["vp"], 1)
        v.wait_ge(em.h["vp"], em.inc("vp", 1))
        for h in range(H_PER):
            bank = ps["q2"] if h < 4 else ps["q3"]
            vi = v.tensor_scalar(sb["partS"][:, sub, h * 128:(h + 1) * 128],
                                 bank[:, (h % 4) * 128:(h % 4 + 1) * 128],
                                 sb["tmp"][:, 8 + h:9 + h],
                                 sb["g1m"][:, h:h + 1], ALU.mult, ALU.mult)
        vi.then_inc(em.h["vp"], 1)
        env[f"p1_outev_{sc}_{sub}"] = em.inc("vp", 1)


def phase_rel(env):
    em, sb, ps, cfg = env["em"], env["sb"], env["ps"], env["cfg"]

    @em.on("vector")
    def _(v):
        v.tensor_copy(sb["relaccb"][...], sb["relacc"][...])
        v.reduce_sum(sb["tmp"][0:M * 16, 0:1], sb["nrmFs"][...],
                     axis=AX.X).then_inc(em.h["vp"], 1)
        env["p2_v0"] = em.inc("vp", 1)

    @em.on("tensor")
    def _(t):
        em.W(t, "vp", env["p2_v0"])
        for h in range(H_PER):
            t.matmul(ps["s"][0:M, h:h + 1], sb["nrmTb"][:, h, :],
                     sb["relaccb"][:, h:h + 1], start=True, stop=True)
        mmi = t.matmul(ps["s"][0:M, 8:9], sb["c_esel"][...],
                       sb["tmp"][0:M * 16, 0:1], start=True, stop=True)
        mmi.then_inc(em.h["tp"], 1)
        env["p2_t0"] = em.inc("tp", 1)

    @em.on("vector")
    def _(v):
        em.W(v, "tp", env["p2_t0"])
        v.tensor_copy(sb["rel"][0:M, 0:H_PER], ps["s"][0:M, 0:H_PER])
        v.tensor_scalar(sb["rel"][0:M, 8:9], ps["s"][0:M, 8:9], 1e-6, None,
                        ALU.is_lt).then_inc(em.h["vp"], 1)
        env["p2_v1"] = em.inc("vp", 1)

    @em.on("tensor")
    def _(t):
        em.W(t, "vp", env["p2_v1"])
        t.matmul(ps["s"][0:H_PER, 16:16 + M], sb["rel"][0:M, 0:H_PER],
                 sb["c_identf"][0:M, 0:M], start=True, stop=True)
        mmi = t.matmul(ps["s"][0:1, 24:24 + M], sb["rel"][0:M, 8:9],
                       sb["c_identf"][0:M, 0:M], start=True, stop=True)
        mmi.then_inc(em.h["tp"], 1)
        env["p2_t1"] = em.inc("tp", 1)

    @em.on("vector")
    def _(v):
        em.W(v, "tp", env["p2_t1"])
        v.tensor_scalar(sb["rel"][0:H_PER, 0:M], ps["s"][0:H_PER, 16:16 + M],
                        SCALE / env["cfg"].S, None, ALU.mult)
        v.tensor_copy(sb["rel"][0:1, 12:12 + M],
                      ps["s"][0:1, 24:24 + M]).then_inc(em.h["vp"], 1)
        env["p2_v2"] = em.inc("vp", 1)

    @em.on("tensor")
    def _(t):
        em.W(t, "vp", env["p2_v2"])
        mmi = t.matmul(ps["s"][0:H_PER, 28:28 + M],
                       sb["c_onesrow"][0:1, 0:H_PER],
                       sb["rel"][0:1, 12:12 + M], start=True, stop=True)
        mmi.then_inc(em.h["tp"], 1)
        env["p2_t2"] = em.inc("tp", 1)

    @em.on("vector")
    def _(v):
        em.W(v, "tp", env["p2_t2"])
        v.scalar_tensor_tensor(sb["rel"][0:H_PER, 0:M],
                               ps["s"][0:H_PER, 28:28 + M], NEG_BIG,
                               sb["rel"][0:H_PER, 0:M], ALU.mult,
                               ALU.add).then_inc(em.h["vp"], 1)
        ga = em.inc("vp", 1)
        v.tensor_scalar(sb["rel"][0:H_PER, 4:4 + M],
                        ps["s"][0:H_PER, 28:28 + M], -1.0, 1.0, ALU.mult,
                        ALU.add)
        v.wait_ge(em.h["vp"], ga)
        v.reduce_max(sb["rel"][0:H_PER, 8:9], sb["rel"][0:H_PER, 0:M],
                     axis=AX.X).then_inc(em.h["vp"], 1)
        v.wait_ge(em.h["vp"], em.inc("vp", 1))
        v.tensor_scalar(sb["rel"][0:H_PER, 0:M], sb["rel"][0:H_PER, 0:M],
                        sb["rel"][0:H_PER, 8:9], None,
                        ALU.subtract).then_inc(em.h["vp"], 1)
        env["p2_v3"] = em.inc("vp", 1)

    @em.on("scalar")
    def _(s):
        em.W(s, "vp", env["p2_v3"])
        s.activation(sb["rel"][0:H_PER, 16:16 + M], sb["rel"][0:H_PER, 0:M],
                     AF.Exp).then_inc(em.h["sp"], 1)
        env["p2_s0"] = em.inc("sp", 1)

    @em.on("vector")
    def _(v):
        em.W(v, "sp", env["p2_s0"])
        v.reduce_sum(sb["rel"][0:H_PER, 9:10], sb["rel"][0:H_PER, 16:16 + M],
                     axis=AX.X).then_inc(em.h["vp"], 1)
        v.wait_ge(em.h["vp"], em.inc("vp", 1))
        v.reciprocal(sb["rel"][0:H_PER, 10:11],
                     sb["rel"][0:H_PER, 9:10]).then_inc(em.h["vp"], 1)
        v.wait_ge(em.h["vp"], em.inc("vp", 1))
        v.tensor_scalar(sb["rel"][0:H_PER, 16:16 + M],
                        sb["rel"][0:H_PER, 16:16 + M],
                        sb["rel"][0:H_PER, 10:11], None,
                        ALU.mult).then_inc(em.h["vp"], 1)
        v.wait_ge(em.h["vp"], em.inc("vp", 1))
        v.tensor_tensor(sb["rel"][0:H_PER, 16:16 + M],
                        sb["rel"][0:H_PER, 16:16 + M],
                        sb["rel"][0:H_PER, 4:4 + M], ALU.mult)
        v.reduce_sum(sb["rel"][0:1, 13:14], sb["rel"][0:1, 12:12 + M],
                     axis=AX.X).then_inc(em.h["vp"], 1)
        v.wait_ge(em.h["vp"], em.inc("vp", 1))
        v.tensor_scalar(sb["rel"][0:1, 14:15], sb["rel"][0:1, 13:14],
                        M - 0.5, None, ALU.is_lt).then_inc(em.h["vp"], 1)
        env["p2_v4"] = em.inc("vp", 1)

    @em.on("tensor")
    def _(t):
        em.W(t, "vp", env["p2_v4"])
        t.matmul(ps["s"][0:H_PER, 32:33], sb["c_onesrow"][0:1, 0:H_PER],
                 sb["rel"][0:1, 14:15], start=True, stop=True)
        mmi = t.matmul(ps["s"][0:H_PER, 33:34], sb["grow"][...],
                       sb["c_identf"][0:1, 0:1], start=True, stop=True)
        mmi.then_inc(em.h["tp"], 1)
        env["p2_t3"] = em.inc("tp", 1)

    @em.on("vector")
    def _(v):
        em.W(v, "tp", env["p2_t3"])
        v.tensor_scalar(sb["rel"][0:H_PER, 16:16 + M],
                        sb["rel"][0:H_PER, 16:16 + M],
                        ps["s"][0:H_PER, 32:33], ps["s"][0:H_PER, 33:34],
                        ALU.mult, ALU.mult).then_inc(em.h["vp"], 1)
        env["p2_v6"] = em.inc("vp", 1)

    @em.on("sync")
    def _(sy):
        em.W(sy, "vp", env["p2_v6"])
        sy.dma_start(out=sb["tmp"][0:1, 0:H_PER * M],
                     in_=sb["rel"][0:H_PER, 16:16 + M]).then_inc(
            em.h["ld"], 16)
        em.inc("ld", 16)
        em.qw(sy, "ld")
    env["p2_fl"] = em.v["ld"]

    @em.on("tensor")
    def _(t):
        em.W(t, "ld", env["p2_fl"])
        mmi = t.matmul(ps["s"][:, 34:34 + H_PER * M], sb["c_onesrow"][...],
                       sb["tmp"][0:1, 0:H_PER * M], start=True, stop=True)
        mmi.then_inc(em.h["tp"], 1)
        env["p2_t5"] = em.inc("tp", 1)

    @em.on("vector")
    def _(v):
        em.W(v, "tp", env["p2_t5"])
        v.tensor_copy(sb["weff"][...],
                      ps["s"][:, 34:34 + H_PER * M]).then_inc(em.h["vp"], 1)
        env["p2_done"] = em.inc("vp", 1)


def phase_attn_pass2(env):
    em = env["em"]
    for sc in range(env["cfg"].NSC):
        _p3_super(env, sc)
    env["p3_tdone"] = em.v["tp"]

    @em.on("gpsimd")
    def _(gp):
        em.W(gp, "scr", env[f"p3_scr_{env['cfg'].NSC - 1}"])
        gp.collective_compute(
            "AllGather", ALU.bypass,
            replica_groups=[[0, 1], [2, 3], [4, 5], [6, 7]],
            ins=[env["agin"].ap().opt()],
            outs=[env["agout"].ap().opt()],
        ).then_inc(em.h["cc"], 1)
        em.inc("cc", 1)
    env["a2a_done"] = em.v["cc"]


def _p3_super(env, sc):
    em, sb, cfg = env["em"], env["sb"], env["cfg"]
    t0 = sc * 512

    @em.on("scalar")
    def _(s):
        em.W(s, "scr", env[f"p1_scr_{cfg.NSC - 1}"])
        em.W(s, "vp", env["p2_done"])
        if sc > 0:
            em.W(s, "vp", env[f"p3_combev_{sc - 1}_3"])
        for et in range(H_PER):
            s.dma_start(out=sb["sqT"][:, et, :],
                        in_=env["SQT"][et, :, t0:t0 + 512]).then_inc(
                em.h["scr"], 16)
            em.inc("scr", 16)
        s.dma_start(out=sb["partS"][...],
                    in_=env["PART"][t0:t0 + 512, :].rearrange(
                        "(s p) e -> p s e", p=128)).then_inc(em.h["scr"], 16)
        em.inc("scr", 16)
        em.qw(s, "scr")
    env[f"p3_ld_{sc}"] = em.v["scr"]

    for sub in range(4):
        _p3_chunk(env, sc, sub)
    env[f"p3_scr_{sc}"] = em.v["scr"]


def _p3_chunk(env, sc, sub):
    em, sb, ps, cfg = env["em"], env["sb"], env["ps"], env["cfg"]
    ch = sc * 4 + sub

    @em.on("tensor")
    def _(t):
        em.W(t, "scr", env[f"p3_ld_{sc}"])
        em.W(t, "wl", env["wl_mem"])
        if "p3_cmev" in env:
            em.W(t, "vp", env["p3_cmev"])
        for h in range(H_PER):
            mmi = t.matmul(ps["s"][:, h * M:(h + 1) * M],
                           sb["sqT"][:, h, sub * 128:(sub + 1) * 128],
                           sb["nrmTb"][:, h, :], start=True, stop=True)
        mmi.then_inc(em.h["tp"], 1)
        env["p3_den_ps"] = em.inc("tp", 1)

    @em.on("vector")
    def _(v):
        em.W(v, "tp", env["p3_den_ps"])
        v.tensor_scalar(sb["cm"][...], ps["s"][:, 0:M * H_PER], EPS, None,
                        ALU.max).then_inc(em.h["vp"], 1)
        v.wait_ge(em.h["vp"], em.inc("vp", 1))
        v.reciprocal(sb["cm"][...], sb["cm"][...]).then_inc(em.h["vp"], 1)
        v.wait_ge(em.h["vp"], em.inc("vp", 1))
        v.tensor_tensor(sb["cm"][...], sb["cm"][...], sb["weff"][...],
                        ALU.mult).then_inc(em.h["vp"], 1)
        env["p3_cmev"] = em.inc("vp", 1)

    for m in range(M):
        bA = ps["q0"] if m % 2 == 0 else ps["q2"]
        bB = ps["q1"] if m % 2 == 0 else ps["q3"]

        @em.on("tensor")
        def _(t, m=m, bA=bA, bB=bB):
            guard = f"p3_mev_{m - 2}" if m >= 2 else f"p3_mev_{m + 2}"
            if guard in env:
                em.W(t, "vp", env[guard])
            if m == 0:
                em.W(t, "vp", env["p3_cmev"])
            for h in range(H_PER):
                bank = bA if h < 4 else bB
                mmi = t.matmul(bank[:, (h % 4) * 128:(h % 4 + 1) * 128],
                               sb["sqT"][:, h, sub * 128:(sub + 1) * 128],
                               sb["memb"][:, m * H_PER + h, :],
                               start=True, stop=True)
            mmi.then_inc(em.h["tp"], 1)
            env[f"p3_mps_{m}"] = em.inc("tp", 1)

        @em.on("vector")
        def _(v, m=m, bA=bA, bB=bB):
            em.W(v, "tp", env[f"p3_mps_{m}"])
            if m == 0:
                em.W(v, "scr", env.get("p3_aginst", 0))
            else:
                em.W(v, "vp", env[f"p3_mev_{m - 1}"])
            for h in range(H_PER):
                bank = bA if h < 4 else bB
                src1 = (sb["partS"][:, sub, h * 128:(h + 1) * 128] if m == 0
                        else sb["comb"][:, h * 128:(h + 1) * 128])
                vi = v.scalar_tensor_tensor(
                    sb["comb"][:, h * 128:(h + 1) * 128],
                    bank[:, (h % 4) * 128:(h % 4 + 1) * 128],
                    sb["cm"][:, h * M + m:h * M + m + 1],
                    src1, ALU.mult, ALU.add)
            vi.then_inc(em.h["vp"], 1)
            env[f"p3_mev_{m}"] = em.inc("vp", 1)

    @em.on("scalar")
    def _(s):
        em.W(s, "vp", env[f"p3_mev_{M - 1}"])
        s.dma_start(out=env["agin"].ap()[ch * 128:(ch + 1) * 128, :],
                    in_=sb["comb"][...]).then_inc(em.h["scr"], 16)
        env["p3_aginst"] = em.inc("scr", 16)
        em.qw(s, "scr")

    @em.on("vector")
    def _(v):
        em.W(v, "scr", env["p3_aginst"])
        v.nop().then_inc(em.h["vp"], 1)
        env[f"p3_combev_{sc}_{sub}"] = em.inc("vp", 1)


def alloc_scopeC(env, st):
    sbuf = env["sbuf"]
    sbuf("woTs", [128, 16, D], BF16, st)
    sbuf("wolh", [128, 8, 256], BF16, st)
    sbuf("wolh2", [128, 8, 256], BF16, st)
    sbuf("gwo", [128, 16], I16, st)
    sbuf("res", [128, D], F32, st)
    sbuf("res2", [128, D], F32, st)
    sbuf("s1", [128, D], F32, st)
    sbuf("scrA", [128, D], F32, st)
    sbuf("xp", [128, D], F32, st)
    sbuf("xphb", [128, D], BF16, st)
    sbuf("xpT", [128, 16, 128], F32, st)


def ln_stats_apply(env, src, dst, tc, scr_name="scrA", pre=None,
                   src_ready=None):
    em, sb = env["em"], env["sb"]

    @em.on("vector")
    def _(v):
        if src_ready is not None:
            em.W(v, src_ready[0], src_ready[1])
        v.reduce_sum(sb["tmp"][:, tc:tc + 1], src, axis=AX.X).then_inc(
            em.h["vp"], 1)
        env["ln_v0"] = em.inc("vp", 1)

    @em.on("scalar")
    def _(s):
        em.W(s, "vp", env["ln_v0"])
        s.activation(sb[scr_name][...], src, AF.Square,
                     accum_out=sb["tmp"][:, tc + 1:tc + 2]).then_inc(
            em.h["sp"], 1)
        env["ln_s0"] = em.inc("sp", 1)

    @em.on("vector")
    def _(v):
        em.W(v, "sp", env["ln_s0"])
        v.tensor_scalar(sb["tmp"][:, tc + 2:tc + 3], sb["tmp"][:, tc:tc + 1],
                        1.0 / D, None, ALU.mult).then_inc(em.h["vp"], 1)
        g1 = em.inc("vp", 1)
        v.tensor_scalar(sb["tmp"][:, tc + 3:tc + 4],
                        sb["tmp"][:, tc + 1:tc + 2], 1.0 / D, None,
                        ALU.mult).then_inc(em.h["vp"], 1)
        g2 = em.inc("vp", 1)
        v.wait_ge(em.h["vp"], g1)
        v.wait_ge(em.h["vp"], g2)
        v.tensor_tensor(sb["tmp"][:, tc + 6:tc + 7],
                        sb["tmp"][:, tc + 2:tc + 3],
                        sb["tmp"][:, tc + 2:tc + 3],
                        ALU.mult).then_inc(em.h["vp"], 1)
        g3 = em.inc("vp", 1)
        v.wait_ge(em.h["vp"], g3)
        v.tensor_tensor(sb["tmp"][:, tc + 3:tc + 4],
                        sb["tmp"][:, tc + 3:tc + 4],
                        sb["tmp"][:, tc + 6:tc + 7],
                        ALU.subtract).then_inc(em.h["vp"], 1)
        g4 = em.inc("vp", 1)
        v.wait_ge(em.h["vp"], g4)
        v.tensor_scalar(sb["tmp"][:, tc + 3:tc + 4],
                        sb["tmp"][:, tc + 3:tc + 4], 0.0, LN_EPS,
                        ALU.max, ALU.add).then_inc(em.h["vp"], 1)
        env["ln_v1"] = em.inc("vp", 1)

    @em.on("scalar")
    def _(s):
        em.W(s, "vp", env["ln_v1"])
        s.activation(sb["tmp"][:, tc + 6:tc + 7], sb["tmp"][:, tc + 3:tc + 4],
                     AF.Sqrt).then_inc(em.h["sp"], 1)
        env["ln_s1"] = em.inc("sp", 1)

    @em.on("vector")
    def _(v):
        em.W(v, "sp", env["ln_s1"])
        v.reciprocal(sb["tmp"][:, tc + 4:tc + 5],
                     sb["tmp"][:, tc + 6:tc + 7]).then_inc(em.h["vp"], 1)
        v.wait_ge(em.h["vp"], em.inc("vp", 1))
        v.tensor_tensor(sb["tmp"][:, tc + 7:tc + 8],
                        sb["tmp"][:, tc + 4:tc + 5],
                        sb["tmp"][:, tc + 4:tc + 5], ALU.mult).then_inc(
            em.h["vp"], 1)
        v.wait_ge(em.h["vp"], em.inc("vp", 1))
        v.tensor_tensor(sb["tmp"][:, tc + 7:tc + 8],
                        sb["tmp"][:, tc + 7:tc + 8],
                        sb["tmp"][:, tc + 3:tc + 4], ALU.mult).then_inc(
            em.h["vp"], 1)
        v.wait_ge(em.h["vp"], em.inc("vp", 1))
        v.tensor_scalar(sb["tmp"][:, tc + 7:tc + 8],
                        sb["tmp"][:, tc + 7:tc + 8], -0.5, 1.5, ALU.mult,
                        ALU.add).then_inc(em.h["vp"], 1)
        v.wait_ge(em.h["vp"], em.inc("vp", 1))
        v.tensor_tensor(sb["tmp"][:, tc + 4:tc + 5],
                        sb["tmp"][:, tc + 4:tc + 5],
                        sb["tmp"][:, tc + 7:tc + 8], ALU.mult).then_inc(
            em.h["vp"], 1)
        v.wait_ge(em.h["vp"], em.inc("vp", 1))
        v.tensor_scalar(sb["tmp"][:, tc + 5:tc + 6],
                        sb["tmp"][:, tc + 2:tc + 3], -1.0, None,
                        ALU.mult).then_inc(em.h["vp"], 1)
        v.wait_ge(em.h["vp"], em.inc("vp", 1))
        if pre is not None:
            em.W(v, pre[0], pre[1])
        v.tensor_scalar(dst, src, sb["tmp"][:, tc + 5:tc + 6],
                        sb["tmp"][:, tc + 4:tc + 5], ALU.add,
                        ALU.mult).then_inc(em.h["vp"], 1)
        env["ln_done"] = em.inc("vp", 1)


def phase_wo_ln1_routing(env):
    em, sb, cfg = env["em"], env["sb"], env["cfg"]
    env["sbuf"]("gwoall", [128, cfg.OCH * 16], I16, env["_stC"])

    @em.on("sync")
    def _(sy):
        sy.dma_start(out=sb["gwoall"][...],
                     in_=env["gidxwo"]).then_inc(em.h["ld"], 16)
        em.inc("ld", 16)
        em.qw(sy, "ld")
    env["p5_gwold"] = em.v["ld"]

    @em.on("gpsimd")
    def _(gp):
        em.W(gp, "tp", env["p3_tdone"])
        gp.dma_start(out=sb["woTs"][...],
                     in_=env["woT"].rearrange("(k p) e -> p k e", p=128)
                     ).then_inc(em.h["wl"], 16)
        em.inc("wl", 16)
        em.W(gp, "cc", env["a2a_done"])
        em.W(gp, "ld", env["p5_gwold"])
        gp.nop().then_inc(em.h["mk"], 1)
        em.inc("mk", 1)
    env["p5_w"] = em.v["wl"]
    env["p5_ag"] = em.v["mk"]

    for oc in range(cfg.OCH):
        _p5_chunk(env, oc)
    env["p5_tdone"] = em.v["tp"]


def _p5_chunk(env, oc):
    em, sb, ps, cfg = env["em"], env["sb"], env["ps"], env["cfg"]
    r0 = oc * 128
    lh = sb["wolh"] if oc % 2 == 0 else sb["wolh2"]
    res = sb["res"] if oc % 2 == 0 else sb["res2"]

    @em.on("sync")
    def _(sy):
        if oc >= 2:
            em.W(sy, "vp", env[f"p5_s1_{oc - 2}"])
        sy.dma_start(out=res[...],
                     in_=env["xown"][r0:r0 + 128, :]).then_inc(em.h["ld"], 16)
        em.inc("ld", 16)
        em.qw(sy, "ld")
    ldv = em.v["ld"]

    gsem = "qa" if oc % 2 == 0 else "qb"

    @em.on("gpsimd")
    def _(gp):
        em.W(gp, "mk", env["p5_ag"])
        if oc >= 2:
            em.W(gp, "tp", env[f"p5_wops_{oc - 2}"])
        gp.dma_gather(lh[...], env["agout"].ap(),
                      sb["gwoall"][:, oc * 16:(oc + 1) * 16],
                      num_idxs=256, num_idxs_reg=env["regs"][256], elem_size=HC,
                      elem_step=HC, transpose=True).then_inc(em.h[gsem], 16)
        em.inc(gsem, 16)
    ghv = (gsem, em.v[gsem])

    @em.on("tensor")
    def _(t):
        em.W(t, "ld", ldv)
        em.W(t, ghv[0], ghv[1])
        em.W(t, "wl", env["p5_w"])
        if oc > 0:
            em.W(t, "vp", env[f"p5_s1_{oc - 1}"])
        for n in range(4):
            for k in range(16):
                lhT = (lh[:, k, 0:128] if k < 8 else lh[:, k - 8, 128:256])
                mmi = t.matmul(ps[f"q{n}"][:, 0:512], lhT,
                               sb["woTs"][:, k, n * 512:(n + 1) * 512],
                               start=(k == 0), stop=(k == 15))
        mmi.then_inc(em.h["tp"], 1)
        env[f"p5_wops_{oc}"] = em.inc("tp", 1)

    @em.on("vector")
    def _(v):
        em.W(v, "tp", env[f"p5_wops_{oc}"])
        em.W(v, "sp", env.get("p5_sq_prev", 0))
        for n in range(4):
            vi = v.tensor_tensor(sb["s1"][:, n * 512:(n + 1) * 512],
                                 ps[f"q{n}"][:, 0:512],
                                 res[:, n * 512:(n + 1) * 512], ALU.add)
        vi.then_inc(em.h["vp"], 1)
        env[f"p5_s1_{oc}"] = em.inc("vp", 1)

    ln_stats_apply(env, sb["s1"][...], sb["xp"][...], 0,
                   pre=("scr", env.get(f"p5_xpst_{oc - 1}", 0)),
                   src_ready=("vp", env[f"p5_s1_{oc}"]))
    env["p5_sq_prev"] = env["ln_s0"]
    xp_done = env["ln_done"]

    @em.on("vector")
    def _(v):
        em.W(v, "scr", env.get(f"p5_xpst_{oc - 1}", 0))
        v.wait_ge(em.h["vp"], xp_done)
        v.tensor_copy(sb["xphb"][...], sb["xp"][...]).then_inc(em.h["vp"], 1)
        env["p5_xph"] = em.inc("vp", 1)

    @em.on("scalar")
    def _(s):
        em.W(s, "vp", env["p5_xph"])
        s.dma_start(out=env["XP"][r0:r0 + 128, :],
                    in_=sb["xp"][...]).then_inc(em.h["scr"], 16)
        em.inc("scr", 16)
        s.dma_start(out=env["XPH"][r0:r0 + 128, :],
                    in_=sb["xphb"][...]).then_inc(em.h["scr"], 16)
        em.inc("scr", 16)
        em.qw(s, "scr")
    env[f"p5_xpst_{oc}"] = em.v["scr"]

    for k in range(16):
        bank = ps["d0"] if k % 2 == 0 else ps["d1"]

        @em.on("tensor")
        def _(t, k=k, bank=bank):
            if k == 0:
                em.W(t, "vp", xp_done)
            if k >= 2:
                em.W(t, "sp", env[f"p5_xpTc_{k - 2}"])
            mmi = t.matmul(bank[:, 0:128], sb["xp"][:, k * 128:(k + 1) * 128],
                           sb["c_identf"][...], start=True, stop=True)
            mmi.then_inc(em.h["tp"], 1)
            env[f"p5_xpT_{k}"] = em.inc("tp", 1)

        @em.on("scalar")
        def _(s, k=k, bank=bank):
            em.W(s, "tp", env[f"p5_xpT_{k}"])
            s.copy(sb["xpT"][:, k, :], bank[:, 0:128]).then_inc(em.h["sp"], 1)
            env[f"p5_xpTc_{k}"] = em.inc("sp", 1)

    @em.on("tensor")
    def _(t):
        em.W(t, "sp", env["p5_xpTc_15"])
        if "p5_rtdone" in env:
            em.W(t, "vp", env["p5_rtdone"])
        for k in range(16):
            mmi = t.matmul(ps["s"][:, 0:E], sb["xpT"][:, k, :],
                           sb["routerS"][:, k, :], start=(k == 0),
                           stop=(k == 15))
        mmi.then_inc(em.h["tp"], 1)
        env["p5_lg"] = em.inc("tp", 1)

    _p5_routing(env, oc)


def _p5_routing(env, oc):
    em, sb, ps = env["em"], env["sb"], env["ps"]

    @em.on("vector")
    def _(v):
        def st(ins):
            ins.then_inc(em.h["vp"], 1)
            v.wait_ge(em.h["vp"], em.inc("vp", 1))

        em.W(v, "tp", env["p5_lg"])
        tm = sb["tmp"]
        rst = sb["rstash"]
        lg = ps["s"][:, 0:E]
        st(v.reduce_max(tm[:, 16:17], lg, axis=AX.X))
        st(v.tensor_scalar(tm[:, 17:17 + E], lg, tm[:, 16:17], None,
                           ALU.is_ge))
        st(v.tensor_scalar(tm[:, 25:25 + E], tm[:, 17:17 + E], -1e4, 1e4,
                           ALU.mult, ALU.add))
        st(v.tensor_tensor(tm[:, 25:25 + E], tm[:, 25:25 + E],
                           sb["c_iotae"][...], ALU.add))
        st(v.tensor_reduce(rst[:, oc, 0:1], tm[:, 25:25 + E], AX.X, ALU.min))
        st(v.tensor_scalar(tm[:, 17:17 + E], sb["c_iotae"][...],
                           rst[:, oc, 0:1], None, ALU.is_equal))
        st(v.scalar_tensor_tensor(tm[:, 33:33 + E], tm[:, 17:17 + E],
                                  NEG_BIG, lg, ALU.mult, ALU.add))
        st(v.reduce_max(tm[:, 41:42], tm[:, 33:33 + E], axis=AX.X))
        st(v.tensor_scalar(tm[:, 42:42 + E], tm[:, 33:33 + E], tm[:, 41:42],
                           None, ALU.is_ge))
        st(v.tensor_scalar(tm[:, 50:50 + E], tm[:, 42:42 + E], -1e4, 1e4,
                           ALU.mult, ALU.add))
        st(v.tensor_tensor(tm[:, 50:50 + E], tm[:, 50:50 + E],
                           sb["c_iotae"][...], ALU.add))
        st(v.tensor_reduce(rst[:, oc, 1:2], tm[:, 50:50 + E], AX.X, ALU.min))
        st(v.tensor_scalar(tm[:, 42:42 + E], sb["c_iotae"][...],
                           rst[:, oc, 1:2], None, ALU.is_equal))
        st(v.tensor_tensor(tm[:, 58:59], tm[:, 41:42], tm[:, 16:17],
                           ALU.subtract))
        env["p5_rt_d"] = em.v["vp"]

    @em.on("scalar")
    def _(s):
        em.W(s, "vp", env["p5_rt_d"])
        s.activation(sb["tmp"][:, 59:60], sb["tmp"][:, 58:59],
                     AF.Exp).then_inc(em.h["sp"], 1)
        env["p5_rt_e"] = em.inc("sp", 1)

    @em.on("vector")
    def _(v):
        def st(ins):
            ins.then_inc(em.h["vp"], 1)
            v.wait_ge(em.h["vp"], em.inc("vp", 1))

        em.W(v, "sp", env["p5_rt_e"])
        tm = sb["tmp"]
        rst = sb["rstash"]
        st(v.tensor_scalar(tm[:, 60:61], tm[:, 59:60], 1.0, None, ALU.add))
        st(v.reciprocal(rst[:, oc, 2:3], tm[:, 60:61]))
        st(v.tensor_scalar(rst[:, oc, 3:4], rst[:, oc, 2:3], -1.0, 1.0,
                           ALU.mult, ALU.add))
        st(v.tensor_tensor(tm[:, 25:25 + E], tm[:, 17:17 + E],
                           tm[:, 42:42 + E], ALU.add))
        st(v.tensor_copy(sb["tmpb"][:, 0:E], tm[:, 25:25 + E]))
        env["p5_oh"] = em.v["vp"]

    @em.on("tensor")
    def _(t):
        em.W(t, "vp", env["p5_oh"])
        t.matmul(ps["s"][:, 8:8 + E], sb["c_triu"][...], sb["tmpb"][:, 0:E],
                 start=True, stop=True)
        mmi = t.matmul(ps["s"][0:1, 24:24 + E], sb["c_onesb"][...],
                       sb["tmpb"][:, 0:E], start=True, stop=True)
        mmi.then_inc(em.h["tp"], 1)
        env["p5_Sx"] = em.inc("tp", 1)

    @em.on("vector")
    def _(v):
        def st(ins):
            ins.then_inc(em.h["vp"], 1)
            v.wait_ge(em.h["vp"], em.inc("vp", 1))

        em.W(v, "tp", env["p5_Sx"])
        tm = sb["tmp"]
        rst = sb["rstash"]
        st(v.tensor_tensor(tm[:, 33:33 + E], ps["s"][:, 8:8 + E],
                           tm[:, 17:17 + E], ALU.mult))
        st(v.reduce_sum(rst[:, oc, 4:5], tm[:, 33:33 + E], axis=AX.X))
        st(v.tensor_tensor(tm[:, 50:50 + E], ps["s"][:, 8:8 + E],
                           tm[:, 42:42 + E], ALU.mult))
        st(v.reduce_sum(rst[:, oc, 5:6], tm[:, 50:50 + E], axis=AX.X))
        st(v.tensor_copy(sb["ctotrow"][0:1, oc * E:(oc + 1) * E],
                         ps["s"][0:1, 24:24 + E]))
        env["p5_rtdone"] = em.v["vp"]
    env[f"p5_rt_{oc}"] = env["p5_rtdone"]


def alloc_scopeD(env, st):
    sbuf, cfg = env["sbuf"], env["cfg"]
    sbuf("tb", [128, E * cfg.CC, 64], F32, st)
    sbuf("gidxall", [128, E, cfg.CAP // 16], I16, st)
    sbuf("wupg", [128, 16, 256], BF16, st)
    sbuf("wupg2", [128, 16, 256], BF16, st)
    sbuf("wupg3", [128, 16, 256], BF16, st)
    sbuf("wdg", [128, 512], BF16, st)
    sbuf("wdg2", [128, 512], BF16, st)
    sbuf("wdg3", [128, 512], BF16, st)
    sbuf("wdg4", [128, 512], BF16, st)
    sbuf("fS", [128, 1024], F32, st)
    sbuf("fS2", [128, 2048], F32, st)
    sbuf("xprow", [128, 512], F32, st)
    sbuf("raccrow", [128, 512], F32, st)
    sbuf("orow", [128, D], F32, st)


def alloc_scopeD1(env, st):
    sbuf, cfg = env["sbuf"], env["cfg"]
    CAP = cfg.CAP
    sbuf("xeT", [128, 16, CAP], BF16, st)
    sbuf("xeT2", [128, 16, CAP], BF16, st)
    sbuf("hT", [128, 11, CAP], BF16, st)
    sbuf("wdq", [128, 11, 512], BF16, st)
    sbuf("wdq2", [128, 11, 512], BF16, st)
    sbuf("stg", [128, cfg.CC, D], F32, st)


def alloc_scopeD2(env, st):
    sbuf, cfg = env["sbuf"], env["cfg"]
    sbuf("scrD", [128, D], F32, st)
    sbuf("shxT", [128, 16, cfg.SHG], BF16, st)
    sbuf("shT", [128, 44, cfg.SHG], BF16, st)
    sbuf("sacc", [128, 4, D], F32, st)


def phase_routing_finalize(env):
    em, sb, ps, cfg = env["em"], env["sb"], env["ps"], env["cfg"]
    OCH = cfg.OCH

    @em.on("sync")
    def _(sy):
        sy.dma_start(out=sb["ctot"][...],
                     in_=sb["ctotrow"][0:1, :]).then_inc(em.h["ld"], 16)
        em.inc("ld", 16)
        em.qw(sy, "ld")
    env["p6_ctl"] = em.v["ld"]

    @em.on("vector")
    def _(v):
        em.W(v, "ld", env["p6_ctl"])
        v.tensor_copy(sb["ctotb"][...], sb["ctot"][...]).then_inc(
            em.h["vp"], 1)
        env["p6_ctb"] = em.inc("vp", 1)

    @em.on("tensor")
    def _(t):
        em.W(t, "vp", env["p6_ctb"])
        t.matmul(ps["s"][0:1, 0:E], sb["c_onesb"][0:OCH, :],
                 sb["ctotb"][...], start=True, stop=True)
        mmi = t.matmul(ps["s"][0:OCH, 8:8 + E], sb["c_triu"][0:OCH, 0:OCH],
                       sb["ctotb"][...], start=True, stop=True)
        mmi.then_inc(em.h["tp"], 1)
        env["p6_t0"] = em.inc("tp", 1)

    @em.on("vector")
    def _(v):
        em.W(v, "tp", env["p6_t0"])
        v.tensor_copy(sb["tmp"][0:1, 0:E], ps["s"][0:1, 0:E])
        v.tensor_copy(sb["ctot"][...], ps["s"][0:OCH, 8:8 + E]).then_inc(
            em.h["vp"], 1)
        env["p6_v0"] = em.inc("vp", 1)

    @em.on("sync")
    def _(sy):
        em.W(sy, "vp", env["p6_v0"])
        sy.dma_start(out=env["cnt_in"].ap(),
                     in_=sb["tmp"][0:1, 0:E]).then_inc(em.h["ld"], 16)
        em.inc("ld", 16)
        sy.dma_start(out=sb["cprerow"][0:1, 0:OCH * E],
                     in_=sb["ctot"][...]).then_inc(em.h["ld"], 16)
        em.inc("ld", 16)
        em.qw(sy, "ld")
    env["p6_cst"] = em.v["ld"]
    env["p6_fl"] = em.v["ld"]

    @em.on("gpsimd")
    def _(gp):
        em.W(gp, "ld", env["p6_cst"])
        gp.collective_compute(
            "AllGather", ALU.bypass,
            replica_groups=[[0, 1, 2, 3, 4, 5, 6, 7]],
            ins=[env["cnt_in"].ap().opt()],
            outs=[env["cnt_out"].ap().opt()],
        ).then_inc(em.h["cc"], 1)
        em.inc("cc", 1)
    env["p6_ag"] = em.v["cc"]

    @em.on("sync")
    def _(sy):
        em.W(sy, "cc", env["p6_ag"])
        sy.dma_start(out=sb["cnt"][...],
                     in_=env["cnt_out"].ap()).then_inc(em.h["ld"], 16)
        em.inc("ld", 16)
        em.qw(sy, "ld")
    env["p6_cl"] = em.v["ld"]

    @em.on("tensor")
    def _(t):
        em.W(t, "ld", env["p6_cl"])
        mmi = t.matmul(ps["s"][0:1, 16:16 + E], sb["priorS"][...],
                       sb["cnt"][...], start=True, stop=True)
        mmi.then_inc(em.h["tp"], 1)
        env["p6_t1"] = em.inc("tp", 1)

    @em.on("vector")
    def _(v):
        em.W(v, "tp", env["p6_t1"])
        v.tensor_copy(sb["priorrow"][...],
                      ps["s"][0:1, 16:16 + E]).then_inc(em.h["vp"], 1)
        env["p6_v1"] = em.inc("vp", 1)

    @em.on("tensor")
    def _(t):
        em.W(t, "vp", env["p6_v1"])
        em.W(t, "ld", env["p6_fl"])
        t.matmul(ps["s"][:, 24:24 + E], sb["c_onesrow"][...],
                 sb["priorrow"][...], start=True, stop=True)
        nrep = (OCH * E + 63) // 64
        for i in range(nrep):
            w = min(64, OCH * E - i * 64)
            mmi = t.matmul(ps["d0"][:, i * 64:i * 64 + w],
                           sb["c_onesrow"][...],
                           sb["cprerow"][0:1, i * 64:i * 64 + w],
                           start=True, stop=True)
        mmi.then_inc(em.h["tp"], 1)
        env["p6_t2"] = em.inc("tp", 1)

    @em.on("vector")
    def _(v):
        em.W(v, "tp", env["p6_t2"])
        v.tensor_copy(sb["priorrep"][...], ps["s"][:, 24:24 + E])
        v.tensor_copy(sb["fS"][:, 0:OCH * E],
                      ps["d0"][:, 0:OCH * E]).then_inc(em.h["vp"], 1)
        env["p6_v3"] = em.inc("vp", 1)

    for oc in range(OCH):
        for k in range(2):
            @em.on("vector")
            def _(v, oc=oc, k=k):
                def st(ins):
                    ins.then_inc(em.h["vp"], 1)
                    v.wait_ge(em.h["vp"], em.inc("vp", 1))

                if oc == 0 and k == 0:
                    em.W(v, "vp", env["p6_v3"])
                tm = sb["tmp"]
                rst = sb["rstash"]
                j = k * OCH + oc
                st(v.tensor_scalar(tm[:, 0:E], sb["c_iotae"][...],
                                   rst[:, oc, k:k + 1], None, ALU.is_equal))
                st(v.tensor_tensor(tm[:, 8:8 + E], tm[:, 0:E],
                                   sb["priorrep"][...], ALU.mult))
                st(v.reduce_sum(tm[:, 16:17], tm[:, 8:8 + E], axis=AX.X))
                st(v.tensor_tensor(tm[:, 24:24 + E], tm[:, 0:E],
                                   sb["fS"][:, oc * E:(oc + 1) * E],
                                   ALU.mult))
                st(v.reduce_sum(tm[:, 32:33], tm[:, 24:24 + E], axis=AX.X))
                st(v.tensor_tensor(tm[:, 33:34], rst[:, oc, 4 + k:5 + k],
                                   tm[:, 32:33], ALU.add))
                st(v.tensor_tensor(tm[:, 34:35], tm[:, 33:34], tm[:, 16:17],
                                   ALU.add))
                st(v.tensor_scalar(tm[:, 35:36], tm[:, 34:35],
                                   float(cfg.CREF), None, ALU.is_lt))
                st(v.tensor_tensor(sb["pay"][:, j, 1:2],
                                   rst[:, oc, 2 + k:3 + k], tm[:, 35:36],
                                   ALU.mult))
                st(v.tensor_copy(sb["pay"][:, j, 0:1],
                                 sb["c_iotat"][:, oc:oc + 1]))
                st(v.tensor_scalar(tm[:, 36:37], tm[:, 33:34],
                                   float(cfg.CAP), None, ALU.is_ge))
                st(v.tensor_scalar(tm[:, 37:38], rst[:, oc, k:k + 1],
                                   float(cfg.CAP), None, ALU.mult))
                st(v.tensor_tensor(tm[:, 37:38], tm[:, 37:38], tm[:, 33:34],
                                   ALU.add))
                st(v.tensor_scalar(tm[:, 38:39], sb["c_iotas"][:, j:j + 1],
                                   1.0, float(E * cfg.CAP), ALU.mult,
                                   ALU.add))
                st(v.tensor_tensor(tm[:, 39:40], tm[:, 38:39], tm[:, 37:38],
                                   ALU.subtract))
                st(v.tensor_tensor(tm[:, 39:40], tm[:, 39:40], tm[:, 36:37],
                                   ALU.mult))
                st(v.tensor_tensor(sb["gdst"][:, j:j + 1], tm[:, 37:38],
                                   tm[:, 39:40], ALU.add))
    env["p6_slots"] = em.v["vp"]

    @em.on("tensor")
    def _(t):
        em.W(t, "vp", env["p6_slots"])
        for a in range(8):
            mmi = t.matmul(ps["s"][0:16, a * 2 * OCH:(a + 1) * 2 * OCH],
                           sb["c_pa"][:, a, :], sb["gdst"][...],
                           start=True, stop=True)
        mmi.then_inc(em.h["tp"], 1)
        env["p6_fold"] = em.inc("tp", 1)

    @em.on("vector")
    def _(v):
        em.W(v, "tp", env["p6_fold"])
        for a in range(8):
            vi = v.tensor_copy(
                sb["tidx"][0:16, :].rearrange("b (g a) -> b g a", a=8)[:, :, a],
                ps["s"][0:16, a * 2 * OCH:(a + 1) * 2 * OCH])
        vi.then_inc(em.h["vp"], 1)
        env["p6_tidx"] = em.inc("vp", 1)

    @em.on("sync")
    def _(sy):
        em.W(sy, "vp", env["p6_tidx"])
        sy.dma_start(out=sb["tidx"][16:32, :],
                     in_=sb["tidx"][0:16, :]).then_inc(em.h["ld"], 16)
        em.inc("ld", 16)
        em.qw(sy, "ld")
    env["p6_trep"] = em.v["ld"]

    @em.on("gpsimd")
    def _(gp):
        em.W(gp, "ld", env["p6_trep"])
        em.W(gp, "gs", env["zeros_done"])
        gp.dma_scatter_add(
            env["TBL"][:, 0:2], sb["pay"][...], sb["tidx"][...],
            num_idxs=cfg.NSLOT, num_idxs_reg=env["regs"][cfg.NSLOT],
            elem_size=2, elem_step=cfg.TROW).then_inc(em.h["gq"], 16)
        em.inc("gq", 16)
    env["p6_scat"] = em.v["gq"]

    @em.on("sync")
    def _(sy):
        em.W(sy, "gq", env["p6_scat"])
        sy.dma_start(out=sb["tb"][...],
                     in_=env["TBL"][0:E * cfg.CAP, :].rearrange(
                         "(g p) c -> p g c", p=128)).then_inc(em.h["ld"], 16)
        em.inc("ld", 16)
        em.qw(sy, "ld")
    env["p6_tb"] = em.v["ld"]

    @em.on("vector")
    def _(v):
        def st(ins):
            ins.then_inc(em.h["vp"], 1)
            v.wait_ge(em.h["vp"], em.inc("vp", 1))
        em.W(v, "ld", env["p6_tb"])
        nsl = E * cfg.CC
        idv = sb["tb"][:, :, 0:1].rearrange("p g c -> p (g c)")
        cwv = sb["tb"][:, :, 1:2].rearrange("p g c -> p (g c)")
        st(v.tensor_scalar(sb["xstage"][:, 0:nsl], idv, 0.0, None,
                           ALU.is_equal))
        st(v.tensor_scalar(sb["xstage"][:, 64:64 + nsl], cwv, 0.0, None,
                           ALU.is_equal))
        st(v.tensor_tensor(sb["xstage"][:, 0:nsl], sb["xstage"][:, 0:nsl],
                           sb["xstage"][:, 64:64 + nsl], ALU.mult))
        st(v.tensor_tensor(sb["tmp"][:, 0:nsl], sb["xstage"][:, 0:nsl],
                           sb["c_dmp"][...], ALU.mult))
        st(v.tensor_tensor(idv, idv, sb["tmp"][:, 0:nsl], ALU.add))
    env["p6_patch"] = em.v["vp"]

    CC = cfg.CC

    @em.on("vector")
    def _(v):
        v.memset(sb["gidxall"][...], 0).then_inc(em.h["vp"], 1)
        env["p6_giz"] = em.inc("vp", 1)

    for eh in range(0, E, 2):
        @em.on("tensor")
        def _(t, eh=eh):
            em.W(t, "ld", env["p6_tb"])
            em.W(t, "vp", env["p6_patch"])
            if eh >= 2:
                em.W(t, "vp", env[f"p6_gcp_{eh - 2}"])
            for e2 in (eh, eh + 1):
                for a in range(8):
                    mmi = t.matmul(
                        ps["s"][0:16, (e2 % 2) * 256 + a * CC:
                                (e2 % 2) * 256 + (a + 1) * CC],
                        sb["c_pa"][:, a, :],
                        sb["tb"][:, e2 * CC:(e2 + 1) * CC, 0:1].rearrange(
                            "p g c -> p (g c)"),
                        start=True, stop=True)
            mmi.then_inc(em.h["tp"], 1)
            env[f"p6_gf_{eh}"] = em.inc("tp", 1)

        @em.on("vector")
        def _(v, eh=eh):
            em.W(v, "tp", env[f"p6_gf_{eh}"])
            if eh == 0:
                em.W(v, "vp", env["p6_giz"])
            for e2 in (eh, eh + 1):
                for a in range(8):
                    vi = v.tensor_copy(
                        sb["gidxall"][0:16, e2, :].rearrange(
                            "b (g a) -> b g a", a=8)[:, :, a],
                        ps["s"][0:16, (e2 % 2) * 256 + a * CC:
                                (e2 % 2) * 256 + (a + 1) * CC])
            vi.then_inc(em.h["vp"], 1)
            env[f"p6_gcp_{eh}"] = em.inc("vp", 1)

    @em.on("sync")
    def _(sy):
        em.W(sy, "vp", env[f"p6_gcp_{E - 2}"])
        sy.dma_start(out=sb["gidxall"][16:32, :, :],
                     in_=sb["gidxall"][0:16, :, :]).then_inc(em.h["ld"], 16)
        em.inc("ld", 16)
        em.qw(sy, "ld")
    env["p6_gall"] = em.v["ld"]


def phase_experts(env):
    for e in range(E):
        _p7_expert(env, e)
    _p7_scatters(env, E - 1)


def _p7_expert(env, e):
    em, sb, ps, cfg = env["em"], env["sb"], env["ps"], env["cfg"]
    CAP, CC = cfg.CAP, cfg.CC
    xeT = sb["xeT"] if e % 2 == 0 else sb["xeT2"]
    gidx = sb["gidxall"][:, e, :]
    gsem = "qa" if e % 2 == 0 else "qb"

    @em.on("gpsimd")
    def _(gp):
        em.W(gp, "ld", env["p6_gall"])
        em.W(gp, "scr", env[f"p5_xpst_{cfg.OCH - 1}"])
        if e >= 2:
            em.W(gp, "tp", env[f"p7_updone_{e - 2}"])
        gp.dma_gather(xeT[...], env["XPH"], gidx,
                      num_idxs=CAP, num_idxs_reg=env["regs"][CAP], elem_size=D,
                      elem_step=D, transpose=True).then_inc(em.h[gsem], 16)
        env[f"p7_gath_{e}"] = (gsem, em.inc(gsem, 16))

    NP = 11
    for pi in range(NP):
        if pi == 3 and e >= 1:
            _p7_scatters(env, e - 1)
        g = e * NP + pi
        wub = [sb["wupg"], sb["wupg2"], sb["wupg3"]][g % 3]
        wsem = ["wla", "wlb", "wl"][g % 3]

        @em.on("gpsimd")
        def _(gp, pi=pi, wub=wub, wsem=wsem, g=g):
            if g >= 3:
                ep, pp = divmod(g - 3, NP)
                em.W(gp, "tp", env[f"p7_up_{ep}_{pp}"])
            gp.dma_start(out=wub[:, :, 0:128],
                         in_=env["wug"][e, :, pi * 128:(pi + 1) * 128]
                         .rearrange("(k p) m -> p k m", p=128)).then_inc(
                em.h[wsem], 16)
            em.inc(wsem, 16)
            gp.dma_start(out=wub[:, :, 128:256],
                         in_=env["wug"][e, :,
                                        (NP + pi) * 128:(NP + pi + 1) * 128]
                         .rearrange("(k p) m -> p k m", p=128)).then_inc(
                em.h[wsem], 16)
            em.inc(wsem, 16)
        wld = (wsem, em.v[wsem])

        bset = (["q0", "q1", "q2", "q3"] if pi % 2 == 0
                else ["d0", "d1", "s", "e"])

        @em.on("tensor")
        def _(t, pi=pi, wub=wub, bset=bset):
            em.W(t, wld[0], wld[1])
            if pi == 0:
                em.W(t, env[f"p7_gath_{e}"][0], env[f"p7_gath_{e}"][1])
            if pi < 2:
                if e >= 1:
                    em.W(t, "vp", env[f"p7_dnev_{e - 1}_3_{CC - 1}"])
            if pi >= 2:
                em.W(t, "vp", env[f"p7_upev_{e}_{pi - 2}"])
            slc = ([(0, 512), (512, CAP)] if CAP > 512 else [(0, CAP)])
            for si, (a, b) in enumerate(slc):
                for k in range(16):
                    t.matmul(ps[bset[si]][:, 0:b - a], wub[:, k, 0:128],
                             xeT[:, k, a:b], start=(k == 0), stop=(k == 15))
            for si, (a, b) in enumerate(slc):
                for k in range(16):
                    mmi = t.matmul(ps[bset[2 + si]][:, 0:b - a],
                                   wub[:, k, 128:256], xeT[:, k, a:b],
                                   start=(k == 0), stop=(k == 15))
            mmi.then_inc(em.h["tp"], 1)
            env[f"p7_up_{e}_{pi}"] = em.inc("tp", 1)

        @em.on("scalar")
        def _(s, pi=pi, bset=bset):
            em.W(s, "tp", env[f"p7_up_{e}_{pi}"])
            fofs = 0 if pi % 2 == 0 else 1024
            slc = ([(0, 512), (512, CAP)] if CAP > 512 else [(0, CAP)])
            for si, (a, b) in enumerate(slc):
                si_i = s.activation(sb["fS2"][:, fofs + a:fofs + b],
                                    ps[bset[si]][:, 0:b - a], AF.Silu)
            si_i.then_inc(em.h["sp"], 1)
            env[f"p7_silu_{e}_{pi}"] = em.inc("sp", 1)

        @em.on("vector")
        def _(v, pi=pi, bset=bset):
            em.W(v, "sp", env[f"p7_silu_{e}_{pi}"])
            fofs = 0 if pi % 2 == 0 else 1024
            slc = ([(0, 512), (512, CAP)] if CAP > 512 else [(0, CAP)])
            for si, (a, b) in enumerate(slc):
                vi = v.tensor_tensor(sb["hT"][:, pi, a:b],
                                     sb["fS2"][:, fofs + a:fofs + b],
                                     ps[bset[2 + si]][:, 0:b - a], ALU.mult)
            vi.then_inc(em.h["vp"], 1)
            env[f"p7_upev_{e}_{pi}"] = em.inc("vp", 1)
    env[f"p7_updone_{e}"] = em.v["tp"]

    dbanks = ["q0", "q1", "q2", "q3", "d0"]
    for n in range(4):
        wdb = sb["wdq"] if n % 2 == 0 else sb["wdq2"]
        dsem = "scb" if n % 2 == 0 else "gl"

        @em.on("gpsimd")
        def _(gp, n=n, wdb=wdb, dsem=dsem):
            guard = env.get(f"p7_dnmm_{e}_{n - 2}",
                            env.get(f"p7_dndone_{e - 1}", 0))
            em.W(gp, "tp", guard)
            gp.dma_start(out=wdb[...],
                         in_=env["wdn"][e, :, n * 512:(n + 1) * 512]
                         .rearrange("(j p) d -> p j d", p=128)).then_inc(
                em.h[dsem], 16)
            em.inc(dsem, 16)
        dld = (dsem, em.v[dsem])

        @em.on("tensor")
        def _(t, n=n, wdb=wdb, dld=dld):
            em.W(t, dld[0], dld[1])
            if n == 0:
                em.W(t, "vp", env[f"p7_upev_{e}_{10}"])
            for cc in range(CC):
                if n > 0:
                    em.W(t, "vp", env[f"p7_dnev_{e}_{n - 1}_{cc}"])
                for j in range(11):
                    mmi = t.matmul(ps[dbanks[cc]][:, 0:512],
                                   sb["hT"][:, j, cc * 128:(cc + 1) * 128],
                                   wdb[:, j, :],
                                   start=(j == 0), stop=(j == 10))
                mmi.then_inc(em.h["tp"], 1)
                env[f"p7_dnmmc_{e}_{n}_{cc}"] = em.inc("tp", 1)
            env[f"p7_dnmm_{e}_{n}"] = em.v["tp"]

        @em.on("vector")
        def _(v, n=n):
            if n == 0:
                prev = env.get(f"p7_scat_{e - 1}_{CC - 1}")
                if prev is not None:
                    em.W(v, "sca", prev)
            for cc in range(CC):
                em.W(v, "tp", env[f"p7_dnmmc_{e}_{n}_{cc}"])
                vi = v.tensor_scalar(sb["stg"][:, cc, n * 512:(n + 1) * 512],
                                     ps[dbanks[cc]][:, 0:512],
                                     sb["tb"][:, e * CC + cc, 1:2], None,
                                     ALU.mult)
                vi.then_inc(em.h["vp"], 1)
                env[f"p7_dnev_{e}_{n}_{cc}"] = em.inc("vp", 1)

    env[f"p7_dndone_{e}"] = em.v["tp"]


def _p7_scatters(env, e):
    em, sb, cfg = env["em"], env["sb"], env["cfg"]
    CC = cfg.CC
    gidx = sb["gidxall"][:, e, :]

    @em.on("gpsimd")
    def _(gp):
        em.W(gp, "sca", env.get(f"p7_scat_{e - 1}_{CC - 1}", 0))
        for cc in range(CC):
            em.W(gp, "vp", env[f"p7_dnev_{e}_3_{cc}"])
            gp.dma_scatter_add(
                env["RACC"],
                sb["stg"][:, cc, :].rearrange("p (g d) -> p g d", g=1),
                gidx[:, cc * 8:(cc + 1) * 8],
                num_idxs=128,
                num_idxs_reg=env["regs"][128],
                elem_size=D,
                elem_step=D).then_inc(em.h["sca"], 16)
            env[f"p7_scat_{e}_{cc}"] = em.inc("sca", 16)
    env["p7_allscat"] = [("sca", em.v["sca"])]


def phase_shared_epilogue(env):
    for sg in range(env["cfg"].NSHG):
        _p8_group(env, sg)


def _p8_group(env, sg):
    em, sb, ps, cfg = env["em"], env["sb"], env["ps"], env["cfg"]
    SHG = cfg.SHG
    g0 = sg * SHG
    NS = SHG // 512

    @em.on("gpsimd")
    def _(gp):
        em.W(gp, "scr", env[f"p5_xpst_{cfg.OCH - 1}"])
        if sg == 0:
            for sn, sv in env["p7_allscat"]:
                em.W(gp, sn, sv)
            em.W(gp, "tp", env[f"p7_dndone_{E - 1}"])
        else:
            em.W(gp, "tp", env[f"p8_updone_{sg - 1}"])
        gp.dma_gather(sb["shxT"][...], env["XPH"],
                      sb["c_gidxsh"][:, g0 // 16:(g0 + SHG) // 16],
                      num_idxs=SHG, num_idxs_reg=env["regs"][SHG],
                      elem_size=D,
                      elem_step=D, transpose=True).then_inc(em.h["gq"], 16)
        env[f"p8_gath_{sg}"] = em.inc("gq", 16)

    for pi in range(44):
        wub = sb["wupg"] if pi % 2 == 0 else sb["wupg2"]

        wsem = "wsa" if pi % 2 == 0 else "wsb"

        @em.on("sync")
        def _(sy, pi=pi, wub=wub, wsem=wsem):
            if pi < 2:
                em.W(sy, "gs", env["wcast_done"])
                em.W(sy, "tp", env[f"p7_updone_{E - 1}"] if sg == 0
                     else env[f"p8_up_{sg - 1}_{42 + pi}"])
            else:
                em.W(sy, "tp", env[f"p8_up_{sg}_{pi - 2}"])
            sy.dma_start(out=wub[:, :, 0:128],
                         in_=env["WSUB"][:, pi * 128:(pi + 1) * 128]
                         .rearrange("(k p) m -> p k m", p=128)).then_inc(
                em.h[wsem], 16)
            em.inc(wsem, 16)
            sy.dma_start(out=wub[:, :, 128:256],
                         in_=env["WSUB"][:, (44 + pi) * 128:(45 + pi) * 128]
                         .rearrange("(k p) m -> p k m", p=128)).then_inc(
                em.h[wsem], 16)
            em.inc(wsem, 16)
        wld = (wsem, em.v[wsem])

        bset = (["q0", "q1", "q2", "q3"] if pi % 2 == 0
                else ["d0", "d1", "s", "e"])

        @em.on("tensor")
        def _(t, pi=pi, wub=wub, bset=bset):
            em.W(t, wld[0], wld[1])
            if pi == 0:
                em.W(t, "gq", env[f"p8_gath_{sg}"])
            if pi < 2:
                em.W(t, "vp", env.get("p8_evlast",
                                      env[f"p7_dnev_{E - 1}_3_{env["cfg"].CC - 1}"]))
            if pi >= 2:
                em.W(t, "vp", env[f"p8_upev_{sg}_{pi - 2}"])
            for ns in range(NS):
                for k in range(16):
                    t.matmul(ps[bset[0] if ns == 0 else bset[1]][:, 0:512],
                             wub[:, k, 0:128],
                             sb["shxT"][:, k, ns * 512:(ns + 1) * 512],
                             start=(k == 0), stop=(k == 15))
            for ns in range(NS):
                for k in range(16):
                    mmi = t.matmul(ps[bset[2] if ns == 0 else bset[3]][:, 0:512],
                                   wub[:, k, 128:256],
                                   sb["shxT"][:, k, ns * 512:(ns + 1) * 512],
                                   start=(k == 0), stop=(k == 15))
            mmi.then_inc(em.h["tp"], 1)
            env[f"p8_up_{sg}_{pi}"] = em.inc("tp", 1)

        @em.on("scalar")
        def _(s, pi=pi, bset=bset):
            em.W(s, "tp", env[f"p8_up_{sg}_{pi}"])
            fofs = 0 if pi % 2 == 0 else 1024
            si_i = s.activation(sb["fS2"][:, fofs:fofs + 512],
                                ps[bset[0]][:, 0:512], AF.Silu)
            if NS > 1:
                si_i = s.activation(sb["fS2"][:, fofs + 512:fofs + 1024],
                                    ps[bset[1]][:, 0:512], AF.Silu)
            si_i.then_inc(em.h["sp"], 1)
            env[f"p8_silu_{sg}_{pi}"] = em.inc("sp", 1)

        @em.on("vector")
        def _(v, pi=pi, bset=bset):
            em.W(v, "sp", env[f"p8_silu_{sg}_{pi}"])
            fofs = 0 if pi % 2 == 0 else 1024
            vi = v.tensor_tensor(sb["shT"][:, pi, 0:512],
                                 sb["fS2"][:, fofs:fofs + 512],
                                 ps[bset[2]][:, 0:512], ALU.mult)
            if NS > 1:
                vi = v.tensor_tensor(sb["shT"][:, pi, 512:1024],
                                     sb["fS2"][:, fofs + 512:fofs + 1024],
                                     ps[bset[3]][:, 0:512], ALU.mult)
            vi.then_inc(em.h["vp"], 1)
            env[f"p8_upev_{sg}_{pi}"] = em.inc("vp", 1)
    env[f"p8_updone_{sg}"] = em.v["tp"]

    NCB = (SHG // 128) // 4
    for ccb in range(NCB):
        for dq in range(4):
            di = ccb * 4 + dq
            banks = (["q0", "q1", "q2", "q3"] if di % 2 == 0
                     else ["d0", "d1", "s", "e"])
            for j in range(44):
                gi = (ccb * 4 + dq) * 44 + j
                wdb = [sb["wdg"], sb["wdg2"], sb["wdg3"],
                       sb["wdg4"]][gi % 4]
                wsem = ["mk", "mk2", "mk3", "mk4"][gi % 4]

                @em.on("scalar")
                def _(s, gi=gi, dq=dq, j=j, wdb=wdb, wsem=wsem):
                    if gi < 4:
                        if sg == 0:
                            em.W(s, "gs", env["wcast_done"])
                        else:
                            em.W(s, "tp", env[f"p8_dn_{sg - 1}_{gi + 172}"])
                    else:
                        em.W(s, "tp", env[f"p8_dn_{sg}_{gi - 4}"])
                    s.dma_start(out=wdb[...],
                                in_=env["WSDB"][j * 128:(j + 1) * 128,
                                                dq * 512:(dq + 1) * 512]
                                ).then_inc(em.h[wsem], 16)
                    em.inc(wsem, 16)
                wld = (wsem, em.v[wsem])

                @em.on("tensor")
                def _(t, gi=gi, ccb=ccb, dq=dq, j=j, wdb=wdb, banks=banks,
                      di=di):
                    em.W(t, wld[0], wld[1])
                    if j == 0:
                        if di < 2:
                            em.W(t, "vp", env[f"p8_upev_{sg}_{43}"])
                        g = env.get(f"p8_evlast_{di % 2}")
                        if g is not None:
                            em.W(t, "vp", g)
                    for cc4 in range(4):
                        cc = ccb * 4 + cc4
                        mmi = t.matmul(ps[banks[cc4]][:, 0:512],
                                       sb["shT"][:, j, cc * 128:(cc + 1) * 128],
                                       wdb[...], start=(j == 0),
                                       stop=(j == 43))
                    mmi.then_inc(em.h["tp"], 1)
                    env[f"p8_dn_{sg}_{gi}"] = em.inc("tp", 1)

            for cc4 in range(4):
                cc = ccb * 4 + cc4
                rows = g0 + cc * 128

                @em.on("scalar")
                def _(s, cc4=cc4, rows=rows, dq=dq):
                    if "p8_rowused" not in env:
                        for sn, sv in env["p7_allscat"]:
                            em.W(s, sn, sv)
                        em.W(s, "scr", env[f"p5_xpst_{cfg.OCH - 1}"])
                    em.W(s, "vp", env.get("p8_rowused", 0))
                    s.dma_start(out=sb["xprow"][...],
                                in_=env["XP"][rows:rows + 128,
                                              dq * 512:(dq + 1) * 512]
                                ).then_inc(em.h["ld"], 16)
                    em.inc("ld", 16)
                    s.dma_start(out=sb["raccrow"][...],
                                in_=env["RACC"][rows:rows + 128,
                                                dq * 512:(dq + 1) * 512]
                                ).then_inc(em.h["ld"], 16)
                    em.inc("ld", 16)
                    em.qw(s, "ld")
                rowld = em.v["ld"]

                @em.on("vector")
                def _(v, cc4=cc4, dq=dq, ccb=ccb, rowld=rowld, banks=banks,
                      di=di):
                    em.W(v, "ld", rowld)
                    em.W(v, "tp", env[f"p8_dn_{sg}_{(ccb * 4 + dq) * 44 + 43}"])
                    em.W(v, "sp", env.get("p8_lnsq", 0))
                    v.tensor_tensor(sb["xprow"][...], sb["xprow"][...],
                                    sb["raccrow"][...],
                                    ALU.add).then_inc(em.h["vp"], 1)
                    v.wait_ge(em.h["vp"], em.inc("vp", 1))
                    vi = v.tensor_tensor(
                        sb["sacc"][:, cc4, dq * 512:(dq + 1) * 512],
                        ps[banks[cc4]][:, 0:512], sb["xprow"][...], ALU.add)
                    vi.then_inc(em.h["vp"], 1)
                    env["p8_rowused"] = em.inc("vp", 1)
                    if cc4 == 3:
                        env["p8_evlast"] = env["p8_rowused"]
                        env[f"p8_evlast_{di % 2}"] = env["p8_rowused"]

        for cc4 in range(4):
            cc = ccb * 4 + cc4
            rows = g0 + cc * 128
            ln_stats_apply(env, sb["sacc"][:, cc4, :], sb["orow"][...], 16,
                           scr_name="scrD",
                           pre=("ost", env.get("p8_ost", 0)),
                           src_ready=("vp", env["p8_rowused"]))
            env["p8_lnsq"] = env["ln_s0"]

            @em.on("scalar")
            def _(s, rows=rows, od=env["ln_done"]):
                em.W(s, "vp", od)
                s.dma_start(out=env["out"][rows:rows + 128, :],
                            in_=sb["orow"][...]).then_inc(em.h["ost"], 16)
                env["p8_ost"] = em.inc("ost", 16)
                em.qw(s, "ost")
    env[f"p8_dndone_{sg}"] = em.v["tp"]


def phase_finish(env, upto):
    em, cfg, sb = env["em"], env["cfg"], env["sb"]
    dbg = env["dbg"]

    @em.on("sync")
    def _(sy):
        if "x1" in dbg:
            em.W(sy, "scr", env[f"p5_xpst_{cfg.OCH - 1}"])
            for i in range(cfg.T // 128):
                sy.dma_start(out=dbg["x1"][i * 128:(i + 1) * 128, :],
                             in_=env["XP"][i * 128:(i + 1) * 128, :]
                             ).then_inc(em.h["scr"], 16)
                em.inc("scr", 16)
        if "part" in dbg:
            em.W(sy, "scr", env[f"p1_scr_{cfg.NSC - 1}"])
            for i in range(cfg.S // 128):
                sy.dma_start(out=dbg["part"][i * 128:(i + 1) * 128, :],
                             in_=env["PART"][i * 128:(i + 1) * 128, :]
                             ).then_inc(em.h["scr"], 16)
                em.inc("scr", 16)
        if "sqt" in dbg:
            em.W(sy, "scr", env[f"p1_scr_{cfg.NSC - 1}"])
            for h in range(H_PER):
                sy.dma_start(out=dbg["sqt"][h * 128:(h + 1) * 128, :],
                             in_=env["SQT"][h, :, :]).then_inc(
                    em.h["ost"], 16)
                em.inc("scr", 16)
        if "agout" in dbg:
            em.W(sy, "cc", env["a2a_done"])
            for i in range(2 * cfg.S // 128):
                sy.dma_start(out=dbg["agout"][i * 128:(i + 1) * 128, :],
                             in_=env["agout"].ap()[i * 128:(i + 1) * 128, :]
                             ).then_inc(em.h["scr"], 16)
                em.inc("scr", 16)
        if "rstash" in dbg:
            em.W(sy, "vp", env.get("p5_rtdone", 0))
            sy.dma_start(out=dbg["rstash"][...],
                         in_=sb["rstash"][...].rearrange("p o c -> p (o c)")
                         ).then_inc(em.h["scr"], 16)
            em.inc("scr", 16)
        if "racc" in dbg:
            for sn, sv in env.get("p7_allscat", []):
                em.W(sy, sn, sv)
            for i in range(cfg.T // 128):
                sy.dma_start(out=dbg["racc"][i * 128:(i + 1) * 128, :],
                             in_=env["RACC"][i * 128:(i + 1) * 128, :]
                             ).then_inc(em.h["scr"], 16)
                em.inc("scr", 16)
        if "tblv" in dbg:
            em.W(sy, "gq", env.get("p6_scat", 0))
            n = dbg["tblv"].shape[0]
            sy.dma_start(out=dbg["tblv"],
                         in_=env["TBL"][0:n, :]).then_inc(em.h["scr"], 16)
            em.inc("scr", 16)
        em.W(sy, "ost", em.v["ost"])
        em.W(sy, "scr", em.v["scr"])

    @em.on("gpsimd")
    def _(gp):
        em.W(gp, "gq", em.v["gq"])
        em.W(gp, "cc", em.v["cc"])


def _gidxwo(cfg, hh):
    cols = np.zeros((128, cfg.OCH * 16), np.int16)
    for oc in range(cfg.OCH):
        ids = []
        for p in range(128):
            ids.append(hh * cfg.T + oc * 128 + p)
        for p in range(128):
            ids.append(cfg.S + hh * cfg.T + oc * 128 + p)
        cols[:, oc * 16:(oc + 1) * 16] = _wrap16(ids)[0:128]
    return cols


def _prep_inputs(cfg, inputs, core, consts):
    b, hh = core // 2, core % 2
    h0 = hh * H_PER
    x = np.asarray(inputs["L0_attn_hidden_states"])
    wq = np.asarray(inputs["L0_attn_w_q"])
    wk = np.asarray(inputs["L0_attn_w_k"])
    wv = np.asarray(inputs["L0_attn_w_v"])
    wo = np.asarray(inputs["L0_attn_w_o"])
    mems = np.asarray(inputs["L0_attn_memories"])
    nrms = np.asarray(inputs["L0_attn_memory_norms"])
    im = {
        "xb": np.ascontiguousarray(x[b]),
        "xown": np.ascontiguousarray(x[b, hh * cfg.T:(hh + 1) * cfg.T]),
        "wqT": np.ascontiguousarray(wq[h0 * HD:(h0 + H_PER) * HD, :].T),
        "wkT": np.ascontiguousarray(wk[h0 * HD:(h0 + H_PER) * HD, :].T),
        "wvT": np.ascontiguousarray(wv[h0 * HD:(h0 + H_PER) * HD, :].T),
        "woT": np.ascontiguousarray(wo.T),
        "memi": np.ascontiguousarray(
            mems[:, h0:h0 + H_PER].reshape(M * H_PER, HD * HD)),
        "nrmT": np.ascontiguousarray(
            nrms[:, h0:h0 + H_PER].transpose(1, 2, 0)),
        "nrmF": np.ascontiguousarray(nrms.reshape(M * 16, HD)),
        "gatei": np.ascontiguousarray(
            np.asarray(inputs["L0_attn_gate"])[h0:h0 + H_PER][None, :]),
        "routeri": np.ascontiguousarray(inputs["L0_ffn_router"]),
        "wug": np.asarray(inputs["L0_ffn_w_up_gate"]),
        "wdn": np.asarray(inputs["L0_ffn_w_down"]),
        "wsu": np.asarray(inputs["L0_ffn_shared_w_up_gate"]),
        "wsd": np.asarray(inputs["L0_ffn_shared_w_down"]),
        "prior_sel": (np.arange(8) < core).astype(np.float32)[:, None],
        "gidxwo": _gidxwo(cfg, hh),
    }
    for n, v in consts.items():
        im["cin_" + n] = v
    return im


def kernel(**inputs):
    cfg = Cfg()
    nc = build_program(cfg)
    consts = make_consts(cfg)
    in_maps = [_prep_inputs(cfg, inputs, c, consts) for c in range(8)]
    res = run_bass_kernel_spmd(nc, in_maps, core_ids=list(range(8)))
    full = np.empty((4, 4096, 2048), np.float32)
    for c in range(8):
        b, hh = c // 2, c % 2
        full[b, hh * cfg.T:(hh + 1) * cfg.T, :] = res.results[c]["out"]
    return full
